# version 03
# baseline (speedup 1.0000x reference)
import numpy as np
import ml_dtypes
from contextlib import ExitStack, contextmanager

import concourse.bass as bass
import concourse.mybir as mybir
from concourse.bass_utils import run_bass_kernel_spmd

F32 = mybir.dt.float32
BF16 = mybir.dt.bfloat16
I32 = mybir.dt.int32
AF = mybir.ActivationFunctionType
ALU = mybir.AluOpType
AX = mybir.AxisListType
BFNP = ml_dtypes.bfloat16

D = 2048
B = 4
PT = 128
F32EPS = float(np.finfo(np.float32).eps)
NEG = -1e30


class K:

    def __init__(self, nc, ctx):
        self.nc = nc
        self.ctx = ctx
        self._ctxs = [ctx]
        self.eng = dict(v=nc.vector, s=nc.scalar, t=nc.tensor, g=nc.gpsimd,
                        y=nc.sync)
        self.sem = {}
        self.cnt = {}
        for nm in ("v", "s", "t", "g", "y", "col"):
            self.sem[nm] = ctx.enter_context(nc.semaphore("sem_" + nm))
            self.cnt[nm] = 0
        self.waited = {e: {} for e in self.eng}

    def barrier(self):
        toks = [(ch, v) for ch, v in self.cnt.items() if v > 0]
        for e in self.eng:
            self.wait(e, *toks)

    @contextmanager
    def scope(self):
        with ExitStack() as c:
            self._ctxs.append(c)
            try:
                yield c
            finally:
                self.barrier()
                self._ctxs.pop()

    def inc(self, ch, inst, n=1):
        self.cnt[ch] += n
        inst.then_inc(self.sem[ch], n)
        return (ch, self.cnt[ch])

    def wait(self, e, *toks):
        for tk in toks:
            if tk is None:
                continue
            ch, v = tk
            w = self.waited[e]
            if w.get(ch, 0) < v:
                self.eng[e].wait_ge(self.sem[ch], v)
                w[ch] = v

    def chan(self, name):
        if name not in self.sem:
            self.sem[name] = self._ctxs[0].enter_context(
                self.nc.semaphore("sem_" + name))
            self.cnt[name] = 0
        return name

    def dma(self, q, ch, out, in_, wait=(), **kw):
        ch = self.chan(ch)
        self.wait(q, *[t for t in wait if t is not None])
        inst = self.eng[q].dma_start(out=out, in_=in_, **kw)
        self.cnt[ch] += 16
        inst.then_inc(self.sem[ch], 16)
        return (ch, self.cnt[ch])

    def dma_fence(self, e='y'):
        for ch in list(self.sem):
            if ch.startswith("c_") and self.cnt[ch] > 0:
                self.wait(e, (ch, self.cnt[ch]))

    def sb(self, name, shape, dt, root=False):
        self._uid = getattr(self, "_uid", 0) + 1
        c = self._ctxs[0] if root else self._ctxs[-1]
        return c.enter_context(
            self.nc.sbuf_tensor(f"{name}_{self._uid}", list(shape), dt))

    def constcol(self, val):
        if not hasattr(self, "_consts"):
            self._consts = {}
        if val not in self._consts:
            t = self.sb(f"const", (128, 1), F32, root=True)
            tok = self.inc('v', self.nc.vector.memset(t[:, :], float(val)))
            self._consts[val] = (t, tok)
        return self._consts[val]

    def ps(self, name, shape, dt=F32):
        self._uid = getattr(self, "_uid", 0) + 1
        return self._ctxs[-1].enter_context(
            self.nc.psum_tensor(f"{name}_{self._uid}", list(shape), dt))


def tmax(*toks):
    best = {}
    for t in toks:
        if t is None:
            continue
        ch, v = t
        if best.get(ch, -1) < v:
            best[ch] = v
    return [(ch, v) for ch, v in best.items()]


def emit_ln(k, xt, xn_out, sc, x_tok, war_tok=None, eps=1e-5):
    nc = k.nc
    F = xt.shape[1]
    nch = max(1, F // 512)
    k.wait('v', x_tok)
    fsz = F // nch
    for c in range(nch):
        t1 = k.inc('v', nc.vector.bn_stats(
            sc['st6'][:, c * 6:(c + 1) * 6],
            xt[:, c * fsz:(c + 1) * fsz]))
    k.wait('v', t1)
    t2 = k.inc('v', nc.vector.bn_aggr(sc['st2'][:, :], sc['st6'][:, :3 * 2 * nch]))
    eps_ap, eps_tok = k.constcol(eps)
    k.wait('s', t2, eps_tok)
    tl = k.inc('s', nc.scalar.activation(sc['rc'][:, 0:1], sc['st2'][:, 1:2],
                                         AF.Ln, bias=eps_ap[:, 0:1]))
    k.wait('s', tl)
    tr = k.inc('s', nc.scalar.activation(sc['rc'][:, 0:1], sc['rc'][:, 0:1],
                                         AF.Exp, scale=-0.5))
    k.wait('v', tr, war_tok)
    tx = k.inc('v', nc.vector.tensor_scalar(
        xn_out[:, :], xt[:, :], sc['st2'][:, 0:1], sc['rc'][:, 0:1],
        ALU.subtract, ALU.mult))
    return tx


class TrPool:

    def __init__(self, k, ident, pb0, pb1):
        self.k = k
        self.ident = ident
        self.banks = [pb0, pb1]
        self.busy = [None, None]
        self.i = 0

    def start(self, src_ap, *toks):
        k, nc = self.k, self.k.nc
        b = self.i % 2
        self.i += 1
        k.wait('t', self.busy[b], *toks)
        tm = k.inc('t', nc.tensor.transpose(self.banks[b][:, :], src_ap,
                                            self.ident[:, :]))
        return self.banks[b], tm, b

    def done(self, b, tok):
        self.busy[b] = tok

    def transpose(self, src_ap, dst_ap, src_tok, evac='v', war_tok=None):
        k, nc = self.k, self.k.nc
        pbk, tm, b = self.start(src_ap, src_tok)
        k.wait(evac, tm, war_tok)
        if evac == 'v':
            te = k.inc('v', nc.vector.tensor_copy(dst_ap, pbk[:, :]))
        else:
            te = k.inc('s', nc.scalar.copy(dst_ap, pbk[:, :]))
        self.busy[b] = te
        return te


def emit_f0_prep(k, trp, wq_ap, s_ap, wT_dram, key_by_row=False):
    nc = k.nc
    n_rows, n_cols = wq_ap.shape
    CC = 512
    s2d = s_ap.rearrange("(t p) -> t p", p=PT)
    with k.scope():
        it = [k.sb(f"dq_i{i}", (PT, CC), I32) for i in range(2)]
        ft = [k.sb(f"dq_f{i}", (PT, CC), F32) for i in range(2)]
        bt = [k.sb(f"dq_b{i}", (PT, CC), BF16) for i in range(2)]
        scol = k.sb("dq_s", (PT, 1), F32)
        stg = [k.sb(f"dq_g{i}", (PT, PT), BF16) for i in range(2)]
        tok_f = [None, None]
        tok_b = [None, None]
        tok_s = None
        stok = [None, None]
        last = None
        for rt in range(n_rows // PT):
            rsl = slice(rt * PT, (rt + 1) * PT)
            ts = k.dma('y', 'c_ps', scol[:, :], s2d[rt].unsqueeze(1),
                       wait=(tok_s,))
            for cc in range(n_cols // CC):
                pi = (rt * (n_cols // CC) + cc) % 2
                csl = slice(cc * CC, (cc + 1) * CC)
                tw = k.dma('y', f'c_pi{pi}', it[pi][:, :], wq_ap[rsl, csl], wait=(tok_f[pi],))
                k.wait('v', tw, tok_b[pi])
                tf = k.inc('v', nc.vector.tensor_copy(ft[pi][:, :], it[pi][:, :]))
                tok_f[pi] = tf
                k.wait('v', tf, ts)
                tok_s = tb = k.inc('v', nc.vector.tensor_scalar(
                    bt[pi][:, :], ft[pi][:, :], -128.0, scol[:, :],
                    ALU.add, ALU.mult))
                for ct in range(CC // PT):
                    si = ct % 2
                    pbk, tm, b = trp.start(bt[pi][:, ct * PT:(ct + 1) * PT], tb)
                    k.wait('v', tm, stok[si])
                    te = k.inc('v', nc.vector.tensor_copy(stg[si][:, :], pbk[:, :]))
                    trp.done(b, te)
                    gct = cc * (CC // PT) + ct
                    if key_by_row:
                        dst = wT_dram[rt, :, gct * PT:(gct + 1) * PT]
                    else:
                        dst = wT_dram[gct, :, rsl]
                    last = k.dma('s', f'c_pg{si}', dst, stg[si][:, :], wait=(te,))
                    stok[si] = last
                tok_b[pi] = last
        return tmax(stok[0], stok[1])


def emit_f0(k, cfg, io):
    nc = k.nc
    n_own = cfg['n_own']
    ngrp = n_own // 4
    HID = 5632
    NH = HID // PT
    NHC = NH * PT // 512
    ident = cfg['ident']
    pb = cfg['pbanks']
    trp = cfg['trp']
    wgT, wuT, wdT = io['wgT'], io['wuT'], io['wdT']
    x_in, x_out = io['x_in'], io['x_out']
    w_tok = io.get('w_tok')

    with k.scope():
        xt = k.sb("f0_xt", (PT, D), F32)
        xn = k.sb("f0_xn", (PT, D), BF16)
        sc = dict(st6=k.sb("f0_st6", (PT, 24), F32),
                  st2=k.sb("f0_st2", (PT, 2), F32),
                  rc=k.sb("f0_rc", (PT, 2), F32))
        xnT = k.sb("f0_xnT", (PT, 16 * 512), BF16)
        wg_sb = k.sb("f0_wg", (PT, 16 * 512), BF16)
        wu_sb = k.sb("f0_wu", (PT, 16 * 512), BF16)
        wd_sb = k.sb("f0_wd", (PT, NH * PT), BF16)
        sg_sb = k.sb("f0_sg", (PT, 512), BF16)
        su_sb = k.sb("f0_su", (PT, 512), BF16)
        actT = k.sb("f0_act", (PT, NH * 512), BF16)
        outT = k.sb("f0_outT", (PT, 16 * 512), BF16)
        xo = k.sb("f0_xo", (PT, D), F32)

        tok_bank = {}
        tok_xt_r = None
        tok_xn_r = None
        tok_xnT_r = None
        tok_act_w = None
        tok_act_r = None
        tok_sg_r = None
        tok_su_r = None
        tok_outT_w = {}
        tok_outT_r = None
        tok_xo_r = None
        tok_wg_r = None
        tok_wu_r = None
        tok_wd_r = None
        tst = None

        for g in range(ngrp):
            tiles = list(range(4 * g, 4 * g + 4))
            t_tr_last = None
            for jj, tile in enumerate(tiles):
                tx = k.dma('y', 'c_xt', xt[:, :], x_in[tile * PT:(tile + 1) * PT, :],
                           wait=(tok_xt_r,))
                txn = emit_ln(k, xt, xn, sc, tx, war_tok=tok_xn_r)
                tok_xt_r = txn
                for dt in range(16):
                    t_tr_last = trp.transpose(
                        xn[:, dt * PT:(dt + 1) * PT],
                        xnT[:, dt * 512 + jj * PT: dt * 512 + (jj + 1) * PT],
                        txn if dt == 0 else None,
                        war_tok=tok_xnT_r if (jj == 0 and dt == 0) else None)
                tok_xn_r = ('t', k.cnt['t'])
            tok_xnT_w = t_tr_last

            for hc in range(NHC):
                for dt in range(16):
                    wtoks = (w_tok if (g == 0 and hc == 0 and dt == 0)
                             else [None])
                    twg = k.dma('y', 'c_wg', wg_sb[:, dt * 512:(dt + 1) * 512],
                                wgT[dt, :, hc * 512:(hc + 1) * 512],
                                wait=(tok_wg_r, *wtoks))
                    twu = k.dma('y', 'c_wu', wu_sb[:, dt * 512:(dt + 1) * 512],
                                wuT[dt, :, hc * 512:(hc + 1) * 512],
                                wait=(tok_wu_r,))
                for hs4 in range(4):
                    hsg = hc * 4 + hs4
                    bg, bu = hs4 % 2, 2 + hs4 % 2
                    k.wait('t', twg, twu, tok_xnT_w,
                           tok_bank.get(bg), tok_bank.get(bu))
                    for dt in range(16):
                        mm = nc.tensor.matmul(
                            pb[bg][:, :],
                            wg_sb[:, dt * 512 + hs4 * PT: dt * 512 + (hs4 + 1) * PT],
                            xnT[:, dt * 512:(dt + 1) * 512],
                            start=(dt == 0), stop=(dt == 15))
                    tmg = k.inc('t', mm)
                    for dt in range(16):
                        mm = nc.tensor.matmul(
                            pb[bu][:, :],
                            wu_sb[:, dt * 512 + hs4 * PT: dt * 512 + (hs4 + 1) * PT],
                            xnT[:, dt * 512:(dt + 1) * 512],
                            start=(dt == 0), stop=(dt == 15))
                    tmu = k.inc('t', mm)
                    tok_xnT_r = tmu
                    k.wait('s', tmg, tok_sg_r)
                    tsg = k.inc('s', nc.scalar.activation(sg_sb[:, :],
                                                          pb[bg][:, :], AF.Silu))
                    tok_bank[bg] = tsg
                    k.wait('v', tmu, tok_su_r)
                    tsu = k.inc('v', nc.vector.tensor_copy(su_sb[:, :],
                                                           pb[bu][:, :]))
                    tok_bank[bu] = tsu
                    k.wait('v', tsg, tsu, tok_act_r if hsg == 0 and g > 0 else None)
                    ta = k.inc('v', nc.vector.tensor_mul(
                        actT[:, hsg * 512:(hsg + 1) * 512], sg_sb[:, :],
                        su_sb[:, :]))
                    tok_sg_r = tok_su_r = ta
                tok_wg_r = tmg
                tok_wu_r = tmu
            tok_act_w = ta

            for dmt in range(16):
                twd = k.dma('y', 'c_wd', wd_sb[:, :], wdT[dmt], wait=(tok_wd_r,))
                bd = 4 + dmt % 2
                k.wait('t', twd, tok_act_w, tok_bank.get(bd))
                for ht in range(NH):
                    mm = nc.tensor.matmul(pb[bd][:, :],
                                          wd_sb[:, ht * PT:(ht + 1) * PT],
                                          actT[:, ht * 512:(ht + 1) * 512],
                                          start=(ht == 0), stop=(ht == NH - 1))
                tmd = k.inc('t', mm)
                tok_wd_r = tmd
                tok_act_r = tmd
                k.wait('v', tmd, tok_outT_r)
                tev = k.inc('v', nc.vector.tensor_copy(
                    outT[:, dmt * 512:(dmt + 1) * 512], pb[bd][:, :]))
                tok_bank[bd] = tev
                tok_outT_w[dmt] = tev

            for jj, tile in enumerate(tiles):
                tx = k.dma('y', 'c_xt', xt[:, :], x_in[tile * PT:(tile + 1) * PT, :],
                           wait=(tok_xt_r,))
                adds = []
                for dmt in range(16):
                    pbk, tm, b = trp.start(
                        outT[:, dmt * 512 + jj * PT: dmt * 512 + (jj + 1) * PT],
                        tok_outT_w[dmt])
                    k.wait('v', tm, tx, tok_xo_r)
                    te = k.inc('v', nc.vector.tensor_tensor(
                        xo[:, dmt * PT:(dmt + 1) * PT],
                        xt[:, dmt * PT:(dmt + 1) * PT],
                        pbk[:, :], ALU.add))
                    trp.done(b, te)
                    adds.append(te)
                tok_xt_r = adds[-1]
                tok_outT_r = ('t', k.cnt['t'])
                tst = k.dma('s', 'c_xo', x_out[tile * PT:(tile + 1) * PT, :],
                            xo[:, :], wait=tuple(tmax(*adds)))
                tok_xo_r = tst
        return tst



def emit_attn(k, cfg, io):
    nc = k.nc
    n_tiles = cfg['n_tiles']
    own = cfg['own_tiles']
    n_own = len(own)
    nH, nKV, DH = cfg['nH'], cfg['nKV'], cfg['DH']
    QW, KW = nH * DH, nKV * DH
    half = DH // 2
    scale = cfg['scale']
    ident = cfg['ident']
    trp = cfg['trp']
    pb = cfg['pbanks']
    tabc, tabs = cfg['tabc'], cfg['tabs']
    x_in, x_out = io['x_in'], io['x_out']
    g_per_h = nH // nKV
    NHD = QW // PT

    with k.scope():
        xt = k.sb("at_xt", (PT, D), F32)
        xn = k.sb("at_xn", (PT, D), BF16)
        sc = dict(st6=k.sb("at_st6", (PT, 24), F32),
                  st2=k.sb("at_st2", (PT, 2), F32),
                  rc=k.sb("at_rc", (PT, 2), F32))
        xnT = k.sb("at_xnT", (PT, 16 * 512), BF16)
        wch = [k.sb(f"at_w{i}", (PT, 512), BF16) for i in range(2)]
        qg = k.sb("at_qg", (PT, 4 * QW), BF16)
        kvg = k.sb("at_kvg", (PT, 4 * 2 * KW), BF16)
        rop = k.sb("at_rop", (PT, QW), BF16)
        rop2 = k.sb("at_rop2", (PT, QW), BF16)
        qTt = k.sb("at_qTt", (PT, QW), BF16)

        tok_xt_r = None
        tok_xn_r = None
        tok_xnT_r = None
        tok_w_r = [None, None]
        tok_qg_r = None
        tok_kvg_r = None
        tok_rop_r = None
        tok_rop2_r = None
        tok_qT_r = None
        tok_bank = {}
        wci = [0]

        def proj_group(Wsrc, width, dst, dst_off, dst_stride, war_tok, xnT_tok,
                       ntl):
            tevs = []
            for blk in range(width // 512):
                for dt in range(16):
                    wi = wci[0] % 2
                    wci[0] += 1
                    tw = k.dma('g', f'c_w{wi}', wch[wi][:, :],
                               Wsrc[dt * PT:(dt + 1) * PT,
                                    blk * 512:(blk + 1) * 512],
                               wait=(tok_w_r[wi],))
                    k.wait('t', tw, xnT_tok,
                           *[tok_bank.get(jj) for jj in range(ntl)])
                    for jj in range(ntl):
                        mm = nc.tensor.matmul(
                            pb[jj][:, :],
                            xnT[:, dt * 512 + jj * PT: dt * 512 + (jj + 1) * PT],
                            wch[wi][:, :], start=(dt == 0), stop=(dt == 15))
                    tok_w_r[wi] = k.inc('t', mm)
                tmm = tok_w_r[(wci[0] - 1) % 2]
                for jj in range(ntl):
                    k.wait('v', tmm, war_tok if (blk == 0 and jj == 0) else None)
                    tev = k.inc('v', nc.vector.tensor_copy(
                        dst[:, dst_off + jj * dst_stride + blk * 512:
                            dst_off + jj * dst_stride + (blk + 1) * 512],
                        pb[jj][:, :]))
                    tok_bank[jj] = tev
                    tevs.append(tev)
            return tevs

        def rope(src_ap, dst_ap, gj, width, nheads, src_tok, war_tok, war2):
            base = src_ap
            sw4 = bass.AP(base.tensor, base.offset + half,
                          [list(base.ap[0]), [DH, nheads], [-half, 2],
                           [1, half]])
            tc = tabc[:, gj * DH:(gj + 1) * DH]
            ts_ = tabs[:, gj * DH:(gj + 1) * DH]
            cosb = bass.AP(tc.tensor, tc.offset,
                           [list(tc.ap[0]), [0, nheads], [1, DH]])
            sinb = bass.AP(ts_.tensor, ts_.offset,
                           [list(ts_.ap[0]), [0, nheads], [half, 2], [1, half]])
            d4 = dst_ap.rearrange("p (h s x) -> p h s x", h=nheads, s=2)
            d3 = dst_ap.rearrange("p (h x) -> p h x", h=nheads)
            s3 = src_ap.rearrange("p (h x) -> p h x", h=nheads)
            k.wait('v', src_tok, war_tok, cfg['tab_tok'])
            t1 = k.inc('v', nc.vector.tensor_tensor(d4, sw4, sinb, ALU.mult))
            k.wait('v', t1, war2)
            t2 = k.inc('v', nc.vector.tensor_tensor(
                rop2[:, :width].rearrange("p (h x) -> p h x", h=nheads),
                s3, cosb, ALU.mult))
            k.wait('v', t2)
            t3 = k.inc('v', nc.vector.tensor_tensor(
                dst_ap, dst_ap, rop2[:, :width], ALU.add))
            return t3

        for g in range((n_own + 3) // 4):
            tiles = list(range(4 * g, min(4 * g + 4, n_own)))
            ntl = len(tiles)
            t_tr = None
            for jj, j in enumerate(tiles):
                rsl = slice(j * PT, (j + 1) * PT)
                tx = k.dma('y', 'c_xt', xt[:, :], x_in[rsl, :], wait=(tok_xt_r,))
                txn = emit_ln(k, xt, xn, sc, tx, war_tok=tok_xn_r)
                tok_xt_r = txn
                for dt in range(16):
                    t_tr = trp.transpose(
                        xn[:, dt * PT:(dt + 1) * PT],
                        xnT[:, dt * 512 + jj * PT: dt * 512 + (jj + 1) * PT],
                        txn if dt == 0 else None,
                        war_tok=tok_xnT_r if (jj == 0 and dt == 0) else None)
                tok_xn_r = ('t', k.cnt['t'])
            tq_evs = proj_group(io['Wq'], QW, qg, 0, QW, tok_qg_r, t_tr, ntl)
            tk_evs = proj_group(io['Wk'], KW, kvg, 0, 2 * KW, tok_kvg_r, t_tr,
                                ntl)
            tv_evs = proj_group(io['Wv'], KW, kvg, KW, 2 * KW, None, t_tr, ntl)
            tok_xnT_r = ('t', k.cnt['t'])
            tq_all = tmax(*tq_evs)
            tk_all = tmax(*tk_evs)
            tv_all = tmax(*tv_evs)
            for jj, j in enumerate(tiles):
                trq = rope(qg[:, jj * QW:(jj + 1) * QW], rop[:, :QW], j, QW,
                           nH, tq_all[0] if tq_all else None, tok_rop_r,
                           tok_rop2_r)
                tt = None
                for dqs in range(QW // PT):
                    tt = trp.transpose(rop[:, dqs * PT:(dqs + 1) * PT],
                                       qTt[:, dqs * PT:(dqs + 1) * PT],
                                       trq if dqs == 0 else None,
                                       war_tok=tok_qT_r if dqs == 0 else None)
                tok_rop_r = ('t', k.cnt['t'])
                tok_rop2_r = trq
                tqs = k.dma('s', 'c_qTs', io['qT'][j], qTt[:, :], wait=(tt,))
                tok_qT_r = tqs
                if cfg.get('rope_k', True):
                    trk = rope(kvg[:, jj * 2 * KW: jj * 2 * KW + KW],
                               rop[:, :KW], j, KW, nKV,
                               tk_all[0] if tk_all else None, tok_rop_r,
                               tok_rop2_r)
                    k.wait('v', trk)
                    tcp = k.inc('v', nc.vector.tensor_copy(
                        kvg[:, jj * 2 * KW: jj * 2 * KW + KW], rop[:, :KW]))
                    tok_rop_r = tcp
                    tok_rop2_r = trk
                    tkv = tmax(tcp, *tv_evs)
                else:
                    tkv = tmax(*(tk_evs + tv_evs))
                tst = k.dma('s', 'c_agi', io['ag_in'][j * PT:(j + 1) * PT, :],
                            kvg[:, jj * 2 * KW:(jj + 1) * 2 * KW],
                            wait=tuple(tkv))
                tok_kvg_r = tst
            tok_qg_r = ('v', k.cnt['v'])

    k.barrier()
    ch = k.chan('c_col')
    inst = nc.gpsimd.collective_compute(
        "AllGather", ALU.bypass, cfg['replica_groups'],
        [io['ag_in'].opt()], [io['ag_out'].opt()])
    k.cnt[ch] += 1
    inst.then_inc(k.sem[ch], 1)
    t_ag = (ch, k.cnt[ch])

    with k.scope():
        kT = k.sb("at_kT", (PT, nKV * n_tiles * PT), BF16)
        vf = k.sb("at_vf", (PT, n_tiles * KW), BF16)
        kvt = k.sb("at_kvt", (PT, 2 * KW), BF16)
        strip = k.sb("at_strip", (PT, n_tiles * PT), F32)
        exps = k.sb("at_exps", (PT, n_tiles * PT), BF16)
        mx = k.sb("at_mx", (PT, 2), F32)
        lsum = k.sb("at_l", (PT, 2), F32)
        drl = k.sb("at_drl", (PT, PT), BF16)
        att = [k.sb(f"at_att{i}", (PT, 512), BF16) for i in range(2)]
        qTt2 = k.sb("at_qT2", (PT, QW), BF16)
        oT = k.sb("at_oT", (PT, NHD * 512), BF16)
        woch = [k.sb(f"at_wo{i}", (PT, 512), BF16) for i in range(2)]
        outT = k.sb("at_outT", (PT, 16 * 512), BF16)
        xt2 = k.sb("at_xt2", (PT, D), F32)
        xo = k.sb("at_xo", (PT, D), F32)

        tok_kvt_r = None
        last_tr = None
        tv = None
        for kb in range(n_tiles):
            par, idx = kb % 2, kb // 2
            row0 = par * (n_tiles // 2) * PT + idx * PT
            tl = k.dma('y', 'c_kvt', kvt[:, :],
                       io['ag_out'][row0:row0 + PT, :],
                       wait=(t_ag, *(tok_kvt_r or [])))
            k.wait('v', tl)
            tv = k.inc('v', nc.vector.tensor_copy(
                vf[:, kb * KW:(kb + 1) * KW], kvt[:, KW:2 * KW]))
            for s in range(KW // PT):
                if DH == PT:
                    last_tr = trp.transpose(
                        kvt[:, s * PT:(s + 1) * PT],
                        kT[:, (s * n_tiles + kb) * PT:(s * n_tiles + kb + 1) * PT],
                        tl if s == 0 else None)
                else:
                    pbk, tm, b = trp.start(kvt[:, s * PT:(s + 1) * PT],
                                           tl if s == 0 else None)
                    k.wait('v', tm)
                    tes = []
                    for hh in range(2):
                        kvh = 2 * s + hh
                        c0 = (kvh * n_tiles + kb) * PT
                        for bb in range(2):
                            tes.append(k.inc('v', nc.vector.tensor_copy(
                                kT[bb * DH:(bb + 1) * DH, c0:c0 + PT],
                                pbk[hh * DH:(hh + 1) * DH, 0:PT])))
                    last_tr = tes[-1]
                    trp.done(b, last_tr)
            tok_kvt_r = tmax(tv, ('t', k.cnt['t']))
        tok_kv_ready = tmax(tv, last_tr, ('t', k.cnt['t']))

        tok_qT2_r = None
        tok_strip_r = None
        tok_exps_r = None
        tok_lsum_r = None
        tok_drl_r = None
        tok_oT_r = None
        tok_wo_r = [None, None]
        tok_outT_r = None
        tok_xt2_r = None
        tok_xo_r = None
        tok_b2 = {}
        woi = [0]
        tst = None

        for g in range((n_own + 3) // 4):
            tiles = list(range(4 * g, min(4 * g + 4, n_own)))
            t_oT_w = []
            for jj, j in enumerate(tiles):
                L = (2 * j + 2) * PT
                nkb = 2 * j + 2
                tql = k.dma('y', 'c_qTl', qTt2[:, :], io['qT'][j],
                            wait=(tok_qT2_r,))
                for h in range(nH):
                    kvh = h // g_per_h
                    if DH == PT:
                        lq = qTt2[:, h * PT:(h + 1) * PT]
                    else:
                        colb, rowb = (h * DH) // PT, (h * DH) % PT
                        lq = qTt2[rowb:rowb + DH, colb * PT:(colb + 1) * PT]
                    nblk = (nkb + 3) // 4
                    cps = []
                    for b4 in range(nblk):
                        w = min(512, L - b4 * 512)
                        bank = pb[b4 % 2]
                        if DH == PT:
                            rk = kT[:, kvh * n_tiles * PT + b4 * 512:
                                    kvh * n_tiles * PT + b4 * 512 + w]
                        else:
                            rb = (h % 2) * DH
                            rk = kT[rb:rb + DH,
                                    kvh * n_tiles * PT + b4 * 512:
                                    kvh * n_tiles * PT + b4 * 512 + w]
                        k.wait('t', tql, *tok_kv_ready)
                        k.wait('t', tok_b2.get(('sc', b4 % 2)))
                        mm = k.inc('t', nc.tensor.matmul(
                            bank[:, :w], lq, rk, start=True, stop=True))
                        k.wait('s', mm, tok_strip_r)
                        tcp = k.inc('s', nc.scalar.copy(
                            strip[:, b4 * 512: b4 * 512 + w], bank[:, :w]))
                        tok_b2[('sc', b4 % 2)] = tcp
                        cps.append(tcp)
                    k.wait('v', cps[-1], cfg['dmask_tok'])
                    tmsk = k.inc('v', nc.vector.tensor_tensor(
                        strip[:, L - 2 * PT:L], strip[:, L - 2 * PT:L],
                        cfg['dmask'][:, :], ALU.add))
                    k.wait('v', *cps, tmsk, tok_strip_r)
                    tmx = k.inc('v', nc.vector.reduce_max(
                        mx[:, 0:1], strip[:, :L], axis=AX.X))
                    k.wait('v', tmx)
                    tb = k.inc('v', nc.vector.tensor_scalar_mul(
                        mx[:, 1:2], mx[:, 0:1], -scale))
                    k.wait('s', tb, tok_exps_r, tok_lsum_r)
                    tex = k.inc('s', nc.scalar.activation(
                        exps[:, :L], strip[:, :L], AF.Exp,
                        bias=mx[:, 1:2], scale=scale, accum_out=lsum[:, 0:1]))
                    tok_strip_r = tex
                    k.wait('v', tex)
                    trl = k.inc('v', nc.vector.reciprocal(lsum[:, 1:2],
                                                          lsum[:, 0:1]))
                    tok_lsum_r = trl
                    k.wait('v', trl, tok_drl_r)
                    tdr = k.inc('v', nc.vector.tensor_scalar_mul(
                        drl[:, :], ident[:, :], lsum[:, 1:2]))
                    po = pb[4]
                    tva = None
                    for b4 in range(nblk):
                        nk4 = min(4, nkb - b4 * 4)
                        bank = pb[2 + b4 % 2]
                        ai = b4 % 2
                        k.wait('t', tex, tdr, tok_b2.get(('at', ai)))
                        for i4 in range(nk4):
                            kb = b4 * 4 + i4
                            mm = nc.tensor.matmul(
                                bank[:, i4 * PT:(i4 + 1) * PT],
                                exps[:, kb * PT:(kb + 1) * PT], drl[:, :],
                                start=(i4 == 0), stop=(i4 == nk4 - 1))
                        tmm = k.inc('t', mm)
                        tok_exps_r = tmm
                        tok_drl_r = tmm
                        k.wait('v', tmm, tok_b2.get(('ar', ai)))
                        tae = k.inc('v', nc.vector.tensor_copy(
                            att[ai][:, :nk4 * PT], bank[:, :nk4 * PT]))
                        tok_b2[('at', ai)] = tae
                        k.wait('t', tae,
                               tok_b2.get('po') if b4 == 0 else None)
                        for i4 in range(nk4):
                            kb = b4 * 4 + i4
                            mm = nc.tensor.matmul(
                                po[:DH, 0:PT], vf[:, kb * KW + kvh * DH:
                                               kb * KW + (kvh + 1) * DH],
                                att[ai][:, i4 * PT:(i4 + 1) * PT],
                                start=(kb == 0), stop=(kb == nkb - 1),
                                skip_group_check=True)
                        tva = k.inc('t', mm)
                        tok_b2[('ar', ai)] = tva
                    k.wait('s', tva, tok_oT_r)
                    if DH == PT:
                        dst = oT[:, h * 512 + jj * PT: h * 512 + (jj + 1) * PT]
                    else:
                        colb, rowb = (h * DH) // PT, (h * DH) % PT
                        dst = oT[rowb:rowb + DH,
                                 colb * 512 + jj * PT: colb * 512 + (jj + 1) * PT]
                    tev = k.inc('s', nc.scalar.copy(dst, po[:DH, 0:PT]))
                    tok_b2['po'] = tev
                    t_oT_w.append(tev)
                tok_qT2_r = ('t', k.cnt['t'])
            t_oT = tmax(*t_oT_w)
            for dmq in range(4):
                for hs in range(NHD):
                    wi = woi[0] % 2
                    woi[0] += 1
                    tw = k.dma('g', f'c_wo{wi}', woch[wi][:, :],
                               io['Wo'][hs * PT:(hs + 1) * PT,
                                        dmq * 512:(dmq + 1) * 512],
                               wait=(tok_wo_r[wi],))
                    k.wait('t', tw, *t_oT)
                    if hs == 0:
                        k.wait('t', *[tok_b2.get(('wo', q)) for q in range(4)])
                    for q4 in range(4):
                        mm = nc.tensor.matmul(
                            pb[q4][:, :], woch[wi][:, q4 * PT:(q4 + 1) * PT],
                            oT[:, hs * 512:(hs + 1) * 512],
                            start=(hs == 0), stop=(hs == NHD - 1))
                    tok_wo_r[wi] = k.inc('t', mm)
                tmw = tok_wo_r[(woi[0] - 1) % 2]
                tok_oT_r = tmw
                for q4 in range(4):
                    dms = dmq * 4 + q4
                    k.wait('v', tmw, tok_outT_r)
                    tev = k.inc('v', nc.vector.tensor_copy(
                        outT[:, dms * 512:(dms + 1) * 512], pb[q4][:, :]))
                    tok_b2[('wo', q4)] = tev
            t_outT_w = ('v', k.cnt['v'])
            for jj, j in enumerate(tiles):
                rsl = slice(j * PT, (j + 1) * PT)
                tx = k.dma('y', 'c_xt2', xt2[:, :], x_in[rsl, :],
                           wait=(tok_xt2_r,))
                adds = []
                for dms in range(16):
                    pbk, tm, b = trp.start(
                        outT[:, dms * 512 + jj * PT: dms * 512 + (jj + 1) * PT],
                        t_outT_w)
                    k.wait('v', tm, tx, tok_xo_r)
                    te = k.inc('v', nc.vector.tensor_tensor(
                        xo[:, dms * PT:(dms + 1) * PT],
                        xt2[:, dms * PT:(dms + 1) * PT], pbk[:, :], ALU.add))
                    trp.done(b, te)
                    adds.append(te)
                tok_xt2_r = adds[-1]
                tok_outT_r = ('t', k.cnt['t'])
                tst = k.dma('s', 'c_xo', x_out[rsl, :], xo[:, :],
                            wait=tuple(tmax(*adds)))
                tok_xo_r = tst
    return tst


def emit_f1(k, cfg, io):
    nc = k.nc
    n_own = cfg['n_own']
    ngrp = n_own // 4
    E, IM, ISH = 8, 1024, 4096
    ident = cfg['ident']
    pb = cfg['pbanks']
    trp = cfg['trp']
    x_in, x_out = io['x_in'], io['x_out']
    wgu, wd = io['w_gate_up'], io['w_down']
    wsg, wsd = io['w_sh_gate_up'], io['w_sh_down']
    esel = cfg['esel']
    esel_tok = cfg['esel_tok']

    with k.scope():
        xt = k.sb("f1_xt", (PT, D), F32)
        xn = k.sb("f1_xn", (PT, D), BF16)
        sc = dict(st6=k.sb("f1_st6", (PT, 24), F32),
                  st2=k.sb("f1_st2", (PT, 2), F32),
                  rc=k.sb("f1_rc", (PT, 2), F32))
        xnT = k.sb("f1_xnT", (PT, 16 * 512), BF16)
        wg8 = k.sb("f1_wg8", (PT, 16 * 8), BF16)
        lgT = k.sb("f1_lgT", (8, 512), F32)
        lg = k.sb("f1_lg", (PT, 8), F32)
        r1 = k.sb("f1_r1", (PT, 8), F32)
        r2 = k.sb("f1_r2", (PT, 8), F32)
        rcol = k.sb("f1_rcol", (PT, 4), F32)
        cmb = k.sb("f1_cmb", (PT, 8), BF16)
        cmbT = k.sb("f1_cmbT", (8, 512), BF16)
        bc = k.sb("f1_bc", (PT, 512), BF16)
        wch = [k.sb(f"f1_w{i}", (PT, 512), BF16) for i in range(2)]
        gsl = k.sb("f1_gs", (PT, 4 * 512), BF16)
        usl = k.sb("f1_us", (PT, 4 * 512), BF16)
        acts = k.sb("f1_act", (PT, 96 * 512), BF16)
        xo = k.sb("f1_xo", (PT, D), F32)

        tok = {}
        wci = [0]
        tst = None

        def castdma(dst, src, ch_wait=None):
            wi = wci[0] % 2
            wci[0] += 1
            t = k.dma('g', f'c_fw{wi}', dst if dst is not None else wch[wi][:, :],
                      src, wait=(tok.get(('fw', wi)), ch_wait))
            return (wi, t)

        for g in range(ngrp):
            tiles = list(range(4 * g, 4 * g + 4))
            t_tr = None
            for jj, j in enumerate(tiles):
                tx = k.dma('y', 'c_xt', xt[:, :], x_in[j * PT:(j + 1) * PT, :],
                           wait=(tok.get('xt_r'),))
                txn = emit_ln(k, xt, xn, sc, tx, war_tok=tok.get('xn_r'))
                tok['xt_r'] = txn
                for dt in range(16):
                    t_tr = trp.transpose(
                        xn[:, dt * PT:(dt + 1) * PT],
                        xnT[:, dt * 512 + jj * PT: dt * 512 + (jj + 1) * PT],
                        txn if dt == 0 else None,
                        war_tok=tok.get('xnT_r') if (jj == 0 and dt == 0) else None)
                tok['xn_r'] = ('t', k.cnt['t'])
            if g == 0:
                tw8 = k.dma('g', 'c_wg8',
                            wg8[:, :].rearrange("p (a e) -> p a e", e=8),
                            io['w_gate'].rearrange("(a p) e -> p a e", p=PT))
            k.wait('t', tw8, t_tr, tok.get('ps8_r'))
            for dt in range(16):
                mm = nc.tensor.matmul(pb[4][0:8, :], wg8[:, dt * 8:(dt + 1) * 8],
                                      xnT[:, dt * 512:(dt + 1) * 512],
                                      start=(dt == 0), stop=(dt == 15))
            tmm = k.inc('t', mm)
            tok['xnT_r'] = tmm
            k.wait('v', tmm, tok.get('lgT_r'))
            tlgT = k.inc('v', nc.vector.tensor_copy(lgT[:, :], pb[4][0:8, :]))
            tok['ps8_r'] = tlgT
            tcmbT = []
            for st in range(4):
                k.wait('t', tlgT, tok.get('bc_ps_r'))
                tm = k.inc('t', nc.tensor.transpose(
                    pb[5][:, 0:8], lgT[:, st * PT:(st + 1) * PT],
                    cfg['identf'][0:8, 0:8]))
                k.wait('v', tm, tok.get('lg_r'))
                tlg = k.inc('v', nc.vector.tensor_copy(lg[:, :], pb[5][:, 0:8]))
                tok['bc_ps_r'] = tlg
                k.wait('v', tlg)
                t1 = k.inc('v', nc.vector.reduce_max(rcol[:, 0:1], lg[:, :],
                                                     axis=AX.X))
                k.wait('v', t1)
                t2 = k.inc('v', nc.vector.tensor_scalar(
                    r1[:, :], lg[:, :], rcol[:, 0:1], None, ALU.is_equal))
                k.wait('v', t2)
                t3 = k.inc('v', nc.vector.scalar_tensor_tensor(
                    r2[:, :], r1[:, :], NEG, lg[:, :], ALU.mult, ALU.add))
                k.wait('v', t3)
                t4 = k.inc('v', nc.vector.reduce_max(rcol[:, 1:2], r2[:, :],
                                                     axis=AX.X))
                k.wait('v', t4)
                t5 = k.inc('v', nc.vector.tensor_scalar(
                    r1[:, :], lg[:, :], rcol[:, 1:2], None, ALU.is_ge))
                t5b = k.inc('v', nc.vector.tensor_scalar_mul(
                    rcol[:, 2:3], rcol[:, 0:1], -1.0))
                k.wait('s', t5b, t2)
                t6 = k.inc('s', nc.scalar.activation(r2[:, :], lg[:, :], AF.Exp,
                                                     bias=rcol[:, 2:3]))
                k.wait('v', t6, t5)
                t7 = k.inc('v', nc.vector.tensor_mul(r2[:, :], r2[:, :], r1[:, :]))
                k.wait('v', t7)
                t8 = k.inc('v', nc.vector.reduce_sum(rcol[:, 3:4], r2[:, :],
                                                     axis=AX.X))
                k.wait('v', t8)
                t9 = k.inc('v', nc.vector.reciprocal(rcol[:, 3:4], rcol[:, 3:4]))
                k.wait('v', t9)
                t10 = k.inc('v', nc.vector.tensor_scalar_mul(
                    cmb[:, :], r2[:, :], rcol[:, 3:4]))
                tok['lg_r'] = t10
                b = trp.i % 2
                trp.i += 1
                pbk = trp.banks[b]
                k.wait('t', t10, trp.busy[b], tok.get('cmbT_r') if st == 0 else None)
                tm2 = k.inc('t', nc.tensor.transpose(pbk[0:8, 0:PT], cmb[:, :],
                                                     ident[:, :]))
                k.wait('v', tm2)
                tc = k.inc('v', nc.vector.tensor_copy(
                    cmbT[0:8, st * PT:(st + 1) * PT], pbk[0:8, 0:PT]))
                trp.done(b, tc)
                tcmbT.append(tc)
            t_cmbT = tmax(*tcmbT)

            def gu_phase(Wsrc, bc_tok, actbase, width2):
                half = width2 // 2
                tacts = []
                for fb in range(half // 512):
                    for part, base_col in ((0, fb * 512), (1, half + fb * 512)):
                        dst = gsl if part == 0 else usl
                        for dt in range(16):
                            wi, tw = castdma(None,
                                             Wsrc[dt * PT:(dt + 1) * PT,
                                                  base_col:base_col + 512])
                            k.wait('t', tw, t_tr,
                                   *[tok.get(('gub', q)) for q in range(4)])
                            for q4 in range(4):
                                mm = nc.tensor.matmul(
                                    pb[q4][:, :],
                                    wch[wi][:, q4 * PT:(q4 + 1) * PT],
                                    xnT[:, dt * 512:(dt + 1) * 512],
                                    start=(dt == 0), stop=(dt == 15))
                            tok[('fw', wi)] = k.inc('t', mm)
                        tmm = tok[('fw', wi)]
                        for q4 in range(4):
                            k.wait('v', tmm, tok.get(('gu_r', part, q4)))
                            tev = k.inc('v', nc.vector.tensor_copy(
                                dst[:, q4 * 512:(q4 + 1) * 512], pb[q4][:, :]))
                            tok[('gub', q4)] = tev
                            tok[('gu_w', part, q4)] = tev
                    for q4 in range(4):
                        fs = fb * 4 + q4
                        tg = tok[('gu_w', 0, q4)]
                        tu = tok[('gu_w', 1, q4)]
                        k.wait('s', tg, tok.get(('sg_r', q4)))
                        ts1 = k.inc('s', nc.scalar.activation(
                            gsl[:, q4 * 512:(q4 + 1) * 512],
                            gsl[:, q4 * 512:(q4 + 1) * 512], AF.Silu))
                        tok[('gu_r', 0, q4)] = ts1
                        k.wait('v', ts1, tu, tok.get(('act_r', actbase + fs)))
                        tm1 = k.inc('v', nc.vector.tensor_mul(
                            usl[:, q4 * 512:(q4 + 1) * 512],
                            gsl[:, q4 * 512:(q4 + 1) * 512],
                            usl[:, q4 * 512:(q4 + 1) * 512]))
                        tok[('sg_r', q4)] = tm1
                        k.wait('v', tm1, bc_tok)
                        if bc_tok is not None:
                            tm2 = k.inc('v', nc.vector.tensor_mul(
                                acts[:, (actbase + fs) * 512:
                                     (actbase + fs + 1) * 512],
                                usl[:, q4 * 512:(q4 + 1) * 512], bc[:, :]))
                        else:
                            tm2 = k.inc('v', nc.vector.tensor_copy(
                                acts[:, (actbase + fs) * 512:
                                     (actbase + fs + 1) * 512],
                                usl[:, q4 * 512:(q4 + 1) * 512]))
                        tok[('gu_r', 1, q4)] = tm2
                        tok[('act_w', actbase + fs)] = tm2
                        tacts.append(tm2)
                return tacts

            all_act = []
            for e in range(E):
                k.wait('t', *t_cmbT, cfg['esel_tok'], tok.get('bc_ps_r'))
                mmb = k.inc('t', nc.tensor.matmul(
                    pb[5][:, :], esel[0:8, e * PT:(e + 1) * PT], cmbT[0:8, :],
                    start=True, stop=True))
                tok['cmbT_r'] = mmb
                k.wait('v', mmb, tok.get('bc_r'))
                tbc = k.inc('v', nc.vector.tensor_copy(bc[:, :], pb[5][:, :]))
                tok['bc_ps_r'] = tbc
                all_act += gu_phase(wgu[e], tbc, e * 8, 2 * IM)
                tok['bc_r'] = ('v', k.cnt['v'])
            all_act += gu_phase(wsg, None, 64, 2 * ISH)
            t_acts = tmax(*all_act)

            NSL_E, NSL_S = IM // PT, ISH // PT
            for jj, j in enumerate(tiles):
                tx = k.dma('y', 'c_xt2', xt[:, :], x_in[j * PT:(j + 1) * PT, :],
                           wait=(tok.get('xt_r'),))
                adds = []
                for dmb in range(4):
                    bank = pb[dmb % 2]
                    first = True
                    k.wait('t', *t_acts, tok.get(('dps', dmb % 2)))
                    for e in range(E):
                        for it in range(NSL_E):
                            wi, tw = castdma(None,
                                             wd[e][it * PT:(it + 1) * PT,
                                                   dmb * 512:(dmb + 1) * 512])
                            k.wait('t', tw)
                            mm = nc.tensor.matmul(
                                bank[:, :],
                                acts[:, (e * 8 + it) * 512 + jj * PT:
                                     (e * 8 + it) * 512 + (jj + 1) * PT],
                                wch[wi][:, :], start=first, stop=False,
                                skip_group_check=True)
                            first = False
                            tok[('fw', wi)] = k.inc('t', mm)
                    for it in range(NSL_S):
                        wi, tw = castdma(None,
                                         wsd[it * PT:(it + 1) * PT,
                                             dmb * 512:(dmb + 1) * 512])
                        k.wait('t', tw)
                        mm = nc.tensor.matmul(
                            bank[:, :],
                            acts[:, (64 + it) * 512 + jj * PT:
                                 (64 + it) * 512 + (jj + 1) * PT],
                            wch[wi][:, :], start=False, stop=(it == NSL_S - 1),
                            skip_group_check=True)
                        tok[('fw', wi)] = k.inc('t', mm)
                    tmm = tok[('fw', wi)]
                    k.wait('v', tmm, tx, tok.get('xo_r'))
                    te = k.inc('v', nc.vector.tensor_tensor(
                        xo[:, dmb * 512:(dmb + 1) * 512],
                        xt[:, dmb * 512:(dmb + 1) * 512], bank[:, :], ALU.add))
                    tok[('dps', dmb % 2)] = te
                    adds.append(te)
                tok['xt_r'] = adds[-1]
                tst = k.dma('s', 'c_xo', x_out[j * PT:(j + 1) * PT, :], xo[:, :],
                            wait=tuple(tmax(*adds)))
                tok['xo_r'] = tst
            for key in list(tok):
                if isinstance(key, tuple) and key[0] == 'act_r':
                    pass
            tok['xnT_r'] = ('t', k.cnt['t'])
            for i in range(96):
                tok[('act_r', i)] = ('t', k.cnt['t'])
    return tst


def emit_a0b(k, cfg, io):
    nc = k.nc
    n_own = cfg['n_own']
    n_tiles = 2 * n_own
    H, KV, DH = 16, 4, 128
    QW, KW = H * DH, KV * DH
    gph = H // KV
    half = DH // 2
    scale = DH ** -0.5
    ident = cfg['ident']
    pb = cfg['pbanks']
    trp = cfg['trp']
    x_in, x_out = io['x_in'], io['x_out']
    tok = {}
    wci = [0]

    def rope3(src_ap, dst_ap, tabc, tabs, ti, width, nheads, src_tok, war_tok,
              war2, scratch, tab_tok=None):
        base = src_ap
        sw4 = bass.AP(base.tensor, base.offset + half,
                      [list(base.ap[0]), [DH, nheads], [-half, 2], [1, half]])
        tc = tabc[:, ti * DH:(ti + 1) * DH]
        ts_ = tabs[:, ti * DH:(ti + 1) * DH]
        cosb = bass.AP(tc.tensor, tc.offset,
                       [list(tc.ap[0]), [0, nheads], [1, DH]])
        sinb = bass.AP(ts_.tensor, ts_.offset,
                       [list(ts_.ap[0]), [0, nheads], [half, 2], [1, half]])
        d4 = dst_ap.rearrange("p (h s x) -> p h s x", h=nheads, s=2)
        s3 = src_ap.rearrange("p (h x) -> p h x", h=nheads)
        k.wait('v', src_tok, war_tok, tab_tok)
        t1 = k.inc('v', nc.vector.tensor_tensor(d4, sw4, sinb, ALU.mult))
        k.wait('v', t1, war2)
        t2 = k.inc('v', nc.vector.tensor_tensor(
            scratch[:, :width].rearrange("p (h x) -> p h x", h=nheads), s3, cosb,
            ALU.mult))
        k.wait('v', t2)
        return k.inc('v', nc.vector.tensor_tensor(dst_ap, dst_ap,
                                                  scratch[:, :width], ALU.add))

    def softmax_heads(src_ap, dst_ap, nheads, mxb, src_tok, war_tok):
        v3s = src_ap.rearrange("p (h x) -> p h x", h=nheads)
        k.wait('v', src_tok)
        t1 = k.inc('v', nc.vector.reduce_max(
            mxb[:, 0:nheads].rearrange("p (h o) -> p h o", h=nheads), v3s,
            axis=AX.X))
        k.wait('v', t1)
        t2 = k.inc('v', nc.vector.tensor_scalar_mul(mxb[:, 16:16 + nheads],
                                                    mxb[:, 0:nheads], -1.0))
        texp = None
        for h in range(nheads):
            k.wait('s', t2, war_tok if h == 0 else None)
            texp = k.inc('s', nc.scalar.activation(
                dst_ap[:, h * DH:(h + 1) * DH], src_ap[:, h * DH:(h + 1) * DH],
                AF.Exp, bias=mxb[:, 16 + h:17 + h]))
        v3d = dst_ap.rearrange("p (h x) -> p h x", h=nheads)
        k.wait('v', texp)
        t3 = k.inc('v', nc.vector.reduce_sum(
            mxb[:, 0:nheads].rearrange("p (h o) -> p h o", h=nheads), v3d,
            axis=AX.X))
        k.wait('v', t3)
        t4 = k.inc('v', nc.vector.reciprocal(mxb[:, 16:16 + nheads],
                                             mxb[:, 0:nheads]))
        tl = None
        k.wait('v', t4)
        for h in range(nheads):
            tl = k.inc('v', nc.vector.tensor_scalar_mul(
                dst_ap[:, h * DH:(h + 1) * DH], dst_ap[:, h * DH:(h + 1) * DH],
                mxb[:, 16 + h:17 + h]))
        return tl

    def gl_pipeline(ksrc_ap, A_bank, Eq_sb, Ek_sb, gl_sb, src_tok, war_q,
                    war_k):
        k.wait('s', src_tok, war_q)
        tg1 = k.inc('s', nc.scalar.activation(gl_sb[:, :], ksrc_ap, AF.Sigmoid))
        k.wait('s', tg1)
        tg2 = k.inc('s', nc.scalar.activation(gl_sb[:, :], gl_sb[:, :], AF.Ln))
        k.wait('t', tg2, cfg['utri_tok'], tok.get('Abank_r'))
        tA = k.inc('t', nc.tensor.matmul(A_bank[:, 0:KW], cfg['utri'][:, :],
                                         gl_sb[:, :], start=True, stop=True))
        k.wait('s', tA, war_k)
        te1 = k.inc('s', nc.scalar.activation(Eq_sb[:, :], A_bank[:, 0:KW],
                                              AF.Exp, scale=1.0 / 16.0))
        te2 = k.inc('s', nc.scalar.activation(Ek_sb[:, :], A_bank[:, 0:KW],
                                              AF.Exp, scale=-1.0 / 16.0))
        tok['Abank_r'] = te2
        return tg2, tA, te1, te2

    with k.scope():
        xt = k.sb("ab_xt", (PT, D), F32)
        xn = k.sb("ab_xn", (PT, D), BF16)
        sc = dict(st6=k.sb("ab_st6", (PT, 24), F32),
                  st2=k.sb("ab_st2", (PT, 2), F32),
                  rc=k.sb("ab_rc", (PT, 2), F32))
        xnT = k.sb("ab_xnT", (PT, 16 * 512), BF16)
        wch = [k.sb(f"ab_w{i}", (PT, 512), BF16) for i in range(2)]
        qg = k.sb("ab_qg", (PT, 4 * QW), BF16)
        kvg = k.sb("ab_kvg", (PT, 4 * 2 * KW), BF16)
        rop = k.sb("ab_rop", (PT, QW), BF16)
        rop2 = k.sb("ab_rop2", (PT, QW), BF16)
        trb = k.sb("ab_trb", (PT, QW), BF16)
        qs_sb = k.sb("ab_qs", (PT, QW), BF16)
        mxb = k.sb("ab_mx", (PT, 32), F32)
        gl_sb = k.sb("ab_gl", (PT, KW), F32)
        Eq_sb = k.sb("ab_Eq", (PT, KW), F32)
        Ek_sb = k.sb("ab_Ek", (PT, KW), F32)
        qh_sb = k.sb("ab_qh", (PT, QW), BF16)
        kh_sb = k.sb("ab_kh", (PT, KW), BF16)
        tco = k.sb("ab_tco", (PT, n_own * DH), BF16)
        tso = k.sb("ab_tso", (PT, n_own * DH), BF16)
        ttab = k.dma('y', 'c_tab', tco[:, :], cfg['tabc_o_d'])
        ttab = k.dma('y', 'c_tab', tso[:, :], cfg['tabs_o_d'])

        def proj_group(Wsrc, width, dst, dst_off, dst_stride, war_toks, x_tok):
            tevs = []
            for blk in range(width // 512):
                for dt in range(16):
                    wi = wci[0] % 2
                    wci[0] += 1
                    tw = k.dma('g', f'c_w{wi}', wch[wi][:, :],
                               Wsrc[dt * PT:(dt + 1) * PT,
                                    blk * 512:(blk + 1) * 512],
                               wait=(tok.get(('w', wi)),))
                    k.wait('t', tw, x_tok, *[tok.get(('pg', q)) for q in range(4)])
                    for q4 in range(4):
                        mm = nc.tensor.matmul(
                            pb[q4][:, :],
                            xnT[:, dt * 512 + q4 * PT: dt * 512 + (q4 + 1) * PT],
                            wch[wi][:, :], start=(dt == 0), stop=(dt == 15))
                    tok[('w', wi)] = k.inc('t', mm)
                tmm = tok[('w', (wci[0] - 1) % 2)]
                for q4 in range(4):
                    k.wait('v', tmm,
                           *(war_toks if (blk == 0 and q4 == 0) else []))
                    tev = k.inc('v', nc.vector.tensor_copy(
                        dst[:, dst_off + q4 * dst_stride + blk * 512:
                            dst_off + q4 * dst_stride + (blk + 1) * 512],
                        pb[q4][:, :]))
                    tok[('pg', q4)] = tev
                    tevs.append(tev)
            return tmax(*tevs)

        def store_T(src_ap, nsl, src_tok, dram_dst, chname):
            tt = None
            for dt in range(nsl):
                tt = trp.transpose(src_ap[:, dt * PT:(dt + 1) * PT],
                                   trb[:, dt * PT:(dt + 1) * PT],
                                   src_tok if dt == 0 else None,
                                   war_tok=tok.get('trb_r') if dt == 0 else None)
            ts = k.dma('s', chname, dram_dst, trb[:, 0:nsl * PT], wait=(tt,))
            tok['trb_r'] = ts
            return ts

        ngrp = n_own // 4
        for g in range(ngrp):
            tiles = list(range(4 * g, 4 * g + 4))
            t_tr = None
            for jj, j in enumerate(tiles):
                tx = k.dma('y', 'c_xt', xt[:, :], x_in[j * PT:(j + 1) * PT, :],
                           wait=(tok.get('xt_r'),))
                txn = emit_ln(k, xt, xn, sc, tx, war_tok=tok.get('xn_r'))
                tok['xt_r'] = txn
                for dt in range(16):
                    t_tr = trp.transpose(
                        xn[:, dt * PT:(dt + 1) * PT],
                        xnT[:, dt * 512 + jj * PT: dt * 512 + (jj + 1) * PT],
                        txn if dt == 0 else None,
                        war_tok=tok.get('xnT_r') if (jj == 0 and dt == 0) else None)
                tok['xn_r'] = ('t', k.cnt['t'])
            tq = proj_group(io['Wq'], QW, qg, 0, QW, tok.get('qg_r') or [],
                            t_tr)
            tkk = proj_group(io['Wk'], KW, kvg, 0, 2 * KW,
                             tok.get('kvg_r') or [], t_tr)
            tvv = proj_group(io['Wv'], KW, kvg, KW, 2 * KW, [], t_tr)
            tok['xnT_r'] = ('t', k.cnt['t'])
            for jj, j in enumerate(tiles):
                qsl = qg[:, jj * QW:(jj + 1) * QW]
                ksl = kvg[:, jj * 2 * KW: jj * 2 * KW + KW]
                vsl = kvg[:, jj * 2 * KW + KW: (jj + 1) * 2 * KW]
                trq = rope3(qsl, rop[:, :QW], tco[:, :], tso[:, :], j,
                            QW, H, tq[0] if tq else None, tok.get('rop_r'),
                            tok.get('rop2_r'), rop2, tab_tok=ttab)
                tok['rop2_r'] = trq
                tsq = store_T(rop[:, :QW], 16, trq, io['qT'][j], 'c_sq')
                tok['rop_r'] = ('t', k.cnt['t'])
                tqs = softmax_heads(qsl, qs_sb[:, :], H, mxb,
                                    tq[0] if tq else None, tok.get('qs_r'))
                tg2, tA, te1, te2 = gl_pipeline(ksl, pb[4], Eq_sb, Ek_sb, gl_sb,
                                                tkk[0] if tkk else None,
                                                tok.get('Eq_r'), tok.get('Ek_r'))
                eqb = bass.AP(Eq_sb.ap().tensor, Eq_sb.ap().offset,
                              [list(Eq_sb.ap().ap[0]), [DH, KV], [0, gph],
                               [1, DH]])
                k.wait('v', tqs, te1, tok.get('qh_r'))
                tqh = k.inc('v', nc.vector.tensor_mul(
                    qh_sb[:, :].rearrange("p (a b x) -> p a b x", a=KV, b=gph),
                    qs_sb[:, :].rearrange("p (a b x) -> p a b x", a=KV, b=gph),
                    eqb))
                tok['Eq_r'] = tqh
                tsh = store_T(qh_sb[:, :], 16, tqh, io['qhT'][j], 'c_sh')
                tok['qh_r'] = ('t', k.cnt['t'])
                tok['qs_r'] = tqh
                tks = softmax_heads(ksl, kh_sb[:, :], KV, mxb,
                                    tkk[0] if tkk else None, tok.get('kh_r'))
                k.wait('v', tks, te2)
                tkh = k.inc('v', nc.vector.tensor_mul(kh_sb[:, :], kh_sb[:, :],
                                                      Ek_sb[:, :]))
                tok['Ek_r'] = tkh
                tsk = store_T(kh_sb[:, :], 4, tkh, io['khT'][j], 'c_skh')
                tok['kh_r'] = ('t', k.cnt['t'])
                ta1 = k.dma('s', 'c_agi', io['ag_in'][j * PT:(j + 1) * PT, :],
                            kvg[:, jj * 2 * KW:(jj + 1) * 2 * KW],
                            wait=tuple(tmax(*tkk, *tvv)))
                tv2 = k.dma('s', 'c_sv', io['vT'][j], vsl,
                            wait=tuple(tmax(*tvv)))
                tok.setdefault('pstores', []).extend([tsq, tsh, tsk, ta1, tv2])
            tok['qg_r'] = tmax(*tok.pop('pstores'))
            tok['kvg_r'] = tok['qg_r']

    k.barrier()
    ch = k.chan('c_col')
    inst = nc.gpsimd.collective_compute(
        "AllGather", ALU.bypass, cfg['replica_groups'],
        [io['ag_in'].opt()], [io['ag_out'].opt()])
    k.cnt[ch] += 1
    inst.then_inc(k.sem[ch], 1)
    t_ag = (ch, k.cnt[ch])

    with k.scope():
        kT = k.sb("ab_kT", (PT, KV * n_tiles * PT), BF16)
        vf = k.sb("ab_vf", (PT, n_tiles * KW), BF16)
        khG = k.sb("ab_khG", (PT, n_tiles * KW), BF16)
        ECg = k.sb("ab_EC", (PT, n_tiles * KV), F32)
        prep_scope = k.scope()
        prep_scope.__enter__()
        kvt = k.sb("ab_kvt", (PT, 2 * KW), BF16)
        rop_b = k.sb("ab_ropb", (PT, KW), BF16)
        rop2_b = k.sb("ab_rop2b", (PT, KW), BF16)
        gl2 = k.sb("ab_gl2", (PT, KW), F32)
        Ek2 = k.sb("ab_Ek2", (PT, KW), F32)
        ks2 = k.sb("ab_ks2", (PT, KW), BF16)
        mxb2 = k.sb("ab_mx2", (PT, 32), F32)
        tca = k.sb("ab_tca", (PT, n_tiles * DH), BF16)
        tsa = k.sb("ab_tsa", (PT, n_tiles * DH), BF16)
        ttab2 = k.dma('y', 'c_tab', tca[:, :], cfg['tabc_a_d'])
        ttab2 = k.dma('y', 'c_tab', tsa[:, :], cfg['tabs_a_d'])
        onec, onec_tok = k.constcol(1.0)

        tok2 = {}
        last_pr = []
        for kb in range(n_tiles):
            par, idx = kb % 2, kb // 2
            row0 = par * n_own * PT + idx * PT
            tl = k.dma('y', 'c_kvt', kvt[:, :], io['ag_out'][row0:row0 + PT, :],
                       wait=(t_ag, *(tok2.get('kvt_r') or [])))
            k.wait('v', tl)
            tv = k.inc('v', nc.vector.tensor_copy(vf[:, kb * KW:(kb + 1) * KW],
                                                  kvt[:, KW:2 * KW]))
            trk = rope3(kvt[:, 0:KW], rop_b[:, :KW], tca[:, :],
                        tsa[:, :], kb, KW, KV, tl, tok2.get('ropb_r'),
                        tok2.get('ropb2_r'), rop2_b, tab_tok=ttab2)
            tok2['ropb2_r'] = trk
            ttx = None
            for s in range(KV):
                ttx = trp.transpose(
                    rop_b[:, s * PT:(s + 1) * PT],
                    kT[:, (s * n_tiles + kb) * PT:(s * n_tiles + kb + 1) * PT],
                    trk if s == 0 else None)
            tok2['ropb_r'] = ('t', k.cnt['t'])
            k.wait('s', tl, tok2.get('gl2_r'))
            tg1 = k.inc('s', nc.scalar.activation(gl2[:, :], kvt[:, 0:KW],
                                                  AF.Sigmoid))
            k.wait('s', tg1)
            tg2 = k.inc('s', nc.scalar.activation(gl2[:, :], gl2[:, :], AF.Ln))
            k.wait('t', tg2, cfg['utri_tok'], tok2.get('A_r'))
            tA = k.inc('t', nc.tensor.matmul(pb[4][:, 0:KW], cfg['utri'][:, :],
                                             gl2[:, :], start=True, stop=True))
            k.wait('s', tA, tok2.get('Ek2_r'))
            te2 = k.inc('s', nc.scalar.activation(Ek2[:, :], pb[4][:, 0:KW],
                                                  AF.Exp, scale=-1.0 / 16.0))
            tok2['A_r'] = te2
            tks = softmax_heads(kvt[:, 0:KW], ks2[:, :], KV, mxb2, tl,
                                tok2.get('ks2_r'))
            k.wait('v', tks, te2, tok2.get('khG_r'))
            tkh = k.inc('v', nc.vector.tensor_mul(
                khG[:, kb * KW:(kb + 1) * KW], ks2[:, :], Ek2[:, :]))
            tok2['Ek2_r'] = tkh
            tok2['ks2_r'] = tkh
            tecs = []
            for s in range(KV):
                k.wait('t', tg2, onec_tok, tok2.get(('EC_r', s)))
                tmc = k.inc('t', nc.tensor.matmul(
                    pb[5][:, s:s + 1], gl2[:, s * PT:(s + 1) * PT],
                    onec[:, 0:1], start=True, stop=True))
                k.wait('s', tmc)
                tec = k.inc('s', nc.scalar.activation(
                    ECg[:, kb * KV + s: kb * KV + s + 1], pb[5][:, s:s + 1],
                    AF.Exp, scale=1.0 / 16.0))
                tok2[('EC_r', s)] = tec
                tecs.append(tec)
            tok2['gl2_r'] = tA
            tok2['kvt_r'] = tmax(tv, tks, tg1, ('t', k.cnt['t']))
            last_pr += [tv, tkh, ttx, *tecs]
        t_prep = tmax(*last_pr, ('t', k.cnt['t']))
        prep_scope.__exit__(None, None, None)

        strip = k.sb("ab_strip", (PT, n_tiles * PT), F32)
        exps = k.sb("ab_exps", (PT, n_tiles * PT), BF16)
        mx = k.sb("ab_mxs", (PT, 2), F32)
        lsum = k.sb("ab_l", (PT, 2), F32)
        drl = k.sb("ab_drl", (PT, PT), BF16)
        att = [k.sb(f"ab_att{i}", (PT, 512), BF16) for i in range(2)]
        qTt = k.sb("ab_qTt", (PT, QW), BF16)
        qhTt = k.sb("ab_qhTt", (PT, QW), BF16)
        khTt = k.sb("ab_khTt", (PT, KW), BF16)
        vtt = k.sb("ab_vtt", (PT, KW), BF16)
        S_sb = k.sb("ab_S", (PT, KV * PT), F32)
        Sb = k.sb("ab_Sb", (PT, KV * PT), BF16)
        msk = [k.sb(f"ab_msk{i}", (PT, PT), BF16) for i in range(2)]
        mskT = [k.sb(f"ab_mskT{i}", (PT, PT), BF16) for i in range(2)]
        oT = k.sb("ab_oT", (PT, 16 * 512), BF16)
        woch = [k.sb(f"ab_wo{i}", (PT, 512), BF16) for i in range(2)]
        outT = k.sb("ab_outT", (PT, 16 * 512), BF16)
        xt2 = k.sb("ab_xt2", (PT, D), F32)
        xo = k.sb("ab_xo", (PT, D), F32)
        blend = k.sb("ab_blend", (PT, 2), F32)

        tbl = k.dma('y', 'c_bl', blend[:, :], io['blend'])
        tz1 = k.inc('v', nc.vector.memset(S_sb[:, :], 0.0))
        tz2 = k.inc('v', nc.vector.memset(Sb[:, :], 0.0))
        tok3 = {}
        woi = [0]
        tst = None

        for g in range(n_own // 4):
            tiles = list(range(4 * g, 4 * g + 4))
            t_oT_w = []
            for jj, j in enumerate(tiles):
                L = (2 * j + 2) * PT
                nkb = 2 * j + 2
                tql = k.dma('y', 'c_qTl', qTt[:, :], io['qT'][j],
                            wait=(tok3.get('qTt_r'),))
                tqh = k.dma('y', 'c_qhl', qhTt[:, :], io['qhT'][j],
                            wait=(tok3.get('qhTt_r'),))
                tkh2 = k.dma('y', 'c_khl', khTt[:, :], io['khT'][j],
                             wait=(tok3.get('khTt_r'),))
                tvl = k.dma('y', 'c_vl', vtt[:, :], io['vT'][j],
                            wait=(tok3.get('vtt_r'),))
                for h in range(H):
                    kvh = h // gph
                    lq = qTt[:, h * PT:(h + 1) * PT]
                    nblk = (nkb + 3) // 4
                    cps = []
                    for b4 in range(nblk):
                        w = min(512, L - b4 * 512)
                        bank = pb[b4 % 2]
                        rk = kT[:, kvh * n_tiles * PT + b4 * 512:
                                kvh * n_tiles * PT + b4 * 512 + w]
                        k.wait('t', tql, *t_prep)
                        k.wait('t', tok3.get(('sc_r', b4 % 2)))
                        mm = k.inc('t', nc.tensor.matmul(
                            bank[:, :w], lq, rk, start=True, stop=True))
                        k.wait('s', mm, tok3.get('strip_r'))
                        tcp = k.inc('s', nc.scalar.copy(
                            strip[:, b4 * 512: b4 * 512 + w], bank[:, :w]))
                        tok3[('sc_r', b4 % 2)] = tcp
                        cps.append(tcp)
                    k.wait('v', cps[-1], cfg['dmask_tok'])
                    tmsk = k.inc('v', nc.vector.tensor_tensor(
                        strip[:, L - 2 * PT:L], strip[:, L - 2 * PT:L],
                        cfg['dmask'][:, :], ALU.add))
                    k.wait('v', *cps, tmsk, tok3.get('strip_r'))
                    tmx = k.inc('v', nc.vector.reduce_max(
                        mx[:, 0:1], strip[:, :L], axis=AX.X))
                    k.wait('v', tmx)
                    tb = k.inc('v', nc.vector.tensor_scalar_mul(
                        mx[:, 1:2], mx[:, 0:1], -scale))
                    k.wait('s', tb, tok3.get('exps_r'), tok3.get('lsum_r'))
                    tex = k.inc('s', nc.scalar.activation(
                        exps[:, :L], strip[:, :L], AF.Exp,
                        bias=mx[:, 1:2], scale=scale, accum_out=lsum[:, 0:1]))
                    tok3['strip_r'] = tex
                    k.wait('v', tex)
                    trl = k.inc('v', nc.vector.reciprocal(lsum[:, 1:2],
                                                          lsum[:, 0:1]))
                    tok3['lsum_r'] = trl
                    k.wait('v', trl, tok3.get('drl_r'))
                    tdr = k.inc('v', nc.vector.tensor_scalar_mul(
                        drl[:, :], ident[:, :], lsum[:, 1:2]))
                    po = pb[4]
                    tva = None
                    for b4 in range(nblk):
                        nk4 = min(4, nkb - b4 * 4)
                        bank = pb[2 + b4 % 2]
                        ai = b4 % 2
                        k.wait('t', tex, tdr, tok3.get(('at', ai)))
                        for i4 in range(nk4):
                            kb2 = b4 * 4 + i4
                            mm = nc.tensor.matmul(
                                bank[:, i4 * PT:(i4 + 1) * PT],
                                exps[:, kb2 * PT:(kb2 + 1) * PT], drl[:, :],
                                start=(i4 == 0), stop=(i4 == nk4 - 1))
                        tmm = k.inc('t', mm)
                        tok3['exps_r'] = tmm
                        tok3['drl_r'] = tmm
                        k.wait('v', tmm, tok3.get(('ar', ai)))
                        tae = k.inc('v', nc.vector.tensor_copy(
                            att[ai][:, :nk4 * PT], bank[:, :nk4 * PT]))
                        tok3[('at', ai)] = tae
                        k.wait('t', tae, tok3.get('po_r') if b4 == 0 else None)
                        for i4 in range(nk4):
                            kb2 = b4 * 4 + i4
                            mm = nc.tensor.matmul(
                                po[:, 0:PT], vf[:, kb2 * KW + kvh * DH:
                                                kb2 * KW + (kvh + 1) * DH],
                                att[ai][:, i4 * PT:(i4 + 1) * PT],
                                start=(kb2 == 0), stop=(kb2 == nkb - 1),
                                skip_group_check=True)
                        tva = k.inc('t', mm)
                        tok3[('ar', ai)] = tva
                    dst = oT[:, h * 512 + jj * PT: h * 512 + (jj + 1) * PT]
                    k.wait('s', tva, tok3.get('oT_r'))
                    tev = k.inc('s', nc.scalar.activation(dst, po[:, 0:PT],
                                                          AF.Copy, scale=0.5))
                    tok3['po_r'] = tev
                    t_oT_w.append(tev)
                tok3['qTt_r'] = ('t', k.cnt['t'])
                for point in range(2):
                    c = 2 * j + point
                    for hg in range(4):
                        bankP = pb[2 + hg % 2]
                        for hh in range(4):
                            h = hg * 4 + hh
                            kvh = h // gph
                            bi = h % 2
                            bankS = pb[bi]
                            k.wait('t', tqh, tkh2, tok3.get(('gsc_r', bi)))
                            mmS = k.inc('t', nc.tensor.matmul(
                                bankS[:, 0:PT], qhTt[:, h * PT:(h + 1) * PT],
                                khTt[:, kvh * PT:(kvh + 1) * PT],
                                start=True, stop=True))
                            k.wait('v', mmS, cfg['tri_tok'],
                                   tok3.get(('gmsk_r', bi)))
                            tmk = k.inc('v', nc.vector.tensor_mul(
                                msk[bi][:, :], bankS[:, 0:PT],
                                cfg['tri01'][:, :]))
                            tok3[('gsc_r', bi)] = tmk
                            pbk, tmt, b = trp.start(msk[bi][:, :], tmk)
                            k.wait('v', tmt, tok3.get(('gmskT_r', bi)))
                            tmT = k.inc('v', nc.vector.tensor_copy(
                                mskT[bi][:, :], pbk[:, :]))
                            trp.done(b, tmT)
                            tok3[('gmsk_r', bi)] = tmT
                            k.wait('t', tmT, tvl, tqh, tz2,
                                   tok3.get(('S_w', kvh)),
                                   tok3.get(('P_r', hg % 2)) if hh == 0 else None)
                            mm = nc.tensor.matmul(
                                bankP[:, hh * PT:(hh + 1) * PT],
                                vtt[:, kvh * PT:(kvh + 1) * PT], mskT[bi][:, :],
                                start=True,
                                stop=False, skip_group_check=True)
                            mm = nc.tensor.matmul(
                                bankP[:, hh * PT:(hh + 1) * PT],
                                Sb[:, kvh * PT:(kvh + 1) * PT],
                                qhTt[:, h * PT:(h + 1) * PT], start=False,
                                stop=True, skip_group_check=True)
                            tP = k.inc('t', mm)
                            tok3[('gmskT_r', bi)] = tP
                            tok3[('Sb_r', kvh)] = tP
                        k.wait('v', tP, tbl, ('s', k.cnt['s']))
                        for hh in range(4):
                            h = hg * 4 + hh
                            dsl = oT[:, h * 512 + jj * PT: h * 512 + (jj + 1) * PT]
                            tev = k.inc('v', nc.vector.scalar_tensor_tensor(
                                dsl, bankP[:, hh * PT:(hh + 1) * PT],
                                blend[:, point:point + 1], dsl,
                                ALU.mult, ALU.add))
                            t_oT_w.append(tev)
                        tok3[('P_r', hg % 2)] = tev
                    for s in range(KV):
                        k.wait('t', *t_prep, tok3.get('KV_r'))
                        mmK = k.inc('t', nc.tensor.matmul(
                            pb[5][:, 0:PT],
                            khG[:, c * KW + s * PT: c * KW + (s + 1) * PT],
                            vf[:, c * KW + s * PT: c * KW + (s + 1) * PT],
                            start=True, stop=True))
                        k.wait('v', mmK, tz1, tok3.get(('S_r', s)))
                        tS1 = k.inc('v', nc.vector.tensor_tensor(
                            S_sb[:, s * PT:(s + 1) * PT],
                            S_sb[:, s * PT:(s + 1) * PT], pb[5][:, 0:PT],
                            ALU.add))
                        tok3['KV_r'] = tS1
                        k.wait('v', tS1)
                        tS2 = k.inc('v', nc.vector.tensor_scalar_mul(
                            S_sb[:, s * PT:(s + 1) * PT],
                            S_sb[:, s * PT:(s + 1) * PT],
                            ECg[:, c * KV + s: c * KV + s + 1]))
                        k.wait('v', tS2, tok3.get(('Sb_r', s)))
                        tSb = k.inc('v', nc.vector.tensor_copy(
                            Sb[:, s * PT:(s + 1) * PT],
                            S_sb[:, s * PT:(s + 1) * PT]))
                        tok3[('S_r', s)] = tSb
                        tok3[('S_w', s)] = tSb
                tok3['qhTt_r'] = ('t', k.cnt['t'])
                tok3['khTt_r'] = tok3['qhTt_r']
                tok3['vtt_r'] = tok3['qhTt_r']
            t_oT = tmax(*t_oT_w)
            NHD = QW // PT
            for dmq in range(4):
                for hs in range(NHD):
                    wi = woi[0] % 2
                    woi[0] += 1
                    tw = k.dma('g', f'c_wo{wi}', woch[wi][:, :],
                               io['Wo'][hs * PT:(hs + 1) * PT,
                                        dmq * 512:(dmq + 1) * 512],
                               wait=(tok3.get(('wo', wi)),))
                    k.wait('t', tw, *t_oT)
                    if hs == 0:
                        k.wait('t', *[tok3.get(('wops', q)) for q in range(4)])
                    for q4 in range(4):
                        mm = nc.tensor.matmul(
                            pb[q4][:, :], woch[wi][:, q4 * PT:(q4 + 1) * PT],
                            oT[:, hs * 512:(hs + 1) * 512],
                            start=(hs == 0), stop=(hs == NHD - 1))
                    tok3[('wo', wi)] = k.inc('t', mm)
                tmw = tok3[('wo', (woi[0] - 1) % 2)]
                tok3['oT_r'] = tmw
                for q4 in range(4):
                    dms = dmq * 4 + q4
                    k.wait('v', tmw, tok3.get('outT_r'))
                    tev = k.inc('v', nc.vector.tensor_copy(
                        outT[:, dms * 512:(dms + 1) * 512], pb[q4][:, :]))
                    tok3[('wops', q4)] = tev
            t_outT_w = ('v', k.cnt['v'])
            for jj, j in enumerate(tiles):
                rsl = slice(j * PT, (j + 1) * PT)
                tx = k.dma('y', 'c_xt2', xt2[:, :], x_in[rsl, :],
                           wait=(tok3.get('xt2_r'),))
                adds = []
                for dms in range(16):
                    pbk, tm, b = trp.start(
                        outT[:, dms * 512 + jj * PT: dms * 512 + (jj + 1) * PT],
                        t_outT_w)
                    k.wait('v', tm, tx, tok3.get('xo_r'))
                    te = k.inc('v', nc.vector.tensor_tensor(
                        xo[:, dms * PT:(dms + 1) * PT],
                        xt2[:, dms * PT:(dms + 1) * PT], pbk[:, :], ALU.add))
                    trp.done(b, te)
                    adds.append(te)
                tok3['xt2_r'] = adds[-1]
                tok3['outT_r'] = ('t', k.cnt['t'])
                tst = k.dma('s', 'c_xo', x_out[rsl, :], xo[:, :],
                            wait=tuple(tmax(*adds)))
                tok3['xo_r'] = tst
            tok3['oT_r'] = ('t', k.cnt['t'])
    return tst


def emit_ternary_prep(k, trp, w_ap, wT_dram, onesrow, onesrow_tok, pbank):
    nc = k.nc
    n_rows, n_cols = w_ap.shape
    NRT = n_rows // PT
    with k.scope():
        ft = [k.sb(f"tq_f{i}", (PT, n_cols), F32) for i in range(2)]
        sgn = k.sb("tq_sg", (PT, n_cols), F32)
        ab = k.sb("tq_ab", (PT, n_cols), F32)
        bt = k.sb("tq_b", (PT, n_cols), BF16)
        acc = k.sb("tq_acc", (PT, 2), F32)
        mcol = k.sb("tq_m", (PT, 2), F32)
        stg = [k.sb(f"tq_g{i}", (PT, PT), BF16) for i in range(2)]
        one_c, one_tok = k.constcol(1.0)
        tok_l = [None, None]
        tz = k.inc('v', nc.vector.memset(acc[:, :], 0.0))
        tacc = tz
        for rt in range(NRT):
            pi = rt % 2
            tw = k.dma('y', f'c_tq{pi}', ft[pi][:, :],
                       w_ap[rt * PT:(rt + 1) * PT, :], wait=(tok_l[pi],))
            k.wait('v', tw, tacc)
            t1 = k.inc('v', nc.vector.reduce_sum(acc[:, 1:2], ft[pi][:, :],
                                                 axis=AX.X,
                                                 apply_absolute_value=True))
            tok_l[pi] = t1
            k.wait('v', t1)
            tacc = k.inc('v', nc.vector.tensor_add(acc[:, 0:1], acc[:, 0:1],
                                                   acc[:, 1:2]))
        k.wait('t', tacc, one_tok)
        tmm = k.inc('t', nc.tensor.matmul(pbank[0:1, 0:1], one_c[:, 0:1],
                                          acc[:, 0:1], start=True, stop=True))
        k.wait('v', tmm)
        t2 = k.inc('v', nc.vector.tensor_scalar(
            acc[0:1, 0:1], pbank[0:1, 0:1], 1.0 / (n_rows * n_cols), None,
            ALU.mult))
        k.wait('v', t2)
        t3 = k.inc('v', nc.vector.tensor_scalar_max(acc[0:1, 0:1],
                                                    acc[0:1, 0:1], 1e-5))
        k.wait('v', t3)
        t4 = k.inc('v', nc.vector.tensor_scalar(acc[0:1, 1:2], acc[0:1, 0:1],
                                                0.5, None, ALU.mult))
        k.wait('t', t4, onesrow_tok, t2)
        tb = k.inc('t', nc.tensor.matmul(pbank[0:PT, 0:2],
                                         onesrow[0:1, :], acc[0:1, 0:2],
                                         start=True, stop=True))
        k.wait('v', tb)
        tm = k.inc('v', nc.vector.tensor_copy(mcol[:, :], pbank[0:PT, 0:2]))
        tok_l = [tacc, tacc]
        stok = [None, None]
        tok_sgn = None
        last = None
        for rt in range(NRT):
            pi = rt % 2
            rsl = slice(rt * PT, (rt + 1) * PT)
            tw = k.dma('y', f'c_tq{pi}', ft[pi][:, :], w_ap[rsl, :],
                       wait=(tok_l[pi],))
            k.wait('s', tw, tok_sgn)
            ts1 = k.inc('s', nc.scalar.sign(sgn[:, :], ft[pi][:, :]))
            k.wait('s', ts1)
            ts2 = k.inc('s', nc.scalar.activation(ab[:, :], ft[pi][:, :], AF.Abs))
            tok_l[pi] = ts2
            k.wait('v', ts2, tm, tok_sgn)
            tv1 = k.inc('v', nc.vector.tensor_scalar(
                ab[:, :], ab[:, :], mcol[:, 1:2], None, ALU.is_ge))
            k.wait('v', tv1)
            tok_sgn = tb2 = k.inc('v', nc.vector.scalar_tensor_tensor(
                bt[:, :], sgn[:, :], mcol[:, 0:1], ab[:, :], ALU.mult, ALU.mult))
            for ct in range(n_cols // PT):
                si = ct % 2
                pbk, tmt, b = trp.start(bt[:, ct * PT:(ct + 1) * PT], tb2)
                k.wait('v', tmt, stok[si])
                te = k.inc('v', nc.vector.tensor_copy(stg[si][:, :], pbk[:, :]))
                trp.done(b, te)
                last = k.dma('s', f'c_tg{si}', wT_dram[ct, :, rsl], stg[si][:, :],
                             wait=(te,))
                stok[si] = last
            tok_sgn = tmax(tok_sgn, None)[0]
        return tmax(stok[0], stok[1])


def emit_a1a(k, cfg, io):
    nc = k.nc
    n_own = cfg['n_own']
    n_tiles = 2 * n_own
    H, DH = 16, 128
    ident = cfg['ident']
    pb = cfg['pbanks']
    trp = cfg['trp']
    x_in, x_out = io['x_in'], io['x_out']
    qkvT, projT = io['qkvT'], io['projT']
    w_tok = io.get('w_tok') or []
    onesrow, onesrow_tok = cfg['onesrow'], cfg['onesrow_tok']
    tok = {}
    wci = [0]

    if True:
        srow = k.sb("aa_srow", (1, 16), F32)
        sbc = k.sb("aa_sbc", (PT, 16), F32)
        Lt = k.sb("aa_Lt", (PT, 16), F32)
        Lrev = k.sb("aa_Lrev", (PT, 16), F32)
        LT = k.sb("aa_LT", (PT, 16), F32)
        Mh = k.sb("aa_Mh", (PT, 16 * PT), BF16)
        tmpm = k.sb("aa_tmpm", (PT, PT), F32)
        wnq = k.sb("aa_wnq", (PT, D), BF16)
        wnn = k.sb("aa_wnn", (PT, D), BF16)
        wnp = k.sb("aa_wnp", (PT, D), BF16)
        vrow = k.sb("aa_vrow", (1, D), F32)
        blend = k.sb("aa_blend", (PT, 2), F32)

        ts_ = k.dma('y', 'c_aa1', srow[0:1, :], io['s'].unsqueeze(0))
        tbl = k.dma('y', 'c_aa2', blend[:, :], io['blend'])
        k.wait('t', ts_, onesrow_tok)
        t1 = k.inc('t', nc.tensor.matmul(pb[0][0:PT, 0:16], onesrow[0:1, :],
                                         srow[0:1, :], start=True, stop=True))
        k.wait('v', t1)
        t2 = k.inc('v', nc.vector.tensor_copy(sbc[:, :], pb[0][0:PT, 0:16]))
        iota_ap, iota_tok = cfg['iota'], cfg['iota_tok']
        k.wait('v', t2, iota_tok)
        ta = k.inc('v', nc.vector.tensor_scalar(Lt[:, :], sbc[:, :],
                                                iota_ap[:, 0:1], None, ALU.mult))
        tb_ = k.inc('v', nc.vector.tensor_scalar(Lrev[:, :], sbc[:, :],
                                                 iota_ap[:, 1:2], None, ALU.mult))
        tc_ = k.inc('v', nc.vector.tensor_scalar(LT[:, :], sbc[:, :], 128.0,
                                                 None, ALU.mult))
        k.wait('s', ta, tb_, tc_)
        tLt = k.inc('s', nc.scalar.activation(Lt[:, :], Lt[:, :], AF.Exp,
                                              scale=-1.0))
        tLr = k.inc('s', nc.scalar.activation(Lrev[:, :], Lrev[:, :], AF.Exp,
                                              scale=-1.0))
        tLT = k.inc('s', nc.scalar.activation(LT[:, :], LT[:, :], AF.Exp,
                                              scale=-1.0))
        tMh = None
        for h in range(H):
            k.wait('v', t2, cfg['diff_tok'], tMh)
            tm1 = k.inc('v', nc.vector.scalar_tensor_tensor(
                tmpm[:, :], cfg['diff'][:, :], sbc[:, h:h + 1],
                cfg['upmask'][:, :], ALU.mult, ALU.add))
            k.wait('s', tm1)
            tMh = k.inc('s', nc.scalar.activation(Mh[:, h * PT:(h + 1) * PT],
                                                  tmpm[:, :], AF.Exp, scale=-1.0))
            tok['tmpm_r'] = tMh
        t_mh = tMh
        wn_toks = []
        for (win, dst) in ((io['qkv_norm_w'], wnq), (io['norm_w'], wnn),
                           (io['proj_norm_w'], wnp)):
            tv = k.dma('y', 'c_aa3', vrow[0:1, :], win.unsqueeze(0),
                       wait=(wn_toks[-1] if wn_toks else None,))
            for b4 in range(4):
                k.wait('t', tv, tok.get('wnps'))
                tmm = k.inc('t', nc.tensor.matmul(
                    pb[1][:, :], onesrow[0:1, :],
                    vrow[0:1, b4 * 512:(b4 + 1) * 512], start=True, stop=True))
                k.wait('v', tmm)
                te = k.inc('v', nc.vector.tensor_copy(
                    dst[:, b4 * 512:(b4 + 1) * 512], pb[1][:, :]))
                tok['wnps'] = te
            wn_toks.append(te)
        t_wn = tmax(*wn_toks)

    with k.scope():
        xt = k.sb("aa_xt", (PT, D), F32)
        xn = k.sb("aa_xn", (PT, D), BF16)
        sc = dict(st6=k.sb("aa_st6", (PT, 24), F32),
                  st2=k.sb("aa_st2", (PT, 2), F32),
                  rc=k.sb("aa_rc", (PT, 4), F32))
        xnT = k.sb("aa_xnT", (PT, 16 * 512), BF16)
        wch = [k.sb(f"aa_w{i}", (PT, 512), BF16) for i in range(2)]
        qg = k.sb("aa_qg", (PT, 4 * D), BF16)
        kg = k.sb("aa_kg", (PT, 4 * D), BF16)
        vg = k.sb("aa_vg", (PT, 4 * D), BF16)
        qp = k.sb("aa_qp", (PT, D), BF16)
        trb = k.sb("aa_trb", (PT, D), BF16)

        def rms_tile(src, dst, wbc, eps, src_tok, war_tok):
            k.wait('v', src_tok)
            for c in range(4):
                tt1 = k.inc('v', nc.vector.bn_stats(
                    sc['st6'][:, c * 6:(c + 1) * 6],
                    src[:, c * 512:(c + 1) * 512]))
            k.wait('v', tt1)
            tt2 = k.inc('v', nc.vector.bn_aggr(sc['st2'][:, :], sc['st6'][:, :]))
            k.wait('v', tt2)
            tt3 = k.inc('v', nc.vector.scalar_tensor_tensor(
                sc['rc'][:, 1:2], sc['st2'][:, 0:1], sc['st2'][:, 0:1],
                sc['st2'][:, 1:2], ALU.mult, ALU.add))
            eps_ap, eps_tok = k.constcol(eps)
            k.wait('s', tt3, eps_tok)
            tl = k.inc('s', nc.scalar.activation(sc['rc'][:, 0:1],
                                                 sc['rc'][:, 1:2], AF.Ln,
                                                 bias=eps_ap[:, 0:1]))
            k.wait('s', tl)
            tr = k.inc('s', nc.scalar.activation(sc['rc'][:, 0:1],
                                                 sc['rc'][:, 0:1], AF.Exp,
                                                 scale=-0.5))
            k.wait('v', tr, war_tok, *t_wn)
            tx1 = k.inc('v', nc.vector.tensor_scalar_mul(dst[:, :], src[:, :],
                                                         sc['rc'][:, 0:1]))
            k.wait('v', tx1)
            return k.inc('v', nc.vector.tensor_mul(dst[:, :], dst[:, :], wbc))

        def store_T(src, src_tok, dram_dst, chname, war_tok):
            tt = None
            for dt in range(16):
                tt = trp.transpose(src[:, dt * PT:(dt + 1) * PT],
                                   trb[:, dt * PT:(dt + 1) * PT],
                                   src_tok if dt == 0 else None,
                                   war_tok=war_tok if dt == 0 else None)
            return k.dma('s', chname, dram_dst, trb[:, :], wait=(tt,))

        ngrp = n_own // 4
        for g in range(ngrp):
            tiles = list(range(4 * g, 4 * g + 4))
            t_tr = None
            for jj, j in enumerate(tiles):
                tx = k.dma('y', 'c_xt', xt[:, :], x_in[j * PT:(j + 1) * PT, :],
                           wait=(tok.get('xt_r'),))
                txn = rms_tile(xt, xn, wnq[:, :], F32EPS, tx, tok.get('xn_r'))
                tok['xt_r'] = txn
                for dt in range(16):
                    t_tr = trp.transpose(
                        xn[:, dt * PT:(dt + 1) * PT],
                        xnT[:, dt * 512 + jj * PT: dt * 512 + (jj + 1) * PT],
                        txn if dt == 0 else None,
                        war_tok=tok.get('xnT_r') if (jj == 0 and dt == 0) else None)
                tok['xn_r'] = ('t', k.cnt['t'])
            for blk in range(12):
                dst = (qg, kg, vg)[blk // 4]
                bof = (blk % 4) * 512
                for dt in range(16):
                    wi = wci[0] % 2
                    wci[0] += 1
                    tw = k.dma('y', f'c_aw{wi}', wch[wi][:, :],
                               qkvT[dt, :, blk * 512:(blk + 1) * 512],
                               wait=(tok.get(('aw', wi)),
                                     *(w_tok if (blk == 0 and dt == 0 and g == 0)
                                       else [])))
                    k.wait('t', tw, t_tr, *[tok.get(('pg', q)) for q in range(4)])
                    for q4 in range(4):
                        mm = nc.tensor.matmul(
                            pb[q4][:, :],
                            xnT[:, dt * 512 + q4 * PT: dt * 512 + (q4 + 1) * PT],
                            wch[wi][:, :], start=(dt == 0), stop=(dt == 15))
                    tok[('aw', wi)] = k.inc('t', mm)
                tmm = tok[('aw', (wci[0] - 1) % 2)]
                for q4 in range(4):
                    k.wait('v', tmm, *(tok.get('qkv_r') or []))
                    te = k.inc('v', nc.vector.tensor_copy(
                        dst[:, q4 * D + bof: q4 * D + bof + 512], pb[q4][:, :]))
                    tok[('pg', q4)] = te
            t_proj = ('v', k.cnt['v'])
            for jj, j in enumerate(tiles):
                tq = store_T(qg[:, jj * D:(jj + 1) * D], t_proj, io['qT'][j],
                             'c_sq', tok.get('trb_r'))
                tok['trb_r'] = tq
                tk = store_T(kg[:, jj * D:(jj + 1) * D], t_proj, io['kT'][j],
                             'c_sk', tok.get('trb_r'))
                tok['trb_r'] = tk
                k.wait('v', tLt, tok.get('qp_r'))
                for h in range(H):
                    tqp = k.inc('v', nc.vector.tensor_scalar_mul(
                        qp[:, h * PT:(h + 1) * PT],
                        qg[:, jj * D + h * PT: jj * D + (h + 1) * PT],
                        Lt[:, h:h + 1]))
                tqs = store_T(qp, tqp, io['qpT'][j], 'c_sp', tok.get('trb_r'))
                tok['trb_r'] = tqs
                tok['qp_r'] = ('t', k.cnt['t'])
                k.wait('v', tLr, tok.get('qp_r'))
                for h in range(H):
                    tk2 = k.inc('v', nc.vector.tensor_scalar_mul(
                        qp[:, h * PT:(h + 1) * PT],
                        kg[:, jj * D + h * PT: jj * D + (h + 1) * PT],
                        Lrev[:, h:h + 1]))
                ta1 = k.dma('s', 'c_agi', io['ag_in'][j * PT:(j + 1) * PT, 0:D],
                            qp[:, :], wait=(tk2,))
                tok['qp_r'] = ta1
                ta2 = k.dma('s', 'c_agi2',
                            io['ag_in'][j * PT:(j + 1) * PT, D:2 * D],
                            vg[:, jj * D:(jj + 1) * D], wait=(t_proj,))
                tv2 = k.dma('s', 'c_sv', io['vT'][j], vg[:, jj * D:(jj + 1) * D],
                            wait=(t_proj,))
                tok.setdefault('qkv_stores', []).extend([tq, tk, ta1, ta2, tv2])
            tok['qkv_r'] = tmax(*tok.pop('qkv_stores'))
            tok['xnT_r'] = ('t', k.cnt['t'])

    k.barrier()
    ch = k.chan('c_col')
    inst = nc.gpsimd.collective_compute(
        "AllGather", ALU.bypass, cfg['replica_groups'],
        [io['ag_in'].opt()], [io['ag_out'].opt()])
    k.cnt[ch] += 1
    inst.then_inc(k.sem[ch], 1)
    t_ag = (ch, k.cnt[ch])

    with k.scope():
        S_sb = k.sb("aa_S", (PT, H * PT), F32)
        Sb = k.sb("aa_Sb", (PT, H * PT), BF16)
        o_sb = k.sb("aa_o", (PT, D), F32)
        o2_sb = k.sb("aa_o2", (PT, D), F32)
        kvc = k.sb("aa_kvc", (PT, 2 * D), BF16)
        qTt = k.sb("aa_qTt", (PT, D), BF16)
        qpTt = k.sb("aa_qpTt", (PT, D), BF16)
        kTt = k.sb("aa_kTt", (PT, D), BF16)
        vt = k.sb("aa_vt", (PT, D), BF16)
        msk = [k.sb(f"aa_msk{i}", (PT, PT), BF16) for i in range(2)]
        mskT = [k.sb(f"aa_mskT{i}", (PT, PT), BF16) for i in range(2)]
        y2 = k.sb("aa_y2", (PT, D), BF16)
        y2T = k.sb("aa_y2T", (PT, D), BF16)
        pch = [k.sb(f"aa_pw{i}", (PT, 512), BF16) for i in range(2)]
        xo = k.sb("aa_xo", (PT, D), F32)
        sc2 = dict(st6=k.sb("aa_st6b", (PT, 24), F32),
                   st2=k.sb("aa_st2b", (PT, 2), F32),
                   rc=k.sb("aa_rcb", (PT, 4), F32))

        tz1 = k.inc('v', nc.vector.memset(S_sb[:, :], 0.0))
        tz2 = k.inc('v', nc.vector.memset(Sb[:, :], 0.0))

        def rms_tile2(src, dst, wbc, eps, src_tok, war_tok):
            k.wait('v', src_tok)
            for c in range(4):
                tt1 = k.inc('v', nc.vector.bn_stats(
                    sc2['st6'][:, c * 6:(c + 1) * 6],
                    src[:, c * 512:(c + 1) * 512]))
            k.wait('v', tt1)
            tt2 = k.inc('v', nc.vector.bn_aggr(sc2['st2'][:, :], sc2['st6'][:, :]))
            k.wait('v', tt2)
            tt3 = k.inc('v', nc.vector.scalar_tensor_tensor(
                sc2['rc'][:, 1:2], sc2['st2'][:, 0:1], sc2['st2'][:, 0:1],
                sc2['st2'][:, 1:2], ALU.mult, ALU.add))
            eps_ap, eps_tok = k.constcol(eps)
            k.wait('s', tt3, eps_tok)
            tl = k.inc('s', nc.scalar.activation(sc2['rc'][:, 0:1],
                                                 sc2['rc'][:, 1:2], AF.Ln,
                                                 bias=eps_ap[:, 0:1]))
            k.wait('s', tl)
            tr = k.inc('s', nc.scalar.activation(sc2['rc'][:, 0:1],
                                                 sc2['rc'][:, 0:1], AF.Exp,
                                                 scale=-0.5))
            k.wait('v', tr, war_tok)
            tx1 = k.inc('v', nc.vector.tensor_scalar_mul(dst[:, :], src[:, :],
                                                         sc2['rc'][:, 0:1]))
            k.wait('v', tx1)
            return k.inc('v', nc.vector.tensor_mul(dst[:, :], dst[:, :], wbc))

        tst = None
        for j in range(n_own):
            tql = k.dma('y', 'c_lq', qTt[:, :], io['qT'][j],
                        wait=(tok.get('qTt_r'),))
            tqp = k.dma('y', 'c_lp', qpTt[:, :], io['qpT'][j],
                        wait=(tok.get('qpTt_r'),))
            tkl = k.dma('y', 'c_lk', kTt[:, :], io['kT'][j],
                        wait=(tok.get('kTt_r'),))
            tvl = k.dma('y', 'c_lv', vt[:, :], io['vT'][j],
                        wait=(tok.get('vt_r'),))
            for point in range(2):
                c = 2 * j + point
                for hg in range(4):
                    bankP = pb[2 + hg % 2]
                    for hh in range(4):
                        h = hg * 4 + hh
                        bi = h % 2
                        bankS = pb[bi]
                        k.wait('t', tql, tkl, tok.get(('sc_r', bi)))
                        mmS = k.inc('t', nc.tensor.matmul(
                            bankS[:, 0:PT], qTt[:, h * PT:(h + 1) * PT],
                            kTt[:, h * PT:(h + 1) * PT], start=True, stop=True))
                        k.wait('v', mmS, t_mh, tok.get(('msk_r', bi)))
                        tmk = k.inc('v', nc.vector.tensor_mul(
                            msk[bi][:, :], bankS[:, 0:PT],
                            Mh[:, h * PT:(h + 1) * PT]))
                        tok[('sc_r', bi)] = tmk
                        pbk, tmt, b = trp.start(msk[bi][:, :], tmk)
                        k.wait('v', tmt, tok.get(('mskT_r', bi)))
                        tmT = k.inc('v', nc.vector.tensor_copy(mskT[bi][:, :],
                                                               pbk[:, :]))
                        trp.done(b, tmT)
                        tok[('msk_r', bi)] = tmT
                        k.wait('t', tmT, tvl, tqp, tz2,
                               tok.get(('S_r', h)),
                               tok.get(('P_r', hg % 2)) if hh == 0 else None)
                        mm = nc.tensor.matmul(
                            bankP[:, hh * PT:(hh + 1) * PT], mskT[bi][:, :],
                            vt[:, h * PT:(h + 1) * PT], start=True,
                            stop=False, skip_group_check=True)
                        mm = nc.tensor.matmul(
                            bankP[:, hh * PT:(hh + 1) * PT],
                            qpTt[:, h * PT:(h + 1) * PT],
                            Sb[:, h * PT:(h + 1) * PT], start=False, stop=True,
                            skip_group_check=True)
                        tok[('mskT_r', bi)] = tP = k.inc('t', mm)
                        tok[('Sb_r', h)] = tP
                    k.wait('v', tP, tbl, tok.get('o_r') if hg == 0 and point == 0 else None)
                    if point == 0:
                        tev = k.inc('v', nc.vector.tensor_scalar_mul(
                            o_sb[:, hg * 512:(hg + 1) * 512], bankP[:, :],
                            blend[:, 0:1]))
                    else:
                        tev = k.inc('v', nc.vector.scalar_tensor_tensor(
                            o_sb[:, hg * 512:(hg + 1) * 512], bankP[:, :],
                            blend[:, 1:2], o_sb[:, hg * 512:(hg + 1) * 512],
                            ALU.mult, ALU.add))
                    tok[('P_r', hg % 2)] = tev
                t_o = tev
                row0 = (c % 2) * n_own * PT + (c // 2) * PT
                tkv = k.dma('y', 'c_kvc', kvc[:, :],
                            io['ag_out'][row0:row0 + PT, :],
                            wait=(t_ag, tok.get('kvc_r')))
                for h in range(H):
                    k.wait('t', tkv, tok.get('KV_r'))
                    mmK = k.inc('t', nc.tensor.matmul(
                        pb[4][:, 0:PT], kvc[:, h * PT:(h + 1) * PT],
                        kvc[:, D + h * PT: D + (h + 1) * PT],
                        start=True, stop=True))
                    k.wait('v', mmK, tLT, tz1,
                           tok.get(('S_r', h)))
                    tS = k.inc('v', nc.vector.scalar_tensor_tensor(
                        S_sb[:, h * PT:(h + 1) * PT],
                        S_sb[:, h * PT:(h + 1) * PT], LT[:, h:h + 1],
                        pb[4][:, 0:PT], ALU.mult, ALU.add))
                    tok['KV_r'] = tS
                    k.wait('v', tS, tok.get(('Sb_r', h)))
                    tSb = k.inc('v', nc.vector.tensor_copy(
                        Sb[:, h * PT:(h + 1) * PT],
                        S_sb[:, h * PT:(h + 1) * PT]))
                    tok[('S_r', h)] = tSb
                tok['kvc_r'] = ('t', k.cnt['t'])
            tok['qTt_r'] = ('t', k.cnt['t'])
            tok['kTt_r'] = tok['qTt_r']
            tok['qpTt_r'] = tok['qTt_r']
            tok['vt_r'] = tok['qTt_r']
            tr1 = rms_tile2(o_sb, o2_sb, wnn[:, :], 1e-6, t_o, tok.get('o2_r'))
            tok['o_r'] = tr1
            tr2 = rms_tile2(o2_sb, y2, wnp[:, :], F32EPS, tr1, tok.get('y2_r'))
            tok['o2_r'] = tr2
            tt = None
            for dt in range(16):
                tt = trp.transpose(y2[:, dt * PT:(dt + 1) * PT],
                                   y2T[:, dt * PT:(dt + 1) * PT],
                                   tr2 if dt == 0 else None,
                                   war_tok=tok.get('y2T_r') if dt == 0 else None)
            tok['y2_r'] = ('t', k.cnt['t'])
            tx = k.dma('y', 'c_xt3', o2_sb[:, :], x_in[j * PT:(j + 1) * PT, :],
                       wait=(tr2,))
            adds = []
            for dmb in range(4):
                bankD = pb[5]
                k.wait('t', tok.get('pD_r'))
                for ft in range(16):
                    wi = wci[0] % 2
                    wci[0] += 1
                    tw = k.dma('y', f'c_aw{wi}', pch[wi % 2][:, :],
                               projT[ft, :, dmb * 512:(dmb + 1) * 512],
                               wait=(tok.get(('aw2', wi % 2)),))
                    k.wait('t', tw, tt)
                    mm = nc.tensor.matmul(bankD[:, :],
                                          y2T[:, ft * PT:(ft + 1) * PT],
                                          pch[wi % 2][:, :],
                                          start=(ft == 0), stop=(ft == 15))
                    tok[('aw2', wi % 2)] = k.inc('t', mm)
                tmm = tok[('aw2', (wci[0] - 1) % 2)]
                tok['y2T_r'] = tmm
                k.wait('v', tmm, tx, tok.get('xo_r'))
                te = k.inc('v', nc.vector.tensor_tensor(
                    xo[:, dmb * 512:(dmb + 1) * 512],
                    o2_sb[:, dmb * 512:(dmb + 1) * 512], bankD[:, :], ALU.add))
                tok['pD_r'] = te
                adds.append(te)
            tok['o2_r'] = adds[-1]
            tst = k.dma('s', 'c_xo', x_out[j * PT:(j + 1) * PT, :], xo[:, :],
                        wait=tuple(tmax(*adds)))
            tok['xo_r'] = tst
    return tst


def build_f0_program(n_own=16):
    nc = bass.Bass("TRN2", target_bir_lowering=False, debug=False)
    HID = 5632
    x_in = nc.dram_tensor("x_shard", [n_own * PT, D], F32,
                          kind="ExternalInput").ap()
    wq_g = nc.dram_tensor("wq_gate", [HID, D], I32, kind="ExternalInput").ap()
    s_g = nc.dram_tensor("s_gate", [HID], F32, kind="ExternalInput").ap()
    wq_u = nc.dram_tensor("wq_up", [HID, D], I32, kind="ExternalInput").ap()
    s_u = nc.dram_tensor("s_up", [HID], F32, kind="ExternalInput").ap()
    wq_d = nc.dram_tensor("wq_down", [D, HID], I32, kind="ExternalInput").ap()
    s_d = nc.dram_tensor("s_down", [D], F32, kind="ExternalInput").ap()
    ident_in = nc.dram_tensor("ident", [PT, PT], BF16, kind="ExternalInput").ap()
    x_out = nc.dram_tensor("y_shard", [n_own * PT, D], F32,
                           kind="ExternalOutput").ap()
    wgT = nc.dram_tensor("wgT", [16, PT, HID], BF16, kind="Internal").ap()
    wuT = nc.dram_tensor("wuT", [16, PT, HID], BF16, kind="Internal").ap()
    wdT = nc.dram_tensor("wdT", [16, PT, 5632], BF16, kind="Internal").ap()

    with ExitStack() as ctx:
        k = K(nc, ctx)
        k.constcol(1e-5)
        ident = k.sb("ident_sb", (PT, PT), BF16)
        pbanks = [k.ps(f"pb{i}", (PT, 512), F32) for i in range(6)]
        ptr = [k.ps(f"ptr{i}", (PT, 1024), BF16) for i in range(2)]
        block = ctx.enter_context(nc.Block())

        @block.sync
        def _(e):
            trp = TrPool(k, ident, ptr[0][:, 0:PT], ptr[1][:, 0:PT])
            ti = k.dma('y', 'c_id', ident[:, :], ident_in)
            k.wait('t', ti)
            emit_f0_prep(k, trp, wq_g, s_g, wgT)
            emit_f0_prep(k, trp, wq_u, s_u, wuT)
            tw = emit_f0_prep(k, trp, wq_d, s_d, wdT, key_by_row=True)
            cfg = dict(n_own=n_own, ident=ident, pbanks=pbanks, trp=trp)
            io = dict(x_in=x_in, x_out=x_out, wgT=wgT, wuT=wuT, wdT=wdT,
                      w_tok=tw)
            tst = emit_f0(k, cfg, io)
            k.dma_fence('y')
    return nc



def host_rope_tables(positions, DH, base=10000.0):
    n_own = len(positions) // PT
    inv = base ** (-np.arange(0, DH, 2, dtype=np.float64) / DH)
    t = positions.astype(np.float64)[:, None] * inv[None, :]
    cos = np.cos(t).astype(np.float32)
    sin = np.sin(t).astype(np.float32)
    cosw = np.concatenate([cos, cos], -1)
    sinw = np.concatenate([-sin, sin], -1)

    def tilemaj(a):
        return np.ascontiguousarray(
            a.reshape(n_own, PT, DH).transpose(1, 0, 2).reshape(PT, n_own * DH))
    return tilemaj(cosw).astype(BFNP), tilemaj(sinw).astype(BFNP)


def host_dmask(par):
    tri = np.where(np.arange(PT)[None, :] <= np.arange(PT)[:, None], 0.0, NEG)
    m = np.zeros((PT, 2 * PT), np.float32)
    if par == 0:
        m[:, :PT] = tri
        m[:, PT:] = NEG
    else:
        m[:, :PT] = 0.0
        m[:, PT:] = tri
    return m.astype(np.float32)


def build_attn_program(n_own, n_tiles, nH, nKV, DH, scale, replica_groups,
                       combined_kv=False, rope_k=True):
    nc = bass.Bass("TRN2", target_bir_lowering=False, debug=False)
    QW, KW = nH * DH, nKV * DH
    x_in = nc.dram_tensor("x_shard", [n_own * PT, D], F32,
                          kind="ExternalInput").ap()
    if combined_kv:
        Wq = nc.dram_tensor("Wq", [D, QW], F32, kind="ExternalInput").ap()
        Wkv = nc.dram_tensor("Wkv", [D, 2 * KW], F32, kind="ExternalInput").ap()
        Wk, Wv = Wkv[:, 0:KW], Wkv[:, KW:2 * KW]
    else:
        Wq = nc.dram_tensor("Wq", [D, QW], F32, kind="ExternalInput").ap()
        Wk = nc.dram_tensor("Wk", [D, KW], F32, kind="ExternalInput").ap()
        Wv = nc.dram_tensor("Wv", [D, KW], F32, kind="ExternalInput").ap()
    Wo = nc.dram_tensor("Wo", [QW, D], F32, kind="ExternalInput").ap()
    tabc_in = nc.dram_tensor("tabc", [PT, n_own * DH], BF16,
                             kind="ExternalInput").ap()
    tabs_in = nc.dram_tensor("tabs", [PT, n_own * DH], BF16,
                             kind="ExternalInput").ap()
    dmask_in = nc.dram_tensor("dmask", [PT, 2 * PT], F32,
                              kind="ExternalInput").ap()
    ident_in = nc.dram_tensor("ident", [PT, PT], BF16,
                              kind="ExternalInput").ap()
    x_out = nc.dram_tensor("y_shard", [n_own * PT, D], F32,
                           kind="ExternalOutput").ap()
    qT = nc.dram_tensor("qT", [n_own, PT, QW], BF16, kind="Internal").ap()
    ag_in = nc.dram_tensor("ag_in", [n_own * PT, 2 * KW], BF16,
                           kind="Internal").ap()
    ag_out = nc.dram_tensor("ag_out", [n_tiles * PT, 2 * KW], BF16,
                            kind="Internal").ap()

    with ExitStack() as ctx:
        k = K(nc, ctx)
        k.constcol(1e-5)
        ident = k.sb("ident_sb", (PT, PT), BF16)
        tabc = k.sb("tabc_sb", (PT, n_own * DH), BF16)
        tabs = k.sb("tabs_sb", (PT, n_own * DH), BF16)
        dmask = k.sb("dmask_sb", (PT, 2 * PT), F32)
        pbanks = [k.ps(f"pb{i}", (PT, 512), F32) for i in range(6)]
        ptr = [k.ps(f"ptr{i}", (PT, 1024), BF16) for i in range(2)]
        block = ctx.enter_context(nc.Block())

        @block.sync
        def _(e):
            trp = TrPool(k, ident, ptr[0][:, 0:PT], ptr[1][:, 0:PT])
            ti = k.dma('y', 'c_id', ident[:, :], ident_in)
            tc = k.dma('y', 'c_id', tabc[:, :], tabc_in)
            ts = k.dma('y', 'c_id', tabs[:, :], tabs_in)
            tdm = k.dma('y', 'c_id', dmask[:, :], dmask_in)
            k.wait('t', tdm)
            k.wait('v', tdm)
            cfg = dict(n_tiles=n_tiles, own_tiles=list(range(n_own)),
                       n_own=n_own, nH=nH, nKV=nKV, DH=DH, scale=scale,
                       ident=ident, pbanks=pbanks, trp=trp,
                       tabc=tabc[:, :], tabs=tabs[:, :], tab_tok=ts,
                       dmask=dmask, dmask_tok=tdm,
                       replica_groups=replica_groups, rope_k=rope_k)
            io = dict(x_in=x_in, x_out=x_out, Wq=Wq, Wk=Wk, Wv=Wv, Wo=Wo,
                      qT=qT, ag_in=ag_in, ag_out=ag_out)
            emit_attn(k, cfg, io)
            k.dma_fence('y')
    return nc



def build_f1_program(n_own=16):
    nc = bass.Bass("TRN2", target_bir_lowering=False, debug=False)
    x_in = nc.dram_tensor("x_shard", [n_own * PT, D], F32,
                          kind="ExternalInput").ap()
    w_gate = nc.dram_tensor("w_gate", [D, 8], F32, kind="ExternalInput").ap()
    w_gu = nc.dram_tensor("w_gate_up", [8, D, D], F32, kind="ExternalInput").ap()
    w_dn = nc.dram_tensor("w_down", [8, 1024, D], F32, kind="ExternalInput").ap()
    w_sg = nc.dram_tensor("w_sh_gate_up", [D, 8192], F32,
                          kind="ExternalInput").ap()
    w_sd = nc.dram_tensor("w_sh_down", [4096, D], F32, kind="ExternalInput").ap()
    ident_in = nc.dram_tensor("ident", [PT, PT], BF16, kind="ExternalInput").ap()
    esel_in = nc.dram_tensor("esel", [8, 8 * PT], BF16, kind="ExternalInput").ap()
    x_out = nc.dram_tensor("y_shard", [n_own * PT, D], F32,
                           kind="ExternalOutput").ap()

    with ExitStack() as ctx:
        k = K(nc, ctx)
        k.constcol(1e-5)
        ident = k.sb("ident_sb", (PT, PT), BF16)
        identf = k.sb("identf_sb", (PT, PT), F32)
        esel = k.sb("esel_sb", (8, 8 * PT), BF16)
        pbanks = [k.ps(f"pb{i}", (PT, 512), F32) for i in range(6)]
        ptr = [k.ps(f"ptr{i}", (PT, 1024), BF16) for i in range(2)]
        block = ctx.enter_context(nc.Block())

        @block.sync
        def _(e):
            trp = TrPool(k, ident, ptr[0][:, 0:PT], ptr[1][:, 0:PT])
            ti = k.dma('y', 'c_id', ident[:, :], ident_in)
            tif = k.dma('g', 'c_id2', identf[:, :], ident_in)
            te = k.dma('y', 'c_id', esel[0:8, :], esel_in)
            k.wait('t', te)
            k.wait('t', tif)
            cfg = dict(n_own=n_own, ident=ident, identf=identf, pbanks=pbanks,
                       trp=trp, esel=esel, esel_tok=te)
            io = dict(x_in=x_in, x_out=x_out, w_gate=w_gate, w_gate_up=w_gu,
                      w_down=w_dn, w_sh_gate_up=w_sg, w_sh_down=w_sd)
            emit_f1(k, cfg, io)
            k.dma_fence('y')
    return nc


def host_esel():
    e = np.zeros((8, 8 * PT), dtype=BFNP)
    for i in range(8):
        e[i, i * PT:(i + 1) * PT] = 1
    return e



def build_a1a_program(n_own, n_tiles, replica_groups):
    nc = bass.Bass("TRN2", target_bir_lowering=False, debug=False)
    x_in = nc.dram_tensor("x_shard", [n_own * PT, D], F32,
                          kind="ExternalInput").ap()
    qkv_w = nc.dram_tensor("qkv_w", [3 * D, D], F32, kind="ExternalInput").ap()
    proj_w = nc.dram_tensor("proj_w", [D, D], F32, kind="ExternalInput").ap()
    qn = nc.dram_tensor("qkv_norm_w", [D], F32, kind="ExternalInput").ap()
    nw = nc.dram_tensor("norm_w", [D], F32, kind="ExternalInput").ap()
    pn = nc.dram_tensor("proj_norm_w", [D], F32, kind="ExternalInput").ap()
    s_in = nc.dram_tensor("s", [16], F32, kind="ExternalInput").ap()
    ident_in = nc.dram_tensor("ident", [PT, PT], BF16, kind="ExternalInput").ap()
    ones_in = nc.dram_tensor("onesrow", [1, PT], F32, kind="ExternalInput").ap()
    blend_in = nc.dram_tensor("blend", [PT, 2], F32, kind="ExternalInput").ap()
    iota_in = nc.dram_tensor("iota2", [PT, 2], F32, kind="ExternalInput").ap()
    diff_in = nc.dram_tensor("diff", [PT, PT], F32, kind="ExternalInput").ap()
    up_in = nc.dram_tensor("upmask", [PT, PT], F32, kind="ExternalInput").ap()
    x_out = nc.dram_tensor("y_shard", [n_own * PT, D], F32,
                           kind="ExternalOutput").ap()
    qkvT = nc.dram_tensor("qkvT", [16, PT, 3 * D], BF16, kind="Internal").ap()
    projT = nc.dram_tensor("projT", [16, PT, D], BF16, kind="Internal").ap()
    qT = nc.dram_tensor("qT_s", [n_own, PT, D], BF16, kind="Internal").ap()
    qpT = nc.dram_tensor("qpT_s", [n_own, PT, D], BF16, kind="Internal").ap()
    kT = nc.dram_tensor("kT_s", [n_own, PT, D], BF16, kind="Internal").ap()
    vT = nc.dram_tensor("vT_s", [n_own, PT, D], BF16, kind="Internal").ap()
    ag_in = nc.dram_tensor("ag_in", [n_own * PT, 2 * D], BF16,
                           kind="Internal").ap()
    ag_out = nc.dram_tensor("ag_out", [n_tiles * PT, 2 * D], BF16,
                            kind="Internal").ap()

    with ExitStack() as ctx:
        k = K(nc, ctx)
        k.constcol(1.0)
        k.constcol(1e-6)
        k.constcol(F32EPS)
        ident = k.sb("ident_sb", (PT, PT), BF16)
        onesrow = k.sb("ones_sb", (1, PT), F32)
        iota2 = k.sb("iota_sb", (PT, 2), F32)
        diff = k.sb("diff_sb", (PT, PT), F32)
        upm = k.sb("up_sb", (PT, PT), F32)
        pbanks = [k.ps(f"pb{i}", (PT, 512), F32) for i in range(6)]
        ptr = [k.ps(f"ptr{i}", (PT, 1024), BF16) for i in range(2)]
        block = ctx.enter_context(nc.Block())

        @block.sync
        def _(e):
            trp = TrPool(k, ident, ptr[0][:, 0:PT], ptr[1][:, 0:PT])
            ti = k.dma('y', 'c_id', ident[:, :], ident_in)
            to = k.dma('y', 'c_id', onesrow[0:1, :], ones_in)
            ti2 = k.dma('y', 'c_id', iota2[:, :], iota_in)
            td = k.dma('y', 'c_id', diff[:, :], diff_in)
            tu = k.dma('y', 'c_id', upm[:, :], up_in)
            k.wait('t', tu)
            k.wait('v', tu)
            t1 = emit_ternary_prep(k, trp, qkv_w, qkvT, onesrow, to, pbanks[5])
            t2 = emit_ternary_prep(k, trp, proj_w, projT, onesrow, to, pbanks[5])
            cfg = dict(n_own=n_own, ident=ident, pbanks=pbanks, trp=trp,
                       onesrow=onesrow, onesrow_tok=to, iota=iota2,
                       iota_tok=ti2, diff=diff, diff_tok=td, upmask=upm,
                       replica_groups=replica_groups)
            io = dict(x_in=x_in, x_out=x_out, qkvT=qkvT, projT=projT,
                      qT=qT, qpT=qpT, kT=kT, vT=vT, ag_in=ag_in, ag_out=ag_out,
                      s=s_in, blend=blend_in, qkv_norm_w=qn, norm_w=nw,
                      proj_norm_w=pn, w_tok=tmax(*(t1 + t2)))
            emit_a1a(k, cfg, io)
            k.dma_fence('y')
    return nc


def host_a1a_consts(par):
    t = np.arange(PT, dtype=np.float32)
    iota2 = np.stack([t, PT - t], 1).astype(np.float32)
    diff = (t[:, None] - t[None, :]).astype(np.float32)
    upm = np.where(t[None, :] <= t[:, None], 0.0, 1e30).astype(np.float32)
    scale = float(128 ** -0.5)
    blend = np.zeros((PT, 2), np.float32)
    blend[:, par] = scale
    ones = np.ones((1, PT), np.float32)
    return iota2, diff, upm, blend, ones



def build_a0b_program(n_own, n_tiles, replica_groups):
    nc = bass.Bass("TRN2", target_bir_lowering=False, debug=False)
    H, KV, DH = 16, 4, 128
    QW, KW = H * DH, KV * DH
    x_in = nc.dram_tensor("x_shard", [n_own * PT, D], F32,
                          kind="ExternalInput").ap()
    Wq = nc.dram_tensor("Wq", [D, QW], F32, kind="ExternalInput").ap()
    Wk = nc.dram_tensor("Wk", [D, KW], F32, kind="ExternalInput").ap()
    Wv = nc.dram_tensor("Wv", [D, KW], F32, kind="ExternalInput").ap()
    Wo = nc.dram_tensor("Wo", [QW, D], F32, kind="ExternalInput").ap()
    tabc_o = nc.dram_tensor("tabc_o", [PT, n_own * DH], BF16,
                            kind="ExternalInput").ap()
    tabs_o = nc.dram_tensor("tabs_o", [PT, n_own * DH], BF16,
                            kind="ExternalInput").ap()
    tabc_a = nc.dram_tensor("tabc_a", [PT, n_tiles * DH], BF16,
                            kind="ExternalInput").ap()
    tabs_a = nc.dram_tensor("tabs_a", [PT, n_tiles * DH], BF16,
                            kind="ExternalInput").ap()
    dmask_in = nc.dram_tensor("dmask", [PT, 2 * PT], F32,
                              kind="ExternalInput").ap()
    tri01_in = nc.dram_tensor("tri01", [PT, PT], BF16,
                              kind="ExternalInput").ap()
    utri_in = nc.dram_tensor("utri", [PT, PT], F32, kind="ExternalInput").ap()
    ident_in = nc.dram_tensor("ident", [PT, PT], BF16,
                              kind="ExternalInput").ap()
    blend_in = nc.dram_tensor("blend", [PT, 2], F32, kind="ExternalInput").ap()
    x_out = nc.dram_tensor("y_shard", [n_own * PT, D], F32,
                           kind="ExternalOutput").ap()
    qT = nc.dram_tensor("qT_s", [n_own, PT, QW], BF16, kind="Internal").ap()
    qhT = nc.dram_tensor("qhT_s", [n_own, PT, QW], BF16, kind="Internal").ap()
    khT = nc.dram_tensor("khT_s", [n_own, PT, KW], BF16, kind="Internal").ap()
    vT = nc.dram_tensor("vT_s", [n_own, PT, KW], BF16, kind="Internal").ap()
    ag_in = nc.dram_tensor("ag_in", [n_own * PT, 2 * KW], BF16,
                           kind="Internal").ap()
    ag_out = nc.dram_tensor("ag_out", [n_tiles * PT, 2 * KW], BF16,
                            kind="Internal").ap()

    with ExitStack() as ctx:
        k = K(nc, ctx)
        k.constcol(1.0)
        k.constcol(1e-5)
        ident = k.sb("ident_sb", (PT, PT), BF16)
        dmask = k.sb("dmask_sb", (PT, 2 * PT), F32)
        tri01 = k.sb("tri01_sb", (PT, PT), BF16)
        utri = k.sb("utri_sb", (PT, PT), F32)
        pbanks = [k.ps(f"pb{i}", (PT, 512), F32) for i in range(6)]
        ptr = [k.ps(f"ptr{i}", (PT, 1024), BF16) for i in range(2)]
        block = ctx.enter_context(nc.Block())

        @block.sync
        def _(e):
            trp = TrPool(k, ident, ptr[0][:, 0:PT], ptr[1][:, 0:PT])
            ti = k.dma('y', 'c_id', ident[:, :], ident_in)
            t5 = k.dma('y', 'c_id', dmask[:, :], dmask_in)
            t6 = k.dma('y', 'c_id', tri01[:, :], tri01_in)
            t7 = k.dma('y', 'c_id', utri[:, :], utri_in)
            k.wait('t', t7)
            k.wait('v', t7)
            k.wait('s', t7)
            cfg = dict(n_own=n_own, ident=ident, pbanks=pbanks, trp=trp,
                       tabc_o_d=tabc_o, tabs_o_d=tabs_o, tabc_a_d=tabc_a,
                       tabs_a_d=tabs_a, dmask=dmask, dmask_tok=t7,
                       tri01=tri01, tri_tok=t7, utri=utri, utri_tok=t7,
                       replica_groups=replica_groups)
            io = dict(x_in=x_in, x_out=x_out, Wq=Wq, Wk=Wk, Wv=Wv, Wo=Wo,
                      qT=qT, qhT=qhT, khT=khT, vT=vT, ag_in=ag_in,
                      ag_out=ag_out, blend=blend_in)
            emit_a0b(k, cfg, io)
            k.dma_fence('y')
    return nc


def host_a0b_consts(par):
    t = np.arange(PT)
    tri01 = (t[None, :] <= t[:, None]).astype(np.float32)
    utri = (t[None, :] >= t[:, None]).astype(np.float32)
    blend = np.zeros((PT, 2), np.float32)
    blend[:, par] = 0.5
    return tri01.astype(BFNP), np.ascontiguousarray(utri), blend


_PCACHE = {}


def _get_prog(key, builder):
    if key not in _PCACHE:
        _PCACHE[key] = builder()
    return _PCACHE[key]


def kernel(**inputs):
    n_own = 16
    n_tiles = 32
    RG = [[0, 1], [2, 3], [4, 5], [6, 7]]
    ident = np.eye(PT).astype(BFNP)
    x = np.asarray(inputs["L0_attn_a_x"], np.float32)

    rows_par = []
    for par in range(2):
        tiles = [2 * j + par for j in range(n_own)]
        rows_par.append(np.concatenate(
            [np.arange(t * PT, (t + 1) * PT) for t in tiles]))

    xs = [np.ascontiguousarray(x[c // 2][rows_par[c % 2]]) for c in range(8)]

    tab64 = [host_rope_tables(rows_par[par].astype(np.float64), 64)
             for par in range(2)]
    tab128_o = [host_rope_tables(rows_par[par].astype(np.float64), 128)
                for par in range(2)]
    tab128_a = host_rope_tables(np.arange(n_tiles * PT).astype(np.float64), 128)
    dmasks = [host_dmask(par) for par in range(2)]

    def run(nc, per_core):
        res = run_bass_kernel_spmd(nc, per_core, core_ids=list(range(8)))
        return [np.asarray(res.results[c]["y_shard"]) for c in range(8)]

    def attn_launch(kind):
        if kind == "a0a":
            nc = _get_prog("a0a", lambda: build_attn_program(
                n_own, n_tiles, 32, 8, 64, scale=64 ** -0.5,
                replica_groups=RG, combined_kv=True))
            maps = [dict(x_shard=xs[c], Wq=inputs["L0_attn_a_Wq"],
                         Wkv=inputs["L0_attn_a_Wkv"], Wo=inputs["L0_attn_a_Wo"],
                         tabc=tab64[c % 2][0], tabs=tab64[c % 2][1],
                         dmask=dmasks[c % 2], ident=ident) for c in range(8)]
        else:
            nc = _get_prog("a1b", lambda: build_attn_program(
                n_own, n_tiles, 16, 4, 128, scale=128 ** -0.5,
                replica_groups=RG))
            maps = [dict(x_shard=xs[c], Wq=inputs["L1_attn_b_Wq"],
                         Wk=inputs["L1_attn_b_Wk"], Wv=inputs["L1_attn_b_Wv"],
                         Wo=inputs["L1_attn_b_Wo"],
                         tabc=tab128_o[c % 2][0], tabs=tab128_o[c % 2][1],
                         dmask=dmasks[c % 2], ident=ident) for c in range(8)]
        return run(nc, maps)

    def f0_launch():
        nc = _get_prog("f0", lambda: build_f0_program(n_own=n_own))
        maps = [dict(x_shard=xs[c],
                     wq_gate=inputs["L0_ffn_wq_gate"],
                     s_gate=inputs["L0_ffn_s_gate"],
                     wq_up=inputs["L0_ffn_wq_up"], s_up=inputs["L0_ffn_s_up"],
                     wq_down=inputs["L0_ffn_wq_down"],
                     s_down=inputs["L0_ffn_s_down"], ident=ident)
                for c in range(8)]
        return run(nc, maps)

    def a0b_launch():
        nc = _get_prog("a0b", lambda: build_a0b_program(n_own, n_tiles, RG))
        maps = []
        for c in range(8):
            par = c % 2
            tri01, utri, blend = host_a0b_consts(par)
            maps.append(dict(
                x_shard=xs[c], Wq=inputs["L0_attn_b_Wq"],
                Wk=inputs["L0_attn_b_Wk"], Wv=inputs["L0_attn_b_Wv"],
                Wo=inputs["L0_attn_b_Wo"],
                tabc_o=tab128_o[par][0], tabs_o=tab128_o[par][1],
                tabc_a=tab128_a[0], tabs_a=tab128_a[1],
                dmask=dmasks[par], tri01=tri01, utri=utri, blend=blend,
                ident=ident))
        return run(nc, maps)

    def a1a_launch():
        nc = _get_prog("a1a", lambda: build_a1a_program(n_own, n_tiles, RG))
        maps = []
        for c in range(8):
            par = c % 2
            iota2, diff, upm, blend, ones = host_a1a_consts(par)
            maps.append(dict(
                x_shard=xs[c], qkv_w=inputs["L1_attn_a_qkv_w"],
                proj_w=inputs["L1_attn_a_proj_w"],
                qkv_norm_w=inputs["L1_attn_a_qkv_norm_w"],
                norm_w=inputs["L1_attn_a_norm_w"],
                proj_norm_w=inputs["L1_attn_a_proj_norm_w"],
                s=inputs["L1_attn_a_s"], ident=ident, onesrow=ones,
                blend=blend, iota2=iota2, diff=diff, upmask=upm))
        return run(nc, maps)

    def f1_launch():
        nc = _get_prog("f1", lambda: build_f1_program(n_own=n_own))
        maps = [dict(x_shard=xs[c], w_gate=inputs["L1_ffn_w_gate"],
                     w_gate_up=inputs["L1_ffn_w_gate_up"],
                     w_down=inputs["L1_ffn_w_down"],
                     w_sh_gate_up=inputs["L1_ffn_w_sh_gate_up"],
                     w_sh_down=inputs["L1_ffn_w_sh_down"], ident=ident,
                     esel=host_esel()) for c in range(8)]
        return run(nc, maps)

    seq = [lambda: attn_launch("a0a"), f0_launch, a0b_launch, f0_launch,
           a1a_launch, f1_launch, lambda: attn_launch("a1b"), f1_launch]
    for step in seq:
        xs = step()

    out = np.empty((4, 4096, D), np.float32)
    for c in range(8):
        out[c // 2][rows_par[c % 2]] = xs[c]
    return out


# version 04
# speedup vs baseline: 1.0421x; 1.0421x over previous
import numpy as np
import ml_dtypes
from contextlib import ExitStack, contextmanager

import concourse.bass as bass
import concourse.mybir as mybir
from concourse.bass_utils import run_bass_kernel_spmd

F32 = mybir.dt.float32
BF16 = mybir.dt.bfloat16
I32 = mybir.dt.int32
AF = mybir.ActivationFunctionType
ALU = mybir.AluOpType
AX = mybir.AxisListType
BFNP = ml_dtypes.bfloat16

D = 2048
B = 4
PT = 128
F32EPS = float(np.finfo(np.float32).eps)
NEG = -1e30


class K:

    def __init__(self, nc, ctx):
        self.nc = nc
        self.ctx = ctx
        self._ctxs = [ctx]
        self.eng = dict(v=nc.vector, s=nc.scalar, t=nc.tensor, g=nc.gpsimd,
                        y=nc.sync)
        self.sem = {}
        self.cnt = {}
        for nm in ("v", "s", "t", "g", "y", "col"):
            self.sem[nm] = ctx.enter_context(nc.semaphore("sem_" + nm))
            self.cnt[nm] = 0
        self.waited = {e: {} for e in self.eng}

    def barrier(self):
        toks = [(ch, v) for ch, v in self.cnt.items() if v > 0]
        for e in self.eng:
            self.wait(e, *toks)

    @contextmanager
    def scope(self):
        with ExitStack() as c:
            self._ctxs.append(c)
            try:
                yield c
            finally:
                self.barrier()
                self._ctxs.pop()

    def inc(self, ch, inst, n=1):
        self.cnt[ch] += n
        inst.then_inc(self.sem[ch], n)
        return (ch, self.cnt[ch])

    def wait(self, e, *toks):
        for tk in toks:
            if tk is None:
                continue
            ch, v = tk
            w = self.waited[e]
            if w.get(ch, 0) < v:
                self.eng[e].wait_ge(self.sem[ch], v)
                w[ch] = v

    def chan(self, name):
        if name not in self.sem:
            self.sem[name] = self._ctxs[0].enter_context(
                self.nc.semaphore("sem_" + name))
            self.cnt[name] = 0
        return name

    def dma(self, q, ch, out, in_, wait=(), **kw):
        ch = self.chan(ch)
        self.wait(q, *[t for t in wait if t is not None])
        inst = self.eng[q].dma_start(out=out, in_=in_, **kw)
        self.cnt[ch] += 16
        inst.then_inc(self.sem[ch], 16)
        return (ch, self.cnt[ch])

    def dma_fence(self, e='y'):
        for ch in list(self.sem):
            if ch.startswith("c_") and self.cnt[ch] > 0:
                self.wait(e, (ch, self.cnt[ch]))

    def sb(self, name, shape, dt, root=False):
        self._uid = getattr(self, "_uid", 0) + 1
        c = self._ctxs[0] if root else self._ctxs[-1]
        return c.enter_context(
            self.nc.sbuf_tensor(f"{name}_{self._uid}", list(shape), dt))

    def constcol(self, val):
        if not hasattr(self, "_consts"):
            self._consts = {}
        if val not in self._consts:
            t = self.sb(f"const", (128, 1), F32, root=True)
            tok = self.inc('v', self.nc.vector.memset(t[:, :], float(val)))
            self._consts[val] = (t, tok)
        return self._consts[val]

    def ps(self, name, shape, dt=F32):
        self._uid = getattr(self, "_uid", 0) + 1
        return self._ctxs[-1].enter_context(
            self.nc.psum_tensor(f"{name}_{self._uid}", list(shape), dt))


def tmax(*toks):
    best = {}
    for t in toks:
        if t is None:
            continue
        ch, v = t
        if best.get(ch, -1) < v:
            best[ch] = v
    return [(ch, v) for ch, v in best.items()]


def emit_ln(k, xt, xn_out, sc, x_tok, war_tok=None, eps=1e-5):
    nc = k.nc
    F = xt.shape[1]
    nch = max(1, F // 512)
    k.wait('v', x_tok)
    fsz = F // nch
    for c in range(nch):
        t1 = k.inc('v', nc.vector.bn_stats(
            sc['st6'][:, c * 6:(c + 1) * 6],
            xt[:, c * fsz:(c + 1) * fsz]))
    k.wait('v', t1)
    t2 = k.inc('v', nc.vector.bn_aggr(sc['st2'][:, :], sc['st6'][:, :3 * 2 * nch]))
    eps_ap, eps_tok = k.constcol(eps)
    k.wait('s', t2, eps_tok)
    tl = k.inc('s', nc.scalar.activation(sc['rc'][:, 0:1], sc['st2'][:, 1:2],
                                         AF.Ln, bias=eps_ap[:, 0:1]))
    k.wait('s', tl)
    tr = k.inc('s', nc.scalar.activation(sc['rc'][:, 0:1], sc['rc'][:, 0:1],
                                         AF.Exp, scale=-0.5))
    k.wait('v', tr, war_tok)
    tx = k.inc('v', nc.vector.tensor_scalar(
        xn_out[:, :], xt[:, :], sc['st2'][:, 0:1], sc['rc'][:, 0:1],
        ALU.subtract, ALU.mult))
    return tx


class TrPool:

    def __init__(self, k, ident, pb0, pb1):
        self.k = k
        self.ident = ident
        self.banks = [pb0, pb1]
        self.busy = [None, None]
        self.i = 0

    def start(self, src_ap, *toks):
        k, nc = self.k, self.k.nc
        b = self.i % 2
        self.i += 1
        k.wait('t', self.busy[b], *toks)
        tm = k.inc('t', nc.tensor.transpose(self.banks[b][:, :], src_ap,
                                            self.ident[:, :]))
        return self.banks[b], tm, b

    def done(self, b, tok):
        self.busy[b] = tok

    def transpose(self, src_ap, dst_ap, src_tok, evac='v', war_tok=None):
        k, nc = self.k, self.k.nc
        pbk, tm, b = self.start(src_ap, src_tok)
        k.wait(evac, tm, war_tok)
        if evac == 'v':
            te = k.inc('v', nc.vector.tensor_copy(dst_ap, pbk[:, :]))
        else:
            te = k.inc('s', nc.scalar.copy(dst_ap, pbk[:, :]))
        self.busy[b] = te
        return te


def emit_f0_prep(k, trp, wq_ap, s_ap, wT_dram, key_by_row=False):
    nc = k.nc
    n_rows, n_cols = wq_ap.shape
    CC = 512
    s2d = s_ap.rearrange("(t p) -> t p", p=PT)
    with k.scope():
        it = [k.sb(f"dq_i{i}", (PT, CC), I32) for i in range(2)]
        ft = [k.sb(f"dq_f{i}", (PT, CC), F32) for i in range(2)]
        bt = [k.sb(f"dq_b{i}", (PT, CC), BF16) for i in range(2)]
        scol = k.sb("dq_s", (PT, 1), F32)
        stg = [k.sb(f"dq_g{i}", (PT, PT), BF16) for i in range(2)]
        tok_f = [None, None]
        tok_b = [None, None]
        tok_s = None
        stok = [None, None]
        last = None
        for rt in range(n_rows // PT):
            rsl = slice(rt * PT, (rt + 1) * PT)
            ts = k.dma('y', 'c_ps', scol[:, :], s2d[rt].unsqueeze(1),
                       wait=(tok_s,))
            for cc in range(n_cols // CC):
                pi = (rt * (n_cols // CC) + cc) % 2
                csl = slice(cc * CC, (cc + 1) * CC)
                tw = k.dma('y', f'c_pi{pi}', it[pi][:, :], wq_ap[rsl, csl], wait=(tok_f[pi],))
                k.wait('v', tw, tok_b[pi])
                tf = k.inc('v', nc.vector.tensor_copy(ft[pi][:, :], it[pi][:, :]))
                tok_f[pi] = tf
                k.wait('v', tf, ts)
                tok_s = tb = k.inc('v', nc.vector.tensor_scalar(
                    bt[pi][:, :], ft[pi][:, :], -128.0, scol[:, :],
                    ALU.add, ALU.mult))
                for ct in range(CC // PT):
                    si = ct % 2
                    pbk, tm, b = trp.start(bt[pi][:, ct * PT:(ct + 1) * PT], tb)
                    k.wait('v', tm, stok[si])
                    te = k.inc('v', nc.vector.tensor_copy(stg[si][:, :], pbk[:, :]))
                    trp.done(b, te)
                    gct = cc * (CC // PT) + ct
                    if key_by_row:
                        dst = wT_dram[rt, :, gct * PT:(gct + 1) * PT]
                    else:
                        dst = wT_dram[gct, :, rsl]
                    last = k.dma('s', f'c_pg{si}', dst, stg[si][:, :], wait=(te,))
                    stok[si] = last
                tok_b[pi] = last
        return tmax(stok[0], stok[1])


def emit_f0(k, cfg, io):
    nc = k.nc
    n_own = cfg['n_own']
    ngrp = n_own // 4
    HID = 5632
    NH = HID // PT
    NHC = NH * PT // 512
    ident = cfg['ident']
    pb = cfg['pbanks']
    trp = cfg['trp']
    wgT, wuT, wdT = io['wgT'], io['wuT'], io['wdT']
    x_in, x_out = io['x_in'], io['x_out']
    w_tok = io.get('w_tok')

    with k.scope():
        xt = k.sb("f0_xt", (PT, D), F32)
        xn = k.sb("f0_xn", (PT, D), BF16)
        sc = dict(st6=k.sb("f0_st6", (PT, 24), F32),
                  st2=k.sb("f0_st2", (PT, 2), F32),
                  rc=k.sb("f0_rc", (PT, 2), F32))
        xnT = k.sb("f0_xnT", (PT, 16 * 512), BF16)
        wg_sb = k.sb("f0_wg", (PT, 16 * 512), BF16)
        wu_sb = k.sb("f0_wu", (PT, 16 * 512), BF16)
        wd_sb = k.sb("f0_wd", (PT, NH * PT), BF16)
        sg_sb = k.sb("f0_sg", (PT, 512), BF16)
        su_sb = k.sb("f0_su", (PT, 512), BF16)
        actT = k.sb("f0_act", (PT, NH * 512), BF16)
        outT = k.sb("f0_outT", (PT, 16 * 512), BF16)
        xo = k.sb("f0_xo", (PT, D), F32)

        tok_bank = {}
        tok_xt_r = None
        tok_xn_r = None
        tok_xnT_r = None
        tok_act_w = None
        tok_act_r = None
        tok_sg_r = None
        tok_su_r = None
        tok_outT_w = {}
        tok_outT_r = None
        tok_xo_r = None
        tok_wg_r = None
        tok_wu_r = None
        tok_wd_r = None
        tst = None

        for g in range(ngrp):
            tiles = list(range(4 * g, 4 * g + 4))
            t_tr_last = None
            for jj, tile in enumerate(tiles):
                tx = k.dma('y', 'c_xt', xt[:, :], x_in[tile * PT:(tile + 1) * PT, :],
                           wait=(tok_xt_r,))
                txn = emit_ln(k, xt, xn, sc, tx, war_tok=tok_xn_r)
                tok_xt_r = txn
                for dt in range(16):
                    t_tr_last = trp.transpose(
                        xn[:, dt * PT:(dt + 1) * PT],
                        xnT[:, dt * 512 + jj * PT: dt * 512 + (jj + 1) * PT],
                        txn if dt == 0 else None,
                        war_tok=tok_xnT_r if (jj == 0 and dt == 0) else None)
                tok_xn_r = ('t', k.cnt['t'])
            tok_xnT_w = t_tr_last

            for hc in range(NHC):
                for dt in range(16):
                    wtoks = ((w_tok or [None])
                             if (g == 0 and hc == 0 and dt == 0) else [None])
                    twg = k.dma('y', 'c_wg', wg_sb[:, dt * 512:(dt + 1) * 512],
                                wgT[dt, :, hc * 512:(hc + 1) * 512],
                                wait=(tok_wg_r, *wtoks))
                    twu = k.dma('y', 'c_wu', wu_sb[:, dt * 512:(dt + 1) * 512],
                                wuT[dt, :, hc * 512:(hc + 1) * 512],
                                wait=(tok_wu_r,))
                for hs4 in range(4):
                    hsg = hc * 4 + hs4
                    bg, bu = hs4 % 2, 2 + hs4 % 2
                    k.wait('t', twg, twu, tok_xnT_w,
                           tok_bank.get(bg), tok_bank.get(bu))
                    for dt in range(16):
                        mm = nc.tensor.matmul(
                            pb[bg][:, :],
                            wg_sb[:, dt * 512 + hs4 * PT: dt * 512 + (hs4 + 1) * PT],
                            xnT[:, dt * 512:(dt + 1) * 512],
                            start=(dt == 0), stop=(dt == 15))
                    tmg = k.inc('t', mm)
                    for dt in range(16):
                        mm = nc.tensor.matmul(
                            pb[bu][:, :],
                            wu_sb[:, dt * 512 + hs4 * PT: dt * 512 + (hs4 + 1) * PT],
                            xnT[:, dt * 512:(dt + 1) * 512],
                            start=(dt == 0), stop=(dt == 15))
                    tmu = k.inc('t', mm)
                    tok_xnT_r = tmu
                    k.wait('s', tmg, tok_sg_r)
                    tsg = k.inc('s', nc.scalar.activation(sg_sb[:, :],
                                                          pb[bg][:, :], AF.Silu))
                    tok_bank[bg] = tsg
                    k.wait('v', tmu, tok_su_r)
                    tsu = k.inc('v', nc.vector.tensor_copy(su_sb[:, :],
                                                           pb[bu][:, :]))
                    tok_bank[bu] = tsu
                    k.wait('v', tsg, tsu, tok_act_r if hsg == 0 and g > 0 else None)
                    ta = k.inc('v', nc.vector.tensor_mul(
                        actT[:, hsg * 512:(hsg + 1) * 512], sg_sb[:, :],
                        su_sb[:, :]))
                    tok_sg_r = tok_su_r = ta
                tok_wg_r = tmg
                tok_wu_r = tmu
            tok_act_w = ta

            for dmt in range(16):
                twd = k.dma('y', 'c_wd', wd_sb[:, :], wdT[dmt], wait=(tok_wd_r,))
                bd = 4 + dmt % 2
                k.wait('t', twd, tok_act_w, tok_bank.get(bd))
                for ht in range(NH):
                    mm = nc.tensor.matmul(pb[bd][:, :],
                                          wd_sb[:, ht * PT:(ht + 1) * PT],
                                          actT[:, ht * 512:(ht + 1) * 512],
                                          start=(ht == 0), stop=(ht == NH - 1))
                tmd = k.inc('t', mm)
                tok_wd_r = tmd
                tok_act_r = tmd
                k.wait('v', tmd, tok_outT_r)
                tev = k.inc('v', nc.vector.tensor_copy(
                    outT[:, dmt * 512:(dmt + 1) * 512], pb[bd][:, :]))
                tok_bank[bd] = tev
                tok_outT_w[dmt] = tev

            for jj, tile in enumerate(tiles):
                tx = k.dma('y', 'c_xt', xt[:, :], x_in[tile * PT:(tile + 1) * PT, :],
                           wait=(tok_xt_r,))
                adds = []
                for dmt in range(16):
                    pbk, tm, b = trp.start(
                        outT[:, dmt * 512 + jj * PT: dmt * 512 + (jj + 1) * PT],
                        tok_outT_w[dmt])
                    k.wait('v', tm, tx, tok_xo_r)
                    te = k.inc('v', nc.vector.tensor_tensor(
                        xo[:, dmt * PT:(dmt + 1) * PT],
                        xt[:, dmt * PT:(dmt + 1) * PT],
                        pbk[:, :], ALU.add))
                    trp.done(b, te)
                    adds.append(te)
                tok_xt_r = adds[-1]
                tok_outT_r = ('t', k.cnt['t'])
                tst = k.dma('s', 'c_xo', x_out[tile * PT:(tile + 1) * PT, :],
                            xo[:, :], wait=tuple(tmax(*adds)))
                tok_xo_r = tst
        return tst



def emit_attn(k, cfg, io):
    nc = k.nc
    n_tiles = cfg['n_tiles']
    own = cfg['own_tiles']
    n_own = len(own)
    nH, nKV, DH = cfg['nH'], cfg['nKV'], cfg['DH']
    QW, KW = nH * DH, nKV * DH
    half = DH // 2
    scale = cfg['scale']
    ident = cfg['ident']
    trp = cfg['trp']
    pb = cfg['pbanks']
    tabc, tabs = cfg['tabc'], cfg['tabs']
    x_in, x_out = io['x_in'], io['x_out']
    g_per_h = nH // nKV
    NHD = QW // PT

    with k.scope():
        xt = k.sb("at_xt", (PT, D), F32)
        xn = k.sb("at_xn", (PT, D), BF16)
        sc = dict(st6=k.sb("at_st6", (PT, 24), F32),
                  st2=k.sb("at_st2", (PT, 2), F32),
                  rc=k.sb("at_rc", (PT, 2), F32))
        xnT = k.sb("at_xnT", (PT, 16 * 512), BF16)
        wch = [k.sb(f"at_w{i}", (PT, 512), BF16) for i in range(2)]
        qg = k.sb("at_qg", (PT, 4 * QW), BF16)
        kvg = k.sb("at_kvg", (PT, 4 * 2 * KW), BF16)
        rop = k.sb("at_rop", (PT, QW), BF16)
        rop2 = k.sb("at_rop2", (PT, QW), BF16)
        qTt = k.sb("at_qTt", (PT, QW), BF16)

        tok_xt_r = None
        tok_xn_r = None
        tok_xnT_r = None
        tok_w_r = [None, None]
        tok_qg_r = None
        tok_kvg_r = None
        tok_rop_r = None
        tok_rop2_r = None
        tok_qT_r = None
        tok_bank = {}
        wci = [0]

        def proj_group(Wsrc, width, dst, dst_off, dst_stride, war_tok, xnT_tok,
                       ntl):
            tevs = []
            for blk in range(width // 512):
                for dt in range(16):
                    wi = wci[0] % 2
                    wci[0] += 1
                    tw = k.dma('g', f'c_w{wi}', wch[wi][:, :],
                               Wsrc[dt * PT:(dt + 1) * PT,
                                    blk * 512:(blk + 1) * 512],
                               wait=(tok_w_r[wi],))
                    k.wait('t', tw, xnT_tok,
                           *[tok_bank.get(jj) for jj in range(ntl)])
                    for jj in range(ntl):
                        mm = nc.tensor.matmul(
                            pb[jj][:, :],
                            xnT[:, dt * 512 + jj * PT: dt * 512 + (jj + 1) * PT],
                            wch[wi][:, :], start=(dt == 0), stop=(dt == 15))
                    tok_w_r[wi] = k.inc('t', mm)
                tmm = tok_w_r[(wci[0] - 1) % 2]
                for jj in range(ntl):
                    k.wait('v', tmm, war_tok if (blk == 0 and jj == 0) else None)
                    tev = k.inc('v', nc.vector.tensor_copy(
                        dst[:, dst_off + jj * dst_stride + blk * 512:
                            dst_off + jj * dst_stride + (blk + 1) * 512],
                        pb[jj][:, :]))
                    tok_bank[jj] = tev
                    tevs.append(tev)
            return tevs

        def rope(src_ap, dst_ap, gj, width, nheads, src_tok, war_tok, war2):
            base = src_ap
            sw4 = bass.AP(base.tensor, base.offset + half,
                          [list(base.ap[0]), [DH, nheads], [-half, 2],
                           [1, half]])
            tc = tabc[:, gj * DH:(gj + 1) * DH]
            ts_ = tabs[:, gj * DH:(gj + 1) * DH]
            cosb = bass.AP(tc.tensor, tc.offset,
                           [list(tc.ap[0]), [0, nheads], [1, DH]])
            sinb = bass.AP(ts_.tensor, ts_.offset,
                           [list(ts_.ap[0]), [0, nheads], [half, 2], [1, half]])
            d4 = dst_ap.rearrange("p (h s x) -> p h s x", h=nheads, s=2)
            d3 = dst_ap.rearrange("p (h x) -> p h x", h=nheads)
            s3 = src_ap.rearrange("p (h x) -> p h x", h=nheads)
            k.wait('v', src_tok, war_tok, cfg['tab_tok'])
            t1 = k.inc('v', nc.vector.tensor_tensor(d4, sw4, sinb, ALU.mult))
            k.wait('v', t1, war2)
            t2 = k.inc('v', nc.vector.tensor_tensor(
                rop2[:, :width].rearrange("p (h x) -> p h x", h=nheads),
                s3, cosb, ALU.mult))
            k.wait('v', t2)
            t3 = k.inc('v', nc.vector.tensor_tensor(
                dst_ap, dst_ap, rop2[:, :width], ALU.add))
            return t3

        for g in range((n_own + 3) // 4):
            tiles = list(range(4 * g, min(4 * g + 4, n_own)))
            ntl = len(tiles)
            t_tr = None
            for jj, j in enumerate(tiles):
                rsl = slice(j * PT, (j + 1) * PT)
                tx = k.dma('y', 'c_xt', xt[:, :], x_in[rsl, :], wait=(tok_xt_r,))
                txn = emit_ln(k, xt, xn, sc, tx, war_tok=tok_xn_r)
                tok_xt_r = txn
                for dt in range(16):
                    t_tr = trp.transpose(
                        xn[:, dt * PT:(dt + 1) * PT],
                        xnT[:, dt * 512 + jj * PT: dt * 512 + (jj + 1) * PT],
                        txn if dt == 0 else None,
                        war_tok=tok_xnT_r if (jj == 0 and dt == 0) else None)
                tok_xn_r = ('t', k.cnt['t'])
            tq_evs = proj_group(io['Wq'], QW, qg, 0, QW, tok_qg_r, t_tr, ntl)
            tk_evs = proj_group(io['Wk'], KW, kvg, 0, 2 * KW, tok_kvg_r, t_tr,
                                ntl)
            tv_evs = proj_group(io['Wv'], KW, kvg, KW, 2 * KW, None, t_tr, ntl)
            tok_xnT_r = ('t', k.cnt['t'])
            tq_all = tmax(*tq_evs)
            tk_all = tmax(*tk_evs)
            tv_all = tmax(*tv_evs)
            for jj, j in enumerate(tiles):
                trq = rope(qg[:, jj * QW:(jj + 1) * QW], rop[:, :QW], j, QW,
                           nH, tq_all[0] if tq_all else None, tok_rop_r,
                           tok_rop2_r)
                tt = None
                for dqs in range(QW // PT):
                    tt = trp.transpose(rop[:, dqs * PT:(dqs + 1) * PT],
                                       qTt[:, dqs * PT:(dqs + 1) * PT],
                                       trq if dqs == 0 else None,
                                       war_tok=tok_qT_r if dqs == 0 else None)
                tok_rop_r = ('t', k.cnt['t'])
                tok_rop2_r = trq
                tqs = k.dma('s', 'c_qTs', io['qT'][j], qTt[:, :], wait=(tt,))
                tok_qT_r = tqs
                if cfg.get('rope_k', True):
                    trk = rope(kvg[:, jj * 2 * KW: jj * 2 * KW + KW],
                               rop[:, :KW], j, KW, nKV,
                               tk_all[0] if tk_all else None, tok_rop_r,
                               tok_rop2_r)
                    k.wait('v', trk)
                    tcp = k.inc('v', nc.vector.tensor_copy(
                        kvg[:, jj * 2 * KW: jj * 2 * KW + KW], rop[:, :KW]))
                    tok_rop_r = tcp
                    tok_rop2_r = trk
                    tkv = tmax(tcp, *tv_evs)
                else:
                    tkv = tmax(*(tk_evs + tv_evs))
                tst = k.dma('s', 'c_agi', io['ag_in'][j * PT:(j + 1) * PT, :],
                            kvg[:, jj * 2 * KW:(jj + 1) * 2 * KW],
                            wait=tuple(tkv))
                tok_kvg_r = tst
            tok_qg_r = ('v', k.cnt['v'])

    k.barrier()
    ch = k.chan('c_col')
    inst = nc.gpsimd.collective_compute(
        "AllGather", ALU.bypass, cfg['replica_groups'],
        [io['ag_in'].opt()], [io['ag_out'].opt()])
    k.cnt[ch] += 1
    inst.then_inc(k.sem[ch], 1)
    t_ag = (ch, k.cnt[ch])

    with k.scope():
        kT = k.sb("at_kT", (PT, nKV * n_tiles * PT), BF16)
        vf = k.sb("at_vf", (PT, n_tiles * KW), BF16)
        kvt = k.sb("at_kvt", (PT, 2 * KW), BF16)
        strip = k.sb("at_strip", (PT, n_tiles * PT), F32)
        exps = k.sb("at_exps", (PT, n_tiles * PT), BF16)
        mx = k.sb("at_mx", (PT, 2), F32)
        lsum = k.sb("at_l", (PT, 2), F32)
        drl = k.sb("at_drl", (PT, PT), BF16)
        att = [k.sb(f"at_att{i}", (PT, 512), BF16) for i in range(2)]
        qTt2 = k.sb("at_qT2", (PT, QW), BF16)
        oT = k.sb("at_oT", (PT, NHD * 512), BF16)
        woch = [k.sb(f"at_wo{i}", (PT, 512), BF16) for i in range(2)]
        outT = k.sb("at_outT", (PT, 16 * 512), BF16)
        xt2 = k.sb("at_xt2", (PT, D), F32)
        xo = k.sb("at_xo", (PT, D), F32)

        tok_kvt_r = None
        last_tr = None
        tv = None
        for kb in range(n_tiles):
            par, idx = kb % 2, kb // 2
            row0 = par * (n_tiles // 2) * PT + idx * PT
            tl = k.dma('y', 'c_kvt', kvt[:, :],
                       io['ag_out'][row0:row0 + PT, :],
                       wait=(t_ag, *(tok_kvt_r or [])))
            k.wait('v', tl)
            tv = k.inc('v', nc.vector.tensor_copy(
                vf[:, kb * KW:(kb + 1) * KW], kvt[:, KW:2 * KW]))
            for s in range(KW // PT):
                if DH == PT:
                    last_tr = trp.transpose(
                        kvt[:, s * PT:(s + 1) * PT],
                        kT[:, (s * n_tiles + kb) * PT:(s * n_tiles + kb + 1) * PT],
                        tl if s == 0 else None)
                else:
                    pbk, tm, b = trp.start(kvt[:, s * PT:(s + 1) * PT],
                                           tl if s == 0 else None)
                    k.wait('v', tm)
                    tes = []
                    for hh in range(2):
                        kvh = 2 * s + hh
                        c0 = (kvh * n_tiles + kb) * PT
                        for bb in range(2):
                            tes.append(k.inc('v', nc.vector.tensor_copy(
                                kT[bb * DH:(bb + 1) * DH, c0:c0 + PT],
                                pbk[hh * DH:(hh + 1) * DH, 0:PT])))
                    last_tr = tes[-1]
                    trp.done(b, last_tr)
            tok_kvt_r = tmax(tv, ('t', k.cnt['t']))
        tok_kv_ready = tmax(tv, last_tr, ('t', k.cnt['t']))

        tok_qT2_r = None
        tok_strip_r = None
        tok_exps_r = None
        tok_lsum_r = None
        tok_drl_r = None
        tok_oT_r = None
        tok_wo_r = [None, None]
        tok_outT_r = None
        tok_xt2_r = None
        tok_xo_r = None
        tok_b2 = {}
        woi = [0]
        tst = None

        for g in range((n_own + 3) // 4):
            tiles = list(range(4 * g, min(4 * g + 4, n_own)))
            t_oT_w = []
            for jj, j in enumerate(tiles):
                L = (2 * j + 2) * PT
                nkb = 2 * j + 2
                tql = k.dma('y', 'c_qTl', qTt2[:, :], io['qT'][j],
                            wait=(tok_qT2_r,))
                for h in range(nH):
                    kvh = h // g_per_h
                    if DH == PT:
                        lq = qTt2[:, h * PT:(h + 1) * PT]
                    else:
                        colb, rowb = (h * DH) // PT, (h * DH) % PT
                        lq = qTt2[rowb:rowb + DH, colb * PT:(colb + 1) * PT]
                    nblk = (nkb + 3) // 4
                    cps = []
                    for b4 in range(nblk):
                        w = min(512, L - b4 * 512)
                        bank = pb[b4 % 2]
                        if DH == PT:
                            rk = kT[:, kvh * n_tiles * PT + b4 * 512:
                                    kvh * n_tiles * PT + b4 * 512 + w]
                        else:
                            rb = (h % 2) * DH
                            rk = kT[rb:rb + DH,
                                    kvh * n_tiles * PT + b4 * 512:
                                    kvh * n_tiles * PT + b4 * 512 + w]
                        k.wait('t', tql, *tok_kv_ready)
                        k.wait('t', tok_b2.get(('sc', b4 % 2)))
                        mm = k.inc('t', nc.tensor.matmul(
                            bank[:, :w], lq, rk, start=True, stop=True))
                        k.wait('s', mm, tok_strip_r)
                        tcp = k.inc('s', nc.scalar.copy(
                            strip[:, b4 * 512: b4 * 512 + w], bank[:, :w]))
                        tok_b2[('sc', b4 % 2)] = tcp
                        cps.append(tcp)
                    k.wait('v', cps[-1], cfg['dmask_tok'])
                    tmsk = k.inc('v', nc.vector.tensor_tensor(
                        strip[:, L - 2 * PT:L], strip[:, L - 2 * PT:L],
                        cfg['dmask'][:, :], ALU.add))
                    k.wait('v', *cps, tmsk, tok_strip_r)
                    tmx = k.inc('v', nc.vector.reduce_max(
                        mx[:, 0:1], strip[:, :L], axis=AX.X))
                    k.wait('v', tmx)
                    tb = k.inc('v', nc.vector.tensor_scalar_mul(
                        mx[:, 1:2], mx[:, 0:1], -scale))
                    k.wait('s', tb, tok_exps_r, tok_lsum_r)
                    tex = k.inc('s', nc.scalar.activation(
                        exps[:, :L], strip[:, :L], AF.Exp,
                        bias=mx[:, 1:2], scale=scale, accum_out=lsum[:, 0:1]))
                    tok_strip_r = tex
                    k.wait('v', tex)
                    trl = k.inc('v', nc.vector.reciprocal(lsum[:, 1:2],
                                                          lsum[:, 0:1]))
                    tok_lsum_r = trl
                    k.wait('v', trl, tok_drl_r)
                    tdr = k.inc('v', nc.vector.tensor_scalar_mul(
                        drl[:, :], ident[:, :], lsum[:, 1:2]))
                    po = pb[4]
                    tva = None
                    for b4 in range(nblk):
                        nk4 = min(4, nkb - b4 * 4)
                        bank = pb[2 + b4 % 2]
                        ai = b4 % 2
                        k.wait('t', tex, tdr, tok_b2.get(('at', ai)))
                        for i4 in range(nk4):
                            kb = b4 * 4 + i4
                            mm = nc.tensor.matmul(
                                bank[:, i4 * PT:(i4 + 1) * PT],
                                exps[:, kb * PT:(kb + 1) * PT], drl[:, :],
                                start=(i4 == 0), stop=(i4 == nk4 - 1))
                        tmm = k.inc('t', mm)
                        tok_exps_r = tmm
                        tok_drl_r = tmm
                        k.wait('v', tmm, tok_b2.get(('ar', ai)))
                        tae = k.inc('v', nc.vector.tensor_copy(
                            att[ai][:, :nk4 * PT], bank[:, :nk4 * PT]))
                        tok_b2[('at', ai)] = tae
                        k.wait('t', tae,
                               tok_b2.get('po') if b4 == 0 else None)
                        for i4 in range(nk4):
                            kb = b4 * 4 + i4
                            mm = nc.tensor.matmul(
                                po[:DH, 0:PT], vf[:, kb * KW + kvh * DH:
                                               kb * KW + (kvh + 1) * DH],
                                att[ai][:, i4 * PT:(i4 + 1) * PT],
                                start=(kb == 0), stop=(kb == nkb - 1),
                                skip_group_check=True)
                        tva = k.inc('t', mm)
                        tok_b2[('ar', ai)] = tva
                    k.wait('s', tva, tok_oT_r)
                    if DH == PT:
                        dst = oT[:, h * 512 + jj * PT: h * 512 + (jj + 1) * PT]
                    else:
                        colb, rowb = (h * DH) // PT, (h * DH) % PT
                        dst = oT[rowb:rowb + DH,
                                 colb * 512 + jj * PT: colb * 512 + (jj + 1) * PT]
                    tev = k.inc('s', nc.scalar.copy(dst, po[:DH, 0:PT]))
                    tok_b2['po'] = tev
                    t_oT_w.append(tev)
                tok_qT2_r = ('t', k.cnt['t'])
            t_oT = tmax(*t_oT_w)
            for dmq in range(4):
                for hs in range(NHD):
                    wi = woi[0] % 2
                    woi[0] += 1
                    tw = k.dma('g', f'c_wo{wi}', woch[wi][:, :],
                               io['Wo'][hs * PT:(hs + 1) * PT,
                                        dmq * 512:(dmq + 1) * 512],
                               wait=(tok_wo_r[wi],))
                    k.wait('t', tw, *t_oT)
                    if hs == 0:
                        k.wait('t', *[tok_b2.get(('wo', q)) for q in range(4)])
                    for q4 in range(4):
                        mm = nc.tensor.matmul(
                            pb[q4][:, :], woch[wi][:, q4 * PT:(q4 + 1) * PT],
                            oT[:, hs * 512:(hs + 1) * 512],
                            start=(hs == 0), stop=(hs == NHD - 1))
                    tok_wo_r[wi] = k.inc('t', mm)
                tmw = tok_wo_r[(woi[0] - 1) % 2]
                tok_oT_r = tmw
                for q4 in range(4):
                    dms = dmq * 4 + q4
                    k.wait('v', tmw, tok_outT_r)
                    tev = k.inc('v', nc.vector.tensor_copy(
                        outT[:, dms * 512:(dms + 1) * 512], pb[q4][:, :]))
                    tok_b2[('wo', q4)] = tev
            t_outT_w = ('v', k.cnt['v'])
            for jj, j in enumerate(tiles):
                rsl = slice(j * PT, (j + 1) * PT)
                tx = k.dma('y', 'c_xt2', xt2[:, :], x_in[rsl, :],
                           wait=(tok_xt2_r,))
                adds = []
                for dms in range(16):
                    pbk, tm, b = trp.start(
                        outT[:, dms * 512 + jj * PT: dms * 512 + (jj + 1) * PT],
                        t_outT_w)
                    k.wait('v', tm, tx, tok_xo_r)
                    te = k.inc('v', nc.vector.tensor_tensor(
                        xo[:, dms * PT:(dms + 1) * PT],
                        xt2[:, dms * PT:(dms + 1) * PT], pbk[:, :], ALU.add))
                    trp.done(b, te)
                    adds.append(te)
                tok_xt2_r = adds[-1]
                tok_outT_r = ('t', k.cnt['t'])
                tst = k.dma('s', 'c_xo', x_out[rsl, :], xo[:, :],
                            wait=tuple(tmax(*adds)))
                tok_xo_r = tst
    return tst


def emit_f1(k, cfg, io):
    nc = k.nc
    n_own = cfg['n_own']
    ngrp = n_own // 4
    E, IM, ISH = 8, 1024, 4096
    ident = cfg['ident']
    pb = cfg['pbanks']
    trp = cfg['trp']
    x_in, x_out = io['x_in'], io['x_out']
    wgu, wd = io['w_gate_up'], io['w_down']
    wsg, wsd = io['w_sh_gate_up'], io['w_sh_down']
    esel = cfg['esel']
    esel_tok = cfg['esel_tok']

    with k.scope():
        xt = k.sb("f1_xt", (PT, D), F32)
        xn = k.sb("f1_xn", (PT, D), BF16)
        sc = dict(st6=k.sb("f1_st6", (PT, 24), F32),
                  st2=k.sb("f1_st2", (PT, 2), F32),
                  rc=k.sb("f1_rc", (PT, 2), F32))
        xnT = k.sb("f1_xnT", (PT, 16 * 512), BF16)
        wg8 = k.sb("f1_wg8", (PT, 16 * 8), BF16)
        lgT = k.sb("f1_lgT", (8, 512), F32)
        lg = k.sb("f1_lg", (PT, 8), F32)
        r1 = k.sb("f1_r1", (PT, 8), F32)
        r2 = k.sb("f1_r2", (PT, 8), F32)
        rcol = k.sb("f1_rcol", (PT, 4), F32)
        cmb = k.sb("f1_cmb", (PT, 8), BF16)
        cmbT = k.sb("f1_cmbT", (8, 512), BF16)
        bc = k.sb("f1_bc", (PT, 512), BF16)
        wch = [k.sb(f"f1_w{i}", (PT, 512), BF16) for i in range(2)]
        gsl = k.sb("f1_gs", (PT, 4 * 512), BF16)
        usl = k.sb("f1_us", (PT, 4 * 512), BF16)
        acts = k.sb("f1_act", (PT, 96 * 512), BF16)
        xo = k.sb("f1_xo", (PT, D), F32)

        tok = {}
        wci = [0]
        tst = None

        def castdma(dst, src, ch_wait=None):
            wi = wci[0] % 2
            wci[0] += 1
            t = k.dma('g', f'c_fw{wi}', dst if dst is not None else wch[wi][:, :],
                      src, wait=(tok.get(('fw', wi)), ch_wait))
            return (wi, t)

        for g in range(ngrp):
            tiles = list(range(4 * g, 4 * g + 4))
            t_tr = None
            for jj, j in enumerate(tiles):
                tx = k.dma('y', 'c_xt', xt[:, :], x_in[j * PT:(j + 1) * PT, :],
                           wait=(tok.get('xt_r'),))
                txn = emit_ln(k, xt, xn, sc, tx, war_tok=tok.get('xn_r'))
                tok['xt_r'] = txn
                for dt in range(16):
                    t_tr = trp.transpose(
                        xn[:, dt * PT:(dt + 1) * PT],
                        xnT[:, dt * 512 + jj * PT: dt * 512 + (jj + 1) * PT],
                        txn if dt == 0 else None,
                        war_tok=tok.get('xnT_r') if (jj == 0 and dt == 0) else None)
                tok['xn_r'] = ('t', k.cnt['t'])
            if g == 0:
                tw8 = k.dma('g', 'c_wg8',
                            wg8[:, :].rearrange("p (a e) -> p a e", e=8),
                            io['w_gate'].rearrange("(a p) e -> p a e", p=PT))
            k.wait('t', tw8, t_tr, tok.get('ps8_r'))
            for dt in range(16):
                mm = nc.tensor.matmul(pb[4][0:8, :], wg8[:, dt * 8:(dt + 1) * 8],
                                      xnT[:, dt * 512:(dt + 1) * 512],
                                      start=(dt == 0), stop=(dt == 15))
            tmm = k.inc('t', mm)
            tok['xnT_r'] = tmm
            k.wait('v', tmm, tok.get('lgT_r'))
            tlgT = k.inc('v', nc.vector.tensor_copy(lgT[:, :], pb[4][0:8, :]))
            tok['ps8_r'] = tlgT
            tcmbT = []
            for st in range(4):
                k.wait('t', tlgT, tok.get('bc_ps_r'))
                tm = k.inc('t', nc.tensor.transpose(
                    pb[5][:, 0:8], lgT[:, st * PT:(st + 1) * PT],
                    cfg['identf'][0:8, 0:8]))
                k.wait('v', tm, tok.get('lg_r'))
                tlg = k.inc('v', nc.vector.tensor_copy(lg[:, :], pb[5][:, 0:8]))
                tok['bc_ps_r'] = tlg
                k.wait('v', tlg)
                t1 = k.inc('v', nc.vector.reduce_max(rcol[:, 0:1], lg[:, :],
                                                     axis=AX.X))
                k.wait('v', t1)
                t2 = k.inc('v', nc.vector.tensor_scalar(
                    r1[:, :], lg[:, :], rcol[:, 0:1], None, ALU.is_equal))
                k.wait('v', t2)
                t3 = k.inc('v', nc.vector.scalar_tensor_tensor(
                    r2[:, :], r1[:, :], NEG, lg[:, :], ALU.mult, ALU.add))
                k.wait('v', t3)
                t4 = k.inc('v', nc.vector.reduce_max(rcol[:, 1:2], r2[:, :],
                                                     axis=AX.X))
                k.wait('v', t4)
                t5 = k.inc('v', nc.vector.tensor_scalar(
                    r1[:, :], lg[:, :], rcol[:, 1:2], None, ALU.is_ge))
                t5b = k.inc('v', nc.vector.tensor_scalar_mul(
                    rcol[:, 2:3], rcol[:, 0:1], -1.0))
                k.wait('s', t5b, t2)
                t6 = k.inc('s', nc.scalar.activation(r2[:, :], lg[:, :], AF.Exp,
                                                     bias=rcol[:, 2:3]))
                k.wait('v', t6, t5)
                t7 = k.inc('v', nc.vector.tensor_mul(r2[:, :], r2[:, :], r1[:, :]))
                k.wait('v', t7)
                t8 = k.inc('v', nc.vector.reduce_sum(rcol[:, 3:4], r2[:, :],
                                                     axis=AX.X))
                k.wait('v', t8)
                t9 = k.inc('v', nc.vector.reciprocal(rcol[:, 3:4], rcol[:, 3:4]))
                k.wait('v', t9)
                t10 = k.inc('v', nc.vector.tensor_scalar_mul(
                    cmb[:, :], r2[:, :], rcol[:, 3:4]))
                tok['lg_r'] = t10
                b = trp.i % 2
                trp.i += 1
                pbk = trp.banks[b]
                k.wait('t', t10, trp.busy[b], tok.get('cmbT_r') if st == 0 else None)
                tm2 = k.inc('t', nc.tensor.transpose(pbk[0:8, 0:PT], cmb[:, :],
                                                     ident[:, :]))
                k.wait('v', tm2)
                tc = k.inc('v', nc.vector.tensor_copy(
                    cmbT[0:8, st * PT:(st + 1) * PT], pbk[0:8, 0:PT]))
                trp.done(b, tc)
                tcmbT.append(tc)
            t_cmbT = tmax(*tcmbT)

            def gu_phase(Wsrc, bc_tok, actbase, width2):
                half = width2 // 2
                tacts = []
                for fb in range(half // 512):
                    for part, base_col in ((0, fb * 512), (1, half + fb * 512)):
                        dst = gsl if part == 0 else usl
                        for dt in range(16):
                            wi, tw = castdma(None,
                                             Wsrc[dt * PT:(dt + 1) * PT,
                                                  base_col:base_col + 512])
                            k.wait('t', tw, t_tr,
                                   *[tok.get(('gub', q)) for q in range(4)])
                            for q4 in range(4):
                                mm = nc.tensor.matmul(
                                    pb[q4][:, :],
                                    wch[wi][:, q4 * PT:(q4 + 1) * PT],
                                    xnT[:, dt * 512:(dt + 1) * 512],
                                    start=(dt == 0), stop=(dt == 15))
                            tok[('fw', wi)] = k.inc('t', mm)
                        tmm = tok[('fw', wi)]
                        for q4 in range(4):
                            k.wait('v', tmm, tok.get(('gu_r', part, q4)))
                            tev = k.inc('v', nc.vector.tensor_copy(
                                dst[:, q4 * 512:(q4 + 1) * 512], pb[q4][:, :]))
                            tok[('gub', q4)] = tev
                            tok[('gu_w', part, q4)] = tev
                    for q4 in range(4):
                        fs = fb * 4 + q4
                        tg = tok[('gu_w', 0, q4)]
                        tu = tok[('gu_w', 1, q4)]
                        k.wait('s', tg, tok.get(('sg_r', q4)))
                        ts1 = k.inc('s', nc.scalar.activation(
                            gsl[:, q4 * 512:(q4 + 1) * 512],
                            gsl[:, q4 * 512:(q4 + 1) * 512], AF.Silu))
                        tok[('gu_r', 0, q4)] = ts1
                        k.wait('v', ts1, tu, tok.get(('act_r', actbase + fs)))
                        tm1 = k.inc('v', nc.vector.tensor_mul(
                            usl[:, q4 * 512:(q4 + 1) * 512],
                            gsl[:, q4 * 512:(q4 + 1) * 512],
                            usl[:, q4 * 512:(q4 + 1) * 512]))
                        tok[('sg_r', q4)] = tm1
                        k.wait('v', tm1, bc_tok)
                        if bc_tok is not None:
                            tm2 = k.inc('v', nc.vector.tensor_mul(
                                acts[:, (actbase + fs) * 512:
                                     (actbase + fs + 1) * 512],
                                usl[:, q4 * 512:(q4 + 1) * 512], bc[:, :]))
                        else:
                            tm2 = k.inc('v', nc.vector.tensor_copy(
                                acts[:, (actbase + fs) * 512:
                                     (actbase + fs + 1) * 512],
                                usl[:, q4 * 512:(q4 + 1) * 512]))
                        tok[('gu_r', 1, q4)] = tm2
                        tok[('act_w', actbase + fs)] = tm2
                        tacts.append(tm2)
                return tacts

            all_act = []
            for e in range(E):
                k.wait('t', *t_cmbT, cfg['esel_tok'], tok.get('bc_ps_r'))
                mmb = k.inc('t', nc.tensor.matmul(
                    pb[5][:, :], esel[0:8, e * PT:(e + 1) * PT], cmbT[0:8, :],
                    start=True, stop=True))
                tok['cmbT_r'] = mmb
                k.wait('v', mmb, tok.get('bc_r'))
                tbc = k.inc('v', nc.vector.tensor_copy(bc[:, :], pb[5][:, :]))
                tok['bc_ps_r'] = tbc
                all_act += gu_phase(wgu[e], tbc, e * 8, 2 * IM)
                tok['bc_r'] = ('v', k.cnt['v'])
            all_act += gu_phase(wsg, None, 64, 2 * ISH)
            t_acts = tmax(*all_act)

            NSL_E, NSL_S = IM // PT, ISH // PT
            for jj, j in enumerate(tiles):
                tx = k.dma('y', 'c_xt2', xt[:, :], x_in[j * PT:(j + 1) * PT, :],
                           wait=(tok.get('xt_r'),))
                adds = []
                for dmb in range(4):
                    bank = pb[dmb % 2]
                    first = True
                    k.wait('t', *t_acts, tok.get(('dps', dmb % 2)))
                    for e in range(E):
                        for it in range(NSL_E):
                            wi, tw = castdma(None,
                                             wd[e][it * PT:(it + 1) * PT,
                                                   dmb * 512:(dmb + 1) * 512])
                            k.wait('t', tw)
                            mm = nc.tensor.matmul(
                                bank[:, :],
                                acts[:, (e * 8 + it) * 512 + jj * PT:
                                     (e * 8 + it) * 512 + (jj + 1) * PT],
                                wch[wi][:, :], start=first, stop=False,
                                skip_group_check=True)
                            first = False
                            tok[('fw', wi)] = k.inc('t', mm)
                    for it in range(NSL_S):
                        wi, tw = castdma(None,
                                         wsd[it * PT:(it + 1) * PT,
                                             dmb * 512:(dmb + 1) * 512])
                        k.wait('t', tw)
                        mm = nc.tensor.matmul(
                            bank[:, :],
                            acts[:, (64 + it) * 512 + jj * PT:
                                 (64 + it) * 512 + (jj + 1) * PT],
                            wch[wi][:, :], start=False, stop=(it == NSL_S - 1),
                            skip_group_check=True)
                        tok[('fw', wi)] = k.inc('t', mm)
                    tmm = tok[('fw', wi)]
                    k.wait('v', tmm, tx, tok.get('xo_r'))
                    te = k.inc('v', nc.vector.tensor_tensor(
                        xo[:, dmb * 512:(dmb + 1) * 512],
                        xt[:, dmb * 512:(dmb + 1) * 512], bank[:, :], ALU.add))
                    tok[('dps', dmb % 2)] = te
                    adds.append(te)
                tok['xt_r'] = adds[-1]
                tst = k.dma('s', 'c_xo', x_out[j * PT:(j + 1) * PT, :], xo[:, :],
                            wait=tuple(tmax(*adds)))
                tok['xo_r'] = tst
            for key in list(tok):
                if isinstance(key, tuple) and key[0] == 'act_r':
                    pass
            tok['xnT_r'] = ('t', k.cnt['t'])
            for i in range(96):
                tok[('act_r', i)] = ('t', k.cnt['t'])
    return tst


def emit_a0b(k, cfg, io):
    nc = k.nc
    n_own = cfg['n_own']
    n_tiles = 2 * n_own
    H, KV, DH = 16, 4, 128
    QW, KW = H * DH, KV * DH
    gph = H // KV
    half = DH // 2
    scale = DH ** -0.5
    ident = cfg['ident']
    pb = cfg['pbanks']
    trp = cfg['trp']
    x_in, x_out = io['x_in'], io['x_out']
    tok = {}
    wci = [0]

    def rope3(src_ap, dst_ap, tabc, tabs, ti, width, nheads, src_tok, war_tok,
              war2, scratch, tab_tok=None):
        base = src_ap
        sw4 = bass.AP(base.tensor, base.offset + half,
                      [list(base.ap[0]), [DH, nheads], [-half, 2], [1, half]])
        tc = tabc[:, ti * DH:(ti + 1) * DH]
        ts_ = tabs[:, ti * DH:(ti + 1) * DH]
        cosb = bass.AP(tc.tensor, tc.offset,
                       [list(tc.ap[0]), [0, nheads], [1, DH]])
        sinb = bass.AP(ts_.tensor, ts_.offset,
                       [list(ts_.ap[0]), [0, nheads], [half, 2], [1, half]])
        d4 = dst_ap.rearrange("p (h s x) -> p h s x", h=nheads, s=2)
        s3 = src_ap.rearrange("p (h x) -> p h x", h=nheads)
        k.wait('v', src_tok, war_tok, tab_tok)
        t1 = k.inc('v', nc.vector.tensor_tensor(d4, sw4, sinb, ALU.mult))
        k.wait('v', t1, war2)
        t2 = k.inc('v', nc.vector.tensor_tensor(
            scratch[:, :width].rearrange("p (h x) -> p h x", h=nheads), s3, cosb,
            ALU.mult))
        k.wait('v', t2)
        return k.inc('v', nc.vector.tensor_tensor(dst_ap, dst_ap,
                                                  scratch[:, :width], ALU.add))

    def softmax_heads(src_ap, dst_ap, nheads, mxb, src_tok, war_tok):
        v3s = src_ap.rearrange("p (h x) -> p h x", h=nheads)
        k.wait('v', src_tok)
        t1 = k.inc('v', nc.vector.reduce_max(
            mxb[:, 0:nheads].rearrange("p (h o) -> p h o", h=nheads), v3s,
            axis=AX.X))
        k.wait('v', t1)
        t2 = k.inc('v', nc.vector.tensor_scalar_mul(mxb[:, 16:16 + nheads],
                                                    mxb[:, 0:nheads], -1.0))
        texp = None
        for h in range(nheads):
            k.wait('s', t2, war_tok if h == 0 else None)
            texp = k.inc('s', nc.scalar.activation(
                dst_ap[:, h * DH:(h + 1) * DH], src_ap[:, h * DH:(h + 1) * DH],
                AF.Exp, bias=mxb[:, 16 + h:17 + h]))
        v3d = dst_ap.rearrange("p (h x) -> p h x", h=nheads)
        k.wait('v', texp)
        t3 = k.inc('v', nc.vector.reduce_sum(
            mxb[:, 0:nheads].rearrange("p (h o) -> p h o", h=nheads), v3d,
            axis=AX.X))
        k.wait('v', t3)
        t4 = k.inc('v', nc.vector.reciprocal(mxb[:, 16:16 + nheads],
                                             mxb[:, 0:nheads]))
        tl = None
        k.wait('v', t4)
        for h in range(nheads):
            tl = k.inc('v', nc.vector.tensor_scalar_mul(
                dst_ap[:, h * DH:(h + 1) * DH], dst_ap[:, h * DH:(h + 1) * DH],
                mxb[:, 16 + h:17 + h]))
        return tl

    def gl_pipeline(ksrc_ap, A_bank, Eq_sb, Ek_sb, gl_sb, src_tok, war_q,
                    war_k):
        k.wait('s', src_tok, war_q)
        tg1 = k.inc('s', nc.scalar.activation(gl_sb[:, :], ksrc_ap, AF.Sigmoid))
        k.wait('s', tg1)
        tg2 = k.inc('s', nc.scalar.activation(gl_sb[:, :], gl_sb[:, :], AF.Ln))
        k.wait('t', tg2, cfg['utri_tok'], tok.get('Abank_r'))
        tA = k.inc('t', nc.tensor.matmul(A_bank[:, 0:KW], cfg['utri'][:, :],
                                         gl_sb[:, :], start=True, stop=True))
        k.wait('s', tA, war_k)
        te1 = k.inc('s', nc.scalar.activation(Eq_sb[:, :], A_bank[:, 0:KW],
                                              AF.Exp, scale=1.0 / 16.0))
        te2 = k.inc('s', nc.scalar.activation(Ek_sb[:, :], A_bank[:, 0:KW],
                                              AF.Exp, scale=-1.0 / 16.0))
        tok['Abank_r'] = te2
        return tg2, tA, te1, te2

    with k.scope():
        xt = k.sb("ab_xt", (PT, D), F32)
        xn = k.sb("ab_xn", (PT, D), BF16)
        sc = dict(st6=k.sb("ab_st6", (PT, 24), F32),
                  st2=k.sb("ab_st2", (PT, 2), F32),
                  rc=k.sb("ab_rc", (PT, 2), F32))
        xnT = k.sb("ab_xnT", (PT, 16 * 512), BF16)
        wch = [k.sb(f"ab_w{i}", (PT, 512), BF16) for i in range(2)]
        qg = k.sb("ab_qg", (PT, 4 * QW), BF16)
        kvg = k.sb("ab_kvg", (PT, 4 * 2 * KW), BF16)
        rop = k.sb("ab_rop", (PT, QW), BF16)
        rop2 = k.sb("ab_rop2", (PT, QW), BF16)
        trb = k.sb("ab_trb", (PT, QW), BF16)
        qs_sb = k.sb("ab_qs", (PT, QW), BF16)
        mxb = k.sb("ab_mx", (PT, 32), F32)
        gl_sb = k.sb("ab_gl", (PT, KW), F32)
        Eq_sb = k.sb("ab_Eq", (PT, KW), F32)
        Ek_sb = k.sb("ab_Ek", (PT, KW), F32)
        qh_sb = k.sb("ab_qh", (PT, QW), BF16)
        kh_sb = k.sb("ab_kh", (PT, KW), BF16)
        tco = k.sb("ab_tco", (PT, n_own * DH), BF16)
        tso = k.sb("ab_tso", (PT, n_own * DH), BF16)
        ttab = k.dma('y', 'c_tab', tco[:, :], cfg['tabc_o_d'])
        ttab = k.dma('y', 'c_tab', tso[:, :], cfg['tabs_o_d'])

        def proj_group(Wsrc, width, dst, dst_off, dst_stride, war_toks, x_tok):
            tevs = []
            for blk in range(width // 512):
                for dt in range(16):
                    wi = wci[0] % 2
                    wci[0] += 1
                    tw = k.dma('g', f'c_w{wi}', wch[wi][:, :],
                               Wsrc[dt * PT:(dt + 1) * PT,
                                    blk * 512:(blk + 1) * 512],
                               wait=(tok.get(('w', wi)),))
                    k.wait('t', tw, x_tok, *[tok.get(('pg', q)) for q in range(4)])
                    for q4 in range(4):
                        mm = nc.tensor.matmul(
                            pb[q4][:, :],
                            xnT[:, dt * 512 + q4 * PT: dt * 512 + (q4 + 1) * PT],
                            wch[wi][:, :], start=(dt == 0), stop=(dt == 15))
                    tok[('w', wi)] = k.inc('t', mm)
                tmm = tok[('w', (wci[0] - 1) % 2)]
                for q4 in range(4):
                    k.wait('v', tmm,
                           *(war_toks if (blk == 0 and q4 == 0) else []))
                    tev = k.inc('v', nc.vector.tensor_copy(
                        dst[:, dst_off + q4 * dst_stride + blk * 512:
                            dst_off + q4 * dst_stride + (blk + 1) * 512],
                        pb[q4][:, :]))
                    tok[('pg', q4)] = tev
                    tevs.append(tev)
            return tmax(*tevs)

        def store_T(src_ap, nsl, src_tok, dram_dst, chname):
            tt = None
            for dt in range(nsl):
                tt = trp.transpose(src_ap[:, dt * PT:(dt + 1) * PT],
                                   trb[:, dt * PT:(dt + 1) * PT],
                                   src_tok if dt == 0 else None,
                                   war_tok=tok.get('trb_r') if dt == 0 else None)
            ts = k.dma('s', chname, dram_dst, trb[:, 0:nsl * PT], wait=(tt,))
            tok['trb_r'] = ts
            return ts

        ngrp = n_own // 4
        for g in range(ngrp):
            tiles = list(range(4 * g, 4 * g + 4))
            t_tr = None
            for jj, j in enumerate(tiles):
                tx = k.dma('y', 'c_xt', xt[:, :], x_in[j * PT:(j + 1) * PT, :],
                           wait=(tok.get('xt_r'),))
                txn = emit_ln(k, xt, xn, sc, tx, war_tok=tok.get('xn_r'))
                tok['xt_r'] = txn
                for dt in range(16):
                    t_tr = trp.transpose(
                        xn[:, dt * PT:(dt + 1) * PT],
                        xnT[:, dt * 512 + jj * PT: dt * 512 + (jj + 1) * PT],
                        txn if dt == 0 else None,
                        war_tok=tok.get('xnT_r') if (jj == 0 and dt == 0) else None)
                tok['xn_r'] = ('t', k.cnt['t'])
            tq = proj_group(io['Wq'], QW, qg, 0, QW, tok.get('qg_r') or [],
                            t_tr)
            tkk = proj_group(io['Wk'], KW, kvg, 0, 2 * KW,
                             tok.get('kvg_r') or [], t_tr)
            tvv = proj_group(io['Wv'], KW, kvg, KW, 2 * KW, [], t_tr)
            tok['xnT_r'] = ('t', k.cnt['t'])
            for jj, j in enumerate(tiles):
                qsl = qg[:, jj * QW:(jj + 1) * QW]
                ksl = kvg[:, jj * 2 * KW: jj * 2 * KW + KW]
                vsl = kvg[:, jj * 2 * KW + KW: (jj + 1) * 2 * KW]
                trq = rope3(qsl, rop[:, :QW], tco[:, :], tso[:, :], j,
                            QW, H, tq[0] if tq else None, tok.get('rop_r'),
                            tok.get('rop2_r'), rop2, tab_tok=ttab)
                tok['rop2_r'] = trq
                tsq = store_T(rop[:, :QW], 16, trq, io['qT'][j], 'c_sq')
                tok['rop_r'] = ('t', k.cnt['t'])
                tqs = softmax_heads(qsl, qs_sb[:, :], H, mxb,
                                    tq[0] if tq else None, tok.get('qs_r'))
                tg2, tA, te1, te2 = gl_pipeline(ksl, pb[4], Eq_sb, Ek_sb, gl_sb,
                                                tkk[0] if tkk else None,
                                                tok.get('Eq_r'), tok.get('Ek_r'))
                eqb = bass.AP(Eq_sb.ap().tensor, Eq_sb.ap().offset,
                              [list(Eq_sb.ap().ap[0]), [DH, KV], [0, gph],
                               [1, DH]])
                k.wait('v', tqs, te1, tok.get('qh_r'))
                tqh = k.inc('v', nc.vector.tensor_mul(
                    qh_sb[:, :].rearrange("p (a b x) -> p a b x", a=KV, b=gph),
                    qs_sb[:, :].rearrange("p (a b x) -> p a b x", a=KV, b=gph),
                    eqb))
                tok['Eq_r'] = tqh
                tsh = store_T(qh_sb[:, :], 16, tqh, io['qhT'][j], 'c_sh')
                tok['qh_r'] = ('t', k.cnt['t'])
                tok['qs_r'] = tqh
                tks = softmax_heads(ksl, kh_sb[:, :], KV, mxb,
                                    tkk[0] if tkk else None, tok.get('kh_r'))
                k.wait('v', tks, te2)
                tkh = k.inc('v', nc.vector.tensor_mul(kh_sb[:, :], kh_sb[:, :],
                                                      Ek_sb[:, :]))
                tok['Ek_r'] = tkh
                tsk = store_T(kh_sb[:, :], 4, tkh, io['khT'][j], 'c_skh')
                tok['kh_r'] = ('t', k.cnt['t'])
                ta1 = k.dma('s', 'c_agi', io['ag_in'][j * PT:(j + 1) * PT, :],
                            kvg[:, jj * 2 * KW:(jj + 1) * 2 * KW],
                            wait=tuple(tmax(*tkk, *tvv)))
                tv2 = k.dma('s', 'c_sv', io['vT'][j], vsl,
                            wait=tuple(tmax(*tvv)))
                tok.setdefault('pstores', []).extend([tsq, tsh, tsk, ta1, tv2])
            tok['qg_r'] = tmax(*tok.pop('pstores'))
            tok['kvg_r'] = tok['qg_r']

    k.barrier()
    ch = k.chan('c_col')
    inst = nc.gpsimd.collective_compute(
        "AllGather", ALU.bypass, cfg['replica_groups'],
        [io['ag_in'].opt()], [io['ag_out'].opt()])
    k.cnt[ch] += 1
    inst.then_inc(k.sem[ch], 1)
    t_ag = (ch, k.cnt[ch])

    with k.scope():
        kT = k.sb("ab_kT", (PT, KV * n_tiles * PT), BF16)
        vf = k.sb("ab_vf", (PT, n_tiles * KW), BF16)
        khG = k.sb("ab_khG", (PT, n_tiles * KW), BF16)
        ECg = k.sb("ab_EC", (PT, n_tiles * KV), F32)
        prep_scope = k.scope()
        prep_scope.__enter__()
        kvt = k.sb("ab_kvt", (PT, 2 * KW), BF16)
        rop_b = k.sb("ab_ropb", (PT, KW), BF16)
        rop2_b = k.sb("ab_rop2b", (PT, KW), BF16)
        gl2 = k.sb("ab_gl2", (PT, KW), F32)
        Ek2 = k.sb("ab_Ek2", (PT, KW), F32)
        ks2 = k.sb("ab_ks2", (PT, KW), BF16)
        mxb2 = k.sb("ab_mx2", (PT, 32), F32)
        tca = k.sb("ab_tca", (PT, n_tiles * DH), BF16)
        tsa = k.sb("ab_tsa", (PT, n_tiles * DH), BF16)
        ttab2 = k.dma('y', 'c_tab', tca[:, :], cfg['tabc_a_d'])
        ttab2 = k.dma('y', 'c_tab', tsa[:, :], cfg['tabs_a_d'])
        onec, onec_tok = k.constcol(1.0)

        tok2 = {}
        last_pr = []
        for kb in range(n_tiles):
            par, idx = kb % 2, kb // 2
            row0 = par * n_own * PT + idx * PT
            tl = k.dma('y', 'c_kvt', kvt[:, :], io['ag_out'][row0:row0 + PT, :],
                       wait=(t_ag, *(tok2.get('kvt_r') or [])))
            k.wait('v', tl)
            tv = k.inc('v', nc.vector.tensor_copy(vf[:, kb * KW:(kb + 1) * KW],
                                                  kvt[:, KW:2 * KW]))
            trk = rope3(kvt[:, 0:KW], rop_b[:, :KW], tca[:, :],
                        tsa[:, :], kb, KW, KV, tl, tok2.get('ropb_r'),
                        tok2.get('ropb2_r'), rop2_b, tab_tok=ttab2)
            tok2['ropb2_r'] = trk
            ttx = None
            for s in range(KV):
                ttx = trp.transpose(
                    rop_b[:, s * PT:(s + 1) * PT],
                    kT[:, (s * n_tiles + kb) * PT:(s * n_tiles + kb + 1) * PT],
                    trk if s == 0 else None)
            tok2['ropb_r'] = ('t', k.cnt['t'])
            k.wait('s', tl, tok2.get('gl2_r'))
            tg1 = k.inc('s', nc.scalar.activation(gl2[:, :], kvt[:, 0:KW],
                                                  AF.Sigmoid))
            k.wait('s', tg1)
            tg2 = k.inc('s', nc.scalar.activation(gl2[:, :], gl2[:, :], AF.Ln))
            k.wait('t', tg2, cfg['utri_tok'], tok2.get('A_r'))
            tA = k.inc('t', nc.tensor.matmul(pb[4][:, 0:KW], cfg['utri'][:, :],
                                             gl2[:, :], start=True, stop=True))
            k.wait('s', tA, tok2.get('Ek2_r'))
            te2 = k.inc('s', nc.scalar.activation(Ek2[:, :], pb[4][:, 0:KW],
                                                  AF.Exp, scale=-1.0 / 16.0))
            tok2['A_r'] = te2
            tks = softmax_heads(kvt[:, 0:KW], ks2[:, :], KV, mxb2, tl,
                                tok2.get('ks2_r'))
            k.wait('v', tks, te2, tok2.get('khG_r'))
            tkh = k.inc('v', nc.vector.tensor_mul(
                khG[:, kb * KW:(kb + 1) * KW], ks2[:, :], Ek2[:, :]))
            tok2['Ek2_r'] = tkh
            tok2['ks2_r'] = tkh
            tecs = []
            for s in range(KV):
                k.wait('t', tg2, onec_tok, tok2.get(('EC_r', s)))
                tmc = k.inc('t', nc.tensor.matmul(
                    pb[5][:, s:s + 1], gl2[:, s * PT:(s + 1) * PT],
                    onec[:, 0:1], start=True, stop=True))
                k.wait('s', tmc)
                tec = k.inc('s', nc.scalar.activation(
                    ECg[:, kb * KV + s: kb * KV + s + 1], pb[5][:, s:s + 1],
                    AF.Exp, scale=1.0 / 16.0))
                tok2[('EC_r', s)] = tec
                tecs.append(tec)
            tok2['gl2_r'] = tA
            tok2['kvt_r'] = tmax(tv, tks, tg1, ('t', k.cnt['t']))
            last_pr += [tv, tkh, ttx, *tecs]
        t_prep = tmax(*last_pr, ('t', k.cnt['t']))
        prep_scope.__exit__(None, None, None)

        strip = k.sb("ab_strip", (PT, n_tiles * PT), F32)
        exps = k.sb("ab_exps", (PT, n_tiles * PT), BF16)
        mx = k.sb("ab_mxs", (PT, 2), F32)
        lsum = k.sb("ab_l", (PT, 2), F32)
        drl = k.sb("ab_drl", (PT, PT), BF16)
        att = [k.sb(f"ab_att{i}", (PT, 512), BF16) for i in range(2)]
        qTt = k.sb("ab_qTt", (PT, QW), BF16)
        qhTt = k.sb("ab_qhTt", (PT, QW), BF16)
        khTt = k.sb("ab_khTt", (PT, KW), BF16)
        vtt = k.sb("ab_vtt", (PT, KW), BF16)
        S_sb = k.sb("ab_S", (PT, KV * PT), F32)
        Sb = k.sb("ab_Sb", (PT, KV * PT), BF16)
        msk = [k.sb(f"ab_msk{i}", (PT, PT), BF16) for i in range(2)]
        mskT = [k.sb(f"ab_mskT{i}", (PT, PT), BF16) for i in range(2)]
        oT = k.sb("ab_oT", (PT, 16 * 512), BF16)
        woch = [k.sb(f"ab_wo{i}", (PT, 512), BF16) for i in range(2)]
        outT = k.sb("ab_outT", (PT, 16 * 512), BF16)
        xt2 = k.sb("ab_xt2", (PT, D), F32)
        xo = k.sb("ab_xo", (PT, D), F32)
        blend = k.sb("ab_blend", (PT, 2), F32)

        tbl = k.dma('y', 'c_bl', blend[:, :], io['blend'])
        tz1 = k.inc('v', nc.vector.memset(S_sb[:, :], 0.0))
        tz2 = k.inc('v', nc.vector.memset(Sb[:, :], 0.0))
        tok3 = {}
        woi = [0]
        tst = None

        for g in range(n_own // 4):
            tiles = list(range(4 * g, 4 * g + 4))
            t_oT_w = []
            for jj, j in enumerate(tiles):
                L = (2 * j + 2) * PT
                nkb = 2 * j + 2
                tql = k.dma('y', 'c_qTl', qTt[:, :], io['qT'][j],
                            wait=(tok3.get('qTt_r'),))
                tqh = k.dma('y', 'c_qhl', qhTt[:, :], io['qhT'][j],
                            wait=(tok3.get('qhTt_r'),))
                tkh2 = k.dma('y', 'c_khl', khTt[:, :], io['khT'][j],
                             wait=(tok3.get('khTt_r'),))
                tvl = k.dma('y', 'c_vl', vtt[:, :], io['vT'][j],
                            wait=(tok3.get('vtt_r'),))
                for h in range(H):
                    kvh = h // gph
                    lq = qTt[:, h * PT:(h + 1) * PT]
                    nblk = (nkb + 3) // 4
                    cps = []
                    for b4 in range(nblk):
                        w = min(512, L - b4 * 512)
                        bank = pb[b4 % 2]
                        rk = kT[:, kvh * n_tiles * PT + b4 * 512:
                                kvh * n_tiles * PT + b4 * 512 + w]
                        k.wait('t', tql, *t_prep)
                        k.wait('t', tok3.get(('sc_r', b4 % 2)))
                        mm = k.inc('t', nc.tensor.matmul(
                            bank[:, :w], lq, rk, start=True, stop=True))
                        k.wait('s', mm, tok3.get('strip_r'))
                        tcp = k.inc('s', nc.scalar.copy(
                            strip[:, b4 * 512: b4 * 512 + w], bank[:, :w]))
                        tok3[('sc_r', b4 % 2)] = tcp
                        cps.append(tcp)
                    k.wait('v', cps[-1], cfg['dmask_tok'])
                    tmsk = k.inc('v', nc.vector.tensor_tensor(
                        strip[:, L - 2 * PT:L], strip[:, L - 2 * PT:L],
                        cfg['dmask'][:, :], ALU.add))
                    k.wait('v', *cps, tmsk, tok3.get('strip_r'))
                    tmx = k.inc('v', nc.vector.reduce_max(
                        mx[:, 0:1], strip[:, :L], axis=AX.X))
                    k.wait('v', tmx)
                    tb = k.inc('v', nc.vector.tensor_scalar_mul(
                        mx[:, 1:2], mx[:, 0:1], -scale))
                    k.wait('s', tb, tok3.get('exps_r'), tok3.get('lsum_r'))
                    tex = k.inc('s', nc.scalar.activation(
                        exps[:, :L], strip[:, :L], AF.Exp,
                        bias=mx[:, 1:2], scale=scale, accum_out=lsum[:, 0:1]))
                    tok3['strip_r'] = tex
                    k.wait('v', tex)
                    trl = k.inc('v', nc.vector.reciprocal(lsum[:, 1:2],
                                                          lsum[:, 0:1]))
                    tok3['lsum_r'] = trl
                    k.wait('v', trl, tok3.get('drl_r'))
                    tdr = k.inc('v', nc.vector.tensor_scalar_mul(
                        drl[:, :], ident[:, :], lsum[:, 1:2]))
                    po = pb[4]
                    tva = None
                    for b4 in range(nblk):
                        nk4 = min(4, nkb - b4 * 4)
                        bank = pb[2 + b4 % 2]
                        ai = b4 % 2
                        k.wait('t', tex, tdr, tok3.get(('at', ai)))
                        for i4 in range(nk4):
                            kb2 = b4 * 4 + i4
                            mm = nc.tensor.matmul(
                                bank[:, i4 * PT:(i4 + 1) * PT],
                                exps[:, kb2 * PT:(kb2 + 1) * PT], drl[:, :],
                                start=(i4 == 0), stop=(i4 == nk4 - 1))
                        tmm = k.inc('t', mm)
                        tok3['exps_r'] = tmm
                        tok3['drl_r'] = tmm
                        k.wait('v', tmm, tok3.get(('ar', ai)))
                        tae = k.inc('v', nc.vector.tensor_copy(
                            att[ai][:, :nk4 * PT], bank[:, :nk4 * PT]))
                        tok3[('at', ai)] = tae
                        k.wait('t', tae, tok3.get('po_r') if b4 == 0 else None)
                        for i4 in range(nk4):
                            kb2 = b4 * 4 + i4
                            mm = nc.tensor.matmul(
                                po[:, 0:PT], vf[:, kb2 * KW + kvh * DH:
                                                kb2 * KW + (kvh + 1) * DH],
                                att[ai][:, i4 * PT:(i4 + 1) * PT],
                                start=(kb2 == 0), stop=(kb2 == nkb - 1),
                                skip_group_check=True)
                        tva = k.inc('t', mm)
                        tok3[('ar', ai)] = tva
                    dst = oT[:, h * 512 + jj * PT: h * 512 + (jj + 1) * PT]
                    k.wait('s', tva, tok3.get('oT_r'))
                    tev = k.inc('s', nc.scalar.activation(dst, po[:, 0:PT],
                                                          AF.Copy, scale=0.5))
                    tok3['po_r'] = tev
                    t_oT_w.append(tev)
                tok3['qTt_r'] = ('t', k.cnt['t'])
                for point in range(2):
                    c = 2 * j + point
                    for hg in range(4):
                        bankP = pb[2 + hg % 2]
                        for hh in range(4):
                            h = hg * 4 + hh
                            kvh = h // gph
                            bi = h % 2
                            bankS = pb[bi]
                            k.wait('t', tqh, tkh2, tok3.get(('gsc_r', bi)))
                            mmS = k.inc('t', nc.tensor.matmul(
                                bankS[:, 0:PT], qhTt[:, h * PT:(h + 1) * PT],
                                khTt[:, kvh * PT:(kvh + 1) * PT],
                                start=True, stop=True))
                            k.wait('v', mmS, cfg['tri_tok'],
                                   tok3.get(('gmsk_r', bi)))
                            tmk = k.inc('v', nc.vector.tensor_mul(
                                msk[bi][:, :], bankS[:, 0:PT],
                                cfg['tri01'][:, :]))
                            tok3[('gsc_r', bi)] = tmk
                            pbk, tmt, b = trp.start(msk[bi][:, :], tmk)
                            k.wait('v', tmt, tok3.get(('gmskT_r', bi)))
                            tmT = k.inc('v', nc.vector.tensor_copy(
                                mskT[bi][:, :], pbk[:, :]))
                            trp.done(b, tmT)
                            tok3[('gmsk_r', bi)] = tmT
                            k.wait('t', tmT, tvl, tqh, tz2,
                                   tok3.get(('S_w', kvh)),
                                   tok3.get(('P_r', hg % 2)) if hh == 0 else None)
                            mm = nc.tensor.matmul(
                                bankP[:, hh * PT:(hh + 1) * PT],
                                vtt[:, kvh * PT:(kvh + 1) * PT], mskT[bi][:, :],
                                start=True,
                                stop=False, skip_group_check=True)
                            mm = nc.tensor.matmul(
                                bankP[:, hh * PT:(hh + 1) * PT],
                                Sb[:, kvh * PT:(kvh + 1) * PT],
                                qhTt[:, h * PT:(h + 1) * PT], start=False,
                                stop=True, skip_group_check=True)
                            tP = k.inc('t', mm)
                            tok3[('gmskT_r', bi)] = tP
                            tok3[('Sb_r', kvh)] = tP
                        k.wait('v', tP, tbl, ('s', k.cnt['s']))
                        for hh in range(4):
                            h = hg * 4 + hh
                            dsl = oT[:, h * 512 + jj * PT: h * 512 + (jj + 1) * PT]
                            tev = k.inc('v', nc.vector.scalar_tensor_tensor(
                                dsl, bankP[:, hh * PT:(hh + 1) * PT],
                                blend[:, point:point + 1], dsl,
                                ALU.mult, ALU.add))
                            t_oT_w.append(tev)
                        tok3[('P_r', hg % 2)] = tev
                    for s in range(KV):
                        k.wait('t', *t_prep, tok3.get('KV_r'))
                        mmK = k.inc('t', nc.tensor.matmul(
                            pb[5][:, 0:PT],
                            khG[:, c * KW + s * PT: c * KW + (s + 1) * PT],
                            vf[:, c * KW + s * PT: c * KW + (s + 1) * PT],
                            start=True, stop=True))
                        k.wait('v', mmK, tz1, tok3.get(('S_r', s)))
                        tS1 = k.inc('v', nc.vector.tensor_tensor(
                            S_sb[:, s * PT:(s + 1) * PT],
                            S_sb[:, s * PT:(s + 1) * PT], pb[5][:, 0:PT],
                            ALU.add))
                        tok3['KV_r'] = tS1
                        k.wait('v', tS1)
                        tS2 = k.inc('v', nc.vector.tensor_scalar_mul(
                            S_sb[:, s * PT:(s + 1) * PT],
                            S_sb[:, s * PT:(s + 1) * PT],
                            ECg[:, c * KV + s: c * KV + s + 1]))
                        k.wait('v', tS2, tok3.get(('Sb_r', s)))
                        tSb = k.inc('v', nc.vector.tensor_copy(
                            Sb[:, s * PT:(s + 1) * PT],
                            S_sb[:, s * PT:(s + 1) * PT]))
                        tok3[('S_r', s)] = tSb
                        tok3[('S_w', s)] = tSb
                tok3['qhTt_r'] = ('t', k.cnt['t'])
                tok3['khTt_r'] = tok3['qhTt_r']
                tok3['vtt_r'] = tok3['qhTt_r']
            t_oT = tmax(*t_oT_w)
            NHD = QW // PT
            for dmq in range(4):
                for hs in range(NHD):
                    wi = woi[0] % 2
                    woi[0] += 1
                    tw = k.dma('g', f'c_wo{wi}', woch[wi][:, :],
                               io['Wo'][hs * PT:(hs + 1) * PT,
                                        dmq * 512:(dmq + 1) * 512],
                               wait=(tok3.get(('wo', wi)),))
                    k.wait('t', tw, *t_oT)
                    if hs == 0:
                        k.wait('t', *[tok3.get(('wops', q)) for q in range(4)])
                    for q4 in range(4):
                        mm = nc.tensor.matmul(
                            pb[q4][:, :], woch[wi][:, q4 * PT:(q4 + 1) * PT],
                            oT[:, hs * 512:(hs + 1) * 512],
                            start=(hs == 0), stop=(hs == NHD - 1))
                    tok3[('wo', wi)] = k.inc('t', mm)
                tmw = tok3[('wo', (woi[0] - 1) % 2)]
                tok3['oT_r'] = tmw
                for q4 in range(4):
                    dms = dmq * 4 + q4
                    k.wait('v', tmw, tok3.get('outT_r'))
                    tev = k.inc('v', nc.vector.tensor_copy(
                        outT[:, dms * 512:(dms + 1) * 512], pb[q4][:, :]))
                    tok3[('wops', q4)] = tev
            t_outT_w = ('v', k.cnt['v'])
            for jj, j in enumerate(tiles):
                rsl = slice(j * PT, (j + 1) * PT)
                tx = k.dma('y', 'c_xt2', xt2[:, :], x_in[rsl, :],
                           wait=(tok3.get('xt2_r'),))
                adds = []
                for dms in range(16):
                    pbk, tm, b = trp.start(
                        outT[:, dms * 512 + jj * PT: dms * 512 + (jj + 1) * PT],
                        t_outT_w)
                    k.wait('v', tm, tx, tok3.get('xo_r'))
                    te = k.inc('v', nc.vector.tensor_tensor(
                        xo[:, dms * PT:(dms + 1) * PT],
                        xt2[:, dms * PT:(dms + 1) * PT], pbk[:, :], ALU.add))
                    trp.done(b, te)
                    adds.append(te)
                tok3['xt2_r'] = adds[-1]
                tok3['outT_r'] = ('t', k.cnt['t'])
                tst = k.dma('s', 'c_xo', x_out[rsl, :], xo[:, :],
                            wait=tuple(tmax(*adds)))
                tok3['xo_r'] = tst
            tok3['oT_r'] = ('t', k.cnt['t'])
    return tst


def emit_ternary_prep(k, trp, w_ap, wT_dram, onesrow, onesrow_tok, pbank):
    nc = k.nc
    n_rows, n_cols = w_ap.shape
    NRT = n_rows // PT
    with k.scope():
        ft = [k.sb(f"tq_f{i}", (PT, n_cols), F32) for i in range(2)]
        sgn = k.sb("tq_sg", (PT, n_cols), F32)
        ab = k.sb("tq_ab", (PT, n_cols), F32)
        bt = k.sb("tq_b", (PT, n_cols), BF16)
        acc = k.sb("tq_acc", (PT, 2), F32)
        mcol = k.sb("tq_m", (PT, 2), F32)
        stg = [k.sb(f"tq_g{i}", (PT, PT), BF16) for i in range(2)]
        one_c, one_tok = k.constcol(1.0)
        tok_l = [None, None]
        tz = k.inc('v', nc.vector.memset(acc[:, :], 0.0))
        tacc = tz
        for rt in range(NRT):
            pi = rt % 2
            tw = k.dma('y', f'c_tq{pi}', ft[pi][:, :],
                       w_ap[rt * PT:(rt + 1) * PT, :], wait=(tok_l[pi],))
            k.wait('v', tw, tacc)
            t1 = k.inc('v', nc.vector.reduce_sum(acc[:, 1:2], ft[pi][:, :],
                                                 axis=AX.X,
                                                 apply_absolute_value=True))
            tok_l[pi] = t1
            k.wait('v', t1)
            tacc = k.inc('v', nc.vector.tensor_add(acc[:, 0:1], acc[:, 0:1],
                                                   acc[:, 1:2]))
        k.wait('t', tacc, one_tok)
        tmm = k.inc('t', nc.tensor.matmul(pbank[0:1, 0:1], one_c[:, 0:1],
                                          acc[:, 0:1], start=True, stop=True))
        k.wait('v', tmm)
        t2 = k.inc('v', nc.vector.tensor_scalar(
            acc[0:1, 0:1], pbank[0:1, 0:1], 1.0 / (n_rows * n_cols), None,
            ALU.mult))
        k.wait('v', t2)
        t3 = k.inc('v', nc.vector.tensor_scalar_max(acc[0:1, 0:1],
                                                    acc[0:1, 0:1], 1e-5))
        k.wait('v', t3)
        t4 = k.inc('v', nc.vector.tensor_scalar(acc[0:1, 1:2], acc[0:1, 0:1],
                                                0.5, None, ALU.mult))
        k.wait('t', t4, onesrow_tok, t2)
        tb = k.inc('t', nc.tensor.matmul(pbank[0:PT, 0:2],
                                         onesrow[0:1, :], acc[0:1, 0:2],
                                         start=True, stop=True))
        k.wait('v', tb)
        tm = k.inc('v', nc.vector.tensor_copy(mcol[:, :], pbank[0:PT, 0:2]))
        tok_l = [tacc, tacc]
        stok = [None, None]
        tok_sgn = None
        last = None
        for rt in range(NRT):
            pi = rt % 2
            rsl = slice(rt * PT, (rt + 1) * PT)
            tw = k.dma('y', f'c_tq{pi}', ft[pi][:, :], w_ap[rsl, :],
                       wait=(tok_l[pi],))
            k.wait('s', tw, tok_sgn)
            ts1 = k.inc('s', nc.scalar.sign(sgn[:, :], ft[pi][:, :]))
            k.wait('s', ts1)
            ts2 = k.inc('s', nc.scalar.activation(ab[:, :], ft[pi][:, :], AF.Abs))
            tok_l[pi] = ts2
            k.wait('v', ts2, tm, tok_sgn)
            tv1 = k.inc('v', nc.vector.tensor_scalar(
                ab[:, :], ab[:, :], mcol[:, 1:2], None, ALU.is_ge))
            k.wait('v', tv1)
            tok_sgn = tb2 = k.inc('v', nc.vector.scalar_tensor_tensor(
                bt[:, :], sgn[:, :], mcol[:, 0:1], ab[:, :], ALU.mult, ALU.mult))
            for ct in range(n_cols // PT):
                si = ct % 2
                pbk, tmt, b = trp.start(bt[:, ct * PT:(ct + 1) * PT], tb2)
                k.wait('v', tmt, stok[si])
                te = k.inc('v', nc.vector.tensor_copy(stg[si][:, :], pbk[:, :]))
                trp.done(b, te)
                last = k.dma('s', f'c_tg{si}', wT_dram[ct, :, rsl], stg[si][:, :],
                             wait=(te,))
                stok[si] = last
            tok_sgn = tmax(tok_sgn, None)[0]
        return tmax(stok[0], stok[1])


def emit_a1a(k, cfg, io):
    nc = k.nc
    n_own = cfg['n_own']
    n_tiles = 2 * n_own
    H, DH = 16, 128
    ident = cfg['ident']
    pb = cfg['pbanks']
    trp = cfg['trp']
    x_in, x_out = io['x_in'], io['x_out']
    qkvT, projT = io['qkvT'], io['projT']
    w_tok = io.get('w_tok') or []
    onesrow, onesrow_tok = cfg['onesrow'], cfg['onesrow_tok']
    tok = {}
    wci = [0]

    if True:
        srow = k.sb("aa_srow", (1, 16), F32)
        sbc = k.sb("aa_sbc", (PT, 16), F32)
        Lt = k.sb("aa_Lt", (PT, 16), F32)
        Lrev = k.sb("aa_Lrev", (PT, 16), F32)
        LT = k.sb("aa_LT", (PT, 16), F32)
        Mh = k.sb("aa_Mh", (PT, 16 * PT), BF16)
        tmpm = k.sb("aa_tmpm", (PT, PT), F32)
        wnq = k.sb("aa_wnq", (PT, D), BF16)
        wnn = k.sb("aa_wnn", (PT, D), BF16)
        wnp = k.sb("aa_wnp", (PT, D), BF16)
        vrow = k.sb("aa_vrow", (1, D), F32)
        blend = k.sb("aa_blend", (PT, 2), F32)

        ts_ = k.dma('y', 'c_aa1', srow[0:1, :], io['s'].unsqueeze(0))
        tbl = k.dma('y', 'c_aa2', blend[:, :], io['blend'])
        k.wait('t', ts_, onesrow_tok)
        t1 = k.inc('t', nc.tensor.matmul(pb[0][0:PT, 0:16], onesrow[0:1, :],
                                         srow[0:1, :], start=True, stop=True))
        k.wait('v', t1)
        t2 = k.inc('v', nc.vector.tensor_copy(sbc[:, :], pb[0][0:PT, 0:16]))
        iota_ap, iota_tok = cfg['iota'], cfg['iota_tok']
        k.wait('v', t2, iota_tok)
        ta = k.inc('v', nc.vector.tensor_scalar(Lt[:, :], sbc[:, :],
                                                iota_ap[:, 0:1], None, ALU.mult))
        tb_ = k.inc('v', nc.vector.tensor_scalar(Lrev[:, :], sbc[:, :],
                                                 iota_ap[:, 1:2], None, ALU.mult))
        tc_ = k.inc('v', nc.vector.tensor_scalar(LT[:, :], sbc[:, :], 128.0,
                                                 None, ALU.mult))
        k.wait('s', ta, tb_, tc_)
        tLt = k.inc('s', nc.scalar.activation(Lt[:, :], Lt[:, :], AF.Exp,
                                              scale=-1.0))
        tLr = k.inc('s', nc.scalar.activation(Lrev[:, :], Lrev[:, :], AF.Exp,
                                              scale=-1.0))
        tLT = k.inc('s', nc.scalar.activation(LT[:, :], LT[:, :], AF.Exp,
                                              scale=-1.0))
        tMh = None
        for h in range(H):
            k.wait('v', t2, cfg['diff_tok'], tMh)
            tm1 = k.inc('v', nc.vector.scalar_tensor_tensor(
                tmpm[:, :], cfg['diff'][:, :], sbc[:, h:h + 1],
                cfg['upmask'][:, :], ALU.mult, ALU.add))
            k.wait('s', tm1)
            tMh = k.inc('s', nc.scalar.activation(Mh[:, h * PT:(h + 1) * PT],
                                                  tmpm[:, :], AF.Exp, scale=-1.0))
            tok['tmpm_r'] = tMh
        t_mh = tMh
        wn_toks = []
        for (win, dst) in ((io['qkv_norm_w'], wnq), (io['norm_w'], wnn),
                           (io['proj_norm_w'], wnp)):
            tv = k.dma('y', 'c_aa3', vrow[0:1, :], win.unsqueeze(0),
                       wait=(wn_toks[-1] if wn_toks else None,))
            for b4 in range(4):
                k.wait('t', tv, tok.get('wnps'))
                tmm = k.inc('t', nc.tensor.matmul(
                    pb[1][:, :], onesrow[0:1, :],
                    vrow[0:1, b4 * 512:(b4 + 1) * 512], start=True, stop=True))
                k.wait('v', tmm)
                te = k.inc('v', nc.vector.tensor_copy(
                    dst[:, b4 * 512:(b4 + 1) * 512], pb[1][:, :]))
                tok['wnps'] = te
            wn_toks.append(te)
        t_wn = tmax(*wn_toks)

    with k.scope():
        xt = k.sb("aa_xt", (PT, D), F32)
        xn = k.sb("aa_xn", (PT, D), BF16)
        sc = dict(st6=k.sb("aa_st6", (PT, 24), F32),
                  st2=k.sb("aa_st2", (PT, 2), F32),
                  rc=k.sb("aa_rc", (PT, 4), F32))
        xnT = k.sb("aa_xnT", (PT, 16 * 512), BF16)
        wch = [k.sb(f"aa_w{i}", (PT, 512), BF16) for i in range(2)]
        qg = k.sb("aa_qg", (PT, 4 * D), BF16)
        kg = k.sb("aa_kg", (PT, 4 * D), BF16)
        vg = k.sb("aa_vg", (PT, 4 * D), BF16)
        qp = k.sb("aa_qp", (PT, D), BF16)
        trb = k.sb("aa_trb", (PT, D), BF16)

        def rms_tile(src, dst, wbc, eps, src_tok, war_tok):
            k.wait('v', src_tok)
            for c in range(4):
                tt1 = k.inc('v', nc.vector.bn_stats(
                    sc['st6'][:, c * 6:(c + 1) * 6],
                    src[:, c * 512:(c + 1) * 512]))
            k.wait('v', tt1)
            tt2 = k.inc('v', nc.vector.bn_aggr(sc['st2'][:, :], sc['st6'][:, :]))
            k.wait('v', tt2)
            tt3 = k.inc('v', nc.vector.scalar_tensor_tensor(
                sc['rc'][:, 1:2], sc['st2'][:, 0:1], sc['st2'][:, 0:1],
                sc['st2'][:, 1:2], ALU.mult, ALU.add))
            eps_ap, eps_tok = k.constcol(eps)
            k.wait('s', tt3, eps_tok)
            tl = k.inc('s', nc.scalar.activation(sc['rc'][:, 0:1],
                                                 sc['rc'][:, 1:2], AF.Ln,
                                                 bias=eps_ap[:, 0:1]))
            k.wait('s', tl)
            tr = k.inc('s', nc.scalar.activation(sc['rc'][:, 0:1],
                                                 sc['rc'][:, 0:1], AF.Exp,
                                                 scale=-0.5))
            k.wait('v', tr, war_tok, *t_wn)
            tx1 = k.inc('v', nc.vector.tensor_scalar_mul(dst[:, :], src[:, :],
                                                         sc['rc'][:, 0:1]))
            k.wait('v', tx1)
            return k.inc('v', nc.vector.tensor_mul(dst[:, :], dst[:, :], wbc))

        def store_T(src, src_tok, dram_dst, chname, war_tok):
            tt = None
            for dt in range(16):
                tt = trp.transpose(src[:, dt * PT:(dt + 1) * PT],
                                   trb[:, dt * PT:(dt + 1) * PT],
                                   src_tok if dt == 0 else None,
                                   war_tok=war_tok if dt == 0 else None)
            return k.dma('s', chname, dram_dst, trb[:, :], wait=(tt,))

        ngrp = n_own // 4
        for g in range(ngrp):
            tiles = list(range(4 * g, 4 * g + 4))
            t_tr = None
            for jj, j in enumerate(tiles):
                tx = k.dma('y', 'c_xt', xt[:, :], x_in[j * PT:(j + 1) * PT, :],
                           wait=(tok.get('xt_r'),))
                txn = rms_tile(xt, xn, wnq[:, :], F32EPS, tx, tok.get('xn_r'))
                tok['xt_r'] = txn
                for dt in range(16):
                    t_tr = trp.transpose(
                        xn[:, dt * PT:(dt + 1) * PT],
                        xnT[:, dt * 512 + jj * PT: dt * 512 + (jj + 1) * PT],
                        txn if dt == 0 else None,
                        war_tok=tok.get('xnT_r') if (jj == 0 and dt == 0) else None)
                tok['xn_r'] = ('t', k.cnt['t'])
            for blk in range(12):
                dst = (qg, kg, vg)[blk // 4]
                bof = (blk % 4) * 512
                for dt in range(16):
                    wi = wci[0] % 2
                    wci[0] += 1
                    tw = k.dma('y', f'c_aw{wi}', wch[wi][:, :],
                               qkvT[dt, :, blk * 512:(blk + 1) * 512],
                               wait=(tok.get(('aw', wi)),
                                     *(w_tok if (blk == 0 and dt == 0 and g == 0)
                                       else [])))
                    k.wait('t', tw, t_tr, *[tok.get(('pg', q)) for q in range(4)])
                    for q4 in range(4):
                        mm = nc.tensor.matmul(
                            pb[q4][:, :],
                            xnT[:, dt * 512 + q4 * PT: dt * 512 + (q4 + 1) * PT],
                            wch[wi][:, :], start=(dt == 0), stop=(dt == 15))
                    tok[('aw', wi)] = k.inc('t', mm)
                tmm = tok[('aw', (wci[0] - 1) % 2)]
                for q4 in range(4):
                    k.wait('v', tmm, *(tok.get('qkv_r') or []))
                    te = k.inc('v', nc.vector.tensor_copy(
                        dst[:, q4 * D + bof: q4 * D + bof + 512], pb[q4][:, :]))
                    tok[('pg', q4)] = te
            t_proj = ('v', k.cnt['v'])
            for jj, j in enumerate(tiles):
                tq = store_T(qg[:, jj * D:(jj + 1) * D], t_proj, io['qT'][j],
                             'c_sq', tok.get('trb_r'))
                tok['trb_r'] = tq
                tk = store_T(kg[:, jj * D:(jj + 1) * D], t_proj, io['kT'][j],
                             'c_sk', tok.get('trb_r'))
                tok['trb_r'] = tk
                k.wait('v', tLt, tok.get('qp_r'))
                for h in range(H):
                    tqp = k.inc('v', nc.vector.tensor_scalar_mul(
                        qp[:, h * PT:(h + 1) * PT],
                        qg[:, jj * D + h * PT: jj * D + (h + 1) * PT],
                        Lt[:, h:h + 1]))
                tqs = store_T(qp, tqp, io['qpT'][j], 'c_sp', tok.get('trb_r'))
                tok['trb_r'] = tqs
                tok['qp_r'] = ('t', k.cnt['t'])
                k.wait('v', tLr, tok.get('qp_r'))
                for h in range(H):
                    tk2 = k.inc('v', nc.vector.tensor_scalar_mul(
                        qp[:, h * PT:(h + 1) * PT],
                        kg[:, jj * D + h * PT: jj * D + (h + 1) * PT],
                        Lrev[:, h:h + 1]))
                ta1 = k.dma('s', 'c_agi', io['ag_in'][j * PT:(j + 1) * PT, 0:D],
                            qp[:, :], wait=(tk2,))
                tok['qp_r'] = ta1
                ta2 = k.dma('s', 'c_agi2',
                            io['ag_in'][j * PT:(j + 1) * PT, D:2 * D],
                            vg[:, jj * D:(jj + 1) * D], wait=(t_proj,))
                tv2 = k.dma('s', 'c_sv', io['vT'][j], vg[:, jj * D:(jj + 1) * D],
                            wait=(t_proj,))
                tok.setdefault('qkv_stores', []).extend([tq, tk, ta1, ta2, tv2])
            tok['qkv_r'] = tmax(*tok.pop('qkv_stores'))
            tok['xnT_r'] = ('t', k.cnt['t'])

    k.barrier()
    ch = k.chan('c_col')
    inst = nc.gpsimd.collective_compute(
        "AllGather", ALU.bypass, cfg['replica_groups'],
        [io['ag_in'].opt()], [io['ag_out'].opt()])
    k.cnt[ch] += 1
    inst.then_inc(k.sem[ch], 1)
    t_ag = (ch, k.cnt[ch])

    with k.scope():
        S_sb = k.sb("aa_S", (PT, H * PT), F32)
        Sb = k.sb("aa_Sb", (PT, H * PT), BF16)
        o_sb = k.sb("aa_o", (PT, D), F32)
        o2_sb = k.sb("aa_o2", (PT, D), F32)
        kvc = k.sb("aa_kvc", (PT, 2 * D), BF16)
        qTt = k.sb("aa_qTt", (PT, D), BF16)
        qpTt = k.sb("aa_qpTt", (PT, D), BF16)
        kTt = k.sb("aa_kTt", (PT, D), BF16)
        vt = k.sb("aa_vt", (PT, D), BF16)
        msk = [k.sb(f"aa_msk{i}", (PT, PT), BF16) for i in range(2)]
        mskT = [k.sb(f"aa_mskT{i}", (PT, PT), BF16) for i in range(2)]
        y2 = k.sb("aa_y2", (PT, D), BF16)
        y2T = k.sb("aa_y2T", (PT, D), BF16)
        pch = [k.sb(f"aa_pw{i}", (PT, 512), BF16) for i in range(2)]
        xo = k.sb("aa_xo", (PT, D), F32)
        sc2 = dict(st6=k.sb("aa_st6b", (PT, 24), F32),
                   st2=k.sb("aa_st2b", (PT, 2), F32),
                   rc=k.sb("aa_rcb", (PT, 4), F32))

        tz1 = k.inc('v', nc.vector.memset(S_sb[:, :], 0.0))
        tz2 = k.inc('v', nc.vector.memset(Sb[:, :], 0.0))

        def rms_tile2(src, dst, wbc, eps, src_tok, war_tok):
            k.wait('v', src_tok)
            for c in range(4):
                tt1 = k.inc('v', nc.vector.bn_stats(
                    sc2['st6'][:, c * 6:(c + 1) * 6],
                    src[:, c * 512:(c + 1) * 512]))
            k.wait('v', tt1)
            tt2 = k.inc('v', nc.vector.bn_aggr(sc2['st2'][:, :], sc2['st6'][:, :]))
            k.wait('v', tt2)
            tt3 = k.inc('v', nc.vector.scalar_tensor_tensor(
                sc2['rc'][:, 1:2], sc2['st2'][:, 0:1], sc2['st2'][:, 0:1],
                sc2['st2'][:, 1:2], ALU.mult, ALU.add))
            eps_ap, eps_tok = k.constcol(eps)
            k.wait('s', tt3, eps_tok)
            tl = k.inc('s', nc.scalar.activation(sc2['rc'][:, 0:1],
                                                 sc2['rc'][:, 1:2], AF.Ln,
                                                 bias=eps_ap[:, 0:1]))
            k.wait('s', tl)
            tr = k.inc('s', nc.scalar.activation(sc2['rc'][:, 0:1],
                                                 sc2['rc'][:, 0:1], AF.Exp,
                                                 scale=-0.5))
            k.wait('v', tr, war_tok)
            tx1 = k.inc('v', nc.vector.tensor_scalar_mul(dst[:, :], src[:, :],
                                                         sc2['rc'][:, 0:1]))
            k.wait('v', tx1)
            return k.inc('v', nc.vector.tensor_mul(dst[:, :], dst[:, :], wbc))

        tst = None
        for j in range(n_own):
            tql = k.dma('y', 'c_lq', qTt[:, :], io['qT'][j],
                        wait=(tok.get('qTt_r'),))
            tqp = k.dma('y', 'c_lp', qpTt[:, :], io['qpT'][j],
                        wait=(tok.get('qpTt_r'),))
            tkl = k.dma('y', 'c_lk', kTt[:, :], io['kT'][j],
                        wait=(tok.get('kTt_r'),))
            tvl = k.dma('y', 'c_lv', vt[:, :], io['vT'][j],
                        wait=(tok.get('vt_r'),))
            for point in range(2):
                c = 2 * j + point
                for hg in range(4):
                    bankP = pb[2 + hg % 2]
                    for hh in range(4):
                        h = hg * 4 + hh
                        bi = h % 2
                        bankS = pb[bi]
                        k.wait('t', tql, tkl, tok.get(('sc_r', bi)))
                        mmS = k.inc('t', nc.tensor.matmul(
                            bankS[:, 0:PT], qTt[:, h * PT:(h + 1) * PT],
                            kTt[:, h * PT:(h + 1) * PT], start=True, stop=True))
                        k.wait('v', mmS, t_mh, tok.get(('msk_r', bi)))
                        tmk = k.inc('v', nc.vector.tensor_mul(
                            msk[bi][:, :], bankS[:, 0:PT],
                            Mh[:, h * PT:(h + 1) * PT]))
                        tok[('sc_r', bi)] = tmk
                        pbk, tmt, b = trp.start(msk[bi][:, :], tmk)
                        k.wait('v', tmt, tok.get(('mskT_r', bi)))
                        tmT = k.inc('v', nc.vector.tensor_copy(mskT[bi][:, :],
                                                               pbk[:, :]))
                        trp.done(b, tmT)
                        tok[('msk_r', bi)] = tmT
                        k.wait('t', tmT, tvl, tqp, tz2,
                               tok.get(('S_r', h)),
                               tok.get(('P_r', hg % 2)) if hh == 0 else None)
                        mm = nc.tensor.matmul(
                            bankP[:, hh * PT:(hh + 1) * PT], mskT[bi][:, :],
                            vt[:, h * PT:(h + 1) * PT], start=True,
                            stop=False, skip_group_check=True)
                        mm = nc.tensor.matmul(
                            bankP[:, hh * PT:(hh + 1) * PT],
                            qpTt[:, h * PT:(h + 1) * PT],
                            Sb[:, h * PT:(h + 1) * PT], start=False, stop=True,
                            skip_group_check=True)
                        tok[('mskT_r', bi)] = tP = k.inc('t', mm)
                        tok[('Sb_r', h)] = tP
                    k.wait('v', tP, tbl, tok.get('o_r') if hg == 0 and point == 0 else None)
                    if point == 0:
                        tev = k.inc('v', nc.vector.tensor_scalar_mul(
                            o_sb[:, hg * 512:(hg + 1) * 512], bankP[:, :],
                            blend[:, 0:1]))
                    else:
                        tev = k.inc('v', nc.vector.scalar_tensor_tensor(
                            o_sb[:, hg * 512:(hg + 1) * 512], bankP[:, :],
                            blend[:, 1:2], o_sb[:, hg * 512:(hg + 1) * 512],
                            ALU.mult, ALU.add))
                    tok[('P_r', hg % 2)] = tev
                t_o = tev
                row0 = (c % 2) * n_own * PT + (c // 2) * PT
                tkv = k.dma('y', 'c_kvc', kvc[:, :],
                            io['ag_out'][row0:row0 + PT, :],
                            wait=(t_ag, tok.get('kvc_r')))
                for h in range(H):
                    k.wait('t', tkv, tok.get('KV_r'))
                    mmK = k.inc('t', nc.tensor.matmul(
                        pb[4][:, 0:PT], kvc[:, h * PT:(h + 1) * PT],
                        kvc[:, D + h * PT: D + (h + 1) * PT],
                        start=True, stop=True))
                    k.wait('v', mmK, tLT, tz1,
                           tok.get(('S_r', h)))
                    tS = k.inc('v', nc.vector.scalar_tensor_tensor(
                        S_sb[:, h * PT:(h + 1) * PT],
                        S_sb[:, h * PT:(h + 1) * PT], LT[:, h:h + 1],
                        pb[4][:, 0:PT], ALU.mult, ALU.add))
                    tok['KV_r'] = tS
                    k.wait('v', tS, tok.get(('Sb_r', h)))
                    tSb = k.inc('v', nc.vector.tensor_copy(
                        Sb[:, h * PT:(h + 1) * PT],
                        S_sb[:, h * PT:(h + 1) * PT]))
                    tok[('S_r', h)] = tSb
                tok['kvc_r'] = ('t', k.cnt['t'])
            tok['qTt_r'] = ('t', k.cnt['t'])
            tok['kTt_r'] = tok['qTt_r']
            tok['qpTt_r'] = tok['qTt_r']
            tok['vt_r'] = tok['qTt_r']
            tr1 = rms_tile2(o_sb, o2_sb, wnn[:, :], 1e-6, t_o, tok.get('o2_r'))
            tok['o_r'] = tr1
            tr2 = rms_tile2(o2_sb, y2, wnp[:, :], F32EPS, tr1, tok.get('y2_r'))
            tok['o2_r'] = tr2
            tt = None
            for dt in range(16):
                tt = trp.transpose(y2[:, dt * PT:(dt + 1) * PT],
                                   y2T[:, dt * PT:(dt + 1) * PT],
                                   tr2 if dt == 0 else None,
                                   war_tok=tok.get('y2T_r') if dt == 0 else None)
            tok['y2_r'] = ('t', k.cnt['t'])
            tx = k.dma('y', 'c_xt3', o2_sb[:, :], x_in[j * PT:(j + 1) * PT, :],
                       wait=(tr2,))
            adds = []
            for dmb in range(4):
                bankD = pb[5]
                k.wait('t', tok.get('pD_r'))
                for ft in range(16):
                    wi = wci[0] % 2
                    wci[0] += 1
                    tw = k.dma('y', f'c_aw{wi}', pch[wi % 2][:, :],
                               projT[ft, :, dmb * 512:(dmb + 1) * 512],
                               wait=(tok.get(('aw2', wi % 2)),))
                    k.wait('t', tw, tt)
                    mm = nc.tensor.matmul(bankD[:, :],
                                          y2T[:, ft * PT:(ft + 1) * PT],
                                          pch[wi % 2][:, :],
                                          start=(ft == 0), stop=(ft == 15))
                    tok[('aw2', wi % 2)] = k.inc('t', mm)
                tmm = tok[('aw2', (wci[0] - 1) % 2)]
                tok['y2T_r'] = tmm
                k.wait('v', tmm, tx, tok.get('xo_r'))
                te = k.inc('v', nc.vector.tensor_tensor(
                    xo[:, dmb * 512:(dmb + 1) * 512],
                    o2_sb[:, dmb * 512:(dmb + 1) * 512], bankD[:, :], ALU.add))
                tok['pD_r'] = te
                adds.append(te)
            tok['o2_r'] = adds[-1]
            tst = k.dma('s', 'c_xo', x_out[j * PT:(j + 1) * PT, :], xo[:, :],
                        wait=tuple(tmax(*adds)))
            tok['xo_r'] = tst
    return tst


def build_f0_program(n_own=16):
    nc = bass.Bass("TRN2", target_bir_lowering=False, debug=False)
    HID = 5632
    x_in = nc.dram_tensor("x_shard", [n_own * PT, D], F32,
                          kind="ExternalInput").ap()
    wq_g = nc.dram_tensor("wq_gate", [HID, D], I32, kind="ExternalInput").ap()
    s_g = nc.dram_tensor("s_gate", [HID], F32, kind="ExternalInput").ap()
    wq_u = nc.dram_tensor("wq_up", [HID, D], I32, kind="ExternalInput").ap()
    s_u = nc.dram_tensor("s_up", [HID], F32, kind="ExternalInput").ap()
    wq_d = nc.dram_tensor("wq_down", [D, HID], I32, kind="ExternalInput").ap()
    s_d = nc.dram_tensor("s_down", [D], F32, kind="ExternalInput").ap()
    ident_in = nc.dram_tensor("ident", [PT, PT], BF16, kind="ExternalInput").ap()
    x_out = nc.dram_tensor("y_shard", [n_own * PT, D], F32,
                           kind="ExternalOutput").ap()
    wgT = nc.dram_tensor("wgT", [16, PT, HID], BF16, kind="Internal").ap()
    wuT = nc.dram_tensor("wuT", [16, PT, HID], BF16, kind="Internal").ap()
    wdT = nc.dram_tensor("wdT", [16, PT, 5632], BF16, kind="Internal").ap()

    with ExitStack() as ctx:
        k = K(nc, ctx)
        k.constcol(1e-5)
        ident = k.sb("ident_sb", (PT, PT), BF16)
        pbanks = [k.ps(f"pb{i}", (PT, 512), F32) for i in range(6)]
        ptr = [k.ps(f"ptr{i}", (PT, 1024), BF16) for i in range(2)]
        block = ctx.enter_context(nc.Block())

        @block.sync
        def _(e):
            trp = TrPool(k, ident, ptr[0][:, 0:PT], ptr[1][:, 0:PT])
            ti = k.dma('y', 'c_id', ident[:, :], ident_in)
            k.wait('t', ti)
            emit_f0_prep(k, trp, wq_g, s_g, wgT)
            emit_f0_prep(k, trp, wq_u, s_u, wuT)
            tw = emit_f0_prep(k, trp, wq_d, s_d, wdT, key_by_row=True)
            cfg = dict(n_own=n_own, ident=ident, pbanks=pbanks, trp=trp)
            io = dict(x_in=x_in, x_out=x_out, wgT=wgT, wuT=wuT, wdT=wdT,
                      w_tok=tw)
            tst = emit_f0(k, cfg, io)
            k.dma_fence('y')
    return nc



def host_rope_tables(positions, DH, base=10000.0):
    n_own = len(positions) // PT
    inv = base ** (-np.arange(0, DH, 2, dtype=np.float64) / DH)
    t = positions.astype(np.float64)[:, None] * inv[None, :]
    cos = np.cos(t).astype(np.float32)
    sin = np.sin(t).astype(np.float32)
    cosw = np.concatenate([cos, cos], -1)
    sinw = np.concatenate([-sin, sin], -1)

    def tilemaj(a):
        return np.ascontiguousarray(
            a.reshape(n_own, PT, DH).transpose(1, 0, 2).reshape(PT, n_own * DH))
    return tilemaj(cosw).astype(BFNP), tilemaj(sinw).astype(BFNP)


def host_dmask(par):
    tri = np.where(np.arange(PT)[None, :] <= np.arange(PT)[:, None], 0.0, NEG)
    m = np.zeros((PT, 2 * PT), np.float32)
    if par == 0:
        m[:, :PT] = tri
        m[:, PT:] = NEG
    else:
        m[:, :PT] = 0.0
        m[:, PT:] = tri
    return m.astype(np.float32)


def build_attn_program(n_own, n_tiles, nH, nKV, DH, scale, replica_groups,
                       combined_kv=False, rope_k=True):
    nc = bass.Bass("TRN2", target_bir_lowering=False, debug=False)
    QW, KW = nH * DH, nKV * DH
    x_in = nc.dram_tensor("x_shard", [n_own * PT, D], F32,
                          kind="ExternalInput").ap()
    if combined_kv:
        Wq = nc.dram_tensor("Wq", [D, QW], F32, kind="ExternalInput").ap()
        Wkv = nc.dram_tensor("Wkv", [D, 2 * KW], F32, kind="ExternalInput").ap()
        Wk, Wv = Wkv[:, 0:KW], Wkv[:, KW:2 * KW]
    else:
        Wq = nc.dram_tensor("Wq", [D, QW], F32, kind="ExternalInput").ap()
        Wk = nc.dram_tensor("Wk", [D, KW], F32, kind="ExternalInput").ap()
        Wv = nc.dram_tensor("Wv", [D, KW], F32, kind="ExternalInput").ap()
    Wo = nc.dram_tensor("Wo", [QW, D], F32, kind="ExternalInput").ap()
    tabc_in = nc.dram_tensor("tabc", [PT, n_own * DH], BF16,
                             kind="ExternalInput").ap()
    tabs_in = nc.dram_tensor("tabs", [PT, n_own * DH], BF16,
                             kind="ExternalInput").ap()
    dmask_in = nc.dram_tensor("dmask", [PT, 2 * PT], F32,
                              kind="ExternalInput").ap()
    ident_in = nc.dram_tensor("ident", [PT, PT], BF16,
                              kind="ExternalInput").ap()
    x_out = nc.dram_tensor("y_shard", [n_own * PT, D], F32,
                           kind="ExternalOutput").ap()
    qT = nc.dram_tensor("qT", [n_own, PT, QW], BF16, kind="Internal").ap()
    ag_in = nc.dram_tensor("ag_in", [n_own * PT, 2 * KW], BF16,
                           kind="Internal").ap()
    ag_out = nc.dram_tensor("ag_out", [n_tiles * PT, 2 * KW], BF16,
                            kind="Internal").ap()

    with ExitStack() as ctx:
        k = K(nc, ctx)
        k.constcol(1e-5)
        ident = k.sb("ident_sb", (PT, PT), BF16)
        tabc = k.sb("tabc_sb", (PT, n_own * DH), BF16)
        tabs = k.sb("tabs_sb", (PT, n_own * DH), BF16)
        dmask = k.sb("dmask_sb", (PT, 2 * PT), F32)
        pbanks = [k.ps(f"pb{i}", (PT, 512), F32) for i in range(6)]
        ptr = [k.ps(f"ptr{i}", (PT, 1024), BF16) for i in range(2)]
        block = ctx.enter_context(nc.Block())

        @block.sync
        def _(e):
            trp = TrPool(k, ident, ptr[0][:, 0:PT], ptr[1][:, 0:PT])
            ti = k.dma('y', 'c_id', ident[:, :], ident_in)
            tc = k.dma('y', 'c_id', tabc[:, :], tabc_in)
            ts = k.dma('y', 'c_id', tabs[:, :], tabs_in)
            tdm = k.dma('y', 'c_id', dmask[:, :], dmask_in)
            k.wait('t', tdm)
            k.wait('v', tdm)
            cfg = dict(n_tiles=n_tiles, own_tiles=list(range(n_own)),
                       n_own=n_own, nH=nH, nKV=nKV, DH=DH, scale=scale,
                       ident=ident, pbanks=pbanks, trp=trp,
                       tabc=tabc[:, :], tabs=tabs[:, :], tab_tok=ts,
                       dmask=dmask, dmask_tok=tdm,
                       replica_groups=replica_groups, rope_k=rope_k)
            io = dict(x_in=x_in, x_out=x_out, Wq=Wq, Wk=Wk, Wv=Wv, Wo=Wo,
                      qT=qT, ag_in=ag_in, ag_out=ag_out)
            emit_attn(k, cfg, io)
            k.dma_fence('y')
    return nc



def build_f1_program(n_own=16):
    nc = bass.Bass("TRN2", target_bir_lowering=False, debug=False)
    x_in = nc.dram_tensor("x_shard", [n_own * PT, D], F32,
                          kind="ExternalInput").ap()
    w_gate = nc.dram_tensor("w_gate", [D, 8], F32, kind="ExternalInput").ap()
    w_gu = nc.dram_tensor("w_gate_up", [8, D, D], F32, kind="ExternalInput").ap()
    w_dn = nc.dram_tensor("w_down", [8, 1024, D], F32, kind="ExternalInput").ap()
    w_sg = nc.dram_tensor("w_sh_gate_up", [D, 8192], F32,
                          kind="ExternalInput").ap()
    w_sd = nc.dram_tensor("w_sh_down", [4096, D], F32, kind="ExternalInput").ap()
    ident_in = nc.dram_tensor("ident", [PT, PT], BF16, kind="ExternalInput").ap()
    esel_in = nc.dram_tensor("esel", [8, 8 * PT], BF16, kind="ExternalInput").ap()
    x_out = nc.dram_tensor("y_shard", [n_own * PT, D], F32,
                           kind="ExternalOutput").ap()

    with ExitStack() as ctx:
        k = K(nc, ctx)
        k.constcol(1e-5)
        ident = k.sb("ident_sb", (PT, PT), BF16)
        identf = k.sb("identf_sb", (PT, PT), F32)
        esel = k.sb("esel_sb", (8, 8 * PT), BF16)
        pbanks = [k.ps(f"pb{i}", (PT, 512), F32) for i in range(6)]
        ptr = [k.ps(f"ptr{i}", (PT, 1024), BF16) for i in range(2)]
        block = ctx.enter_context(nc.Block())

        @block.sync
        def _(e):
            trp = TrPool(k, ident, ptr[0][:, 0:PT], ptr[1][:, 0:PT])
            ti = k.dma('y', 'c_id', ident[:, :], ident_in)
            tif = k.dma('g', 'c_id2', identf[:, :], ident_in)
            te = k.dma('y', 'c_id', esel[0:8, :], esel_in)
            k.wait('t', te)
            k.wait('t', tif)
            cfg = dict(n_own=n_own, ident=ident, identf=identf, pbanks=pbanks,
                       trp=trp, esel=esel, esel_tok=te)
            io = dict(x_in=x_in, x_out=x_out, w_gate=w_gate, w_gate_up=w_gu,
                      w_down=w_dn, w_sh_gate_up=w_sg, w_sh_down=w_sd)
            emit_f1(k, cfg, io)
            k.dma_fence('y')
    return nc


def host_esel():
    e = np.zeros((8, 8 * PT), dtype=BFNP)
    for i in range(8):
        e[i, i * PT:(i + 1) * PT] = 1
    return e



def build_a1a_program(n_own, n_tiles, replica_groups):
    nc = bass.Bass("TRN2", target_bir_lowering=False, debug=False)
    x_in = nc.dram_tensor("x_shard", [n_own * PT, D], F32,
                          kind="ExternalInput").ap()
    qkv_w = nc.dram_tensor("qkv_w", [3 * D, D], F32, kind="ExternalInput").ap()
    proj_w = nc.dram_tensor("proj_w", [D, D], F32, kind="ExternalInput").ap()
    qn = nc.dram_tensor("qkv_norm_w", [D], F32, kind="ExternalInput").ap()
    nw = nc.dram_tensor("norm_w", [D], F32, kind="ExternalInput").ap()
    pn = nc.dram_tensor("proj_norm_w", [D], F32, kind="ExternalInput").ap()
    s_in = nc.dram_tensor("s", [16], F32, kind="ExternalInput").ap()
    ident_in = nc.dram_tensor("ident", [PT, PT], BF16, kind="ExternalInput").ap()
    ones_in = nc.dram_tensor("onesrow", [1, PT], F32, kind="ExternalInput").ap()
    blend_in = nc.dram_tensor("blend", [PT, 2], F32, kind="ExternalInput").ap()
    iota_in = nc.dram_tensor("iota2", [PT, 2], F32, kind="ExternalInput").ap()
    diff_in = nc.dram_tensor("diff", [PT, PT], F32, kind="ExternalInput").ap()
    up_in = nc.dram_tensor("upmask", [PT, PT], F32, kind="ExternalInput").ap()
    x_out = nc.dram_tensor("y_shard", [n_own * PT, D], F32,
                           kind="ExternalOutput").ap()
    qkvT = nc.dram_tensor("qkvT", [16, PT, 3 * D], BF16, kind="Internal").ap()
    projT = nc.dram_tensor("projT", [16, PT, D], BF16, kind="Internal").ap()
    qT = nc.dram_tensor("qT_s", [n_own, PT, D], BF16, kind="Internal").ap()
    qpT = nc.dram_tensor("qpT_s", [n_own, PT, D], BF16, kind="Internal").ap()
    kT = nc.dram_tensor("kT_s", [n_own, PT, D], BF16, kind="Internal").ap()
    vT = nc.dram_tensor("vT_s", [n_own, PT, D], BF16, kind="Internal").ap()
    ag_in = nc.dram_tensor("ag_in", [n_own * PT, 2 * D], BF16,
                           kind="Internal").ap()
    ag_out = nc.dram_tensor("ag_out", [n_tiles * PT, 2 * D], BF16,
                            kind="Internal").ap()

    with ExitStack() as ctx:
        k = K(nc, ctx)
        k.constcol(1.0)
        k.constcol(1e-6)
        k.constcol(F32EPS)
        ident = k.sb("ident_sb", (PT, PT), BF16)
        onesrow = k.sb("ones_sb", (1, PT), F32)
        iota2 = k.sb("iota_sb", (PT, 2), F32)
        diff = k.sb("diff_sb", (PT, PT), F32)
        upm = k.sb("up_sb", (PT, PT), F32)
        pbanks = [k.ps(f"pb{i}", (PT, 512), F32) for i in range(6)]
        ptr = [k.ps(f"ptr{i}", (PT, 1024), BF16) for i in range(2)]
        block = ctx.enter_context(nc.Block())

        @block.sync
        def _(e):
            trp = TrPool(k, ident, ptr[0][:, 0:PT], ptr[1][:, 0:PT])
            ti = k.dma('y', 'c_id', ident[:, :], ident_in)
            to = k.dma('y', 'c_id', onesrow[0:1, :], ones_in)
            ti2 = k.dma('y', 'c_id', iota2[:, :], iota_in)
            td = k.dma('y', 'c_id', diff[:, :], diff_in)
            tu = k.dma('y', 'c_id', upm[:, :], up_in)
            k.wait('t', tu)
            k.wait('v', tu)
            t1 = emit_ternary_prep(k, trp, qkv_w, qkvT, onesrow, to, pbanks[5])
            t2 = emit_ternary_prep(k, trp, proj_w, projT, onesrow, to, pbanks[5])
            cfg = dict(n_own=n_own, ident=ident, pbanks=pbanks, trp=trp,
                       onesrow=onesrow, onesrow_tok=to, iota=iota2,
                       iota_tok=ti2, diff=diff, diff_tok=td, upmask=upm,
                       replica_groups=replica_groups)
            io = dict(x_in=x_in, x_out=x_out, qkvT=qkvT, projT=projT,
                      qT=qT, qpT=qpT, kT=kT, vT=vT, ag_in=ag_in, ag_out=ag_out,
                      s=s_in, blend=blend_in, qkv_norm_w=qn, norm_w=nw,
                      proj_norm_w=pn, w_tok=tmax(*(t1 + t2)))
            emit_a1a(k, cfg, io)
            k.dma_fence('y')
    return nc


def host_a1a_consts(par):
    t = np.arange(PT, dtype=np.float32)
    iota2 = np.stack([t, PT - t], 1).astype(np.float32)
    diff = (t[:, None] - t[None, :]).astype(np.float32)
    upm = np.where(t[None, :] <= t[:, None], 0.0, 1e30).astype(np.float32)
    scale = float(128 ** -0.5)
    blend = np.zeros((PT, 2), np.float32)
    blend[:, par] = scale
    ones = np.ones((1, PT), np.float32)
    return iota2, diff, upm, blend, ones



def build_a0b_program(n_own, n_tiles, replica_groups):
    nc = bass.Bass("TRN2", target_bir_lowering=False, debug=False)
    H, KV, DH = 16, 4, 128
    QW, KW = H * DH, KV * DH
    x_in = nc.dram_tensor("x_shard", [n_own * PT, D], F32,
                          kind="ExternalInput").ap()
    Wq = nc.dram_tensor("Wq", [D, QW], F32, kind="ExternalInput").ap()
    Wk = nc.dram_tensor("Wk", [D, KW], F32, kind="ExternalInput").ap()
    Wv = nc.dram_tensor("Wv", [D, KW], F32, kind="ExternalInput").ap()
    Wo = nc.dram_tensor("Wo", [QW, D], F32, kind="ExternalInput").ap()
    tabc_o = nc.dram_tensor("tabc_o", [PT, n_own * DH], BF16,
                            kind="ExternalInput").ap()
    tabs_o = nc.dram_tensor("tabs_o", [PT, n_own * DH], BF16,
                            kind="ExternalInput").ap()
    tabc_a = nc.dram_tensor("tabc_a", [PT, n_tiles * DH], BF16,
                            kind="ExternalInput").ap()
    tabs_a = nc.dram_tensor("tabs_a", [PT, n_tiles * DH], BF16,
                            kind="ExternalInput").ap()
    dmask_in = nc.dram_tensor("dmask", [PT, 2 * PT], F32,
                              kind="ExternalInput").ap()
    tri01_in = nc.dram_tensor("tri01", [PT, PT], BF16,
                              kind="ExternalInput").ap()
    utri_in = nc.dram_tensor("utri", [PT, PT], F32, kind="ExternalInput").ap()
    ident_in = nc.dram_tensor("ident", [PT, PT], BF16,
                              kind="ExternalInput").ap()
    blend_in = nc.dram_tensor("blend", [PT, 2], F32, kind="ExternalInput").ap()
    x_out = nc.dram_tensor("y_shard", [n_own * PT, D], F32,
                           kind="ExternalOutput").ap()
    qT = nc.dram_tensor("qT_s", [n_own, PT, QW], BF16, kind="Internal").ap()
    qhT = nc.dram_tensor("qhT_s", [n_own, PT, QW], BF16, kind="Internal").ap()
    khT = nc.dram_tensor("khT_s", [n_own, PT, KW], BF16, kind="Internal").ap()
    vT = nc.dram_tensor("vT_s", [n_own, PT, KW], BF16, kind="Internal").ap()
    ag_in = nc.dram_tensor("ag_in", [n_own * PT, 2 * KW], BF16,
                           kind="Internal").ap()
    ag_out = nc.dram_tensor("ag_out", [n_tiles * PT, 2 * KW], BF16,
                            kind="Internal").ap()

    with ExitStack() as ctx:
        k = K(nc, ctx)
        k.constcol(1.0)
        k.constcol(1e-5)
        ident = k.sb("ident_sb", (PT, PT), BF16)
        dmask = k.sb("dmask_sb", (PT, 2 * PT), F32)
        tri01 = k.sb("tri01_sb", (PT, PT), BF16)
        utri = k.sb("utri_sb", (PT, PT), F32)
        pbanks = [k.ps(f"pb{i}", (PT, 512), F32) for i in range(6)]
        ptr = [k.ps(f"ptr{i}", (PT, 1024), BF16) for i in range(2)]
        block = ctx.enter_context(nc.Block())

        @block.sync
        def _(e):
            trp = TrPool(k, ident, ptr[0][:, 0:PT], ptr[1][:, 0:PT])
            ti = k.dma('y', 'c_id', ident[:, :], ident_in)
            t5 = k.dma('y', 'c_id', dmask[:, :], dmask_in)
            t6 = k.dma('y', 'c_id', tri01[:, :], tri01_in)
            t7 = k.dma('y', 'c_id', utri[:, :], utri_in)
            k.wait('t', t7)
            k.wait('v', t7)
            k.wait('s', t7)
            cfg = dict(n_own=n_own, ident=ident, pbanks=pbanks, trp=trp,
                       tabc_o_d=tabc_o, tabs_o_d=tabs_o, tabc_a_d=tabc_a,
                       tabs_a_d=tabs_a, dmask=dmask, dmask_tok=t7,
                       tri01=tri01, tri_tok=t7, utri=utri, utri_tok=t7,
                       replica_groups=replica_groups)
            io = dict(x_in=x_in, x_out=x_out, Wq=Wq, Wk=Wk, Wv=Wv, Wo=Wo,
                      qT=qT, qhT=qhT, khT=khT, vT=vT, ag_in=ag_in,
                      ag_out=ag_out, blend=blend_in)
            emit_a0b(k, cfg, io)
            k.dma_fence('y')
    return nc


def host_a0b_consts(par):
    t = np.arange(PT)
    tri01 = (t[None, :] <= t[:, None]).astype(np.float32)
    utri = (t[None, :] >= t[:, None]).astype(np.float32)
    blend = np.zeros((PT, 2), np.float32)
    blend[:, par] = 0.5
    return tri01.astype(BFNP), np.ascontiguousarray(utri), blend



def build_fused_program(n_own=16, n_tiles=32,
                        RG=((0, 1), (2, 3), (4, 5), (6, 7))):
    RG = [list(g) for g in RG]
    nc = bass.Bass("TRN2", target_bir_lowering=False, debug=False)
    HID = 5632

    def ein(name, shape, dt=F32):
        return nc.dram_tensor(name, list(shape), dt, kind="ExternalInput").ap()

    x_in0 = ein("x_shard", (n_own * PT, D))
    a0a_Wq = ein("L0aWq", (D, 2048))
    a0a_Wkv = ein("L0aWkv", (D, 1024))
    a0a_Wo = ein("L0aWo", (2048, D))
    f0_wqg = ein("F0wg", (HID, D), I32)
    f0_sg = ein("F0sg", (HID,))
    f0_wqu = ein("F0wu", (HID, D), I32)
    f0_su = ein("F0su", (HID,))
    f0_wqd = ein("F0wd", (D, HID), I32)
    f0_sd = ein("F0sd", (D,))
    a0b_Wq = ein("L0bWq", (D, 2048))
    a0b_Wk = ein("L0bWk", (D, 512))
    a0b_Wv = ein("L0bWv", (D, 512))
    a0b_Wo = ein("L0bWo", (2048, D))
    a1a_qkv = ein("L1aqkv", (3 * D, D))
    a1a_proj = ein("L1aproj", (D, D))
    a1a_qn = ein("L1aqn", (D,))
    a1a_nw = ein("L1anw", (D,))
    a1a_pn = ein("L1apn", (D,))
    a1a_s = ein("L1as", (16,))
    f1_wg = ein("F1wg", (D, 8))
    f1_wgu = ein("F1wgu", (8, D, D))
    f1_wd = ein("F1wd", (8, 1024, D))
    f1_wsg = ein("F1wsg", (D, 8192))
    f1_wsd = ein("F1wsd", (4096, D))
    a1b_Wq = ein("L1bWq", (D, 2048))
    a1b_Wk = ein("L1bWk", (D, 512))
    a1b_Wv = ein("L1bWv", (D, 512))
    a1b_Wo = ein("L1bWo", (2048, D))
    ident_in = ein("ident", (PT, PT), BF16)
    esel_in = ein("esel", (8, 8 * PT), BF16)
    t64c_in = ein("t64c", (PT, n_own * 64), BF16)
    t64s_in = ein("t64s", (PT, n_own * 64), BF16)
    t128co = ein("t128co", (PT, n_own * 128), BF16)
    t128so = ein("t128so", (PT, n_own * 128), BF16)
    t128ca = ein("t128ca", (PT, n_tiles * 128), BF16)
    t128sa = ein("t128sa", (PT, n_tiles * 128), BF16)
    dmask_in = ein("dmask", (PT, 2 * PT), F32)
    tri01_in = ein("tri01", (PT, PT), BF16)
    utri_in = ein("utri", (PT, PT), F32)
    blab_in = ein("blend_ab", (PT, 2), F32)
    blaa_in = ein("blend_aa", (PT, 2), F32)
    iota_in = ein("iota2", (PT, 2), F32)
    diff_in = ein("diff", (PT, PT), F32)
    up_in = ein("upmask", (PT, PT), F32)
    ones_in = ein("onesrow", (1, PT), F32)
    y_out = nc.dram_tensor("y_shard", [n_own * PT, D], F32,
                           kind="ExternalOutput").ap()

    def intn(name, shape, dt=BF16):
        return nc.dram_tensor(name, list(shape), dt, kind="Internal").ap()

    xA = intn("xA", (n_own * PT, D), F32)
    xB = intn("xB", (n_own * PT, D), F32)
    wgT = intn("wgT", (16, PT, HID))
    wuT = intn("wuT", (16, PT, HID))
    wdT = intn("wdT", (16, PT, HID))
    qkvT = intn("qkvT", (16, PT, 3 * D))
    projT = intn("projT", (16, PT, D))
    qT_a = intn("qT_a", (n_own, PT, 2048))
    ag_in_a = intn("ag_in_a", (n_own * PT, 1024))
    ag_out_a = intn("ag_out_a", (n_tiles * PT, 1024))
    qT_b = intn("qT_b", (n_own, PT, 2048))
    qhT_b = intn("qhT_b", (n_own, PT, 2048))
    khT_b = intn("khT_b", (n_own, PT, 512))
    vT_b = intn("vT_b", (n_own, PT, 512))
    qT_1a = intn("qT_1a", (n_own, PT, D))
    qpT_1a = intn("qpT_1a", (n_own, PT, D))
    kT_1a = intn("kT_1a", (n_own, PT, D))
    vT_1a = intn("vT_1a", (n_own, PT, D))
    ag_in_1a = intn("ag_in_1a", (n_own * PT, 2 * D))
    ag_out_1a = intn("ag_out_1a", (n_tiles * PT, 2 * D))

    with ExitStack() as ctx:
        k = K(nc, ctx)
        k.constcol(1.0)
        k.constcol(1e-5)
        k.constcol(1e-6)
        k.constcol(F32EPS)
        ident = k.sb("ident_sb", (PT, PT), BF16)
        identf = k.sb("identf_sb", (PT, PT), F32)
        esel = k.sb("esel_sb", (8, 8 * PT), BF16)
        dmask = k.sb("dmask_sb", (PT, 2 * PT), F32)
        tri01 = k.sb("tri01_sb", (PT, PT), BF16)
        utri = k.sb("utri_sb", (PT, PT), F32)
        onesrow = k.sb("ones_sb", (1, PT), F32)
        iota2 = k.sb("iota_sb", (PT, 2), F32)
        diff = k.sb("diff_sb", (PT, PT), F32)
        upm = k.sb("up_sb", (PT, PT), F32)
        pbanks = [k.ps(f"pb{i}", (PT, 512), F32) for i in range(6)]
        ptr = [k.ps(f"ptr{i}", (PT, 1024), BF16) for i in range(2)]
        block = ctx.enter_context(nc.Block())

        @block.sync
        def _(e):
            trp = TrPool(k, ident, ptr[0][:, 0:PT], ptr[1][:, 0:PT])
            tid = k.dma('y', 'c_id', ident[:, :], ident_in)
            k.dma('g', 'c_id2', identf[:, :], ident_in)
            k.dma('y', 'c_id', esel[0:8, :], esel_in)
            k.dma('y', 'c_id', dmask[:, :], dmask_in)
            k.dma('y', 'c_id', tri01[:, :], tri01_in)
            k.dma('y', 'c_id', utri[:, :], utri_in)
            k.dma('y', 'c_id', onesrow[0:1, :], ones_in)
            k.dma('y', 'c_id', iota2[:, :], iota_in)
            k.dma('y', 'c_id', diff[:, :], diff_in)
            tlast = k.dma('y', 'c_id', upm[:, :], up_in)
            tid2 = ('c_id2', k.cnt['c_id2'])
            for eng in ('t', 'v', 's', 'g'):
                k.wait(eng, tlast, tid2)

            emit_f0_prep(k, trp, f0_wqg, f0_sg, wgT)
            emit_f0_prep(k, trp, f0_wqu, f0_su, wuT)
            tw_f0 = emit_f0_prep(k, trp, f0_wqd, f0_sd, wdT, key_by_row=True)
            t1p = emit_ternary_prep(k, trp, a1a_qkv, qkvT, onesrow, tlast,
                                    pbanks[5])
            t2p = emit_ternary_prep(k, trp, a1a_proj, projT, onesrow, tlast,
                                    pbanks[5])
            k.barrier()

            base = dict(n_own=n_own, ident=ident, identf=identf, pbanks=pbanks,
                        trp=trp, replica_groups=RG)

            cfg = dict(base, n_tiles=n_tiles, own_tiles=list(range(n_own)),
                       nH=32, nKV=8, DH=64, scale=64 ** -0.5,
                       tabc=k_load_tab(k, t64c_in, 'tb1'),
                       tabs=k_load_tab(k, t64s_in, 'tb1'),
                       tab_tok=('c_tb1', k.cnt['c_tb1']),
                       dmask=dmask, dmask_tok=tlast, rope_k=True)
            io = dict(x_in=x_in0, x_out=xA, Wq=a0a_Wq, Wkv=None,
                      Wk=a0a_Wkv[:, 0:512], Wv=a0a_Wkv[:, 512:1024],
                      Wo=a0a_Wo, qT=qT_a, ag_in=ag_in_a, ag_out=ag_out_a)
            emit_attn(k, cfg, io)
            k.barrier()

            emit_f0(k, dict(base), dict(x_in=xA, x_out=xB, wgT=wgT, wuT=wuT,
                                        wdT=wdT, w_tok=tw_f0))
            k.barrier()

            cfg = dict(base, tabc_o_d=t128co, tabs_o_d=t128so,
                       tabc_a_d=t128ca, tabs_a_d=t128sa,
                       dmask=dmask, dmask_tok=tlast, tri01=tri01,
                       tri_tok=tlast, utri=utri, utri_tok=tlast)
            emit_a0b(k, cfg, dict(x_in=xB, x_out=xA, Wq=a0b_Wq, Wk=a0b_Wk,
                                  Wv=a0b_Wv, Wo=a0b_Wo, qT=qT_b, qhT=qhT_b,
                                  khT=khT_b, vT=vT_b, ag_in=ag_in_a[:, :],
                                  ag_out=ag_out_a[:, :], blend=blab_in))
            k.barrier()

            emit_f0(k, dict(base), dict(x_in=xA, x_out=xB, wgT=wgT, wuT=wuT,
                                        wdT=wdT))
            k.barrier()

            cfg = dict(base, onesrow=onesrow, onesrow_tok=tlast, iota=iota2,
                       iota_tok=tlast, diff=diff, diff_tok=tlast, upmask=upm)
            emit_a1a(k, cfg, dict(x_in=xB, x_out=xA, qkvT=qkvT, projT=projT,
                                  qT=qT_1a, qpT=qpT_1a, kT=kT_1a, vT=vT_1a,
                                  ag_in=ag_in_1a, ag_out=ag_out_1a, s=a1a_s,
                                  blend=blaa_in, qkv_norm_w=a1a_qn,
                                  norm_w=a1a_nw, proj_norm_w=a1a_pn,
                                  w_tok=tmax(*(t1p + t2p))))
            k.barrier()

            cfgf1 = dict(base, esel=esel, esel_tok=tlast)
            emit_f1(k, cfgf1, dict(x_in=xA, x_out=xB, w_gate=f1_wg,
                                   w_gate_up=f1_wgu, w_down=f1_wd,
                                   w_sh_gate_up=f1_wsg, w_sh_down=f1_wsd))
            k.barrier()

            cfg = dict(base, n_tiles=n_tiles, own_tiles=list(range(n_own)),
                       nH=16, nKV=4, DH=128, scale=128 ** -0.5,
                       tabc=k_load_tab(k, t128co, 'tb2'),
                       tabs=k_load_tab(k, t128so, 'tb2'),
                       tab_tok=('c_tb2', k.cnt['c_tb2']),
                       dmask=dmask, dmask_tok=tlast, rope_k=True)
            emit_attn(k, cfg, dict(x_in=xB, x_out=xA, Wq=a1b_Wq, Wk=a1b_Wk,
                                   Wv=a1b_Wv, Wo=a1b_Wo, qT=qT_b,
                                   ag_in=ag_in_a[:, :], ag_out=ag_out_a[:, :]))
            k.barrier()

            emit_f1(k, cfgf1, dict(x_in=xA, x_out=y_out, w_gate=f1_wg,
                                   w_gate_up=f1_wgu, w_down=f1_wd,
                                   w_sh_gate_up=f1_wsg, w_sh_down=f1_wsd))
            k.dma_fence('y')
    return nc


def k_load_tab(k, dram_ap, ch):
    t = k.sb("tab", tuple(dram_ap.shape), BF16)
    k.dma('y', 'c_' + ch, t[:, :], dram_ap)
    return t[:, :]

_PCACHE = {}


def _get_prog(key, builder):
    if key not in _PCACHE:
        _PCACHE[key] = builder()
    return _PCACHE[key]


def kernel_multilaunch(**inputs):
    n_own = 16
    n_tiles = 32
    RG = [[0, 1], [2, 3], [4, 5], [6, 7]]
    ident = np.eye(PT).astype(BFNP)
    x = np.asarray(inputs["L0_attn_a_x"], np.float32)

    rows_par = []
    for par in range(2):
        tiles = [2 * j + par for j in range(n_own)]
        rows_par.append(np.concatenate(
            [np.arange(t * PT, (t + 1) * PT) for t in tiles]))

    xs = [np.ascontiguousarray(x[c // 2][rows_par[c % 2]]) for c in range(8)]

    tab64 = [host_rope_tables(rows_par[par].astype(np.float64), 64)
             for par in range(2)]
    tab128_o = [host_rope_tables(rows_par[par].astype(np.float64), 128)
                for par in range(2)]
    tab128_a = host_rope_tables(np.arange(n_tiles * PT).astype(np.float64), 128)
    dmasks = [host_dmask(par) for par in range(2)]

    def run(nc, per_core):
        res = run_bass_kernel_spmd(nc, per_core, core_ids=list(range(8)))
        return [np.asarray(res.results[c]["y_shard"]) for c in range(8)]

    def attn_launch(kind):
        if kind == "a0a":
            nc = _get_prog("a0a", lambda: build_attn_program(
                n_own, n_tiles, 32, 8, 64, scale=64 ** -0.5,
                replica_groups=RG, combined_kv=True))
            maps = [dict(x_shard=xs[c], Wq=inputs["L0_attn_a_Wq"],
                         Wkv=inputs["L0_attn_a_Wkv"], Wo=inputs["L0_attn_a_Wo"],
                         tabc=tab64[c % 2][0], tabs=tab64[c % 2][1],
                         dmask=dmasks[c % 2], ident=ident) for c in range(8)]
        else:
            nc = _get_prog("a1b", lambda: build_attn_program(
                n_own, n_tiles, 16, 4, 128, scale=128 ** -0.5,
                replica_groups=RG))
            maps = [dict(x_shard=xs[c], Wq=inputs["L1_attn_b_Wq"],
                         Wk=inputs["L1_attn_b_Wk"], Wv=inputs["L1_attn_b_Wv"],
                         Wo=inputs["L1_attn_b_Wo"],
                         tabc=tab128_o[c % 2][0], tabs=tab128_o[c % 2][1],
                         dmask=dmasks[c % 2], ident=ident) for c in range(8)]
        return run(nc, maps)

    def f0_launch():
        nc = _get_prog("f0", lambda: build_f0_program(n_own=n_own))
        maps = [dict(x_shard=xs[c],
                     wq_gate=inputs["L0_ffn_wq_gate"],
                     s_gate=inputs["L0_ffn_s_gate"],
                     wq_up=inputs["L0_ffn_wq_up"], s_up=inputs["L0_ffn_s_up"],
                     wq_down=inputs["L0_ffn_wq_down"],
                     s_down=inputs["L0_ffn_s_down"], ident=ident)
                for c in range(8)]
        return run(nc, maps)

    def a0b_launch():
        nc = _get_prog("a0b", lambda: build_a0b_program(n_own, n_tiles, RG))
        maps = []
        for c in range(8):
            par = c % 2
            tri01, utri, blend = host_a0b_consts(par)
            maps.append(dict(
                x_shard=xs[c], Wq=inputs["L0_attn_b_Wq"],
                Wk=inputs["L0_attn_b_Wk"], Wv=inputs["L0_attn_b_Wv"],
                Wo=inputs["L0_attn_b_Wo"],
                tabc_o=tab128_o[par][0], tabs_o=tab128_o[par][1],
                tabc_a=tab128_a[0], tabs_a=tab128_a[1],
                dmask=dmasks[par], tri01=tri01, utri=utri, blend=blend,
                ident=ident))
        return run(nc, maps)

    def a1a_launch():
        nc = _get_prog("a1a", lambda: build_a1a_program(n_own, n_tiles, RG))
        maps = []
        for c in range(8):
            par = c % 2
            iota2, diff, upm, blend, ones = host_a1a_consts(par)
            maps.append(dict(
                x_shard=xs[c], qkv_w=inputs["L1_attn_a_qkv_w"],
                proj_w=inputs["L1_attn_a_proj_w"],
                qkv_norm_w=inputs["L1_attn_a_qkv_norm_w"],
                norm_w=inputs["L1_attn_a_norm_w"],
                proj_norm_w=inputs["L1_attn_a_proj_norm_w"],
                s=inputs["L1_attn_a_s"], ident=ident, onesrow=ones,
                blend=blend, iota2=iota2, diff=diff, upmask=upm))
        return run(nc, maps)

    def f1_launch():
        nc = _get_prog("f1", lambda: build_f1_program(n_own=n_own))
        maps = [dict(x_shard=xs[c], w_gate=inputs["L1_ffn_w_gate"],
                     w_gate_up=inputs["L1_ffn_w_gate_up"],
                     w_down=inputs["L1_ffn_w_down"],
                     w_sh_gate_up=inputs["L1_ffn_w_sh_gate_up"],
                     w_sh_down=inputs["L1_ffn_w_sh_down"], ident=ident,
                     esel=host_esel()) for c in range(8)]
        return run(nc, maps)

    seq = [lambda: attn_launch("a0a"), f0_launch, a0b_launch, f0_launch,
           a1a_launch, f1_launch, lambda: attn_launch("a1b"), f1_launch]
    for step in seq:
        xs = step()

    out = np.empty((4, 4096, D), np.float32)
    for c in range(8):
        out[c // 2][rows_par[c % 2]] = xs[c]
    return out


def kernel(**inputs):
    n_own, n_tiles = 16, 32
    ident = np.eye(PT).astype(BFNP)
    x = np.asarray(inputs["L0_attn_a_x"], np.float32)
    rows_par = []
    for par in range(2):
        tiles = [2 * j + par for j in range(n_own)]
        rows_par.append(np.concatenate(
            [np.arange(t * PT, (t + 1) * PT) for t in tiles]))
    nc = _get_prog("fused", lambda: build_fused_program(n_own, n_tiles))
    tab128_a = host_rope_tables(np.arange(n_tiles * PT).astype(np.float64), 128)
    in_maps = []
    for c in range(8):
        par = c % 2
        rows = rows_par[par]
        t64 = host_rope_tables(rows.astype(np.float64), 64)
        t128o = host_rope_tables(rows.astype(np.float64), 128)
        tri01, utri, blend_ab = host_a0b_consts(par)
        iota2, diff, upm, blend_aa, ones = host_a1a_consts(par)
        in_maps.append(dict(
            x_shard=np.ascontiguousarray(x[c // 2][rows]),
            L0aWq=inputs["L0_attn_a_Wq"], L0aWkv=inputs["L0_attn_a_Wkv"],
            L0aWo=inputs["L0_attn_a_Wo"],
            F0wg=inputs["L0_ffn_wq_gate"], F0sg=inputs["L0_ffn_s_gate"],
            F0wu=inputs["L0_ffn_wq_up"], F0su=inputs["L0_ffn_s_up"],
            F0wd=inputs["L0_ffn_wq_down"], F0sd=inputs["L0_ffn_s_down"],
            L0bWq=inputs["L0_attn_b_Wq"], L0bWk=inputs["L0_attn_b_Wk"],
            L0bWv=inputs["L0_attn_b_Wv"], L0bWo=inputs["L0_attn_b_Wo"],
            L1aqkv=inputs["L1_attn_a_qkv_w"], L1aproj=inputs["L1_attn_a_proj_w"],
            L1aqn=inputs["L1_attn_a_qkv_norm_w"], L1anw=inputs["L1_attn_a_norm_w"],
            L1apn=inputs["L1_attn_a_proj_norm_w"], L1as=inputs["L1_attn_a_s"],
            F1wg=inputs["L1_ffn_w_gate"], F1wgu=inputs["L1_ffn_w_gate_up"],
            F1wd=inputs["L1_ffn_w_down"], F1wsg=inputs["L1_ffn_w_sh_gate_up"],
            F1wsd=inputs["L1_ffn_w_sh_down"],
            L1bWq=inputs["L1_attn_b_Wq"], L1bWk=inputs["L1_attn_b_Wk"],
            L1bWv=inputs["L1_attn_b_Wv"], L1bWo=inputs["L1_attn_b_Wo"],
            ident=ident, esel=host_esel(),
            t64c=t64[0], t64s=t64[1], t128co=t128o[0], t128so=t128o[1],
            t128ca=tab128_a[0], t128sa=tab128_a[1],
            dmask=host_dmask(par), tri01=tri01, utri=utri,
            blend_ab=blend_ab, blend_aa=blend_aa, iota2=iota2, diff=diff,
            upmask=upm, onesrow=ones))
    res = run_bass_kernel_spmd(nc, in_maps, core_ids=list(range(8)))
    out = np.empty((4, 4096, D), np.float32)
    for c in range(8):
        out[c // 2][rows_par[c % 2]] = np.asarray(res.results[c]["y_shard"])
    return out


# version 06 (final)
# speedup vs baseline: 1.6040x; 1.5391x over previous
import numpy as np
import ml_dtypes
from contextlib import ExitStack, contextmanager

import concourse.bass as bass
import concourse.mybir as mybir
from concourse.bass_utils import run_bass_kernel_spmd

F32 = mybir.dt.float32
BF16 = mybir.dt.bfloat16
I32 = mybir.dt.int32
AF = mybir.ActivationFunctionType
ALU = mybir.AluOpType
AX = mybir.AxisListType
BFNP = ml_dtypes.bfloat16

D = 2048
B = 4
PT = 128
F32EPS = float(np.finfo(np.float32).eps)
NEG = -1e30


class K:

    def __init__(self, nc, ctx):
        self.nc = nc
        self.ctx = ctx
        self._ctxs = [ctx]
        self.eng = dict(v=nc.vector, s=nc.scalar, t=nc.tensor, g=nc.gpsimd,
                        y=nc.sync)
        self.sem = {}
        self.cnt = {}
        for nm in ("v", "s", "t", "g", "y", "col"):
            self.sem[nm] = ctx.enter_context(nc.semaphore("sem_" + nm))
            self.cnt[nm] = 0
        self.waited = {e: {} for e in self.eng}

    def barrier(self):
        toks = [(ch, v) for ch, v in self.cnt.items() if v > 0]
        for e in self.eng:
            self.wait(e, *toks)

    @contextmanager
    def scope(self):
        with ExitStack() as c:
            self._ctxs.append(c)
            try:
                yield c
            finally:
                self.barrier()
                self._ctxs.pop()

    def inc(self, ch, inst, n=1):
        self.cnt[ch] += n
        inst.then_inc(self.sem[ch], n)
        return (ch, self.cnt[ch])

    def wait(self, e, *toks):
        for tk in toks:
            if tk is None:
                continue
            ch, v = tk
            w = self.waited[e]
            if w.get(ch, 0) < v:
                self.eng[e].wait_ge(self.sem[ch], v)
                w[ch] = v

    def chan(self, name):
        if name not in self.sem:
            self.sem[name] = self._ctxs[0].enter_context(
                self.nc.semaphore("sem_" + name))
            self.cnt[name] = 0
        return name

    def dma(self, q, ch, out, in_, wait=(), **kw):
        ch = self.chan(ch)
        self.wait(q, *[t for t in wait if t is not None])
        inst = self.eng[q].dma_start(out=out, in_=in_, **kw)
        self.cnt[ch] += 16
        inst.then_inc(self.sem[ch], 16)
        return (ch, self.cnt[ch])

    def dma_fence(self, e='y'):
        for ch in list(self.sem):
            if ch.startswith("c_") and self.cnt[ch] > 0:
                self.wait(e, (ch, self.cnt[ch]))

    def sb(self, name, shape, dt, root=False):
        self._uid = getattr(self, "_uid", 0) + 1
        c = self._ctxs[0] if root else self._ctxs[-1]
        return c.enter_context(
            self.nc.sbuf_tensor(f"{name}_{self._uid}", list(shape), dt))

    def constcol(self, val):
        if not hasattr(self, "_consts"):
            self._consts = {}
        if val not in self._consts:
            t = self.sb(f"const", (128, 1), F32, root=True)
            tok = self.inc('v', self.nc.vector.memset(t[:, :], float(val)))
            self._consts[val] = (t, tok)
        return self._consts[val]

    def ps(self, name, shape, dt=F32):
        self._uid = getattr(self, "_uid", 0) + 1
        return self._ctxs[-1].enter_context(
            self.nc.psum_tensor(f"{name}_{self._uid}", list(shape), dt))


def tmax(*toks):
    best = {}
    for t in toks:
        if t is None:
            continue
        ch, v = t
        if best.get(ch, -1) < v:
            best[ch] = v
    return [(ch, v) for ch, v in best.items()]


def emit_ln(k, xt, xn_out, sc, x_tok, war_tok=None, eps=1e-5):
    nc = k.nc
    F = xt.shape[1]
    nch = max(1, F // 512)
    k.wait('v', x_tok)
    fsz = F // nch
    for c in range(nch):
        t1 = k.inc('v', nc.vector.bn_stats(
            sc['st6'][:, c * 6:(c + 1) * 6],
            xt[:, c * fsz:(c + 1) * fsz]))
    k.wait('v', t1)
    t2 = k.inc('v', nc.vector.bn_aggr(sc['st2'][:, :], sc['st6'][:, :3 * 2 * nch]))
    eps_ap, eps_tok = k.constcol(eps)
    k.wait('s', t2, eps_tok)
    tl = k.inc('s', nc.scalar.activation(sc['rc'][:, 0:1], sc['st2'][:, 1:2],
                                         AF.Ln, bias=eps_ap[:, 0:1]))
    k.wait('s', tl)
    tr = k.inc('s', nc.scalar.activation(sc['rc'][:, 0:1], sc['rc'][:, 0:1],
                                         AF.Exp, scale=-0.5))
    k.wait('v', tr, war_tok)
    tx = k.inc('v', nc.vector.tensor_scalar(
        xn_out[:, :], xt[:, :], sc['st2'][:, 0:1], sc['rc'][:, 0:1],
        ALU.subtract, ALU.mult))
    return tx


class TrPool:

    def __init__(self, k, ident, pb0, pb1):
        self.k = k
        self.ident = ident
        self.banks = [pb0, pb1]
        self.busy = [None, None]
        self.i = 0

    def start(self, src_ap, *toks):
        k, nc = self.k, self.k.nc
        b = self.i % 2
        self.i += 1
        k.wait('t', self.busy[b], *toks)
        tm = k.inc('t', nc.tensor.transpose(self.banks[b][:, 0:PT], src_ap,
                                            self.ident[:, :]))
        return self.banks[b][:, 0:PT], tm, b

    def quad(self, srcs, *toks):
        k, nc = self.k, self.k.nc
        b = self.i % 2
        self.i += 1
        k.wait('t', self.busy[b], *toks)
        for i, s in enumerate(srcs):
            mm = nc.tensor.transpose(self.banks[b][:, i * PT:(i + 1) * PT], s,
                                     self.ident[:, :])
        tm = k.inc('t', mm)
        return self.banks[b][:, 0:4 * PT], tm, b

    def done(self, b, tok):
        self.busy[b] = tok

    def transpose(self, src_ap, dst_ap, src_tok, evac='v', war_tok=None):
        k, nc = self.k, self.k.nc
        pbk, tm, b = self.start(src_ap, src_tok)
        k.wait(evac, tm, war_tok)
        if evac == 'v':
            te = k.inc('v', nc.vector.tensor_copy(dst_ap, pbk[:, :]))
        else:
            te = k.inc('s', nc.scalar.copy(dst_ap, pbk[:, :]))
        self.busy[b] = te
        return te

    def t16(self, src_base, dst_tensor_ap, dst_off, dst_stride, src_tok,
            war_tok=None):
        k, nc = self.k, self.k.nc
        te = None
        for q in range(4):
            srcs = [src_base[:, (q * 4 + i) * PT:(q * 4 + i + 1) * PT]
                    for i in range(4)]
            bank, tm, b = self.quad(srcs, src_tok if q == 0 else None)
            dst = bass.AP(dst_tensor_ap.tensor,
                          dst_tensor_ap.offset + dst_off + q * 4 * dst_stride,
                          [list(dst_tensor_ap.ap[0]), [dst_stride, 4],
                           [1, PT]])
            k.wait('v', tm, war_tok if q == 0 else None)
            te = k.inc('v', nc.vector.tensor_copy(
                dst, bank.rearrange("p (c x) -> p c x", c=4)))
            self.busy[b] = te
        return te


def emit_cast_prep(k, pairs):
    nc = k.nc
    CC = 4096
    with k.scope():
        bi = [k.sb(f"cp_i{i}", (PT, CC), F32) for i in range(2)]
        bo = [k.sb(f"cp_o{i}", (PT, CC), BF16) for i in range(2)]
        tok_i = [None, None]
        tok_o = [None, None]
        stoks = []
        pi = 0
        for (s_ap, d_ap) in pairs:
            R, C = s_ap.shape
            for r0 in range(0, R, PT):
                for c0 in range(0, C, CC):
                    w = min(CC, C - c0)
                    pi ^= 1
                    tl = k.dma('y', f'c_cp{pi}', bi[pi][:, 0:w],
                               s_ap[r0:r0 + PT, c0:c0 + w],
                               wait=(tok_i[pi],))
                    k.wait('v', tl, tok_o[pi])
                    tc = k.inc('v', nc.vector.tensor_copy(bo[pi][:, 0:w],
                                                          bi[pi][:, 0:w]))
                    tok_i[pi] = tc
                    ts = k.dma('s', f'c_cq{pi}', d_ap[r0:r0 + PT, c0:c0 + w],
                               bo[pi][:, 0:w], wait=(tc,))
                    tok_o[pi] = ts
                    stoks.append(ts)
        return tmax(*stoks)


def emit_f0_prep(k, trp, wq_ap, s_ap, wT_dram, key_by_row=False):
    nc = k.nc
    n_rows, n_cols = wq_ap.shape
    CC = 512
    s2d = s_ap.rearrange("(t p) -> t p", p=PT)
    with k.scope():
        it = [k.sb(f"dq_i{i}", (PT, CC), I32) for i in range(2)]
        ft = [k.sb(f"dq_f{i}", (PT, CC), F32) for i in range(2)]
        bt = [k.sb(f"dq_b{i}", (PT, CC), BF16) for i in range(2)]
        scol = k.sb("dq_s", (PT, 1), F32)
        stg = [k.sb(f"dq_g{i}", (PT, 4 * PT), BF16) for i in range(2)]
        tok_f = [None, None]
        tok_b = [None, None]
        tok_s = None
        stok = [None, None]
        last = None
        for rt in range(n_rows // PT):
            rsl = slice(rt * PT, (rt + 1) * PT)
            ts = k.dma('y', 'c_ps', scol[:, :], s2d[rt].unsqueeze(1),
                       wait=(tok_s,))
            for cc in range(n_cols // CC):
                pi = (rt * (n_cols // CC) + cc) % 2
                csl = slice(cc * CC, (cc + 1) * CC)
                tw = k.dma('y', f'c_pi{pi}', it[pi][:, :], wq_ap[rsl, csl], wait=(tok_f[pi],))
                k.wait('v', tw, tok_b[pi])
                tf = k.inc('v', nc.vector.tensor_copy(ft[pi][:, :], it[pi][:, :]))
                tok_f[pi] = tf
                k.wait('v', tf, ts)
                tok_s = tb = k.inc('v', nc.vector.tensor_scalar(
                    bt[pi][:, :], ft[pi][:, :], -128.0, scol[:, :],
                    ALU.add, ALU.mult))
                si = cc % 2
                srcs = [bt[pi][:, ct * PT:(ct + 1) * PT] for ct in range(4)]
                bank, tm, b = trp.quad(srcs, tb)
                k.wait('v', tm, stok[si])
                te = k.inc('v', nc.vector.tensor_copy(stg[si][:, :], bank))
                trp.done(b, te)
                g0 = cc * 4
                if key_by_row:
                    dst = wT_dram[rt, :, g0 * PT:(g0 + 4) * PT]
                    sap = stg[si][:, :]
                else:
                    dst = wT_dram[g0:g0 + 4, :, rsl].rearrange(
                        "c p x -> p c x")
                    sap = stg[si][:, :].rearrange("p (c x) -> p c x", c=4)
                last = k.dma('s', f'c_pg{si}', dst, sap, wait=(te,))
                stok[si] = last
                tok_b[pi] = last
        return tmax(stok[0], stok[1])


def emit_f0(k, cfg, io):
    nc = k.nc
    n_own = cfg['n_own']
    ngrp = n_own // 4
    HID = 5632
    NH = HID // PT
    NHC = NH * PT // 512
    ident = cfg['ident']
    pb = cfg['pbanks']
    trp = cfg['trp']
    wgT, wuT, wdT = io['wgT'], io['wuT'], io['wdT']
    x_in, x_out = io['x_in'], io['x_out']
    w_tok = io.get('w_tok')

    with k.scope():
        xt = k.sb("f0_xt", (PT, D), F32)
        xn = k.sb("f0_xn", (PT, D), BF16)
        sc = dict(st6=k.sb("f0_st6", (PT, 24), F32),
                  st2=k.sb("f0_st2", (PT, 2), F32),
                  rc=k.sb("f0_rc", (PT, 2), F32))
        xnT = k.sb("f0_xnT", (PT, 16 * 512), BF16)
        wg_sb = [k.sb(f"f0_wg{i}", (PT, 16 * 512), BF16) for i in range(2)]
        wu_sb = [k.sb(f"f0_wu{i}", (PT, 16 * 512), BF16) for i in range(2)]
        wd_sb = [k.sb(f"f0_wd{i}", (PT, NH * PT), BF16) for i in range(2)]
        sg_sb = k.sb("f0_sg", (PT, 512), BF16)
        su_sb = k.sb("f0_su", (PT, 512), BF16)
        actT = k.sb("f0_act", (PT, NH * 512), BF16)
        outT = k.sb("f0_outT", (PT, 16 * 512), BF16)
        xo = k.sb("f0_xo", (PT, D), F32)

        tok_bank = {}
        tok_xt_r = None
        tok_xn_r = None
        tok_xnT_r = None
        tok_act_w = None
        tok_act_r = None
        tok_sg_r = None
        tok_su_r = None
        tok_outT_w = {}
        tok_outT_r = None
        tok_xo_r = None
        tok_wg_r = [None, None]
        tok_wu_r = [None, None]
        tok_wd_r = [None, None]
        tst = None

        for g in range(ngrp):
            tiles = list(range(4 * g, 4 * g + 4))
            t_tr_last = None
            for jj, tile in enumerate(tiles):
                tx = k.dma('y', 'c_xt', xt[:, :], x_in[tile * PT:(tile + 1) * PT, :],
                           wait=(tok_xt_r,))
                txn = emit_ln(k, xt, xn, sc, tx, war_tok=tok_xn_r)
                tok_xt_r = txn
                t_tr_last = trp.t16(xn[:, :], xnT[:, :], jj * PT, 512, txn,
                                    war_tok=tok_xnT_r if jj == 0 else None)
                tok_xn_r = ('t', k.cnt['t'])
            tok_xnT_w = t_tr_last

            for hc in range(NHC):
                hcp = hc % 2
                for dt in range(16):
                    wtoks = ((w_tok or [None])
                             if (g == 0 and hc == 0 and dt == 0) else [None])
                    twg = k.dma('y', f'c_wg{hcp}',
                                wg_sb[hcp][:, dt * 512:(dt + 1) * 512],
                                wgT[dt, :, hc * 512:(hc + 1) * 512],
                                wait=(tok_wg_r[hcp], *wtoks))
                    twu = k.dma('y', f'c_wu{hcp}',
                                wu_sb[hcp][:, dt * 512:(dt + 1) * 512],
                                wuT[dt, :, hc * 512:(hc + 1) * 512],
                                wait=(tok_wu_r[hcp],))
                for hs4 in range(4):
                    hsg = hc * 4 + hs4
                    bg, bu = hs4 % 2, 2 + hs4 % 2
                    k.wait('t', twg, twu, tok_xnT_w,
                           tok_bank.get(bg), tok_bank.get(bu))
                    for dt in range(16):
                        mm = nc.tensor.matmul(
                            pb[bg][:, :],
                            wg_sb[hcp][:, dt * 512 + hs4 * PT:
                                       dt * 512 + (hs4 + 1) * PT],
                            xnT[:, dt * 512:(dt + 1) * 512],
                            start=(dt == 0), stop=(dt == 15))
                    tmg = k.inc('t', mm)
                    for dt in range(16):
                        mm = nc.tensor.matmul(
                            pb[bu][:, :],
                            wu_sb[hcp][:, dt * 512 + hs4 * PT:
                                       dt * 512 + (hs4 + 1) * PT],
                            xnT[:, dt * 512:(dt + 1) * 512],
                            start=(dt == 0), stop=(dt == 15))
                    tmu = k.inc('t', mm)
                    tok_xnT_r = tmu
                    k.wait('s', tmg, tok_sg_r)
                    tsg = k.inc('s', nc.scalar.activation(sg_sb[:, :],
                                                          pb[bg][:, :], AF.Silu))
                    tok_bank[bg] = tsg
                    k.wait('v', tmu, tok_su_r)
                    tsu = k.inc('v', nc.vector.tensor_copy(su_sb[:, :],
                                                           pb[bu][:, :]))
                    tok_bank[bu] = tsu
                    k.wait('v', tsg, tsu, tok_act_r if hsg == 0 and g > 0 else None)
                    ta = k.inc('v', nc.vector.tensor_mul(
                        actT[:, hsg * 512:(hsg + 1) * 512], sg_sb[:, :],
                        su_sb[:, :]))
                    tok_sg_r = tok_su_r = ta
                tok_wg_r[hcp] = tmu
                tok_wu_r[hcp] = tmu
            tok_act_w = ta

            for dmt in range(16):
                pi = dmt % 2
                twd = k.dma('y', f'c_wd{pi}', wd_sb[pi][:, :], wdT[dmt],
                            wait=(tok_wd_r[pi],))
                bd = 4 + dmt % 2
                k.wait('t', twd, tok_act_w, tok_bank.get(bd))
                for ht in range(NH):
                    mm = nc.tensor.matmul(pb[bd][:, :],
                                          wd_sb[pi][:, ht * PT:(ht + 1) * PT],
                                          actT[:, ht * 512:(ht + 1) * 512],
                                          start=(ht == 0), stop=(ht == NH - 1))
                tmd = k.inc('t', mm)
                tok_wd_r[pi] = tmd
                tok_act_r = tmd
                k.wait('v', tmd, tok_outT_r)
                tev = k.inc('v', nc.vector.tensor_copy(
                    outT[:, dmt * 512:(dmt + 1) * 512], pb[bd][:, :]))
                tok_bank[bd] = tev
                tok_outT_w[dmt] = tev

            for jj, tile in enumerate(tiles):
                tx = k.dma('y', 'c_xt', xt[:, :], x_in[tile * PT:(tile + 1) * PT, :],
                           wait=(tok_xt_r,))
                adds = []
                for dmq in range(4):
                    srcs = [outT[:, (dmq * 4 + i) * 512 + jj * PT:
                                 (dmq * 4 + i) * 512 + (jj + 1) * PT]
                            for i in range(4)]
                    bank, tm, b = trp.quad(
                        srcs, *[tok_outT_w[dmq * 4 + i] for i in range(4)])
                    k.wait('v', tm, tx, tok_xo_r)
                    te = k.inc('v', nc.vector.tensor_tensor(
                        xo[:, dmq * 512:(dmq + 1) * 512],
                        xt[:, dmq * 512:(dmq + 1) * 512], bank, ALU.add))
                    trp.done(b, te)
                    adds.append(te)
                tok_xt_r = adds[-1]
                tok_outT_r = ('t', k.cnt['t'])
                tst = k.dma('s', 'c_xo', x_out[tile * PT:(tile + 1) * PT, :],
                            xo[:, :], wait=tuple(tmax(*adds)))
                tok_xo_r = tst
        return tst



def emit_attn(k, cfg, io):
    nc = k.nc
    n_tiles = cfg['n_tiles']
    own = cfg['own_tiles']
    n_own = len(own)
    nH, nKV, DH = cfg['nH'], cfg['nKV'], cfg['DH']
    QW, KW = nH * DH, nKV * DH
    half = DH // 2
    scale = cfg['scale']
    ident = cfg['ident']
    trp = cfg['trp']
    pb = cfg['pbanks']
    x_in, x_out = io['x_in'], io['x_out']
    g_per_h = nH // nKV
    NHD = QW // PT

    with k.scope():
        tabc = k.sb("at_tabc", (PT, n_own * DH), BF16)[:, :]
        tabs = k.sb("at_tabs", (PT, n_own * DH), BF16)[:, :]
        k.dma('y', 'c_tab', tabc, cfg['tabc_d'])
        cfg = dict(cfg, tab_tok=k.dma('y', 'c_tab', tabs, cfg['tabs_d']))
        xt = k.sb("at_xt", (PT, D), F32)
        xn = k.sb("at_xn", (PT, D), BF16)
        sc = dict(st6=k.sb("at_st6", (PT, 24), F32),
                  st2=k.sb("at_st2", (PT, 2), F32),
                  rc=k.sb("at_rc", (PT, 2), F32))
        xnT = k.sb("at_xnT", (PT, 16 * 512), BF16)
        wch = [k.sb(f"at_w{i}", (PT, 512), BF16) for i in range(2)]
        qg = k.sb("at_qg", (PT, 4 * QW), BF16)
        kvg = k.sb("at_kvg", (PT, 4 * 2 * KW), BF16)
        rop = k.sb("at_rop", (PT, QW), BF16)
        rop2 = k.sb("at_rop2", (PT, QW), BF16)
        qTt = k.sb("at_qTt", (PT, QW), BF16)

        tok_xt_r = None
        tok_xn_r = None
        tok_xnT_r = None
        tok_w_r = [None, None]
        tok_qg_r = None
        tok_kvg_r = None
        tok_rop_r = None
        tok_rop2_r = None
        tok_qT_r = None
        tok_bank = {}
        wci = [0]

        def proj_group(Wsrc, width, dst, dst_off, dst_stride, war_tok, xnT_tok,
                       ntl):
            tevs = []
            for blk in range(width // 512):
                wq_ = 'y' if Wsrc.dtype == BF16 else 'g'
                for dt in range(16):
                    wi = wci[0] % 2
                    wci[0] += 1
                    tw = k.dma(wq_, f'c_w{wi}', wch[wi][:, :],
                               Wsrc[dt * PT:(dt + 1) * PT,
                                    blk * 512:(blk + 1) * 512],
                               wait=(tok_w_r[wi],))
                    k.wait('t', tw, xnT_tok,
                           *[tok_bank.get(jj) for jj in range(ntl)])
                    for jj in range(ntl):
                        mm = nc.tensor.matmul(
                            pb[jj][:, :],
                            xnT[:, dt * 512 + jj * PT: dt * 512 + (jj + 1) * PT],
                            wch[wi][:, :], start=(dt == 0), stop=(dt == 15))
                    tok_w_r[wi] = k.inc('t', mm)
                tmm = tok_w_r[(wci[0] - 1) % 2]
                for jj in range(ntl):
                    k.wait('v', tmm, war_tok if (blk == 0 and jj == 0) else None)
                    tev = k.inc('v', nc.vector.tensor_copy(
                        dst[:, dst_off + jj * dst_stride + blk * 512:
                            dst_off + jj * dst_stride + (blk + 1) * 512],
                        pb[jj][:, :]))
                    tok_bank[jj] = tev
                    tevs.append(tev)
            return tevs

        def rope(src_ap, dst_ap, gj, width, nheads, src_tok, war_tok, war2):
            base = src_ap
            sw4 = bass.AP(base.tensor, base.offset + half,
                          [list(base.ap[0]), [DH, nheads], [-half, 2],
                           [1, half]])
            tc = tabc[:, gj * DH:(gj + 1) * DH]
            ts_ = tabs[:, gj * DH:(gj + 1) * DH]
            cosb = bass.AP(tc.tensor, tc.offset,
                           [list(tc.ap[0]), [0, nheads], [1, DH]])
            sinb = bass.AP(ts_.tensor, ts_.offset,
                           [list(ts_.ap[0]), [0, nheads], [half, 2], [1, half]])
            d4 = dst_ap.rearrange("p (h s x) -> p h s x", h=nheads, s=2)
            d3 = dst_ap.rearrange("p (h x) -> p h x", h=nheads)
            s3 = src_ap.rearrange("p (h x) -> p h x", h=nheads)
            k.wait('v', src_tok, war_tok, cfg['tab_tok'])
            t1 = k.inc('v', nc.vector.tensor_tensor(d4, sw4, sinb, ALU.mult))
            k.wait('v', t1, war2)
            t2 = k.inc('v', nc.vector.tensor_tensor(
                rop2[:, :width].rearrange("p (h x) -> p h x", h=nheads),
                s3, cosb, ALU.mult))
            k.wait('v', t2)
            t3 = k.inc('v', nc.vector.tensor_tensor(
                dst_ap, dst_ap, rop2[:, :width], ALU.add))
            return t3

        for g in range((n_own + 3) // 4):
            tiles = list(range(4 * g, min(4 * g + 4, n_own)))
            ntl = len(tiles)
            t_tr = None
            for jj, j in enumerate(tiles):
                rsl = slice(j * PT, (j + 1) * PT)
                tx = k.dma('y', 'c_xt', xt[:, :], x_in[rsl, :], wait=(tok_xt_r,))
                txn = emit_ln(k, xt, xn, sc, tx, war_tok=tok_xn_r)
                tok_xt_r = txn
                t_tr = trp.t16(xn[:, :], xnT[:, :], jj * PT, 512, txn,
                               war_tok=tok_xnT_r if jj == 0 else None)
                tok_xn_r = ('t', k.cnt['t'])
            tq_evs = proj_group(io['Wq'], QW, qg, 0, QW, tok_qg_r, t_tr, ntl)
            tk_evs = proj_group(io['Wk'], KW, kvg, 0, 2 * KW, tok_kvg_r, t_tr,
                                ntl)
            tv_evs = proj_group(io['Wv'], KW, kvg, KW, 2 * KW, None, t_tr, ntl)
            tok_xnT_r = ('t', k.cnt['t'])
            tq_all = tmax(*tq_evs)
            tk_all = tmax(*tk_evs)
            tv_all = tmax(*tv_evs)
            for jj, j in enumerate(tiles):
                trq = rope(qg[:, jj * QW:(jj + 1) * QW], rop[:, :QW], j, QW,
                           nH, tq_all[0] if tq_all else None, tok_rop_r,
                           tok_rop2_r)
                tt = trp.t16(rop[:, :], qTt[:, :], 0, PT, trq,
                             war_tok=tok_qT_r)
                tok_rop_r = ('t', k.cnt['t'])
                tok_rop2_r = trq
                tqs = k.dma('s', 'c_qTs', io['qT'][j], qTt[:, :], wait=(tt,))
                tok_qT_r = tqs
                if cfg.get('rope_k', True):
                    trk = rope(kvg[:, jj * 2 * KW: jj * 2 * KW + KW],
                               rop[:, :KW], j, KW, nKV,
                               tk_all[0] if tk_all else None, tok_rop_r,
                               tok_rop2_r)
                    k.wait('v', trk)
                    tcp = k.inc('v', nc.vector.tensor_copy(
                        kvg[:, jj * 2 * KW: jj * 2 * KW + KW], rop[:, :KW]))
                    tok_rop_r = tcp
                    tok_rop2_r = trk
                    tkv = tmax(tcp, *tv_evs)
                else:
                    tkv = tmax(*(tk_evs + tv_evs))
                tst = k.dma('s', 'c_agi', io['ag_in'][j * PT:(j + 1) * PT, :],
                            kvg[:, jj * 2 * KW:(jj + 1) * 2 * KW],
                            wait=tuple(tkv))
                tok_kvg_r = tst
            tok_qg_r = ('v', k.cnt['v'])

    k.barrier()
    ch = k.chan('c_col')
    inst = nc.gpsimd.collective_compute(
        "AllGather", ALU.bypass, cfg['replica_groups'],
        [io['ag_in'].opt()], [io['ag_out'].opt()])
    k.cnt[ch] += 1
    inst.then_inc(k.sem[ch], 1)
    t_ag = (ch, k.cnt[ch])

    with k.scope():
        kT = k.sb("at_kT", (PT, nKV * n_tiles * PT), BF16)
        vf = k.sb("at_vf", (PT, n_tiles * KW), BF16)
        kvt = k.sb("at_kvt", (PT, 2 * KW), BF16)
        strip = k.sb("at_strip", (PT, n_tiles * PT), F32)
        exps = k.sb("at_exps", (PT, n_tiles * PT), BF16)
        mx = k.sb("at_mx", (PT, 2), F32)
        lsum = k.sb("at_l", (PT, 2), F32)
        drl = k.sb("at_drl", (PT, PT), BF16)
        att = [k.sb(f"at_att{i}", (PT, 512), BF16) for i in range(2)]
        qTt2 = k.sb("at_qT2", (PT, QW), BF16)
        oT = k.sb("at_oT", (PT, NHD * 512), BF16)
        woch = [k.sb(f"at_wo{i}", (PT, 512), BF16) for i in range(2)]
        outT = k.sb("at_outT", (PT, 16 * 512), BF16)
        xt2 = k.sb("at_xt2", (PT, D), F32)
        xo = k.sb("at_xo", (PT, D), F32)

        tok_kvt_r = None
        last_tr = None
        tv = None
        for kb in range(n_tiles):
            par, idx = kb % 2, kb // 2
            row0 = par * (n_tiles // 2) * PT + idx * PT
            tl = k.dma('y', 'c_kvt', kvt[:, :],
                       io['ag_out'][row0:row0 + PT, :],
                       wait=(t_ag, *(tok_kvt_r or [])))
            k.wait('v', tl)
            tv = k.inc('v', nc.vector.tensor_copy(
                vf[:, kb * KW:(kb + 1) * KW], kvt[:, KW:2 * KW]))
            if DH == PT:
                srcs = [kvt[:, s * PT:(s + 1) * PT] for s in range(4)]
                bank, tmq, b = trp.quad(srcs, tl)
                kTb = kT[:, :]
                dst = bass.AP(kTb.tensor, kTb.offset + kb * PT,
                              [list(kTb.ap[0]), [n_tiles * PT, 4], [1, PT]])
                k.wait('v', tmq)
                last_tr = k.inc('v', nc.vector.tensor_copy(
                    dst, bank.rearrange("p (c x) -> p c x", c=4)))
                trp.done(b, last_tr)
            for s in range(KW // PT if DH != PT else 0):
                if False:
                    pass
                else:
                    pbk, tm, b = trp.start(kvt[:, s * PT:(s + 1) * PT],
                                           tl if s == 0 else None)
                    k.wait('v', tm)
                    tes = []
                    for hh in range(2):
                        kvh = 2 * s + hh
                        c0 = (kvh * n_tiles + kb) * PT
                        for bb in range(2):
                            tes.append(k.inc('v', nc.vector.tensor_copy(
                                kT[bb * DH:(bb + 1) * DH, c0:c0 + PT],
                                pbk[hh * DH:(hh + 1) * DH, 0:PT])))
                    last_tr = tes[-1]
                    trp.done(b, last_tr)
            tok_kvt_r = tmax(tv, ('t', k.cnt['t']))
        tok_kv_ready = tmax(tv, last_tr, ('t', k.cnt['t']))

        tok_qT2_r = None
        tok_strip_r = None
        tok_exps_r = None
        tok_lsum_r = None
        tok_drl_r = None
        tok_oT_r = None
        tok_wo_r = [None, None]
        tok_outT_r = None
        tok_xt2_r = None
        tok_xo_r = None
        tok_b2 = {}
        woi = [0]
        tst = None

        for g in range((n_own + 3) // 4):
            tiles = list(range(4 * g, min(4 * g + 4, n_own)))
            t_oT_w = []
            for jj, j in enumerate(tiles):
                L = (2 * j + 2) * PT
                nkb = 2 * j + 2
                tql = k.dma('y', 'c_qTl', qTt2[:, :], io['qT'][j],
                            wait=(tok_qT2_r,))
                for h in range(nH):
                    kvh = h // g_per_h
                    if DH == PT:
                        lq = qTt2[:, h * PT:(h + 1) * PT]
                    else:
                        colb, rowb = (h * DH) // PT, (h * DH) % PT
                        lq = qTt2[rowb:rowb + DH, colb * PT:(colb + 1) * PT]
                    nblk = (nkb + 3) // 4
                    cps = []
                    for b4 in range(nblk):
                        w = min(512, L - b4 * 512)
                        bank = pb[b4 % 2]
                        if DH == PT:
                            rk = kT[:, kvh * n_tiles * PT + b4 * 512:
                                    kvh * n_tiles * PT + b4 * 512 + w]
                        else:
                            rb = (h % 2) * DH
                            rk = kT[rb:rb + DH,
                                    kvh * n_tiles * PT + b4 * 512:
                                    kvh * n_tiles * PT + b4 * 512 + w]
                        k.wait('t', tql, *tok_kv_ready)
                        k.wait('t', tok_b2.get(('sc', b4 % 2)))
                        mm = k.inc('t', nc.tensor.matmul(
                            bank[:, :w], lq, rk, start=True, stop=True))
                        k.wait('s', mm, tok_strip_r)
                        tcp = k.inc('s', nc.scalar.copy(
                            strip[:, b4 * 512: b4 * 512 + w], bank[:, :w]))
                        tok_b2[('sc', b4 % 2)] = tcp
                        cps.append(tcp)
                    k.wait('v', cps[-1], cfg['dmask_tok'])
                    tmsk = k.inc('v', nc.vector.tensor_tensor(
                        strip[:, L - 2 * PT:L], strip[:, L - 2 * PT:L],
                        cfg['dmask'][:, :], ALU.add))
                    k.wait('v', *cps, tmsk, tok_strip_r)
                    tmx = k.inc('v', nc.vector.reduce_max(
                        mx[:, 0:1], strip[:, :L], axis=AX.X))
                    k.wait('v', tmx)
                    tb = k.inc('v', nc.vector.tensor_scalar_mul(
                        mx[:, 1:2], mx[:, 0:1], -scale))
                    k.wait('s', tb, tok_exps_r, tok_lsum_r)
                    tex = k.inc('s', nc.scalar.activation(
                        exps[:, :L], strip[:, :L], AF.Exp,
                        bias=mx[:, 1:2], scale=scale, accum_out=lsum[:, 0:1]))
                    tok_strip_r = tex
                    k.wait('v', tex)
                    trl = k.inc('v', nc.vector.reciprocal(lsum[:, 1:2],
                                                          lsum[:, 0:1]))
                    tok_lsum_r = trl
                    k.wait('v', trl, tok_drl_r)
                    tdr = k.inc('v', nc.vector.tensor_scalar_mul(
                        drl[:, :], ident[:, :], lsum[:, 1:2]))
                    po = pb[4]
                    tva = None
                    for b4 in range(nblk):
                        nk4 = min(4, nkb - b4 * 4)
                        bank = pb[2 + b4 % 2]
                        ai = b4 % 2
                        k.wait('t', tex, tdr, tok_b2.get(('at', ai)))
                        for i4 in range(nk4):
                            kb = b4 * 4 + i4
                            mm = nc.tensor.matmul(
                                bank[:, i4 * PT:(i4 + 1) * PT],
                                exps[:, kb * PT:(kb + 1) * PT], drl[:, :],
                                start=(i4 == 0), stop=(i4 == nk4 - 1))
                        tmm = k.inc('t', mm)
                        tok_exps_r = tmm
                        tok_drl_r = tmm
                        k.wait('v', tmm, tok_b2.get(('ar', ai)))
                        tae = k.inc('v', nc.vector.tensor_copy(
                            att[ai][:, :nk4 * PT], bank[:, :nk4 * PT]))
                        tok_b2[('at', ai)] = tae
                        k.wait('t', tae,
                               tok_b2.get('po') if b4 == 0 else None)
                        for i4 in range(nk4):
                            kb = b4 * 4 + i4
                            mm = nc.tensor.matmul(
                                po[:DH, 0:PT], vf[:, kb * KW + kvh * DH:
                                               kb * KW + (kvh + 1) * DH],
                                att[ai][:, i4 * PT:(i4 + 1) * PT],
                                start=(kb == 0), stop=(kb == nkb - 1),
                                skip_group_check=True)
                        tva = k.inc('t', mm)
                        tok_b2[('ar', ai)] = tva
                    k.wait('s', tva, tok_oT_r)
                    if DH == PT:
                        dst = oT[:, h * 512 + jj * PT: h * 512 + (jj + 1) * PT]
                    else:
                        colb, rowb = (h * DH) // PT, (h * DH) % PT
                        dst = oT[rowb:rowb + DH,
                                 colb * 512 + jj * PT: colb * 512 + (jj + 1) * PT]
                    tev = k.inc('s', nc.scalar.copy(dst, po[:DH, 0:PT]))
                    tok_b2['po'] = tev
                    t_oT_w.append(tev)
                tok_qT2_r = ('t', k.cnt['t'])
            t_oT = tmax(*t_oT_w)
            for dmq in range(4):
                for hs in range(NHD):
                    wi = woi[0] % 2
                    woi[0] += 1
                    tw = k.dma('y' if io['Wo'].dtype == BF16 else 'g',
                               f'c_wo{wi}', woch[wi][:, :],
                               io['Wo'][hs * PT:(hs + 1) * PT,
                                        dmq * 512:(dmq + 1) * 512],
                               wait=(tok_wo_r[wi],))
                    k.wait('t', tw, *t_oT)
                    if hs == 0:
                        k.wait('t', *[tok_b2.get(('wo', q)) for q in range(4)])
                    for q4 in range(4):
                        mm = nc.tensor.matmul(
                            pb[q4][:, :], woch[wi][:, q4 * PT:(q4 + 1) * PT],
                            oT[:, hs * 512:(hs + 1) * 512],
                            start=(hs == 0), stop=(hs == NHD - 1))
                    tok_wo_r[wi] = k.inc('t', mm)
                tmw = tok_wo_r[(woi[0] - 1) % 2]
                tok_oT_r = tmw
                for q4 in range(4):
                    dms = dmq * 4 + q4
                    k.wait('v', tmw, tok_outT_r)
                    tev = k.inc('v', nc.vector.tensor_copy(
                        outT[:, dms * 512:(dms + 1) * 512], pb[q4][:, :]))
                    tok_b2[('wo', q4)] = tev
            t_outT_w = ('v', k.cnt['v'])
            for jj, j in enumerate(tiles):
                rsl = slice(j * PT, (j + 1) * PT)
                tx = k.dma('y', 'c_xt2', xt2[:, :], x_in[rsl, :],
                           wait=(tok_xt2_r,))
                adds = []
                for dmq in range(4):
                    srcs = [outT[:, (dmq * 4 + i) * 512 + jj * PT:
                                 (dmq * 4 + i) * 512 + (jj + 1) * PT]
                            for i in range(4)]
                    bank, tm, b = trp.quad(srcs, t_outT_w)
                    k.wait('v', tm, tx, tok_xo_r)
                    te = k.inc('v', nc.vector.tensor_tensor(
                        xo[:, dmq * 512:(dmq + 1) * 512],
                        xt2[:, dmq * 512:(dmq + 1) * 512], bank, ALU.add))
                    trp.done(b, te)
                    adds.append(te)
                tok_xt2_r = adds[-1]
                tok_outT_r = ('t', k.cnt['t'])
                tst = k.dma('s', 'c_xo', x_out[rsl, :], xo[:, :],
                            wait=tuple(tmax(*adds)))
                tok_xo_r = tst
    return tst


def emit_f1(k, cfg, io):
    nc = k.nc
    n_own = cfg['n_own']
    ngrp = n_own // 4
    E, IM, ISH = 8, 1024, 4096
    ident = cfg['ident']
    pb = cfg['pbanks']
    trp = cfg['trp']
    x_in, x_out = io['x_in'], io['x_out']
    wgu, wd = io['w_gate_up'], io['w_down']
    wsg, wsd = io['w_sh_gate_up'], io['w_sh_down']
    esel = cfg['esel']
    esel_tok = cfg['esel_tok']

    with k.scope():
        xt = k.sb("f1_xt", (PT, D), F32)
        xn = k.sb("f1_xn", (PT, D), BF16)
        sc = dict(st6=k.sb("f1_st6", (PT, 24), F32),
                  st2=k.sb("f1_st2", (PT, 2), F32),
                  rc=k.sb("f1_rc", (PT, 2), F32))
        xnT = k.sb("f1_xnT", (PT, 16 * 512), BF16)
        xtg = [k.sb(f"f1_xtg{i}", (PT, D), F32) for i in range(4)]
        wg8 = k.sb("f1_wg8", (PT, 16 * 8), BF16)
        lgT = k.sb("f1_lgT", (8, 512), F32)
        lg = k.sb("f1_lg", (PT, 8), F32)
        r1 = k.sb("f1_r1", (PT, 8), F32)
        r2 = k.sb("f1_r2", (PT, 8), F32)
        rcol = k.sb("f1_rcol", (PT, 4), F32)
        cmb = k.sb("f1_cmb", (PT, 8), BF16)
        cmbT = k.sb("f1_cmbT", (8, 512), BF16)
        bc = k.sb("f1_bc", (PT, 512), BF16)
        wch = [k.sb(f"f1_w{i}", (PT, 512), BF16) for i in range(2)]
        gsl = k.sb("f1_gs", (PT, 4 * 512), BF16)
        usl = k.sb("f1_us", (PT, 4 * 512), BF16)
        acts = k.sb("f1_act", (PT, 96 * 512), BF16)

        tok = {}
        wci = [0]
        tst = None

        def castdma(dst, src, ch_wait=None):
            wi = wci[0] % 2
            wci[0] += 1
            t = k.dma('y' if src.dtype == BF16 else 'g', f'c_fw{wi}',
                      dst if dst is not None else wch[wi][:, :],
                      src, wait=(tok.get(('fw', wi)), ch_wait))
            return (wi, t)

        for g in range(ngrp):
            tiles = list(range(4 * g, 4 * g + 4))
            t_tr = None
            for jj, j in enumerate(tiles):
                tx = k.dma('y', 'c_xt', xt[:, :], x_in[j * PT:(j + 1) * PT, :],
                           wait=(tok.get('xt_r'),))
                txn = emit_ln(k, xt, xn, sc, tx, war_tok=tok.get('xn_r'))
                tok['xt_r'] = txn
                t_tr = trp.t16(xn[:, :], xnT[:, :], jj * PT, 512, txn,
                               war_tok=tok.get('xnT_r') if jj == 0 else None)
                tok['xn_r'] = ('t', k.cnt['t'])
            if g == 0:
                tw8 = k.dma('g', 'c_wg8',
                            wg8[:, :].rearrange("p (a e) -> p a e", e=8),
                            io['w_gate'].rearrange("(a p) e -> p a e", p=PT))
            k.wait('t', tw8, t_tr, tok.get('ps8_r'))
            for dt in range(16):
                mm = nc.tensor.matmul(pb[4][0:8, :], wg8[:, dt * 8:(dt + 1) * 8],
                                      xnT[:, dt * 512:(dt + 1) * 512],
                                      start=(dt == 0), stop=(dt == 15))
            tmm = k.inc('t', mm)
            tok['xnT_r'] = tmm
            k.wait('v', tmm, tok.get('lgT_r'))
            tlgT = k.inc('v', nc.vector.tensor_copy(lgT[:, :], pb[4][0:8, :]))
            tok['ps8_r'] = tlgT
            tcmbT = []
            for st in range(4):
                k.wait('t', tlgT, tok.get('bc_ps_r'))
                tm = k.inc('t', nc.tensor.transpose(
                    pb[5][:, 0:8], lgT[:, st * PT:(st + 1) * PT],
                    cfg['identf'][0:8, 0:8]))
                k.wait('v', tm, tok.get('lg_r'))
                tlg = k.inc('v', nc.vector.tensor_copy(lg[:, :], pb[5][:, 0:8]))
                tok['bc_ps_r'] = tlg
                k.wait('v', tlg)
                t1 = k.inc('v', nc.vector.reduce_max(rcol[:, 0:1], lg[:, :],
                                                     axis=AX.X))
                k.wait('v', t1)
                t2 = k.inc('v', nc.vector.tensor_scalar(
                    r1[:, :], lg[:, :], rcol[:, 0:1], None, ALU.is_equal))
                k.wait('v', t2)
                t3 = k.inc('v', nc.vector.scalar_tensor_tensor(
                    r2[:, :], r1[:, :], NEG, lg[:, :], ALU.mult, ALU.add))
                k.wait('v', t3)
                t4 = k.inc('v', nc.vector.reduce_max(rcol[:, 1:2], r2[:, :],
                                                     axis=AX.X))
                k.wait('v', t4)
                t5 = k.inc('v', nc.vector.tensor_scalar(
                    r1[:, :], lg[:, :], rcol[:, 1:2], None, ALU.is_ge))
                t5b = k.inc('v', nc.vector.tensor_scalar_mul(
                    rcol[:, 2:3], rcol[:, 0:1], -1.0))
                k.wait('s', t5b, t2)
                t6 = k.inc('s', nc.scalar.activation(r2[:, :], lg[:, :], AF.Exp,
                                                     bias=rcol[:, 2:3]))
                k.wait('v', t6, t5)
                t7 = k.inc('v', nc.vector.tensor_mul(r2[:, :], r2[:, :], r1[:, :]))
                k.wait('v', t7)
                t8 = k.inc('v', nc.vector.reduce_sum(rcol[:, 3:4], r2[:, :],
                                                     axis=AX.X))
                k.wait('v', t8)
                t9 = k.inc('v', nc.vector.reciprocal(rcol[:, 3:4], rcol[:, 3:4]))
                k.wait('v', t9)
                t10 = k.inc('v', nc.vector.tensor_scalar_mul(
                    cmb[:, :], r2[:, :], rcol[:, 3:4]))
                tok['lg_r'] = t10
                b = trp.i % 2
                trp.i += 1
                pbk = trp.banks[b]
                k.wait('t', t10, trp.busy[b], tok.get('cmbT_r') if st == 0 else None)
                tm2 = k.inc('t', nc.tensor.transpose(pbk[0:8, 0:PT], cmb[:, :],
                                                     ident[:, :]))
                k.wait('v', tm2)
                tc = k.inc('v', nc.vector.tensor_copy(
                    cmbT[0:8, st * PT:(st + 1) * PT], pbk[0:8, 0:PT]))
                trp.done(b, tc)
                tcmbT.append(tc)
            t_cmbT = tmax(*tcmbT)

            def gu_phase(Wsrc, bc_tok, actbase, width2):
                half = width2 // 2
                tacts = []
                for fb in range(half // 512):
                    for part, base_col in ((0, fb * 512), (1, half + fb * 512)):
                        dst = gsl if part == 0 else usl
                        for dt in range(16):
                            wi, tw = castdma(None,
                                             Wsrc[dt * PT:(dt + 1) * PT,
                                                  base_col:base_col + 512])
                            k.wait('t', tw, t_tr,
                                   *[tok.get(('gub', q)) for q in range(4)])
                            for q4 in range(4):
                                mm = nc.tensor.matmul(
                                    pb[q4][:, :],
                                    wch[wi][:, q4 * PT:(q4 + 1) * PT],
                                    xnT[:, dt * 512:(dt + 1) * 512],
                                    start=(dt == 0), stop=(dt == 15))
                            tok[('fw', wi)] = k.inc('t', mm)
                        tmm = tok[('fw', wi)]
                        for q4 in range(4):
                            k.wait('v', tmm, tok.get(('gu_r', part, q4)))
                            tev = k.inc('v', nc.vector.tensor_copy(
                                dst[:, q4 * 512:(q4 + 1) * 512], pb[q4][:, :]))
                            tok[('gub', q4)] = tev
                            tok[('gu_w', part, q4)] = tev
                    for q4 in range(4):
                        fs = fb * 4 + q4
                        tg = tok[('gu_w', 0, q4)]
                        tu = tok[('gu_w', 1, q4)]
                        k.wait('s', tg, tok.get(('sg_r', q4)))
                        ts1 = k.inc('s', nc.scalar.activation(
                            gsl[:, q4 * 512:(q4 + 1) * 512],
                            gsl[:, q4 * 512:(q4 + 1) * 512], AF.Silu))
                        tok[('gu_r', 0, q4)] = ts1
                        k.wait('v', ts1, tu, tok.get(('act_r', actbase + fs)))
                        tm1 = k.inc('v', nc.vector.tensor_mul(
                            usl[:, q4 * 512:(q4 + 1) * 512],
                            gsl[:, q4 * 512:(q4 + 1) * 512],
                            usl[:, q4 * 512:(q4 + 1) * 512]))
                        tok[('sg_r', q4)] = tm1
                        k.wait('v', tm1, bc_tok)
                        if bc_tok is not None:
                            tm2 = k.inc('v', nc.vector.tensor_mul(
                                acts[:, (actbase + fs) * 512:
                                     (actbase + fs + 1) * 512],
                                usl[:, q4 * 512:(q4 + 1) * 512], bc[:, :]))
                        else:
                            tm2 = k.inc('v', nc.vector.tensor_copy(
                                acts[:, (actbase + fs) * 512:
                                     (actbase + fs + 1) * 512],
                                usl[:, q4 * 512:(q4 + 1) * 512]))
                        tok[('gu_r', 1, q4)] = tm2
                        tok[('act_w', actbase + fs)] = tm2
                        tacts.append(tm2)
                return tacts

            all_act = []
            for e in range(E):
                k.wait('t', *t_cmbT, cfg['esel_tok'], tok.get('bc_ps_r'))
                mmb = k.inc('t', nc.tensor.matmul(
                    pb[5][:, :], esel[0:8, e * PT:(e + 1) * PT], cmbT[0:8, :],
                    start=True, stop=True))
                tok['cmbT_r'] = mmb
                k.wait('v', mmb, tok.get('bc_r'))
                tbc = k.inc('v', nc.vector.tensor_copy(bc[:, :], pb[5][:, :]))
                tok['bc_ps_r'] = tbc
                all_act += gu_phase(wgu[e], tbc, e * 8, 2 * IM)
                tok['bc_r'] = ('v', k.cnt['v'])
            all_act += gu_phase(wsg, None, 64, 2 * ISH)
            t_acts = tmax(*all_act)

            NSL_E, NSL_S = IM // PT, ISH // PT
            txs = []
            for jj, j in enumerate(tiles):
                txs.append(k.dma('y', f'c_xt2{jj}', xtg[jj][:, :],
                                 x_in[j * PT:(j + 1) * PT, :],
                                 wait=(tok.get(('xtg_r', jj)),)))
            for dmb in range(4):
                k.wait('t', *t_acts, *[tok.get(('dps', q)) for q in range(4)])
                first = True
                chains = [(e, it) for e in range(E) for it in range(NSL_E)]
                chains += [(-1, it) for it in range(NSL_S)]
                for (e, it) in chains:
                    if e >= 0:
                        wsrc = wd[e][it * PT:(it + 1) * PT,
                                     dmb * 512:(dmb + 1) * 512]
                        acol = (e * 8 + it) * 512
                    else:
                        wsrc = wsd[it * PT:(it + 1) * PT,
                                   dmb * 512:(dmb + 1) * 512]
                        acol = (64 + it) * 512
                    wi, tw = castdma(None, wsrc)
                    k.wait('t', tw)
                    last_c = (e, it) == chains[-1]
                    for jj in range(4):
                        mm = nc.tensor.matmul(
                            pb[jj][:, :],
                            acts[:, acol + jj * PT: acol + (jj + 1) * PT],
                            wch[wi][:, :], start=first, stop=last_c,
                            skip_group_check=True)
                    first = False
                    tok[('fw', wi)] = k.inc('t', mm)
                tmm = tok[('fw', (wci[0] - 1) % 2)]
                for jj in range(4):
                    k.wait('v', tmm, txs[jj], tok.get('xo_r'))
                    te = k.inc('v', nc.vector.tensor_tensor(
                        xtg[jj][:, dmb * 512:(dmb + 1) * 512],
                        xtg[jj][:, dmb * 512:(dmb + 1) * 512], pb[jj][:, :],
                        ALU.add))
                    tok[('dps', jj)] = te
                    tok.setdefault('dadds', {}).setdefault(jj, []).append(te)
            dadds = tok.pop('dadds')
            tst = None
            for jj, j in enumerate(tiles):
                tst = k.dma('s', f'c_xo{jj}', x_out[j * PT:(j + 1) * PT, :],
                            xtg[jj][:, :], wait=tuple(tmax(*dadds[jj])))
                tok[('xtg_r', jj)] = tst
            tok['xo_r'] = ('v', k.cnt['v'])
            for key in list(tok):
                if isinstance(key, tuple) and key[0] == 'act_r':
                    pass
            tok['xnT_r'] = ('t', k.cnt['t'])
            for i in range(96):
                tok[('act_r', i)] = ('t', k.cnt['t'])
    return tst


def emit_a0b(k, cfg, io):
    nc = k.nc
    n_own = cfg['n_own']
    n_tiles = 2 * n_own
    H, KV, DH = 16, 4, 128
    QW, KW = H * DH, KV * DH
    gph = H // KV
    half = DH // 2
    scale = DH ** -0.5
    ident = cfg['ident']
    pb = cfg['pbanks']
    trp = cfg['trp']
    x_in, x_out = io['x_in'], io['x_out']
    tok = {}
    wci = [0]

    def rope3(src_ap, dst_ap, tabc, tabs, ti, width, nheads, src_tok, war_tok,
              war2, scratch, tab_tok=None):
        base = src_ap
        sw4 = bass.AP(base.tensor, base.offset + half,
                      [list(base.ap[0]), [DH, nheads], [-half, 2], [1, half]])
        tc = tabc[:, ti * DH:(ti + 1) * DH]
        ts_ = tabs[:, ti * DH:(ti + 1) * DH]
        cosb = bass.AP(tc.tensor, tc.offset,
                       [list(tc.ap[0]), [0, nheads], [1, DH]])
        sinb = bass.AP(ts_.tensor, ts_.offset,
                       [list(ts_.ap[0]), [0, nheads], [half, 2], [1, half]])
        d4 = dst_ap.rearrange("p (h s x) -> p h s x", h=nheads, s=2)
        s3 = src_ap.rearrange("p (h x) -> p h x", h=nheads)
        k.wait('v', src_tok, war_tok, tab_tok)
        t1 = k.inc('v', nc.vector.tensor_tensor(d4, sw4, sinb, ALU.mult))
        k.wait('v', t1, war2)
        t2 = k.inc('v', nc.vector.tensor_tensor(
            scratch[:, :width].rearrange("p (h x) -> p h x", h=nheads), s3, cosb,
            ALU.mult))
        k.wait('v', t2)
        return k.inc('v', nc.vector.tensor_tensor(dst_ap, dst_ap,
                                                  scratch[:, :width], ALU.add))

    def softmax_heads(src_ap, dst_ap, nheads, mxb, src_tok, war_tok):
        v3s = src_ap.rearrange("p (h x) -> p h x", h=nheads)
        k.wait('v', src_tok)
        t1 = k.inc('v', nc.vector.reduce_max(
            mxb[:, 0:nheads].rearrange("p (h o) -> p h o", h=nheads), v3s,
            axis=AX.X))
        k.wait('v', t1)
        t2 = k.inc('v', nc.vector.tensor_scalar_mul(mxb[:, 16:16 + nheads],
                                                    mxb[:, 0:nheads], -1.0))
        texp = None
        for h in range(nheads):
            k.wait('s', t2, war_tok if h == 0 else None)
            texp = k.inc('s', nc.scalar.activation(
                dst_ap[:, h * DH:(h + 1) * DH], src_ap[:, h * DH:(h + 1) * DH],
                AF.Exp, bias=mxb[:, 16 + h:17 + h]))
        v3d = dst_ap.rearrange("p (h x) -> p h x", h=nheads)
        k.wait('v', texp)
        t3 = k.inc('v', nc.vector.reduce_sum(
            mxb[:, 0:nheads].rearrange("p (h o) -> p h o", h=nheads), v3d,
            axis=AX.X))
        k.wait('v', t3)
        t4 = k.inc('v', nc.vector.reciprocal(mxb[:, 16:16 + nheads],
                                             mxb[:, 0:nheads]))
        tl = None
        k.wait('v', t4)
        for h in range(nheads):
            tl = k.inc('v', nc.vector.tensor_scalar_mul(
                dst_ap[:, h * DH:(h + 1) * DH], dst_ap[:, h * DH:(h + 1) * DH],
                mxb[:, 16 + h:17 + h]))
        return tl

    def gl_pipeline(ksrc_ap, A_bank, Eq_sb, Ek_sb, gl_sb, src_tok, war_q,
                    war_k):
        k.wait('s', src_tok, war_q)
        tg1 = k.inc('s', nc.scalar.activation(gl_sb[:, :], ksrc_ap, AF.Sigmoid))
        k.wait('s', tg1)
        tg2 = k.inc('s', nc.scalar.activation(gl_sb[:, :], gl_sb[:, :], AF.Ln))
        k.wait('t', tg2, cfg['utri_tok'], tok.get('Abank_r'))
        tA = k.inc('t', nc.tensor.matmul(A_bank[:, 0:KW], cfg['utri'][:, :],
                                         gl_sb[:, :], start=True, stop=True))
        k.wait('s', tA, war_k)
        te1 = k.inc('s', nc.scalar.activation(Eq_sb[:, :], A_bank[:, 0:KW],
                                              AF.Exp, scale=1.0 / 16.0))
        te2 = k.inc('s', nc.scalar.activation(Ek_sb[:, :], A_bank[:, 0:KW],
                                              AF.Exp, scale=-1.0 / 16.0))
        tok['Abank_r'] = te2
        return tg2, tA, te1, te2

    with k.scope():
        xt = k.sb("ab_xt", (PT, D), F32)
        xn = k.sb("ab_xn", (PT, D), BF16)
        sc = dict(st6=k.sb("ab_st6", (PT, 24), F32),
                  st2=k.sb("ab_st2", (PT, 2), F32),
                  rc=k.sb("ab_rc", (PT, 2), F32))
        xnT = k.sb("ab_xnT", (PT, 16 * 512), BF16)
        wch = [k.sb(f"ab_w{i}", (PT, 512), BF16) for i in range(2)]
        qg = k.sb("ab_qg", (PT, 4 * QW), BF16)
        kvg = k.sb("ab_kvg", (PT, 4 * 2 * KW), BF16)
        rop = k.sb("ab_rop", (PT, QW), BF16)
        rop2 = k.sb("ab_rop2", (PT, QW), BF16)
        trb = k.sb("ab_trb", (PT, QW), BF16)
        qs_sb = k.sb("ab_qs", (PT, QW), BF16)
        mxb = k.sb("ab_mx", (PT, 32), F32)
        gl_sb = k.sb("ab_gl", (PT, KW), F32)
        Eq_sb = k.sb("ab_Eq", (PT, KW), F32)
        Ek_sb = k.sb("ab_Ek", (PT, KW), F32)
        qh_sb = k.sb("ab_qh", (PT, QW), BF16)
        kh_sb = k.sb("ab_kh", (PT, KW), BF16)
        tco = k.sb("ab_tco", (PT, n_own * DH), BF16)
        tso = k.sb("ab_tso", (PT, n_own * DH), BF16)
        ttab = k.dma('y', 'c_tab', tco[:, :], cfg['tabc_o_d'])
        ttab = k.dma('y', 'c_tab', tso[:, :], cfg['tabs_o_d'])

        def proj_group(Wsrc, width, dst, dst_off, dst_stride, war_toks, x_tok):
            tevs = []
            wq_ = 'y' if Wsrc.dtype == BF16 else 'g'
            for blk in range(width // 512):
                for dt in range(16):
                    wi = wci[0] % 2
                    wci[0] += 1
                    tw = k.dma(wq_, f'c_w{wi}', wch[wi][:, :],
                               Wsrc[dt * PT:(dt + 1) * PT,
                                    blk * 512:(blk + 1) * 512],
                               wait=(tok.get(('w', wi)),))
                    k.wait('t', tw, x_tok, *[tok.get(('pg', q)) for q in range(4)])
                    for q4 in range(4):
                        mm = nc.tensor.matmul(
                            pb[q4][:, :],
                            xnT[:, dt * 512 + q4 * PT: dt * 512 + (q4 + 1) * PT],
                            wch[wi][:, :], start=(dt == 0), stop=(dt == 15))
                    tok[('w', wi)] = k.inc('t', mm)
                tmm = tok[('w', (wci[0] - 1) % 2)]
                for q4 in range(4):
                    k.wait('v', tmm,
                           *(war_toks if (blk == 0 and q4 == 0) else []))
                    tev = k.inc('v', nc.vector.tensor_copy(
                        dst[:, dst_off + q4 * dst_stride + blk * 512:
                            dst_off + q4 * dst_stride + (blk + 1) * 512],
                        pb[q4][:, :]))
                    tok[('pg', q4)] = tev
                    tevs.append(tev)
            return tmax(*tevs)

        def store_T(src_ap, nsl, src_tok, dram_dst, chname):
            if nsl == 16:
                tt = trp.t16(src_ap, trb[:, :], 0, PT, src_tok,
                             war_tok=tok.get('trb_r'))
            else:
                srcs = [src_ap[:, i * PT:(i + 1) * PT] for i in range(nsl)]
                bank, tmq, b = trp.quad(srcs, src_tok)
                k.wait('v', tmq, tok.get('trb_r'))
                tt = k.inc('v', nc.vector.tensor_copy(trb[:, 0:nsl * PT],
                                                      bank[:, 0:nsl * PT]))
                trp.done(b, tt)
            ts = k.dma('s', chname, dram_dst, trb[:, 0:nsl * PT], wait=(tt,))
            tok['trb_r'] = ts
            return ts

        ngrp = n_own // 4
        for g in range(ngrp):
            tiles = list(range(4 * g, 4 * g + 4))
            t_tr = None
            for jj, j in enumerate(tiles):
                tx = k.dma('y', 'c_xt', xt[:, :], x_in[j * PT:(j + 1) * PT, :],
                           wait=(tok.get('xt_r'),))
                txn = emit_ln(k, xt, xn, sc, tx, war_tok=tok.get('xn_r'))
                tok['xt_r'] = txn
                t_tr = trp.t16(xn[:, :], xnT[:, :], jj * PT, 512, txn,
                               war_tok=tok.get('xnT_r') if jj == 0 else None)
                tok['xn_r'] = ('t', k.cnt['t'])
            tq = proj_group(io['Wq'], QW, qg, 0, QW, tok.get('qg_r') or [],
                            t_tr)
            tkk = proj_group(io['Wk'], KW, kvg, 0, 2 * KW,
                             tok.get('kvg_r') or [], t_tr)
            tvv = proj_group(io['Wv'], KW, kvg, KW, 2 * KW, [], t_tr)
            tok['xnT_r'] = ('t', k.cnt['t'])
            for jj, j in enumerate(tiles):
                qsl = qg[:, jj * QW:(jj + 1) * QW]
                ksl = kvg[:, jj * 2 * KW: jj * 2 * KW + KW]
                vsl = kvg[:, jj * 2 * KW + KW: (jj + 1) * 2 * KW]
                trq = rope3(qsl, rop[:, :QW], tco[:, :], tso[:, :], j,
                            QW, H, tq[0] if tq else None, tok.get('rop_r'),
                            tok.get('rop2_r'), rop2, tab_tok=ttab)
                tok['rop2_r'] = trq
                tsq = store_T(rop[:, :QW], 16, trq, io['qT'][j], 'c_sq')
                tok['rop_r'] = ('t', k.cnt['t'])
                tqs = softmax_heads(qsl, qs_sb[:, :], H, mxb,
                                    tq[0] if tq else None, tok.get('qs_r'))
                tg2, tA, te1, te2 = gl_pipeline(ksl, pb[4], Eq_sb, Ek_sb, gl_sb,
                                                tkk[0] if tkk else None,
                                                tok.get('Eq_r'), tok.get('Ek_r'))
                eqb = bass.AP(Eq_sb.ap().tensor, Eq_sb.ap().offset,
                              [list(Eq_sb.ap().ap[0]), [DH, KV], [0, gph],
                               [1, DH]])
                k.wait('v', tqs, te1, tok.get('qh_r'))
                tqh = k.inc('v', nc.vector.tensor_mul(
                    qh_sb[:, :].rearrange("p (a b x) -> p a b x", a=KV, b=gph),
                    qs_sb[:, :].rearrange("p (a b x) -> p a b x", a=KV, b=gph),
                    eqb))
                tok['Eq_r'] = tqh
                tsh = store_T(qh_sb[:, :], 16, tqh, io['qhT'][j], 'c_sh')
                tok['qh_r'] = ('t', k.cnt['t'])
                tok['qs_r'] = tqh
                tks = softmax_heads(ksl, kh_sb[:, :], KV, mxb,
                                    tkk[0] if tkk else None, tok.get('kh_r'))
                k.wait('v', tks, te2)
                tkh = k.inc('v', nc.vector.tensor_mul(kh_sb[:, :], kh_sb[:, :],
                                                      Ek_sb[:, :]))
                tok['Ek_r'] = tkh
                tsk = store_T(kh_sb[:, :], 4, tkh, io['khT'][j], 'c_skh')
                tok['kh_r'] = ('t', k.cnt['t'])
                ta1 = k.dma('s', 'c_agi', io['ag_in'][j * PT:(j + 1) * PT, :],
                            kvg[:, jj * 2 * KW:(jj + 1) * 2 * KW],
                            wait=tuple(tmax(*tkk, *tvv)))
                tv2 = k.dma('s', 'c_sv', io['vT'][j], vsl,
                            wait=tuple(tmax(*tvv)))
                tok.setdefault('pstores', []).extend([tsq, tsh, tsk, ta1, tv2])
            tok['qg_r'] = tmax(*tok.pop('pstores'))
            tok['kvg_r'] = tok['qg_r']

    k.barrier()
    ch = k.chan('c_col')
    inst = nc.gpsimd.collective_compute(
        "AllGather", ALU.bypass, cfg['replica_groups'],
        [io['ag_in'].opt()], [io['ag_out'].opt()])
    k.cnt[ch] += 1
    inst.then_inc(k.sem[ch], 1)
    t_ag = (ch, k.cnt[ch])

    with k.scope():
        kT = k.sb("ab_kT", (PT, KV * n_tiles * PT), BF16)
        vf = k.sb("ab_vf", (PT, n_tiles * KW), BF16)
        khG = k.sb("ab_khG", (PT, n_tiles * KW), BF16)
        ECg = k.sb("ab_EC", (PT, n_tiles * KV), F32)
        prep_scope = k.scope()
        prep_scope.__enter__()
        kvt = k.sb("ab_kvt", (PT, 2 * KW), BF16)
        rop_b = k.sb("ab_ropb", (PT, KW), BF16)
        rop2_b = k.sb("ab_rop2b", (PT, KW), BF16)
        gl2 = k.sb("ab_gl2", (PT, KW), F32)
        Ek2 = k.sb("ab_Ek2", (PT, KW), F32)
        ks2 = k.sb("ab_ks2", (PT, KW), BF16)
        mxb2 = k.sb("ab_mx2", (PT, 32), F32)
        tca = k.sb("ab_tca", (PT, n_tiles * DH), BF16)
        tsa = k.sb("ab_tsa", (PT, n_tiles * DH), BF16)
        ttab2 = k.dma('y', 'c_tab', tca[:, :], cfg['tabc_a_d'])
        ttab2 = k.dma('y', 'c_tab', tsa[:, :], cfg['tabs_a_d'])
        onec, onec_tok = k.constcol(1.0)

        tok2 = {}
        last_pr = []
        for kb in range(n_tiles):
            par, idx = kb % 2, kb // 2
            row0 = par * n_own * PT + idx * PT
            tl = k.dma('y', 'c_kvt', kvt[:, :], io['ag_out'][row0:row0 + PT, :],
                       wait=(t_ag, *(tok2.get('kvt_r') or [])))
            k.wait('v', tl)
            tv = k.inc('v', nc.vector.tensor_copy(vf[:, kb * KW:(kb + 1) * KW],
                                                  kvt[:, KW:2 * KW]))
            trk = rope3(kvt[:, 0:KW], rop_b[:, :KW], tca[:, :],
                        tsa[:, :], kb, KW, KV, tl, tok2.get('ropb_r'),
                        tok2.get('ropb2_r'), rop2_b, tab_tok=ttab2)
            tok2['ropb2_r'] = trk
            srcs = [rop_b[:, s * PT:(s + 1) * PT] for s in range(4)]
            bankq, tmq, bq = trp.quad(srcs, trk)
            kTb = kT[:, :]
            dstq = bass.AP(kTb.tensor, kTb.offset + kb * PT,
                           [list(kTb.ap[0]), [n_tiles * PT, 4], [1, PT]])
            k.wait('v', tmq)
            ttx = k.inc('v', nc.vector.tensor_copy(
                dstq, bankq.rearrange("p (c x) -> p c x", c=4)))
            trp.done(bq, ttx)
            tok2['ropb_r'] = ('t', k.cnt['t'])
            k.wait('s', tl, tok2.get('gl2_r'))
            tg1 = k.inc('s', nc.scalar.activation(gl2[:, :], kvt[:, 0:KW],
                                                  AF.Sigmoid))
            k.wait('s', tg1)
            tg2 = k.inc('s', nc.scalar.activation(gl2[:, :], gl2[:, :], AF.Ln))
            k.wait('t', tg2, cfg['utri_tok'], tok2.get('A_r'))
            tA = k.inc('t', nc.tensor.matmul(pb[4][:, 0:KW], cfg['utri'][:, :],
                                             gl2[:, :], start=True, stop=True))
            k.wait('s', tA, tok2.get('Ek2_r'))
            te2 = k.inc('s', nc.scalar.activation(Ek2[:, :], pb[4][:, 0:KW],
                                                  AF.Exp, scale=-1.0 / 16.0))
            tok2['A_r'] = te2
            tks = softmax_heads(kvt[:, 0:KW], ks2[:, :], KV, mxb2, tl,
                                tok2.get('ks2_r'))
            k.wait('v', tks, te2, tok2.get('khG_r'))
            tkh = k.inc('v', nc.vector.tensor_mul(
                khG[:, kb * KW:(kb + 1) * KW], ks2[:, :], Ek2[:, :]))
            tok2['Ek2_r'] = tkh
            tok2['ks2_r'] = tkh
            tecs = []
            for s in range(KV):
                k.wait('t', tg2, onec_tok, tok2.get(('EC_r', s)))
                tmc = k.inc('t', nc.tensor.matmul(
                    pb[5][:, s:s + 1], gl2[:, s * PT:(s + 1) * PT],
                    onec[:, 0:1], start=True, stop=True))
                k.wait('s', tmc)
                tec = k.inc('s', nc.scalar.activation(
                    ECg[:, kb * KV + s: kb * KV + s + 1], pb[5][:, s:s + 1],
                    AF.Exp, scale=1.0 / 16.0))
                tok2[('EC_r', s)] = tec
                tecs.append(tec)
            tok2['gl2_r'] = tA
            tok2['kvt_r'] = tmax(tv, tks, tg1, ('t', k.cnt['t']))
            last_pr += [tv, tkh, ttx, *tecs]
        t_prep = tmax(*last_pr, ('t', k.cnt['t']))
        prep_scope.__exit__(None, None, None)

        strip = k.sb("ab_strip", (PT, n_tiles * PT), F32)
        exps = k.sb("ab_exps", (PT, n_tiles * PT), BF16)
        mx = k.sb("ab_mxs", (PT, 2), F32)
        lsum = k.sb("ab_l", (PT, 2), F32)
        drl = k.sb("ab_drl", (PT, PT), BF16)
        att = [k.sb(f"ab_att{i}", (PT, 512), BF16) for i in range(2)]
        qTt = k.sb("ab_qTt", (PT, QW), BF16)
        qhTt = k.sb("ab_qhTt", (PT, QW), BF16)
        khTt = k.sb("ab_khTt", (PT, KW), BF16)
        vtt = k.sb("ab_vtt", (PT, KW), BF16)
        S_sb = k.sb("ab_S", (PT, KV * PT), F32)
        Sb = k.sb("ab_Sb", (PT, KV * PT), BF16)
        msk = [k.sb(f"ab_msk{i}", (PT, PT), BF16) for i in range(2)]
        mskT = [k.sb(f"ab_mskT{i}", (PT, PT), BF16) for i in range(2)]
        oT = k.sb("ab_oT", (PT, 16 * 512), BF16)
        woch = [k.sb(f"ab_wo{i}", (PT, 512), BF16) for i in range(2)]
        outT = k.sb("ab_outT", (PT, 16 * 512), BF16)
        xt2 = k.sb("ab_xt2", (PT, D), F32)
        xo = k.sb("ab_xo", (PT, D), F32)
        blend = k.sb("ab_blend", (PT, 2), F32)

        tbl = k.dma('y', 'c_bl', blend[:, :], io['blend'])
        tz1 = k.inc('v', nc.vector.memset(S_sb[:, :], 0.0))
        tz2 = k.inc('v', nc.vector.memset(Sb[:, :], 0.0))
        tok3 = {}
        woi = [0]
        tst = None

        for g in range(n_own // 4):
            tiles = list(range(4 * g, 4 * g + 4))
            t_oT_w = []
            for jj, j in enumerate(tiles):
                L = (2 * j + 2) * PT
                nkb = 2 * j + 2
                tql = k.dma('y', 'c_qTl', qTt[:, :], io['qT'][j],
                            wait=(tok3.get('qTt_r'),))
                tqh = k.dma('y', 'c_qhl', qhTt[:, :], io['qhT'][j],
                            wait=(tok3.get('qhTt_r'),))
                tkh2 = k.dma('y', 'c_khl', khTt[:, :], io['khT'][j],
                             wait=(tok3.get('khTt_r'),))
                tvl = k.dma('y', 'c_vl', vtt[:, :], io['vT'][j],
                            wait=(tok3.get('vtt_r'),))
                for h in range(H):
                    kvh = h // gph
                    lq = qTt[:, h * PT:(h + 1) * PT]
                    nblk = (nkb + 3) // 4
                    cps = []
                    for b4 in range(nblk):
                        w = min(512, L - b4 * 512)
                        bank = pb[b4 % 2]
                        rk = kT[:, kvh * n_tiles * PT + b4 * 512:
                                kvh * n_tiles * PT + b4 * 512 + w]
                        k.wait('t', tql, *t_prep)
                        k.wait('t', tok3.get(('sc_r', b4 % 2)))
                        mm = k.inc('t', nc.tensor.matmul(
                            bank[:, :w], lq, rk, start=True, stop=True))
                        k.wait('s', mm, tok3.get('strip_r'))
                        tcp = k.inc('s', nc.scalar.copy(
                            strip[:, b4 * 512: b4 * 512 + w], bank[:, :w]))
                        tok3[('sc_r', b4 % 2)] = tcp
                        cps.append(tcp)
                    k.wait('v', cps[-1], cfg['dmask_tok'])
                    tmsk = k.inc('v', nc.vector.tensor_tensor(
                        strip[:, L - 2 * PT:L], strip[:, L - 2 * PT:L],
                        cfg['dmask'][:, :], ALU.add))
                    k.wait('v', *cps, tmsk, tok3.get('strip_r'))
                    tmx = k.inc('v', nc.vector.reduce_max(
                        mx[:, 0:1], strip[:, :L], axis=AX.X))
                    k.wait('v', tmx)
                    tb = k.inc('v', nc.vector.tensor_scalar_mul(
                        mx[:, 1:2], mx[:, 0:1], -scale))
                    k.wait('s', tb, tok3.get('exps_r'), tok3.get('lsum_r'))
                    tex = k.inc('s', nc.scalar.activation(
                        exps[:, :L], strip[:, :L], AF.Exp,
                        bias=mx[:, 1:2], scale=scale, accum_out=lsum[:, 0:1]))
                    tok3['strip_r'] = tex
                    k.wait('v', tex)
                    trl = k.inc('v', nc.vector.reciprocal(lsum[:, 1:2],
                                                          lsum[:, 0:1]))
                    tok3['lsum_r'] = trl
                    k.wait('v', trl, tok3.get('drl_r'))
                    tdr = k.inc('v', nc.vector.tensor_scalar_mul(
                        drl[:, :], ident[:, :], lsum[:, 1:2]))
                    po = pb[4]
                    tva = None
                    for b4 in range(nblk):
                        nk4 = min(4, nkb - b4 * 4)
                        bank = pb[2 + b4 % 2]
                        ai = b4 % 2
                        k.wait('t', tex, tdr, tok3.get(('at', ai)))
                        for i4 in range(nk4):
                            kb2 = b4 * 4 + i4
                            mm = nc.tensor.matmul(
                                bank[:, i4 * PT:(i4 + 1) * PT],
                                exps[:, kb2 * PT:(kb2 + 1) * PT], drl[:, :],
                                start=(i4 == 0), stop=(i4 == nk4 - 1))
                        tmm = k.inc('t', mm)
                        tok3['exps_r'] = tmm
                        tok3['drl_r'] = tmm
                        k.wait('v', tmm, tok3.get(('ar', ai)))
                        tae = k.inc('v', nc.vector.tensor_copy(
                            att[ai][:, :nk4 * PT], bank[:, :nk4 * PT]))
                        tok3[('at', ai)] = tae
                        k.wait('t', tae, tok3.get('po_r') if b4 == 0 else None)
                        for i4 in range(nk4):
                            kb2 = b4 * 4 + i4
                            mm = nc.tensor.matmul(
                                po[:, 0:PT], vf[:, kb2 * KW + kvh * DH:
                                                kb2 * KW + (kvh + 1) * DH],
                                att[ai][:, i4 * PT:(i4 + 1) * PT],
                                start=(kb2 == 0), stop=(kb2 == nkb - 1),
                                skip_group_check=True)
                        tva = k.inc('t', mm)
                        tok3[('ar', ai)] = tva
                    dst = oT[:, h * 512 + jj * PT: h * 512 + (jj + 1) * PT]
                    k.wait('s', tva, tok3.get('oT_r'))
                    tev = k.inc('s', nc.scalar.activation(dst, po[:, 0:PT],
                                                          AF.Copy, scale=0.5))
                    tok3['po_r'] = tev
                    t_oT_w.append(tev)
                tok3['qTt_r'] = ('t', k.cnt['t'])
                for point in range(2):
                    c = 2 * j + point
                    for hg in range(4):
                        bankP = pb[2 + hg % 2]
                        for hh in range(4):
                            h = hg * 4 + hh
                            kvh = h // gph
                            bi = h % 2
                            bankS = pb[bi]
                            k.wait('t', tqh, tkh2, tok3.get(('gsc_r', bi)))
                            mmS = k.inc('t', nc.tensor.matmul(
                                bankS[:, 0:PT], qhTt[:, h * PT:(h + 1) * PT],
                                khTt[:, kvh * PT:(kvh + 1) * PT],
                                start=True, stop=True))
                            k.wait('v', mmS, cfg['tri_tok'],
                                   tok3.get(('gmsk_r', bi)))
                            tmk = k.inc('v', nc.vector.tensor_mul(
                                msk[bi][:, :], bankS[:, 0:PT],
                                cfg['tri01'][:, :]))
                            tok3[('gsc_r', bi)] = tmk
                            pbk, tmt, b = trp.start(msk[bi][:, :], tmk)
                            k.wait('v', tmt, tok3.get(('gmskT_r', bi)))
                            tmT = k.inc('v', nc.vector.tensor_copy(
                                mskT[bi][:, :], pbk[:, :]))
                            trp.done(b, tmT)
                            tok3[('gmsk_r', bi)] = tmT
                            k.wait('t', tmT, tvl, tqh, tz2,
                                   tok3.get(('S_w', kvh)),
                                   tok3.get(('P_r', hg % 2)) if hh == 0 else None)
                            mm = nc.tensor.matmul(
                                bankP[:, hh * PT:(hh + 1) * PT],
                                vtt[:, kvh * PT:(kvh + 1) * PT], mskT[bi][:, :],
                                start=True,
                                stop=False, skip_group_check=True)
                            mm = nc.tensor.matmul(
                                bankP[:, hh * PT:(hh + 1) * PT],
                                Sb[:, kvh * PT:(kvh + 1) * PT],
                                qhTt[:, h * PT:(h + 1) * PT], start=False,
                                stop=True, skip_group_check=True)
                            tP = k.inc('t', mm)
                            tok3[('gmskT_r', bi)] = tP
                            tok3[('Sb_r', kvh)] = tP
                        k.wait('v', tP, tbl, ('s', k.cnt['s']))
                        for hh in range(4):
                            h = hg * 4 + hh
                            dsl = oT[:, h * 512 + jj * PT: h * 512 + (jj + 1) * PT]
                            tev = k.inc('v', nc.vector.scalar_tensor_tensor(
                                dsl, bankP[:, hh * PT:(hh + 1) * PT],
                                blend[:, point:point + 1], dsl,
                                ALU.mult, ALU.add))
                            t_oT_w.append(tev)
                        tok3[('P_r', hg % 2)] = tev
                    for s in range(KV):
                        k.wait('t', *t_prep, tok3.get('KV_r'))
                        mmK = k.inc('t', nc.tensor.matmul(
                            pb[5][:, 0:PT],
                            khG[:, c * KW + s * PT: c * KW + (s + 1) * PT],
                            vf[:, c * KW + s * PT: c * KW + (s + 1) * PT],
                            start=True, stop=True))
                        k.wait('v', mmK, tz1, tok3.get(('S_r', s)))
                        tS1 = k.inc('v', nc.vector.tensor_tensor(
                            S_sb[:, s * PT:(s + 1) * PT],
                            S_sb[:, s * PT:(s + 1) * PT], pb[5][:, 0:PT],
                            ALU.add))
                        tok3['KV_r'] = tS1
                        k.wait('v', tS1)
                        tS2 = k.inc('v', nc.vector.tensor_scalar_mul(
                            S_sb[:, s * PT:(s + 1) * PT],
                            S_sb[:, s * PT:(s + 1) * PT],
                            ECg[:, c * KV + s: c * KV + s + 1]))
                        k.wait('v', tS2, tok3.get(('Sb_r', s)))
                        tSb = k.inc('v', nc.vector.tensor_copy(
                            Sb[:, s * PT:(s + 1) * PT],
                            S_sb[:, s * PT:(s + 1) * PT]))
                        tok3[('S_r', s)] = tSb
                        tok3[('S_w', s)] = tSb
                tok3['qhTt_r'] = ('t', k.cnt['t'])
                tok3['khTt_r'] = tok3['qhTt_r']
                tok3['vtt_r'] = tok3['qhTt_r']
            t_oT = tmax(*t_oT_w)
            NHD = QW // PT
            for dmq in range(4):
                for hs in range(NHD):
                    wi = woi[0] % 2
                    woi[0] += 1
                    tw = k.dma('y' if io['Wo'].dtype == BF16 else 'g',
                               f'c_wo{wi}', woch[wi][:, :],
                               io['Wo'][hs * PT:(hs + 1) * PT,
                                        dmq * 512:(dmq + 1) * 512],
                               wait=(tok3.get(('wo', wi)),))
                    k.wait('t', tw, *t_oT)
                    if hs == 0:
                        k.wait('t', *[tok3.get(('wops', q)) for q in range(4)])
                    for q4 in range(4):
                        mm = nc.tensor.matmul(
                            pb[q4][:, :], woch[wi][:, q4 * PT:(q4 + 1) * PT],
                            oT[:, hs * 512:(hs + 1) * 512],
                            start=(hs == 0), stop=(hs == NHD - 1))
                    tok3[('wo', wi)] = k.inc('t', mm)
                tmw = tok3[('wo', (woi[0] - 1) % 2)]
                tok3['oT_r'] = tmw
                for q4 in range(4):
                    dms = dmq * 4 + q4
                    k.wait('v', tmw, tok3.get('outT_r'))
                    tev = k.inc('v', nc.vector.tensor_copy(
                        outT[:, dms * 512:(dms + 1) * 512], pb[q4][:, :]))
                    tok3[('wops', q4)] = tev
            t_outT_w = ('v', k.cnt['v'])
            for jj, j in enumerate(tiles):
                rsl = slice(j * PT, (j + 1) * PT)
                tx = k.dma('y', 'c_xt2', xt2[:, :], x_in[rsl, :],
                           wait=(tok3.get('xt2_r'),))
                adds = []
                for dmq in range(4):
                    srcs = [outT[:, (dmq * 4 + i) * 512 + jj * PT:
                                 (dmq * 4 + i) * 512 + (jj + 1) * PT]
                            for i in range(4)]
                    bank, tm, b = trp.quad(srcs, t_outT_w)
                    k.wait('v', tm, tx, tok3.get('xo_r'))
                    te = k.inc('v', nc.vector.tensor_tensor(
                        xo[:, dmq * 512:(dmq + 1) * 512],
                        xt2[:, dmq * 512:(dmq + 1) * 512], bank, ALU.add))
                    trp.done(b, te)
                    adds.append(te)
                tok3['xt2_r'] = adds[-1]
                tok3['outT_r'] = ('t', k.cnt['t'])
                tst = k.dma('s', 'c_xo', x_out[rsl, :], xo[:, :],
                            wait=tuple(tmax(*adds)))
                tok3['xo_r'] = tst
            tok3['oT_r'] = ('t', k.cnt['t'])
    return tst


def emit_ternary_prep(k, trp, w_ap, wT_dram, onesrow, onesrow_tok, pbank):
    nc = k.nc
    n_rows, n_cols = w_ap.shape
    NRT = n_rows // PT
    with k.scope():
        ft = [k.sb(f"tq_f{i}", (PT, n_cols), F32) for i in range(2)]
        sgn = k.sb("tq_sg", (PT, n_cols), F32)
        ab = k.sb("tq_ab", (PT, n_cols), F32)
        bt = k.sb("tq_b", (PT, n_cols), BF16)
        acc = k.sb("tq_acc", (PT, 2), F32)
        mcol = k.sb("tq_m", (PT, 2), F32)
        stg = [k.sb(f"tq_g{i}", (PT, 4 * PT), BF16) for i in range(2)]
        one_c, one_tok = k.constcol(1.0)
        tok_l = [None, None]
        tz = k.inc('v', nc.vector.memset(acc[:, :], 0.0))
        tacc = tz
        for rt in range(NRT):
            pi = rt % 2
            tw = k.dma('y', f'c_tq{pi}', ft[pi][:, :],
                       w_ap[rt * PT:(rt + 1) * PT, :], wait=(tok_l[pi],))
            k.wait('v', tw, tacc)
            t1 = k.inc('v', nc.vector.reduce_sum(acc[:, 1:2], ft[pi][:, :],
                                                 axis=AX.X,
                                                 apply_absolute_value=True))
            tok_l[pi] = t1
            k.wait('v', t1)
            tacc = k.inc('v', nc.vector.tensor_add(acc[:, 0:1], acc[:, 0:1],
                                                   acc[:, 1:2]))
        k.wait('t', tacc, one_tok)
        tmm = k.inc('t', nc.tensor.matmul(pbank[0:1, 0:1], one_c[:, 0:1],
                                          acc[:, 0:1], start=True, stop=True))
        k.wait('v', tmm)
        t2 = k.inc('v', nc.vector.tensor_scalar(
            acc[0:1, 0:1], pbank[0:1, 0:1], 1.0 / (n_rows * n_cols), None,
            ALU.mult))
        k.wait('v', t2)
        t3 = k.inc('v', nc.vector.tensor_scalar_max(acc[0:1, 0:1],
                                                    acc[0:1, 0:1], 1e-5))
        k.wait('v', t3)
        t4 = k.inc('v', nc.vector.tensor_scalar(acc[0:1, 1:2], acc[0:1, 0:1],
                                                0.5, None, ALU.mult))
        k.wait('t', t4, onesrow_tok, t2)
        tb = k.inc('t', nc.tensor.matmul(pbank[0:PT, 0:2],
                                         onesrow[0:1, :], acc[0:1, 0:2],
                                         start=True, stop=True))
        k.wait('v', tb)
        tm = k.inc('v', nc.vector.tensor_copy(mcol[:, :], pbank[0:PT, 0:2]))
        tok_l = [tacc, tacc]
        stok = [None, None]
        tok_sgn = None
        last = None
        for rt in range(NRT):
            pi = rt % 2
            rsl = slice(rt * PT, (rt + 1) * PT)
            tw = k.dma('y', f'c_tq{pi}', ft[pi][:, :], w_ap[rsl, :],
                       wait=(tok_l[pi],))
            k.wait('s', tw, tok_sgn)
            ts1 = k.inc('s', nc.scalar.sign(sgn[:, :], ft[pi][:, :]))
            k.wait('s', ts1)
            ts2 = k.inc('s', nc.scalar.activation(ab[:, :], ft[pi][:, :], AF.Abs))
            tok_l[pi] = ts2
            k.wait('v', ts2, tm, tok_sgn)
            tv1 = k.inc('v', nc.vector.tensor_scalar(
                ab[:, :], ab[:, :], mcol[:, 1:2], None, ALU.is_ge))
            k.wait('v', tv1)
            tok_sgn = tb2 = k.inc('v', nc.vector.scalar_tensor_tensor(
                bt[:, :], sgn[:, :], mcol[:, 0:1], ab[:, :], ALU.mult, ALU.mult))
            for cq in range(n_cols // 512):
                si = cq % 2
                srcs = [bt[:, (cq * 4 + i) * PT:(cq * 4 + i + 1) * PT]
                        for i in range(4)]
                bank, tmt, b = trp.quad(srcs, tb2)
                k.wait('v', tmt, stok[si])
                te = k.inc('v', nc.vector.tensor_copy(stg[si][:, :], bank))
                trp.done(b, te)
                last = k.dma('s', f'c_tg{si}',
                             wT_dram[cq * 4:cq * 4 + 4, :, rsl].rearrange(
                                 "c p x -> p c x"),
                             stg[si][:, :].rearrange("p (c x) -> p c x", c=4),
                             wait=(te,))
                stok[si] = last
        return tmax(stok[0], stok[1])


def emit_a1a(k, cfg, io):
    nc = k.nc
    n_own = cfg['n_own']
    n_tiles = 2 * n_own
    H, DH = 16, 128
    ident = cfg['ident']
    pb = cfg['pbanks']
    trp = cfg['trp']
    x_in, x_out = io['x_in'], io['x_out']
    qkvT, projT = io['qkvT'], io['projT']
    w_tok = io.get('w_tok') or []
    onesrow, onesrow_tok = cfg['onesrow'], cfg['onesrow_tok']
    tok = {}
    wci = [0]

    _outer = k.scope()
    _outer.__enter__()
    if True:
        srow = k.sb("aa_srow", (1, 16), F32)
        sbc = k.sb("aa_sbc", (PT, 16), F32)
        Lt = k.sb("aa_Lt", (PT, 16), F32)
        Lrev = k.sb("aa_Lrev", (PT, 16), F32)
        LT = k.sb("aa_LT", (PT, 16), F32)
        Mh = k.sb("aa_Mh", (PT, 16 * PT), BF16)
        tmpm = k.sb("aa_tmpm", (PT, PT), F32)
        wnq = k.sb("aa_wnq", (PT, D), BF16)
        wnn = k.sb("aa_wnn", (PT, D), BF16)
        wnp = k.sb("aa_wnp", (PT, D), BF16)
        vrow = k.sb("aa_vrow", (1, D), F32)
        blend = k.sb("aa_blend", (PT, 2), F32)

        ts_ = k.dma('y', 'c_aa1', srow[0:1, :], io['s'].unsqueeze(0))
        tbl = k.dma('y', 'c_aa2', blend[:, :], io['blend'])
        k.wait('t', ts_, onesrow_tok)
        t1 = k.inc('t', nc.tensor.matmul(pb[0][0:PT, 0:16], onesrow[0:1, :],
                                         srow[0:1, :], start=True, stop=True))
        k.wait('v', t1)
        t2 = k.inc('v', nc.vector.tensor_copy(sbc[:, :], pb[0][0:PT, 0:16]))
        iota_ap, iota_tok = cfg['iota'], cfg['iota_tok']
        k.wait('v', t2, iota_tok)
        ta = k.inc('v', nc.vector.tensor_scalar(Lt[:, :], sbc[:, :],
                                                iota_ap[:, 0:1], None, ALU.mult))
        tb_ = k.inc('v', nc.vector.tensor_scalar(Lrev[:, :], sbc[:, :],
                                                 iota_ap[:, 1:2], None, ALU.mult))
        tc_ = k.inc('v', nc.vector.tensor_scalar(LT[:, :], sbc[:, :], 128.0,
                                                 None, ALU.mult))
        k.wait('s', ta, tb_, tc_)
        tLt = k.inc('s', nc.scalar.activation(Lt[:, :], Lt[:, :], AF.Exp,
                                              scale=-1.0))
        tLr = k.inc('s', nc.scalar.activation(Lrev[:, :], Lrev[:, :], AF.Exp,
                                              scale=-1.0))
        tLT = k.inc('s', nc.scalar.activation(LT[:, :], LT[:, :], AF.Exp,
                                              scale=-1.0))
        tMh = None
        for h in range(H):
            k.wait('v', t2, cfg['diff_tok'], tMh)
            tm1 = k.inc('v', nc.vector.scalar_tensor_tensor(
                tmpm[:, :], cfg['diff'][:, :], sbc[:, h:h + 1],
                cfg['upmask'][:, :], ALU.mult, ALU.add))
            k.wait('s', tm1)
            tMh = k.inc('s', nc.scalar.activation(Mh[:, h * PT:(h + 1) * PT],
                                                  tmpm[:, :], AF.Exp, scale=-1.0))
            tok['tmpm_r'] = tMh
        t_mh = tMh
        wn_toks = []
        for (win, dst) in ((io['qkv_norm_w'], wnq), (io['norm_w'], wnn),
                           (io['proj_norm_w'], wnp)):
            tv = k.dma('y', 'c_aa3', vrow[0:1, :], win.unsqueeze(0),
                       wait=(wn_toks[-1] if wn_toks else None,))
            for b4 in range(4):
                k.wait('t', tv, tok.get('wnps'))
                tmm = k.inc('t', nc.tensor.matmul(
                    pb[1][:, :], onesrow[0:1, :],
                    vrow[0:1, b4 * 512:(b4 + 1) * 512], start=True, stop=True))
                k.wait('v', tmm)
                te = k.inc('v', nc.vector.tensor_copy(
                    dst[:, b4 * 512:(b4 + 1) * 512], pb[1][:, :]))
                tok['wnps'] = te
            wn_toks.append(te)
        t_wn = tmax(*wn_toks)

    with k.scope():
        xt = k.sb("aa_xt", (PT, D), F32)
        xn = k.sb("aa_xn", (PT, D), BF16)
        sc = dict(st6=k.sb("aa_st6", (PT, 24), F32),
                  st2=k.sb("aa_st2", (PT, 2), F32),
                  rc=k.sb("aa_rc", (PT, 4), F32))
        xnT = k.sb("aa_xnT", (PT, 16 * 512), BF16)
        wch = [k.sb(f"aa_w{i}", (PT, 512), BF16) for i in range(2)]
        qg = k.sb("aa_qg", (PT, 4 * D), BF16)
        kg = k.sb("aa_kg", (PT, 4 * D), BF16)
        vg = k.sb("aa_vg", (PT, 4 * D), BF16)
        qp = k.sb("aa_qp", (PT, D), BF16)
        trb = k.sb("aa_trb", (PT, D), BF16)

        def rms_tile(src, dst, wbc, eps, src_tok, war_tok):
            k.wait('v', src_tok)
            for c in range(4):
                tt1 = k.inc('v', nc.vector.bn_stats(
                    sc['st6'][:, c * 6:(c + 1) * 6],
                    src[:, c * 512:(c + 1) * 512]))
            k.wait('v', tt1)
            tt2 = k.inc('v', nc.vector.bn_aggr(sc['st2'][:, :], sc['st6'][:, :]))
            k.wait('v', tt2)
            tt3 = k.inc('v', nc.vector.scalar_tensor_tensor(
                sc['rc'][:, 1:2], sc['st2'][:, 0:1], sc['st2'][:, 0:1],
                sc['st2'][:, 1:2], ALU.mult, ALU.add))
            eps_ap, eps_tok = k.constcol(eps)
            k.wait('s', tt3, eps_tok)
            tl = k.inc('s', nc.scalar.activation(sc['rc'][:, 0:1],
                                                 sc['rc'][:, 1:2], AF.Ln,
                                                 bias=eps_ap[:, 0:1]))
            k.wait('s', tl)
            tr = k.inc('s', nc.scalar.activation(sc['rc'][:, 0:1],
                                                 sc['rc'][:, 0:1], AF.Exp,
                                                 scale=-0.5))
            k.wait('v', tr, war_tok, *t_wn)
            tx1 = k.inc('v', nc.vector.tensor_scalar_mul(dst[:, :], src[:, :],
                                                         sc['rc'][:, 0:1]))
            k.wait('v', tx1)
            return k.inc('v', nc.vector.tensor_mul(dst[:, :], dst[:, :], wbc))

        def store_T(src, src_tok, dram_dst, chname, war_tok):
            tt = trp.t16(src[:, :] if hasattr(src, 'shape') and not isinstance(src, bass.AP) else src,
                         trb[:, :], 0, PT, src_tok, war_tok=war_tok)
            return k.dma('s', chname, dram_dst, trb[:, :], wait=(tt,))

        ngrp = n_own // 4
        for g in range(ngrp):
            tiles = list(range(4 * g, 4 * g + 4))
            t_tr = None
            for jj, j in enumerate(tiles):
                tx = k.dma('y', 'c_xt', xt[:, :], x_in[j * PT:(j + 1) * PT, :],
                           wait=(tok.get('xt_r'),))
                txn = rms_tile(xt, xn, wnq[:, :], F32EPS, tx, tok.get('xn_r'))
                tok['xt_r'] = txn
                t_tr = trp.t16(xn[:, :], xnT[:, :], jj * PT, 512, txn,
                               war_tok=tok.get('xnT_r') if jj == 0 else None)
                tok['xn_r'] = ('t', k.cnt['t'])
            for blk in range(12):
                dst = (qg, kg, vg)[blk // 4]
                bof = (blk % 4) * 512
                for dt in range(16):
                    wi = wci[0] % 2
                    wci[0] += 1
                    tw = k.dma('y', f'c_aw{wi}', wch[wi][:, :],
                               qkvT[dt, :, blk * 512:(blk + 1) * 512],
                               wait=(tok.get(('aw', wi)),
                                     *(w_tok if (blk == 0 and dt == 0 and g == 0)
                                       else [])))
                    k.wait('t', tw, t_tr, *[tok.get(('pg', q)) for q in range(4)])
                    for q4 in range(4):
                        mm = nc.tensor.matmul(
                            pb[q4][:, :],
                            xnT[:, dt * 512 + q4 * PT: dt * 512 + (q4 + 1) * PT],
                            wch[wi][:, :], start=(dt == 0), stop=(dt == 15))
                    tok[('aw', wi)] = k.inc('t', mm)
                tmm = tok[('aw', (wci[0] - 1) % 2)]
                for q4 in range(4):
                    k.wait('v', tmm, *(tok.get('qkv_r') or []))
                    te = k.inc('v', nc.vector.tensor_copy(
                        dst[:, q4 * D + bof: q4 * D + bof + 512], pb[q4][:, :]))
                    tok[('pg', q4)] = te
            t_proj = ('v', k.cnt['v'])
            for jj, j in enumerate(tiles):
                tq = store_T(qg[:, jj * D:(jj + 1) * D], t_proj, io['qT'][j],
                             'c_sq', tok.get('trb_r'))
                tok['trb_r'] = tq
                tk = store_T(kg[:, jj * D:(jj + 1) * D], t_proj, io['kT'][j],
                             'c_sk', tok.get('trb_r'))
                tok['trb_r'] = tk
                k.wait('v', tLt, tok.get('qp_r'))
                for h in range(H):
                    tqp = k.inc('v', nc.vector.tensor_scalar_mul(
                        qp[:, h * PT:(h + 1) * PT],
                        qg[:, jj * D + h * PT: jj * D + (h + 1) * PT],
                        Lt[:, h:h + 1]))
                tqs = store_T(qp, tqp, io['qpT'][j], 'c_sp', tok.get('trb_r'))
                tok['trb_r'] = tqs
                tok['qp_r'] = ('t', k.cnt['t'])
                k.wait('v', tLr, tok.get('qp_r'))
                for h in range(H):
                    tk2 = k.inc('v', nc.vector.tensor_scalar_mul(
                        qp[:, h * PT:(h + 1) * PT],
                        kg[:, jj * D + h * PT: jj * D + (h + 1) * PT],
                        Lrev[:, h:h + 1]))
                ta1 = k.dma('s', 'c_agi', io['ag_in'][j * PT:(j + 1) * PT, 0:D],
                            qp[:, :], wait=(tk2,))
                tok['qp_r'] = ta1
                ta2 = k.dma('s', 'c_agi2',
                            io['ag_in'][j * PT:(j + 1) * PT, D:2 * D],
                            vg[:, jj * D:(jj + 1) * D], wait=(t_proj,))
                tv2 = k.dma('s', 'c_sv', io['vT'][j], vg[:, jj * D:(jj + 1) * D],
                            wait=(t_proj,))
                tok.setdefault('qkv_stores', []).extend([tq, tk, ta1, ta2, tv2])
            tok['qkv_r'] = tmax(*tok.pop('qkv_stores'))
            tok['xnT_r'] = ('t', k.cnt['t'])

    k.barrier()
    ch = k.chan('c_col')
    inst = nc.gpsimd.collective_compute(
        "AllGather", ALU.bypass, cfg['replica_groups'],
        [io['ag_in'].opt()], [io['ag_out'].opt()])
    k.cnt[ch] += 1
    inst.then_inc(k.sem[ch], 1)
    t_ag = (ch, k.cnt[ch])

    with k.scope():
        S_sb = k.sb("aa_S", (PT, H * PT), F32)
        Sb = k.sb("aa_Sb", (PT, H * PT), BF16)
        o_sb = k.sb("aa_o", (PT, D), F32)
        o2_sb = k.sb("aa_o2", (PT, D), F32)
        kvc = k.sb("aa_kvc", (PT, 2 * D), BF16)
        qTt = k.sb("aa_qTt", (PT, D), BF16)
        qpTt = k.sb("aa_qpTt", (PT, D), BF16)
        kTt = k.sb("aa_kTt", (PT, D), BF16)
        vt = k.sb("aa_vt", (PT, D), BF16)
        msk = [k.sb(f"aa_msk{i}", (PT, PT), BF16) for i in range(2)]
        mskT = [k.sb(f"aa_mskT{i}", (PT, PT), BF16) for i in range(2)]
        y2 = k.sb("aa_y2", (PT, D), BF16)
        y2T = k.sb("aa_y2T", (PT, D), BF16)
        pch = [k.sb(f"aa_pw{i}", (PT, 512), BF16) for i in range(2)]
        xo = k.sb("aa_xo", (PT, D), F32)
        sc2 = dict(st6=k.sb("aa_st6b", (PT, 24), F32),
                   st2=k.sb("aa_st2b", (PT, 2), F32),
                   rc=k.sb("aa_rcb", (PT, 4), F32))

        tz1 = k.inc('v', nc.vector.memset(S_sb[:, :], 0.0))
        tz2 = k.inc('v', nc.vector.memset(Sb[:, :], 0.0))

        def rms_tile2(src, dst, wbc, eps, src_tok, war_tok):
            k.wait('v', src_tok)
            for c in range(4):
                tt1 = k.inc('v', nc.vector.bn_stats(
                    sc2['st6'][:, c * 6:(c + 1) * 6],
                    src[:, c * 512:(c + 1) * 512]))
            k.wait('v', tt1)
            tt2 = k.inc('v', nc.vector.bn_aggr(sc2['st2'][:, :], sc2['st6'][:, :]))
            k.wait('v', tt2)
            tt3 = k.inc('v', nc.vector.scalar_tensor_tensor(
                sc2['rc'][:, 1:2], sc2['st2'][:, 0:1], sc2['st2'][:, 0:1],
                sc2['st2'][:, 1:2], ALU.mult, ALU.add))
            eps_ap, eps_tok = k.constcol(eps)
            k.wait('s', tt3, eps_tok)
            tl = k.inc('s', nc.scalar.activation(sc2['rc'][:, 0:1],
                                                 sc2['rc'][:, 1:2], AF.Ln,
                                                 bias=eps_ap[:, 0:1]))
            k.wait('s', tl)
            tr = k.inc('s', nc.scalar.activation(sc2['rc'][:, 0:1],
                                                 sc2['rc'][:, 0:1], AF.Exp,
                                                 scale=-0.5))
            k.wait('v', tr, war_tok)
            tx1 = k.inc('v', nc.vector.tensor_scalar_mul(dst[:, :], src[:, :],
                                                         sc2['rc'][:, 0:1]))
            k.wait('v', tx1)
            return k.inc('v', nc.vector.tensor_mul(dst[:, :], dst[:, :], wbc))

        tst = None
        for j in range(n_own):
            tql = k.dma('y', 'c_lq', qTt[:, :], io['qT'][j],
                        wait=(tok.get('qTt_r'),))
            tqp = k.dma('y', 'c_lp', qpTt[:, :], io['qpT'][j],
                        wait=(tok.get('qpTt_r'),))
            tkl = k.dma('y', 'c_lk', kTt[:, :], io['kT'][j],
                        wait=(tok.get('kTt_r'),))
            tvl = k.dma('y', 'c_lv', vt[:, :], io['vT'][j],
                        wait=(tok.get('vt_r'),))
            for point in range(2):
                c = 2 * j + point
                for hg in range(4):
                    bankP = pb[2 + hg % 2]
                    for hh in range(4):
                        h = hg * 4 + hh
                        bi = h % 2
                        bankS = pb[bi]
                        k.wait('t', tql, tkl, tok.get(('sc_r', bi)))
                        mmS = k.inc('t', nc.tensor.matmul(
                            bankS[:, 0:PT], qTt[:, h * PT:(h + 1) * PT],
                            kTt[:, h * PT:(h + 1) * PT], start=True, stop=True))
                        k.wait('v', mmS, t_mh, tok.get(('msk_r', bi)))
                        tmk = k.inc('v', nc.vector.tensor_mul(
                            msk[bi][:, :], bankS[:, 0:PT],
                            Mh[:, h * PT:(h + 1) * PT]))
                        tok[('sc_r', bi)] = tmk
                        pbk, tmt, b = trp.start(msk[bi][:, :], tmk)
                        k.wait('v', tmt, tok.get(('mskT_r', bi)))
                        tmT = k.inc('v', nc.vector.tensor_copy(mskT[bi][:, :],
                                                               pbk[:, :]))
                        trp.done(b, tmT)
                        tok[('msk_r', bi)] = tmT
                        k.wait('t', tmT, tvl, tqp, tz2,
                               tok.get(('S_r', h)),
                               tok.get(('P_r', hg % 2)) if hh == 0 else None)
                        mm = nc.tensor.matmul(
                            bankP[:, hh * PT:(hh + 1) * PT], mskT[bi][:, :],
                            vt[:, h * PT:(h + 1) * PT], start=True,
                            stop=False, skip_group_check=True)
                        mm = nc.tensor.matmul(
                            bankP[:, hh * PT:(hh + 1) * PT],
                            qpTt[:, h * PT:(h + 1) * PT],
                            Sb[:, h * PT:(h + 1) * PT], start=False, stop=True,
                            skip_group_check=True)
                        tok[('mskT_r', bi)] = tP = k.inc('t', mm)
                        tok[('Sb_r', h)] = tP
                    k.wait('v', tP, tbl, tok.get('o_r') if hg == 0 and point == 0 else None)
                    if point == 0:
                        tev = k.inc('v', nc.vector.tensor_scalar_mul(
                            o_sb[:, hg * 512:(hg + 1) * 512], bankP[:, :],
                            blend[:, 0:1]))
                    else:
                        tev = k.inc('v', nc.vector.scalar_tensor_tensor(
                            o_sb[:, hg * 512:(hg + 1) * 512], bankP[:, :],
                            blend[:, 1:2], o_sb[:, hg * 512:(hg + 1) * 512],
                            ALU.mult, ALU.add))
                    tok[('P_r', hg % 2)] = tev
                t_o = tev
                row0 = (c % 2) * n_own * PT + (c // 2) * PT
                tkv = k.dma('y', 'c_kvc', kvc[:, :],
                            io['ag_out'][row0:row0 + PT, :],
                            wait=(t_ag, tok.get('kvc_r')))
                for h in range(H):
                    k.wait('t', tkv, tok.get('KV_r'))
                    mmK = k.inc('t', nc.tensor.matmul(
                        pb[4][:, 0:PT], kvc[:, h * PT:(h + 1) * PT],
                        kvc[:, D + h * PT: D + (h + 1) * PT],
                        start=True, stop=True))
                    k.wait('v', mmK, tLT, tz1,
                           tok.get(('S_r', h)))
                    tS = k.inc('v', nc.vector.scalar_tensor_tensor(
                        S_sb[:, h * PT:(h + 1) * PT],
                        S_sb[:, h * PT:(h + 1) * PT], LT[:, h:h + 1],
                        pb[4][:, 0:PT], ALU.mult, ALU.add))
                    tok['KV_r'] = tS
                    k.wait('v', tS, tok.get(('Sb_r', h)))
                    tSb = k.inc('v', nc.vector.tensor_copy(
                        Sb[:, h * PT:(h + 1) * PT],
                        S_sb[:, h * PT:(h + 1) * PT]))
                    tok[('S_r', h)] = tSb
                tok['kvc_r'] = ('t', k.cnt['t'])
            tok['qTt_r'] = ('t', k.cnt['t'])
            tok['kTt_r'] = tok['qTt_r']
            tok['qpTt_r'] = tok['qTt_r']
            tok['vt_r'] = tok['qTt_r']
            tr1 = rms_tile2(o_sb, o2_sb, wnn[:, :], 1e-6, t_o, tok.get('o2_r'))
            tok['o_r'] = tr1
            tr2 = rms_tile2(o2_sb, y2, wnp[:, :], F32EPS, tr1, tok.get('y2_r'))
            tok['o2_r'] = tr2
            tt = trp.t16(y2[:, :], y2T[:, :], 0, PT, tr2,
                         war_tok=tok.get('y2T_r'))
            tok['y2_r'] = ('t', k.cnt['t'])
            tx = k.dma('y', 'c_xt3', o2_sb[:, :], x_in[j * PT:(j + 1) * PT, :],
                       wait=(tr2,))
            adds = []
            for dmb in range(4):
                bankD = pb[5]
                k.wait('t', tok.get('pD_r'))
                for ft in range(16):
                    wi = wci[0] % 2
                    wci[0] += 1
                    tw = k.dma('y', f'c_aw{wi}', pch[wi % 2][:, :],
                               projT[ft, :, dmb * 512:(dmb + 1) * 512],
                               wait=(tok.get(('aw2', wi % 2)),))
                    k.wait('t', tw, tt)
                    mm = nc.tensor.matmul(bankD[:, :],
                                          y2T[:, ft * PT:(ft + 1) * PT],
                                          pch[wi % 2][:, :],
                                          start=(ft == 0), stop=(ft == 15))
                    tok[('aw2', wi % 2)] = k.inc('t', mm)
                tmm = tok[('aw2', (wci[0] - 1) % 2)]
                tok['y2T_r'] = tmm
                k.wait('v', tmm, tx, tok.get('xo_r'))
                te = k.inc('v', nc.vector.tensor_tensor(
                    xo[:, dmb * 512:(dmb + 1) * 512],
                    o2_sb[:, dmb * 512:(dmb + 1) * 512], bankD[:, :], ALU.add))
                tok['pD_r'] = te
                adds.append(te)
            tok['o2_r'] = adds[-1]
            tst = k.dma('s', 'c_xo', x_out[j * PT:(j + 1) * PT, :], xo[:, :],
                        wait=tuple(tmax(*adds)))
            tok['xo_r'] = tst
    _outer.__exit__(None, None, None)
    return tst


def build_f0_program(n_own=16):
    nc = bass.Bass("TRN2", target_bir_lowering=False, debug=False)
    HID = 5632
    x_in = nc.dram_tensor("x_shard", [n_own * PT, D], F32,
                          kind="ExternalInput").ap()
    wq_g = nc.dram_tensor("wq_gate", [HID, D], I32, kind="ExternalInput").ap()
    s_g = nc.dram_tensor("s_gate", [HID], F32, kind="ExternalInput").ap()
    wq_u = nc.dram_tensor("wq_up", [HID, D], I32, kind="ExternalInput").ap()
    s_u = nc.dram_tensor("s_up", [HID], F32, kind="ExternalInput").ap()
    wq_d = nc.dram_tensor("wq_down", [D, HID], I32, kind="ExternalInput").ap()
    s_d = nc.dram_tensor("s_down", [D], F32, kind="ExternalInput").ap()
    ident_in = nc.dram_tensor("ident", [PT, PT], BF16, kind="ExternalInput").ap()
    x_out = nc.dram_tensor("y_shard", [n_own * PT, D], F32,
                           kind="ExternalOutput").ap()
    wgT = nc.dram_tensor("wgT", [16, PT, HID], BF16, kind="Internal").ap()
    wuT = nc.dram_tensor("wuT", [16, PT, HID], BF16, kind="Internal").ap()
    wdT = nc.dram_tensor("wdT", [16, PT, 5632], BF16, kind="Internal").ap()

    with ExitStack() as ctx:
        k = K(nc, ctx)
        k.constcol(1e-5)
        ident = k.sb("ident_sb", (PT, PT), BF16)
        pbanks = [k.ps(f"pb{i}", (PT, 512), F32) for i in range(6)]
        ptr = [k.ps(f"ptr{i}", (PT, 1024), BF16) for i in range(2)]
        block = ctx.enter_context(nc.Block())

        @block.sync
        def _(e):
            trp = TrPool(k, ident, ptr[0][:, :], ptr[1][:, :])
            ti = k.dma('y', 'c_id', ident[:, :], ident_in)
            k.wait('t', ti)
            emit_f0_prep(k, trp, wq_g, s_g, wgT)
            emit_f0_prep(k, trp, wq_u, s_u, wuT)
            tw = emit_f0_prep(k, trp, wq_d, s_d, wdT, key_by_row=True)
            cfg = dict(n_own=n_own, ident=ident, pbanks=pbanks, trp=trp)
            io = dict(x_in=x_in, x_out=x_out, wgT=wgT, wuT=wuT, wdT=wdT,
                      w_tok=tw)
            tst = emit_f0(k, cfg, io)
            k.dma_fence('y')
    return nc



def host_rope_tables(positions, DH, base=10000.0):
    n_own = len(positions) // PT
    inv = base ** (-np.arange(0, DH, 2, dtype=np.float64) / DH)
    t = positions.astype(np.float64)[:, None] * inv[None, :]
    cos = np.cos(t).astype(np.float32)
    sin = np.sin(t).astype(np.float32)
    cosw = np.concatenate([cos, cos], -1)
    sinw = np.concatenate([-sin, sin], -1)

    def tilemaj(a):
        return np.ascontiguousarray(
            a.reshape(n_own, PT, DH).transpose(1, 0, 2).reshape(PT, n_own * DH))
    return tilemaj(cosw).astype(BFNP), tilemaj(sinw).astype(BFNP)


def host_dmask(par):
    tri = np.where(np.arange(PT)[None, :] <= np.arange(PT)[:, None], 0.0, NEG)
    m = np.zeros((PT, 2 * PT), np.float32)
    if par == 0:
        m[:, :PT] = tri
        m[:, PT:] = NEG
    else:
        m[:, :PT] = 0.0
        m[:, PT:] = tri
    return m.astype(np.float32)


def build_attn_program(n_own, n_tiles, nH, nKV, DH, scale, replica_groups,
                       combined_kv=False, rope_k=True):
    nc = bass.Bass("TRN2", target_bir_lowering=False, debug=False)
    QW, KW = nH * DH, nKV * DH
    x_in = nc.dram_tensor("x_shard", [n_own * PT, D], F32,
                          kind="ExternalInput").ap()
    if combined_kv:
        Wq = nc.dram_tensor("Wq", [D, QW], F32, kind="ExternalInput").ap()
        Wkv = nc.dram_tensor("Wkv", [D, 2 * KW], F32, kind="ExternalInput").ap()
        Wk, Wv = Wkv[:, 0:KW], Wkv[:, KW:2 * KW]
    else:
        Wq = nc.dram_tensor("Wq", [D, QW], F32, kind="ExternalInput").ap()
        Wk = nc.dram_tensor("Wk", [D, KW], F32, kind="ExternalInput").ap()
        Wv = nc.dram_tensor("Wv", [D, KW], F32, kind="ExternalInput").ap()
    Wo = nc.dram_tensor("Wo", [QW, D], F32, kind="ExternalInput").ap()
    tabc_in = nc.dram_tensor("tabc", [PT, n_own * DH], BF16,
                             kind="ExternalInput").ap()
    tabs_in = nc.dram_tensor("tabs", [PT, n_own * DH], BF16,
                             kind="ExternalInput").ap()
    dmask_in = nc.dram_tensor("dmask", [PT, 2 * PT], F32,
                              kind="ExternalInput").ap()
    ident_in = nc.dram_tensor("ident", [PT, PT], BF16,
                              kind="ExternalInput").ap()
    x_out = nc.dram_tensor("y_shard", [n_own * PT, D], F32,
                           kind="ExternalOutput").ap()
    qT = nc.dram_tensor("qT", [n_own, PT, QW], BF16, kind="Internal").ap()
    ag_in = nc.dram_tensor("ag_in", [n_own * PT, 2 * KW], BF16,
                           kind="Internal").ap()
    ag_out = nc.dram_tensor("ag_out", [n_tiles * PT, 2 * KW], BF16,
                            kind="Internal").ap()

    with ExitStack() as ctx:
        k = K(nc, ctx)
        k.constcol(1e-5)
        ident = k.sb("ident_sb", (PT, PT), BF16)
        dmask = k.sb("dmask_sb", (PT, 2 * PT), F32)
        pbanks = [k.ps(f"pb{i}", (PT, 512), F32) for i in range(6)]
        ptr = [k.ps(f"ptr{i}", (PT, 1024), BF16) for i in range(2)]
        block = ctx.enter_context(nc.Block())

        @block.sync
        def _(e):
            trp = TrPool(k, ident, ptr[0][:, :], ptr[1][:, :])
            ti = k.dma('y', 'c_id', ident[:, :], ident_in)
            tdm = k.dma('y', 'c_id', dmask[:, :], dmask_in)
            k.wait('t', tdm)
            k.wait('v', tdm)
            cfg = dict(n_tiles=n_tiles, own_tiles=list(range(n_own)),
                       n_own=n_own, nH=nH, nKV=nKV, DH=DH, scale=scale,
                       ident=ident, pbanks=pbanks, trp=trp,
                       tabc_d=tabc_in, tabs_d=tabs_in,
                       dmask=dmask, dmask_tok=tdm,
                       replica_groups=replica_groups, rope_k=rope_k)
            io = dict(x_in=x_in, x_out=x_out, Wq=Wq, Wk=Wk, Wv=Wv, Wo=Wo,
                      qT=qT, ag_in=ag_in, ag_out=ag_out)
            emit_attn(k, cfg, io)
            k.dma_fence('y')
    return nc



def build_f1_program(n_own=16):
    nc = bass.Bass("TRN2", target_bir_lowering=False, debug=False)
    x_in = nc.dram_tensor("x_shard", [n_own * PT, D], F32,
                          kind="ExternalInput").ap()
    w_gate = nc.dram_tensor("w_gate", [D, 8], F32, kind="ExternalInput").ap()
    w_gu = nc.dram_tensor("w_gate_up", [8, D, D], F32, kind="ExternalInput").ap()
    w_dn = nc.dram_tensor("w_down", [8, 1024, D], F32, kind="ExternalInput").ap()
    w_sg = nc.dram_tensor("w_sh_gate_up", [D, 8192], F32,
                          kind="ExternalInput").ap()
    w_sd = nc.dram_tensor("w_sh_down", [4096, D], F32, kind="ExternalInput").ap()
    ident_in = nc.dram_tensor("ident", [PT, PT], BF16, kind="ExternalInput").ap()
    esel_in = nc.dram_tensor("esel", [8, 8 * PT], BF16, kind="ExternalInput").ap()
    x_out = nc.dram_tensor("y_shard", [n_own * PT, D], F32,
                           kind="ExternalOutput").ap()

    with ExitStack() as ctx:
        k = K(nc, ctx)
        k.constcol(1e-5)
        ident = k.sb("ident_sb", (PT, PT), BF16)
        identf = k.sb("identf_sb", (PT, PT), F32)
        esel = k.sb("esel_sb", (8, 8 * PT), BF16)
        pbanks = [k.ps(f"pb{i}", (PT, 512), F32) for i in range(6)]
        ptr = [k.ps(f"ptr{i}", (PT, 1024), BF16) for i in range(2)]
        block = ctx.enter_context(nc.Block())

        @block.sync
        def _(e):
            trp = TrPool(k, ident, ptr[0][:, :], ptr[1][:, :])
            ti = k.dma('y', 'c_id', ident[:, :], ident_in)
            tif = k.dma('g', 'c_id2', identf[:, :], ident_in)
            te = k.dma('y', 'c_id', esel[0:8, :], esel_in)
            k.wait('t', te)
            k.wait('t', tif)
            cfg = dict(n_own=n_own, ident=ident, identf=identf, pbanks=pbanks,
                       trp=trp, esel=esel, esel_tok=te)
            io = dict(x_in=x_in, x_out=x_out, w_gate=w_gate, w_gate_up=w_gu,
                      w_down=w_dn, w_sh_gate_up=w_sg, w_sh_down=w_sd)
            emit_f1(k, cfg, io)
            k.dma_fence('y')
    return nc


def host_esel():
    e = np.zeros((8, 8 * PT), dtype=BFNP)
    for i in range(8):
        e[i, i * PT:(i + 1) * PT] = 1
    return e



def build_a1a_program(n_own, n_tiles, replica_groups):
    nc = bass.Bass("TRN2", target_bir_lowering=False, debug=False)
    x_in = nc.dram_tensor("x_shard", [n_own * PT, D], F32,
                          kind="ExternalInput").ap()
    qkv_w = nc.dram_tensor("qkv_w", [3 * D, D], F32, kind="ExternalInput").ap()
    proj_w = nc.dram_tensor("proj_w", [D, D], F32, kind="ExternalInput").ap()
    qn = nc.dram_tensor("qkv_norm_w", [D], F32, kind="ExternalInput").ap()
    nw = nc.dram_tensor("norm_w", [D], F32, kind="ExternalInput").ap()
    pn = nc.dram_tensor("proj_norm_w", [D], F32, kind="ExternalInput").ap()
    s_in = nc.dram_tensor("s", [16], F32, kind="ExternalInput").ap()
    ident_in = nc.dram_tensor("ident", [PT, PT], BF16, kind="ExternalInput").ap()
    ones_in = nc.dram_tensor("onesrow", [1, PT], F32, kind="ExternalInput").ap()
    blend_in = nc.dram_tensor("blend", [PT, 2], F32, kind="ExternalInput").ap()
    iota_in = nc.dram_tensor("iota2", [PT, 2], F32, kind="ExternalInput").ap()
    diff_in = nc.dram_tensor("diff", [PT, PT], F32, kind="ExternalInput").ap()
    up_in = nc.dram_tensor("upmask", [PT, PT], F32, kind="ExternalInput").ap()
    x_out = nc.dram_tensor("y_shard", [n_own * PT, D], F32,
                           kind="ExternalOutput").ap()
    qkvT = nc.dram_tensor("qkvT", [16, PT, 3 * D], BF16, kind="Internal").ap()
    projT = nc.dram_tensor("projT", [16, PT, D], BF16, kind="Internal").ap()
    qT = nc.dram_tensor("qT_s", [n_own, PT, D], BF16, kind="Internal").ap()
    qpT = nc.dram_tensor("qpT_s", [n_own, PT, D], BF16, kind="Internal").ap()
    kT = nc.dram_tensor("kT_s", [n_own, PT, D], BF16, kind="Internal").ap()
    vT = nc.dram_tensor("vT_s", [n_own, PT, D], BF16, kind="Internal").ap()
    ag_in = nc.dram_tensor("ag_in", [n_own * PT, 2 * D], BF16,
                           kind="Internal").ap()
    ag_out = nc.dram_tensor("ag_out", [n_tiles * PT, 2 * D], BF16,
                            kind="Internal").ap()

    with ExitStack() as ctx:
        k = K(nc, ctx)
        k.constcol(1.0)
        k.constcol(1e-6)
        k.constcol(F32EPS)
        ident = k.sb("ident_sb", (PT, PT), BF16)
        onesrow = k.sb("ones_sb", (1, PT), F32)
        iota2 = k.sb("iota_sb", (PT, 2), F32)
        diff = k.sb("diff_sb", (PT, PT), F32)
        upm = k.sb("up_sb", (PT, PT), F32)
        pbanks = [k.ps(f"pb{i}", (PT, 512), F32) for i in range(6)]
        ptr = [k.ps(f"ptr{i}", (PT, 1024), BF16) for i in range(2)]
        block = ctx.enter_context(nc.Block())

        @block.sync
        def _(e):
            trp = TrPool(k, ident, ptr[0][:, :], ptr[1][:, :])
            ti = k.dma('y', 'c_id', ident[:, :], ident_in)
            to = k.dma('y', 'c_id', onesrow[0:1, :], ones_in)
            ti2 = k.dma('y', 'c_id', iota2[:, :], iota_in)
            td = k.dma('y', 'c_id', diff[:, :], diff_in)
            tu = k.dma('y', 'c_id', upm[:, :], up_in)
            k.wait('t', tu)
            k.wait('v', tu)
            t1 = emit_ternary_prep(k, trp, qkv_w, qkvT, onesrow, to, pbanks[5])
            t2 = emit_ternary_prep(k, trp, proj_w, projT, onesrow, to, pbanks[5])
            cfg = dict(n_own=n_own, ident=ident, pbanks=pbanks, trp=trp,
                       onesrow=onesrow, onesrow_tok=to, iota=iota2,
                       iota_tok=ti2, diff=diff, diff_tok=td, upmask=upm,
                       replica_groups=replica_groups)
            io = dict(x_in=x_in, x_out=x_out, qkvT=qkvT, projT=projT,
                      qT=qT, qpT=qpT, kT=kT, vT=vT, ag_in=ag_in, ag_out=ag_out,
                      s=s_in, blend=blend_in, qkv_norm_w=qn, norm_w=nw,
                      proj_norm_w=pn, w_tok=tmax(*(t1 + t2)))
            emit_a1a(k, cfg, io)
            k.dma_fence('y')
    return nc


def host_a1a_consts(par):
    t = np.arange(PT, dtype=np.float32)
    iota2 = np.stack([t, PT - t], 1).astype(np.float32)
    diff = (t[:, None] - t[None, :]).astype(np.float32)
    upm = np.where(t[None, :] <= t[:, None], 0.0, 1e30).astype(np.float32)
    scale = float(128 ** -0.5)
    blend = np.zeros((PT, 2), np.float32)
    blend[:, par] = scale
    ones = np.ones((1, PT), np.float32)
    return iota2, diff, upm, blend, ones



def build_a0b_program(n_own, n_tiles, replica_groups):
    nc = bass.Bass("TRN2", target_bir_lowering=False, debug=False)
    H, KV, DH = 16, 4, 128
    QW, KW = H * DH, KV * DH
    x_in = nc.dram_tensor("x_shard", [n_own * PT, D], F32,
                          kind="ExternalInput").ap()
    Wq = nc.dram_tensor("Wq", [D, QW], F32, kind="ExternalInput").ap()
    Wk = nc.dram_tensor("Wk", [D, KW], F32, kind="ExternalInput").ap()
    Wv = nc.dram_tensor("Wv", [D, KW], F32, kind="ExternalInput").ap()
    Wo = nc.dram_tensor("Wo", [QW, D], F32, kind="ExternalInput").ap()
    tabc_o = nc.dram_tensor("tabc_o", [PT, n_own * DH], BF16,
                            kind="ExternalInput").ap()
    tabs_o = nc.dram_tensor("tabs_o", [PT, n_own * DH], BF16,
                            kind="ExternalInput").ap()
    tabc_a = nc.dram_tensor("tabc_a", [PT, n_tiles * DH], BF16,
                            kind="ExternalInput").ap()
    tabs_a = nc.dram_tensor("tabs_a", [PT, n_tiles * DH], BF16,
                            kind="ExternalInput").ap()
    dmask_in = nc.dram_tensor("dmask", [PT, 2 * PT], F32,
                              kind="ExternalInput").ap()
    tri01_in = nc.dram_tensor("tri01", [PT, PT], BF16,
                              kind="ExternalInput").ap()
    utri_in = nc.dram_tensor("utri", [PT, PT], F32, kind="ExternalInput").ap()
    ident_in = nc.dram_tensor("ident", [PT, PT], BF16,
                              kind="ExternalInput").ap()
    blend_in = nc.dram_tensor("blend", [PT, 2], F32, kind="ExternalInput").ap()
    x_out = nc.dram_tensor("y_shard", [n_own * PT, D], F32,
                           kind="ExternalOutput").ap()
    qT = nc.dram_tensor("qT_s", [n_own, PT, QW], BF16, kind="Internal").ap()
    qhT = nc.dram_tensor("qhT_s", [n_own, PT, QW], BF16, kind="Internal").ap()
    khT = nc.dram_tensor("khT_s", [n_own, PT, KW], BF16, kind="Internal").ap()
    vT = nc.dram_tensor("vT_s", [n_own, PT, KW], BF16, kind="Internal").ap()
    ag_in = nc.dram_tensor("ag_in", [n_own * PT, 2 * KW], BF16,
                           kind="Internal").ap()
    ag_out = nc.dram_tensor("ag_out", [n_tiles * PT, 2 * KW], BF16,
                            kind="Internal").ap()

    with ExitStack() as ctx:
        k = K(nc, ctx)
        k.constcol(1.0)
        k.constcol(1e-5)
        ident = k.sb("ident_sb", (PT, PT), BF16)
        dmask = k.sb("dmask_sb", (PT, 2 * PT), F32)
        tri01 = k.sb("tri01_sb", (PT, PT), BF16)
        utri = k.sb("utri_sb", (PT, PT), F32)
        pbanks = [k.ps(f"pb{i}", (PT, 512), F32) for i in range(6)]
        ptr = [k.ps(f"ptr{i}", (PT, 1024), BF16) for i in range(2)]
        block = ctx.enter_context(nc.Block())

        @block.sync
        def _(e):
            trp = TrPool(k, ident, ptr[0][:, :], ptr[1][:, :])
            ti = k.dma('y', 'c_id', ident[:, :], ident_in)
            t5 = k.dma('y', 'c_id', dmask[:, :], dmask_in)
            t6 = k.dma('y', 'c_id', tri01[:, :], tri01_in)
            t7 = k.dma('y', 'c_id', utri[:, :], utri_in)
            k.wait('t', t7)
            k.wait('v', t7)
            k.wait('s', t7)
            cfg = dict(n_own=n_own, ident=ident, pbanks=pbanks, trp=trp,
                       tabc_o_d=tabc_o, tabs_o_d=tabs_o, tabc_a_d=tabc_a,
                       tabs_a_d=tabs_a, dmask=dmask, dmask_tok=t7,
                       tri01=tri01, tri_tok=t7, utri=utri, utri_tok=t7,
                       replica_groups=replica_groups)
            io = dict(x_in=x_in, x_out=x_out, Wq=Wq, Wk=Wk, Wv=Wv, Wo=Wo,
                      qT=qT, qhT=qhT, khT=khT, vT=vT, ag_in=ag_in,
                      ag_out=ag_out, blend=blend_in)
            emit_a0b(k, cfg, io)
            k.dma_fence('y')
    return nc


def host_a0b_consts(par):
    t = np.arange(PT)
    tri01 = (t[None, :] <= t[:, None]).astype(np.float32)
    utri = (t[None, :] >= t[:, None]).astype(np.float32)
    blend = np.zeros((PT, 2), np.float32)
    blend[:, par] = 0.5
    return tri01.astype(BFNP), np.ascontiguousarray(utri), blend



def build_fused_program(n_own=16, n_tiles=32,
                        RG=((0, 1), (2, 3), (4, 5), (6, 7))):
    RG = [list(g) for g in RG]
    nc = bass.Bass("TRN2", target_bir_lowering=False, debug=False)
    HID = 5632

    def ein(name, shape, dt=F32):
        return nc.dram_tensor(name, list(shape), dt, kind="ExternalInput").ap()

    x_in0 = ein("x_shard", (n_own * PT, D))
    a0a_Wq = ein("L0aWq", (D, 2048))
    a0a_Wkv = ein("L0aWkv", (D, 1024))
    a0a_Wo = ein("L0aWo", (2048, D))
    f0_wqg = ein("F0wg", (HID, D), I32)
    f0_sg = ein("F0sg", (HID,))
    f0_wqu = ein("F0wu", (HID, D), I32)
    f0_su = ein("F0su", (HID,))
    f0_wqd = ein("F0wd", (D, HID), I32)
    f0_sd = ein("F0sd", (D,))
    a0b_Wq = ein("L0bWq", (D, 2048))
    a0b_Wk = ein("L0bWk", (D, 512))
    a0b_Wv = ein("L0bWv", (D, 512))
    a0b_Wo = ein("L0bWo", (2048, D))
    a1a_qkv = ein("L1aqkv", (3 * D, D))
    a1a_proj = ein("L1aproj", (D, D))
    a1a_qn = ein("L1aqn", (D,))
    a1a_nw = ein("L1anw", (D,))
    a1a_pn = ein("L1apn", (D,))
    a1a_s = ein("L1as", (16,))
    f1_wg = ein("F1wg", (D, 8))
    f1_wgu = ein("F1wgu", (8, D, D))
    f1_wd = ein("F1wd", (8, 1024, D))
    f1_wsg = ein("F1wsg", (D, 8192))
    f1_wsd = ein("F1wsd", (4096, D))
    a1b_Wq = ein("L1bWq", (D, 2048))
    a1b_Wk = ein("L1bWk", (D, 512))
    a1b_Wv = ein("L1bWv", (D, 512))
    a1b_Wo = ein("L1bWo", (2048, D))
    ident_in = ein("ident", (PT, PT), BF16)
    esel_in = ein("esel", (8, 8 * PT), BF16)
    t64c_in = ein("t64c", (PT, n_own * 64), BF16)
    t64s_in = ein("t64s", (PT, n_own * 64), BF16)
    t128co = ein("t128co", (PT, n_own * 128), BF16)
    t128so = ein("t128so", (PT, n_own * 128), BF16)
    t128ca = ein("t128ca", (PT, n_tiles * 128), BF16)
    t128sa = ein("t128sa", (PT, n_tiles * 128), BF16)
    dmask_in = ein("dmask", (PT, 2 * PT), F32)
    tri01_in = ein("tri01", (PT, PT), BF16)
    utri_in = ein("utri", (PT, PT), F32)
    blab_in = ein("blend_ab", (PT, 2), F32)
    blaa_in = ein("blend_aa", (PT, 2), F32)
    iota_in = ein("iota2", (PT, 2), F32)
    diff_in = ein("diff", (PT, PT), F32)
    up_in = ein("upmask", (PT, PT), F32)
    ones_in = ein("onesrow", (1, PT), F32)
    y_out = nc.dram_tensor("y_shard", [n_own * PT, D], F32,
                           kind="ExternalOutput").ap()

    def intn(name, shape, dt=BF16):
        return nc.dram_tensor(name, list(shape), dt, kind="Internal").ap()

    xA = intn("xA", (n_own * PT, D), F32)
    xB = intn("xB", (n_own * PT, D), F32)
    wgT = intn("wgT", (16, PT, HID))
    wuT = intn("wuT", (16, PT, HID))
    wdT = intn("wdT", (16, PT, HID))
    qkvT = intn("qkvT", (16, PT, 3 * D))
    projT = intn("projT", (16, PT, D))
    qT_a = intn("qT_a", (n_own, PT, 2048))
    ag_in_a = intn("ag_in_a", (n_own * PT, 1024))
    ag_out_a = intn("ag_out_a", (n_tiles * PT, 1024))
    qT_b = intn("qT_b", (n_own, PT, 2048))
    qhT_b = intn("qhT_b", (n_own, PT, 2048))
    khT_b = intn("khT_b", (n_own, PT, 512))
    vT_b = intn("vT_b", (n_own, PT, 512))
    qT_1a = intn("qT_1a", (n_own, PT, D))
    qpT_1a = intn("qpT_1a", (n_own, PT, D))
    kT_1a = intn("kT_1a", (n_own, PT, D))
    vT_1a = intn("vT_1a", (n_own, PT, D))
    ag_in_1a = intn("ag_in_1a", (n_own * PT, 2 * D))
    ag_out_1a = intn("ag_out_1a", (n_tiles * PT, 2 * D))
    a0aWq_b = intn("a0aWq_b", (D, 2048))
    a0aWkv_b = intn("a0aWkv_b", (D, 1024))
    a0aWo_b = intn("a0aWo_b", (2048, D))
    a0bWq_b = intn("a0bWq_b", (D, 2048))
    a0bWk_b = intn("a0bWk_b", (D, 512))
    a0bWv_b = intn("a0bWv_b", (D, 512))
    a0bWo_b = intn("a0bWo_b", (2048, D))
    a1bWq_b = intn("a1bWq_b", (D, 2048))
    a1bWk_b = intn("a1bWk_b", (D, 512))
    a1bWv_b = intn("a1bWv_b", (D, 512))
    a1bWo_b = intn("a1bWo_b", (2048, D))
    f1wgu_b = intn("f1wgu_b", (8, D, D))
    f1wd_b = intn("f1wd_b", (8, 1024, D))
    f1wsg_b = intn("f1wsg_b", (D, 8192))
    f1wsd_b = intn("f1wsd_b", (4096, D))

    with ExitStack() as ctx:
        k = K(nc, ctx)
        k.constcol(1.0)
        k.constcol(1e-5)
        k.constcol(1e-6)
        k.constcol(F32EPS)
        ident = k.sb("ident_sb", (PT, PT), BF16)
        identf = k.sb("identf_sb", (PT, PT), F32)
        esel = k.sb("esel_sb", (8, 8 * PT), BF16)
        dmask = k.sb("dmask_sb", (PT, 2 * PT), F32)
        tri01 = k.sb("tri01_sb", (PT, PT), BF16)
        utri = k.sb("utri_sb", (PT, PT), F32)
        onesrow = k.sb("ones_sb", (1, PT), F32)
        iota2 = k.sb("iota_sb", (PT, 2), F32)
        diff = k.sb("diff_sb", (PT, PT), F32)
        upm = k.sb("up_sb", (PT, PT), F32)
        pbanks = [k.ps(f"pb{i}", (PT, 512), F32) for i in range(6)]
        ptr = [k.ps(f"ptr{i}", (PT, 1024), BF16) for i in range(2)]
        block = ctx.enter_context(nc.Block())

        @block.sync
        def _(e):
            trp = TrPool(k, ident, ptr[0][:, :], ptr[1][:, :])
            tid = k.dma('y', 'c_id', ident[:, :], ident_in)
            k.dma('g', 'c_id2', identf[:, :], ident_in)
            k.dma('y', 'c_id', esel[0:8, :], esel_in)
            k.dma('y', 'c_id', dmask[:, :], dmask_in)
            k.dma('y', 'c_id', tri01[:, :], tri01_in)
            k.dma('y', 'c_id', utri[:, :], utri_in)
            k.dma('y', 'c_id', onesrow[0:1, :], ones_in)
            k.dma('y', 'c_id', iota2[:, :], iota_in)
            k.dma('y', 'c_id', diff[:, :], diff_in)
            tlast = k.dma('y', 'c_id', upm[:, :], up_in)
            tid2 = ('c_id2', k.cnt['c_id2'])
            for eng in ('t', 'v', 's', 'g'):
                k.wait(eng, tlast, tid2)

            emit_cast_prep(k, [
                (a0a_Wq, a0aWq_b), (a0a_Wkv, a0aWkv_b), (a0a_Wo, a0aWo_b),
                (a0b_Wq, a0bWq_b), (a0b_Wk, a0bWk_b), (a0b_Wv, a0bWv_b),
                (a0b_Wo, a0bWo_b),
                (a1b_Wq, a1bWq_b), (a1b_Wk, a1bWk_b), (a1b_Wv, a1bWv_b),
                (a1b_Wo, a1bWo_b),
                (f1_wgu.rearrange("e d f -> (e d) f"),
                 f1wgu_b.rearrange("e d f -> (e d) f")),
                (f1_wd.rearrange("e d f -> (e d) f"),
                 f1wd_b.rearrange("e d f -> (e d) f")),
                (f1_wsg, f1wsg_b), (f1_wsd, f1wsd_b)])
            emit_f0_prep(k, trp, f0_wqg, f0_sg, wgT)
            emit_f0_prep(k, trp, f0_wqu, f0_su, wuT)
            tw_f0 = emit_f0_prep(k, trp, f0_wqd, f0_sd, wdT, key_by_row=True)
            t1p = emit_ternary_prep(k, trp, a1a_qkv, qkvT, onesrow, tlast,
                                    pbanks[5])
            t2p = emit_ternary_prep(k, trp, a1a_proj, projT, onesrow, tlast,
                                    pbanks[5])
            k.barrier()

            base = dict(n_own=n_own, ident=ident, identf=identf, pbanks=pbanks,
                        trp=trp, replica_groups=RG)

            cfg = dict(base, n_tiles=n_tiles, own_tiles=list(range(n_own)),
                       nH=32, nKV=8, DH=64, scale=64 ** -0.5,
                       tabc_d=t64c_in, tabs_d=t64s_in,
                       dmask=dmask, dmask_tok=tlast, rope_k=True)
            io = dict(x_in=x_in0, x_out=xA, Wq=a0aWq_b[:, :],
                      Wk=a0aWkv_b[:, 0:512], Wv=a0aWkv_b[:, 512:1024],
                      Wo=a0aWo_b[:, :], qT=qT_a, ag_in=ag_in_a,
                      ag_out=ag_out_a)
            emit_attn(k, cfg, io)
            k.barrier()

            emit_f0(k, dict(base), dict(x_in=xA, x_out=xB, wgT=wgT, wuT=wuT,
                                        wdT=wdT, w_tok=tw_f0))
            k.barrier()

            cfg = dict(base, tabc_o_d=t128co, tabs_o_d=t128so,
                       tabc_a_d=t128ca, tabs_a_d=t128sa,
                       dmask=dmask, dmask_tok=tlast, tri01=tri01,
                       tri_tok=tlast, utri=utri, utri_tok=tlast)
            emit_a0b(k, cfg, dict(x_in=xB, x_out=xA, Wq=a0bWq_b[:, :],
                                  Wk=a0bWk_b[:, :],
                                  Wv=a0bWv_b[:, :], Wo=a0bWo_b[:, :],
                                  qT=qT_b, qhT=qhT_b,
                                  khT=khT_b, vT=vT_b, ag_in=ag_in_a[:, :],
                                  ag_out=ag_out_a[:, :], blend=blab_in))
            k.barrier()

            emit_f0(k, dict(base), dict(x_in=xA, x_out=xB, wgT=wgT, wuT=wuT,
                                        wdT=wdT))
            k.barrier()

            cfg = dict(base, onesrow=onesrow, onesrow_tok=tlast, iota=iota2,
                       iota_tok=tlast, diff=diff, diff_tok=tlast, upmask=upm)
            emit_a1a(k, cfg, dict(x_in=xB, x_out=xA, qkvT=qkvT, projT=projT,
                                  qT=qT_1a, qpT=qpT_1a, kT=kT_1a, vT=vT_1a,
                                  ag_in=ag_in_1a, ag_out=ag_out_1a, s=a1a_s,
                                  blend=blaa_in, qkv_norm_w=a1a_qn,
                                  norm_w=a1a_nw, proj_norm_w=a1a_pn,
                                  w_tok=tmax(*(t1p + t2p))))
            k.barrier()

            cfgf1 = dict(base, esel=esel, esel_tok=tlast)
            emit_f1(k, cfgf1, dict(x_in=xA, x_out=xB, w_gate=f1_wg,
                                   w_gate_up=f1wgu_b, w_down=f1wd_b,
                                   w_sh_gate_up=f1wsg_b, w_sh_down=f1wsd_b))
            k.barrier()

            cfg = dict(base, n_tiles=n_tiles, own_tiles=list(range(n_own)),
                       nH=16, nKV=4, DH=128, scale=128 ** -0.5,
                       tabc_d=t128co, tabs_d=t128so,
                       dmask=dmask, dmask_tok=tlast, rope_k=True)
            emit_attn(k, cfg, dict(x_in=xB, x_out=xA, Wq=a1bWq_b[:, :],
                                   Wk=a1bWk_b[:, :],
                                   Wv=a1bWv_b[:, :], Wo=a1bWo_b[:, :],
                                   qT=qT_b,
                                   ag_in=ag_in_a[:, :], ag_out=ag_out_a[:, :]))
            k.barrier()

            emit_f1(k, cfgf1, dict(x_in=xA, x_out=y_out, w_gate=f1_wg,
                                   w_gate_up=f1wgu_b, w_down=f1wd_b,
                                   w_sh_gate_up=f1wsg_b, w_sh_down=f1wsd_b))
            k.dma_fence('y')
    return nc


_PCACHE = {}


def _get_prog(key, builder):
    if key not in _PCACHE:
        _PCACHE[key] = builder()
    return _PCACHE[key]


def kernel_multilaunch(**inputs):
    n_own = 16
    n_tiles = 32
    RG = [[0, 1], [2, 3], [4, 5], [6, 7]]
    ident = np.eye(PT).astype(BFNP)
    x = np.asarray(inputs["L0_attn_a_x"], np.float32)

    rows_par = []
    for par in range(2):
        tiles = [2 * j + par for j in range(n_own)]
        rows_par.append(np.concatenate(
            [np.arange(t * PT, (t + 1) * PT) for t in tiles]))

    xs = [np.ascontiguousarray(x[c // 2][rows_par[c % 2]]) for c in range(8)]

    tab64 = [host_rope_tables(rows_par[par].astype(np.float64), 64)
             for par in range(2)]
    tab128_o = [host_rope_tables(rows_par[par].astype(np.float64), 128)
                for par in range(2)]
    tab128_a = host_rope_tables(np.arange(n_tiles * PT).astype(np.float64), 128)
    dmasks = [host_dmask(par) for par in range(2)]

    def run(nc, per_core):
        res = run_bass_kernel_spmd(nc, per_core, core_ids=list(range(8)))
        return [np.asarray(res.results[c]["y_shard"]) for c in range(8)]

    def attn_launch(kind):
        if kind == "a0a":
            nc = _get_prog("a0a", lambda: build_attn_program(
                n_own, n_tiles, 32, 8, 64, scale=64 ** -0.5,
                replica_groups=RG, combined_kv=True))
            maps = [dict(x_shard=xs[c], Wq=inputs["L0_attn_a_Wq"],
                         Wkv=inputs["L0_attn_a_Wkv"], Wo=inputs["L0_attn_a_Wo"],
                         tabc=tab64[c % 2][0], tabs=tab64[c % 2][1],
                         dmask=dmasks[c % 2], ident=ident) for c in range(8)]
        else:
            nc = _get_prog("a1b", lambda: build_attn_program(
                n_own, n_tiles, 16, 4, 128, scale=128 ** -0.5,
                replica_groups=RG))
            maps = [dict(x_shard=xs[c], Wq=inputs["L1_attn_b_Wq"],
                         Wk=inputs["L1_attn_b_Wk"], Wv=inputs["L1_attn_b_Wv"],
                         Wo=inputs["L1_attn_b_Wo"],
                         tabc=tab128_o[c % 2][0], tabs=tab128_o[c % 2][1],
                         dmask=dmasks[c % 2], ident=ident) for c in range(8)]
        return run(nc, maps)

    def f0_launch():
        nc = _get_prog("f0", lambda: build_f0_program(n_own=n_own))
        maps = [dict(x_shard=xs[c],
                     wq_gate=inputs["L0_ffn_wq_gate"],
                     s_gate=inputs["L0_ffn_s_gate"],
                     wq_up=inputs["L0_ffn_wq_up"], s_up=inputs["L0_ffn_s_up"],
                     wq_down=inputs["L0_ffn_wq_down"],
                     s_down=inputs["L0_ffn_s_down"], ident=ident)
                for c in range(8)]
        return run(nc, maps)

    def a0b_launch():
        nc = _get_prog("a0b", lambda: build_a0b_program(n_own, n_tiles, RG))
        maps = []
        for c in range(8):
            par = c % 2
            tri01, utri, blend = host_a0b_consts(par)
            maps.append(dict(
                x_shard=xs[c], Wq=inputs["L0_attn_b_Wq"],
                Wk=inputs["L0_attn_b_Wk"], Wv=inputs["L0_attn_b_Wv"],
                Wo=inputs["L0_attn_b_Wo"],
                tabc_o=tab128_o[par][0], tabs_o=tab128_o[par][1],
                tabc_a=tab128_a[0], tabs_a=tab128_a[1],
                dmask=dmasks[par], tri01=tri01, utri=utri, blend=blend,
                ident=ident))
        return run(nc, maps)

    def a1a_launch():
        nc = _get_prog("a1a", lambda: build_a1a_program(n_own, n_tiles, RG))
        maps = []
        for c in range(8):
            par = c % 2
            iota2, diff, upm, blend, ones = host_a1a_consts(par)
            maps.append(dict(
                x_shard=xs[c], qkv_w=inputs["L1_attn_a_qkv_w"],
                proj_w=inputs["L1_attn_a_proj_w"],
                qkv_norm_w=inputs["L1_attn_a_qkv_norm_w"],
                norm_w=inputs["L1_attn_a_norm_w"],
                proj_norm_w=inputs["L1_attn_a_proj_norm_w"],
                s=inputs["L1_attn_a_s"], ident=ident, onesrow=ones,
                blend=blend, iota2=iota2, diff=diff, upmask=upm))
        return run(nc, maps)

    def f1_launch():
        nc = _get_prog("f1", lambda: build_f1_program(n_own=n_own))
        maps = [dict(x_shard=xs[c], w_gate=inputs["L1_ffn_w_gate"],
                     w_gate_up=inputs["L1_ffn_w_gate_up"],
                     w_down=inputs["L1_ffn_w_down"],
                     w_sh_gate_up=inputs["L1_ffn_w_sh_gate_up"],
                     w_sh_down=inputs["L1_ffn_w_sh_down"], ident=ident,
                     esel=host_esel()) for c in range(8)]
        return run(nc, maps)

    seq = [lambda: attn_launch("a0a"), f0_launch, a0b_launch, f0_launch,
           a1a_launch, f1_launch, lambda: attn_launch("a1b"), f1_launch]
    for step in seq:
        xs = step()

    out = np.empty((4, 4096, D), np.float32)
    for c in range(8):
        out[c // 2][rows_par[c % 2]] = xs[c]
    return out


def kernel(**inputs):
    n_own, n_tiles = 16, 32
    ident = np.eye(PT).astype(BFNP)
    x = np.asarray(inputs["L0_attn_a_x"], np.float32)
    rows_par = []
    for par in range(2):
        tiles = [2 * j + par for j in range(n_own)]
        rows_par.append(np.concatenate(
            [np.arange(t * PT, (t + 1) * PT) for t in tiles]))
    nc = _get_prog("fused", lambda: build_fused_program(n_own, n_tiles))
    tab128_a = host_rope_tables(np.arange(n_tiles * PT).astype(np.float64), 128)
    in_maps = []
    for c in range(8):
        par = c % 2
        rows = rows_par[par]
        t64 = host_rope_tables(rows.astype(np.float64), 64)
        t128o = host_rope_tables(rows.astype(np.float64), 128)
        tri01, utri, blend_ab = host_a0b_consts(par)
        iota2, diff, upm, blend_aa, ones = host_a1a_consts(par)
        in_maps.append(dict(
            x_shard=np.ascontiguousarray(x[c // 2][rows]),
            L0aWq=inputs["L0_attn_a_Wq"], L0aWkv=inputs["L0_attn_a_Wkv"],
            L0aWo=inputs["L0_attn_a_Wo"],
            F0wg=inputs["L0_ffn_wq_gate"], F0sg=inputs["L0_ffn_s_gate"],
            F0wu=inputs["L0_ffn_wq_up"], F0su=inputs["L0_ffn_s_up"],
            F0wd=inputs["L0_ffn_wq_down"], F0sd=inputs["L0_ffn_s_down"],
            L0bWq=inputs["L0_attn_b_Wq"], L0bWk=inputs["L0_attn_b_Wk"],
            L0bWv=inputs["L0_attn_b_Wv"], L0bWo=inputs["L0_attn_b_Wo"],
            L1aqkv=inputs["L1_attn_a_qkv_w"], L1aproj=inputs["L1_attn_a_proj_w"],
            L1aqn=inputs["L1_attn_a_qkv_norm_w"], L1anw=inputs["L1_attn_a_norm_w"],
            L1apn=inputs["L1_attn_a_proj_norm_w"], L1as=inputs["L1_attn_a_s"],
            F1wg=inputs["L1_ffn_w_gate"], F1wgu=inputs["L1_ffn_w_gate_up"],
            F1wd=inputs["L1_ffn_w_down"], F1wsg=inputs["L1_ffn_w_sh_gate_up"],
            F1wsd=inputs["L1_ffn_w_sh_down"],
            L1bWq=inputs["L1_attn_b_Wq"], L1bWk=inputs["L1_attn_b_Wk"],
            L1bWv=inputs["L1_attn_b_Wv"], L1bWo=inputs["L1_attn_b_Wo"],
            ident=ident, esel=host_esel(),
            t64c=t64[0], t64s=t64[1], t128co=t128o[0], t128so=t128o[1],
            t128ca=tab128_a[0], t128sa=tab128_a[1],
            dmask=host_dmask(par), tri01=tri01, utri=utri,
            blend_ab=blend_ab, blend_aa=blend_aa, iota2=iota2, diff=diff,
            upmask=upm, onesrow=ones))
    res = run_bass_kernel_spmd(nc, in_maps, core_ids=list(range(8)))
    out = np.empty((4, 4096, D), np.float32)
    for c in range(8):
        out[c // 2][rows_par[c % 2]] = np.asarray(res.results[c]["y_shard"])
    return out
